# version 15
# baseline (speedup 1.0000x reference)
import sys
sys.path.insert(0, "/opt/trn_rl_repo")
import math
from contextlib import ExitStack
import numpy as np
import concourse.bass as bass
import concourse.mybir as mybir
from concourse.bass_utils import run_bass_kernel_spmd

F32 = mybir.dt.float32
BF16 = mybir.dt.bfloat16
AF = mybir.ActivationFunctionType
ALU = mybir.AluOpType
AX = mybir.AxisListType

DEBUG_L2 = False
B, S, D = 4, 4096, 1024
NTOK = B * S
RPC = NTOK // 8
NCH = D // 128
NTT = RPC // 512
EPS = 1e-5
HEADS, DH, NNK = 16, 64, 2

ENGINES = ("tensor", "vector", "scalar", "gpsimd", "sync")


class DrainingEngine:
    _PASS = {"wait_ge", "drain", "engine_nop", "nop"}

    def __init__(self, eng):
        self._e = eng
        self._n = 0
        self._last_dma = False

    def __getattr__(self, k):
        f = getattr(self._e, k)
        if k in DrainingEngine._PASS or not callable(f):
            return f
        is_dma = k == "dma_start"

        def wrapped(*a, **kw):
            if self._n and not (is_dma and self._last_dma):
                self._e.drain()
            self._n += 1
            self._last_dma = is_dma
            return f(*a, **kw)
        return wrapped


class Op:
    __slots__ = ("engine", "fn", "deps", "ndma", "count_after", "name", "skey",
                 "nodrain")

    def __init__(self, engine, fn, deps, ndma, name, skey, nodrain=False):
        self.engine = engine
        self.fn = fn
        self.deps = deps
        self.ndma = ndma
        self.count_after = None
        self.name = name
        self.skey = skey
        self.nodrain = nodrain


class Sched:

    def __init__(self, nc):
        self.nc = nc
        self.ops = []

    def add(self, engine, fn, deps=(), name="", nodrain=False):
        op = Op(engine, fn, [d for d in deps if d is not None], 0, name, engine,
                nodrain)
        self.ops.append(op)
        return op

    def dma(self, engine, fn, deps=(), ndma=1, name="", stream="main"):
        op = Op(engine, fn, [d for d in deps if d is not None], ndma, name,
                "dma_" + stream)
        self.ops.append(op)
        return op

    def emit(self, block):
        skeys = []
        for op in self.ops:
            if op.skey not in skeys:
                skeys.append(op.skey)
        sems = {}
        for k in skeys:
            h = self.nc.semaphore(f"sem_{k}")
            sems[k] = h.__enter__()
        cnt = {k: 0 for k in skeys}
        for op in self.ops:
            cnt[op.skey] += 16 * op.ndma if op.ndma else 1
            op.count_after = cnt[op.skey]
        for e in ENGINES:
            ops_e = [o for o in self.ops if o.engine == e]
            if not ops_e:
                continue
            sec = getattr(block, e)

            def make(ops_e=ops_e, e=e):
                def body(eng):
                    if e in ("vector", "scalar", "gpsimd"):
                        eng = DrainingEngine(eng)
                    done = {}
                    for op in ops_e:
                        waits = {}
                        for d in op.deps:
                            if d.engine == e and d.ndma == 0 and op.ndma == 0:
                                continue
                            if d.engine == e and d.ndma and op.ndma == 0:
                                pass
                            waits[d.skey] = max(waits.get(d.skey, 0), d.count_after)
                        for dk, c in waits.items():
                            if c > done.get(dk, -1):
                                eng.wait_ge(sems[dk], c)
                                done[dk] = c
                        if op.nodrain and hasattr(eng, "_n"):
                            eng._n = 0
                        if op.ndma:
                            n = [0]

                            def put(instr, n=n, op=op):
                                instr.then_inc(sems[op.skey], 16)
                                n[0] += 1
                            op.fn(eng, put)
                            assert n[0] == op.ndma, (op.name, n[0], op.ndma)
                        else:
                            last = op.fn(eng)
                            assert last is not None, f"op {op.name} returned None"
                            last.then_inc(sems[op.skey], 1)
                return body
            sec(make())
        return sems


def tile_lhsT(wT):
    K, M = wT.shape
    return np.ascontiguousarray(wT.reshape(K, M // 128, 128).transpose(1, 0, 2))


def f32c(a):
    return np.ascontiguousarray(np.asarray(a, dtype=np.float32))


def alibi_slopes():
    start = 2.0 ** (-2.0 ** (-(math.log2(HEADS) - 3.0)))
    return np.array([start * start ** i for i in range(HEADS)], dtype=np.float64)


def emit_stats(sd, name, src_tiles, row, consts, psS0, psS1, sqbuf,
               deps, n_tt, width=512, onescol_key="onescol",
               psS0b=None, psS1b=None):
    W = width
    evacs = []
    prev_v = None
    prev_mmb = None
    for tt in range(n_tt):
        sl = slice(tt * W, (tt + 1) * W)
        if psS0b is not None:
            pS0, pS1 = (psS0, psS1) if tt % 2 == 0 else (psS0b, psS1b)
            pair_ev = evacs[tt - 2] if tt >= 2 else None
            sq_dep = prev_mmb
        else:
            pS0, pS1 = psS0, psS1
            pair_ev = prev_v
            sq_dep = prev_v

        def f_sqa(s, sl=sl):
            last = None
            for kc in range(4):
                last = s.activation(sqbuf[:, kc * W:(kc + 1) * W],
                                    src_tiles[kc][:, sl], AF.Square)
            return last
        op_sqa = sd.add("scalar", f_sqa,
                        deps=deps + [x for x in (sq_dep,) if x],
                        name=f"{name}_sqa{tt}")

        def f_mma(t, sl=sl, pS0=pS0, pS1=pS1):
            for kc in range(NCH):
                t.matmul(pS0[0:1, :W], consts[onescol_key], src_tiles[kc][:, sl],
                         start=(kc == 0), stop=(kc == NCH - 1))
            last = None
            for kc in range(4):
                last = t.matmul(pS1[0:1, :W], consts["onescol"],
                                sqbuf[:, kc * W:(kc + 1) * W],
                                start=(kc == 0), stop=False)
            return last
        op_mma = sd.add("tensor", f_mma,
                        deps=deps + [op_sqa] + [x for x in (pair_ev,) if x],
                        name=f"{name}_mma{tt}")

        def f_sqb(s, sl=sl):
            last = None
            for kc in range(4):
                last = s.activation(sqbuf[:, kc * W:(kc + 1) * W],
                                    src_tiles[4 + kc][:, sl], AF.Square)
            return last
        op_sqb = sd.add("scalar", f_sqb, deps=[op_mma], name=f"{name}_sqb{tt}")

        def f_mmb(t, sl=sl, pS1=pS1):
            last = None
            for kc in range(4):
                last = t.matmul(pS1[0:1, :W], consts["onescol"],
                                sqbuf[:, kc * W:(kc + 1) * W],
                                start=False, stop=(kc == 3))
            return last
        op_mmb = sd.add("tensor", f_mmb, deps=[op_sqb], name=f"{name}_mmb{tt}")
        prev_mmb = op_mmb

        def f_ev(v, sl=sl, pS0=pS0, pS1=pS1):
            v.tensor_copy(row(0)[:, sl], pS0[0:1, :W])
            return v.tensor_copy(row(2)[:, sl], pS1[0:1, :W])
        prev_v = sd.add("vector", f_ev, deps=[op_mmb], name=f"{name}_ev{tt}")
        evacs.append(prev_v)

    def f_r1(v):
        v.tensor_scalar(row(0), row(0), 1.0 / D, None, ALU.mult)
        v.tensor_scalar(row(2), row(2), 1.0 / D, None, ALU.mult)
        v.tensor_tensor(row(1), row(0), row(0), ALU.mult)
        return v.tensor_tensor(row(2), row(2), row(1), ALU.subtract)
    op_r1 = sd.add("vector", f_r1, deps=evacs, name=f"{name}_r1")

    def f_r2(s):
        return s.activation(row(2), row(2), AF.Sqrt, bias=consts["eps1"])
    op_r2 = sd.add("scalar", f_r2, deps=[op_r1], name=f"{name}_r2")

    def f_r3(v):
        return v.reciprocal(row(1), row(2))
    return sd.add("vector", f_r3, deps=[op_r2], name=f"{name}_r3")


def emit_apply(sd, name, src_tiles, dst_tiles, row, consts,
               psM, psR, scr, deps, n_tt, residual_tiles=None, width=512,
               psMb=None, psRb=None):
    W = width
    prev = None
    last_op = None
    aps = []
    for tt in range(n_tt):
        sl = slice(tt * W, (tt + 1) * W)
        if psMb is not None:
            pM, pR = (psM, psR) if tt % 2 == 0 else (psMb, psRb)
            pair_ap = aps[tt - 2] if tt >= 2 else None
        else:
            pM, pR = psM, psR
            pair_ap = prev

        def f_bc(t, sl=sl, pM=pM, pR=pR):
            t.matmul(pM[:, :W], consts["ones1"], row(0)[:, sl], start=True, stop=True)
            return t.matmul(pR[:, :W], consts["ones1"], row(1)[:, sl], start=True, stop=True)
        op_bc = sd.add("tensor", f_bc,
                       deps=deps + [x for x in (pair_ap,) if x],
                       name=f"{name}_bc{tt}")

        def f_ap(v, sl=sl, pM=pM, pR=pR):
            last = None
            for kc in range(NCH):
                v.tensor_tensor(scr[:, :W], src_tiles[kc][:, sl], pM[:, :W], ALU.subtract)
                if residual_tiles is None:
                    last = v.tensor_tensor(dst_tiles[kc][:, sl], scr[:, :W],
                                           pR[:, :W], ALU.mult)
                else:
                    v.tensor_tensor(scr[:, :W], scr[:, :W], pR[:, :W], ALU.mult)
                    last = v.tensor_tensor(dst_tiles[kc][:, sl],
                                           residual_tiles[kc][:, sl],
                                           scr[:, :W], ALU.add)
            return last
        prev = sd.add("vector", f_ap, deps=[op_bc] + deps, name=f"{name}_ap{tt}")
        aps.append(prev)
        last_op = prev
    return last_op


def build_l0_ffn():
    nc = bass.Bass("TRN2", target_bir_lowering=False)
    x0s = nc.dram_tensor("x0s", [D, RPC], F32, kind="ExternalInput").ap()
    oa = nc.dram_tensor("oa", [D, RPC], F32, kind="ExternalInput").ap()
    ob = nc.dram_tensor("ob", [D, RPC], F32, kind="ExternalInput").ap()
    x1sc = nc.dram_tensor("x1sc", [D, RPC], F32, kind="Internal").ap()
    bw_t = nc.dram_tensor("bw_t", [8, D, 128], F32, kind="ExternalInput").ap()
    bias_c = nc.dram_tensor("bias_c", [128, 8], F32, kind="ExternalInput").ap()
    A_t = nc.dram_tensor("A_t", [D, 128], F32, kind="ExternalInput").ap()
    B_t = nc.dram_tensor("B_t", [8, 128, 128], F32, kind="ExternalInput").ap()
    rwt = nc.dram_tensor("rwt", [D, 8], F32, kind="ExternalInput").ap()
    rbias = nc.dram_tensor("rbias", [128, 8], F32, kind="ExternalInput").ap()
    Gm = nc.dram_tensor("Gm", [8, 128], F32, kind="ExternalInput").ap()
    ones1 = nc.dram_tensor("ones1", [1, 128], F32, kind="ExternalInput").ap()
    ones65 = nc.dram_tensor("ones65", [65, 128], F32, kind="ExternalInput").ap()
    cst = nc.dram_tensor("cst", [128, 4], F32, kind="ExternalInput").ap()
    onescol = nc.dram_tensor("onescol", [128, 1], F32, kind="ExternalInput").ap()
    ident = nc.dram_tensor("ident", [128, 128], F32, kind="ExternalInput").ap()
    x2f = nc.dram_tensor("x2f", [D, RPC], F32, kind="ExternalOutput").ap()
    if DEBUG_L2:
        dbg = nc.dram_tensor("dbg", [128, (RPC // 128) * 32], F32, kind="ExternalOutput").ap()
    stat3 = nc.dram_tensor("stat3", [2, RPC], F32, kind="ExternalOutput").ap()

    with ExitStack() as ST:
        E = ST.enter_context
        xs = E(nc.sbuf_tensor("xs", [128, NCH * RPC], F32))
        u2 = E(nc.sbuf_tensor("u2", [128, NCH * RPC], BF16))
        bw = E(nc.sbuf_tensor("bw", [128, 8 * NCH * 128], BF16))
        As = E(nc.sbuf_tensor("As", [128, NCH * 128], BF16))
        Bs = E(nc.sbuf_tensor("Bs", [128, 8 * 128], BF16))
        rw = E(nc.sbuf_tensor("rw", [128, NCH * 8], BF16))
        sax = E(nc.sbuf_tensor("sax", [128, RPC], BF16))
        wTf = E(nc.sbuf_tensor("wTf", [8, RPC], F32))
        sq = E(nc.sbuf_tensor("sq", [128, 4 * 512], F32))
        scr = E(nc.sbuf_tensor("scr", [128, 512], F32))
        wa = E(nc.sbuf_tensor("wa", [128, 2 * 512], F32))
        rows = E(nc.sbuf_tensor("rows", [1, 3 * RPC], F32))
        bias_s = E(nc.sbuf_tensor("bias_s", [128, 8], F32))
        ones1_s = E(nc.sbuf_tensor("ones1_s", [1, 128], F32))
        ones65_s = E(nc.sbuf_tensor("ones65_s", [65, 128], F32))
        cst_s = E(nc.sbuf_tensor("cst_s", [128, 4], F32))
        onescol_s = E(nc.sbuf_tensor("onescol_s", [128, 1], F32))
        ident_s = E(nc.sbuf_tensor("ident_s", [128, 128], F32))
        G_s = E(nc.sbuf_tensor("G_s", [8, 128], F32))
        rbias_s = E(nc.sbuf_tensor("rbias_s", [128, 8], F32))
        lgs = E(nc.sbuf_tensor("lgs", [128, 2 * 8], F32))
        tp = E(nc.sbuf_tensor("tp", [128, 2 * 16], F32))
        wsb = E(nc.sbuf_tensor("wsb", [128, 2 * 8], F32))
        dbg_s = E(nc.sbuf_tensor("dbg_s", [128, (RPC // 128) * 32], F32)) if DEBUG_L2 else None
        xrl = E(nc.sbuf_tensor("xrl", [128, 8 * 512], F32))
        P0 = E(nc.psum_tensor("P0", [128, 512], F32))
        P1 = E(nc.psum_tensor("P1", [128, 512], F32))
        P2 = E(nc.psum_tensor("P2", [128, 512], F32))
        P3 = E(nc.psum_tensor("P3", [128, 512], F32))
        P4 = E(nc.psum_tensor("P4", [128, 512], F32))
        P5 = E(nc.psum_tensor("P5", [128, 512], F32))
        S0 = E(nc.psum_tensor("S0", [1, 512], F32))
        S1 = E(nc.psum_tensor("S1", [1, 512], F32))
        block = E(nc.Block())
        sd = Sched(nc)
        consts = {"ones1": ones1_s[:], "onescol": onescol_s[:],
                  "eps1": cst_s[0:1, 0:1], "epsc": cst_s[:, 0:1],
                  "onesat": lambda i: ones65_s[32 * i:32 * i + 1, :]}
        xs_t = [xs[:, ch * RPC:(ch + 1) * RPC] for ch in range(NCH)]
        u2_t = [u2[:, ch * RPC:(ch + 1) * RPC] for ch in range(NCH)]
        f0_t = xs_t
        bw_s = [[bw[:, (oc * NCH + kc) * 128:(oc * NCH + kc + 1) * 128]
                 for kc in range(NCH)] for oc in range(8)]
        As_t = [As[:, kc * 128:(kc + 1) * 128] for kc in range(NCH)]
        Bs_t = [Bs[:, oc * 128:(oc + 1) * 128] for oc in range(8)]
        rw_t = [rw[:, kc * 8:(kc + 1) * 8] for kc in range(NCH)]

        def ld_x(sync, put):
            for ch in range(NCH):
                put(sync.dma_start(out=xs_t[ch], in_=oa[ch * 128:(ch + 1) * 128, :]))
            put(sync.dma_start(out=bias_s[:], in_=bias_c[:]))
            put(sync.dma_start(out=ones1_s[:], in_=ones1[:]))
            put(sync.dma_start(out=ones65_s[:], in_=ones65[:]))
            put(sync.dma_start(out=cst_s[:], in_=cst[:]))
            put(sync.dma_start(out=onescol_s[:], in_=onescol[:]))
            put(sync.dma_start(out=ident_s[:], in_=ident[:]))
            put(sync.dma_start(out=G_s[:], in_=Gm[:]))
            put(sync.dma_start(out=rbias_s[:], in_=rbias[:]))
        op_ldx = sd.dma("sync", ld_x, ndma=NCH + 8, name="ld_x", stream="ldx")

        def ld_w(g, put):
            for oc in range(8):
                put(g.dma_start(
                    out=bw[:, oc * NCH * 128:(oc + 1) * NCH * 128]
                        .rearrange("p (kc o) -> p kc o", kc=NCH),
                    in_=bw_t[oc].rearrange("(kc p) o -> p kc o", p=128)))
            put(g.dma_start(
                out=As[:].rearrange("p (kc r) -> p kc r", kc=NCH),
                in_=A_t.rearrange("(kc p) r -> p kc r", p=128)))
            put(g.dma_start(
                out=Bs[:].rearrange("p (oc o) -> p oc o", oc=8),
                in_=B_t.rearrange("oc p o -> p oc o")))
            put(g.dma_start(
                out=rw[:].rearrange("p (kc e) -> p kc e", kc=NCH),
                in_=rwt.rearrange("(kc p) e -> p kc e", p=128)))
        op_ldw = sd.dma("gpsimd", ld_w, ndma=11, name="ld_w", stream="ldw")

        if DEBUG_L2:
            def f_z(v):
                return v.memset(dbg_s[:], 0.0)
            sd.add("vector", f_z, name="dbg_z")

        def rowf(i):
            return rows[0:1, i * RPC:(i + 1) * RPC]

        osum_ops = []
        prev_os = None
        for tt in range(NTT):
            sl = slice(tt * 512, (tt + 1) * 512)

            def f_obl(sync, put, sl=sl):
                for kc in range(NCH):
                    put(sync.dma_start(out=xrl[:, kc * 512:(kc + 1) * 512],
                                       in_=ob[kc * 128:(kc + 1) * 128, sl]))
            op_obl = sd.dma("sync", f_obl, ndma=NCH, deps=[prev_os],
                            name=f"obl{tt}", stream="rl")

            def f_osum(v, sl=sl):
                last = None
                for kc in range(NCH):
                    last = v.tensor_tensor(xs_t[kc][:, sl], xs_t[kc][:, sl],
                                           xrl[:, kc * 512:(kc + 1) * 512], ALU.add)
                return last
            prev_os = sd.add("vector", f_osum, deps=[op_obl, op_ldx],
                             name=f"osum{tt}")
            osum_ops.append(prev_os)

        op_sto = emit_stats(sd, "sto", xs_t, rowf, consts, S0, S1, sq,
                            osum_ops, NTT)
        prev_x1 = None
        x1_ops = []
        for tt in range(NTT):
            sl = slice(tt * 512, (tt + 1) * 512)

            def f_x0l(sync, put, sl=sl):
                for kc in range(NCH):
                    put(sync.dma_start(out=xrl[:, kc * 512:(kc + 1) * 512],
                                       in_=x0s[kc * 128:(kc + 1) * 128, sl]))
            op_x0l = sd.dma("sync", f_x0l, ndma=NCH, deps=[op_sto, prev_x1],
                            name=f"x0l{tt}", stream="rl")

            def f_x1bc(t, sl=sl):
                t.matmul(P0[:], consts["ones1"], rowf(0)[:, sl], start=True, stop=True)
                return t.matmul(P1[:], consts["ones1"], rowf(1)[:, sl], start=True, stop=True)
            op_x1bc = sd.add("tensor", f_x1bc, deps=[op_sto, prev_x1],
                             name=f"x1bc{tt}")

            def f_x1(v, sl=sl):
                last = None
                for kc in range(NCH):
                    v.tensor_tensor(scr[:], xs_t[kc][:, sl], P0[:], ALU.subtract)
                    v.tensor_tensor(scr[:], scr[:], P1[:], ALU.mult)
                    last = v.tensor_tensor(xs_t[kc][:, sl],
                                           xrl[:, kc * 512:(kc + 1) * 512],
                                           scr[:], ALU.add)
                return last
            prev_x1 = sd.add("vector", f_x1, deps=[op_x0l, op_x1bc], name=f"x1_{tt}")
            x1_ops.append(prev_x1)

        def f_x1sp(sync, put):
            for ch in range(NCH):
                put(sync.dma_start(out=x1sc[ch * 128:(ch + 1) * 128, :], in_=xs_t[ch]))
        op_x1sp = sd.dma("sync", f_x1sp, ndma=NCH, deps=x1_ops, name="x1sp",
                         stream="sp")

        op_st1 = emit_stats(sd, "st1", xs_t, rowf, consts, S0, S1, sq,
                            x1_ops, NTT)
        op_u2 = emit_apply(sd, "u2", xs_t, u2_t, rowf, consts, P0, P1, scr,
                           [op_st1], NTT)

        NST = RPC // 128
        wT_ops = []
        prev = {}
        for st in range(NST):
            par = st % 2
            sl = slice(st * 128, (st + 1) * 128)
            lg_ps = P2[:, par * 8:par * 8 + 8]
            lg = lgs[:, par * 8:par * 8 + 8]
            t0 = tp[:, par * 16:par * 16 + 16]
            w_sb = wsb[:, par * 8:par * 8 + 8]

            def f_lg(t, sl=sl, lg_ps=lg_ps):
                last = None
                for kc in range(NCH):
                    last = t.matmul(lg_ps, u2_t[kc][:, sl], rw_t[kc],
                                    start=(kc == 0), stop=(kc == NCH - 1))
                return last
            op_lg = sd.add("tensor", f_lg,
                           deps=[op_u2, op_ldw, prev.get(("va", par))],
                           name=f"rt_lg{st}")

            def f_va(v, lg=lg, t0=t0, lg_ps=lg_ps):
                v.tensor_tensor(lg, lg_ps, rbias_s[:], ALU.add)
                v.tensor_reduce(t0[:, 0:1], lg, axis=AX.X, op=ALU.max)
                v.tensor_scalar(t0[:, 1:2], t0[:, 0:1], -1.0, None, ALU.mult)
                v.tensor_scalar(t0[:, 8:16], lg, t0[:, 0:1], None, ALU.is_equal)
                v.scalar_tensor_tensor(t0[:, 8:16], t0[:, 8:16], -1e30, lg,
                                       ALU.mult, ALU.add)
                return v.tensor_reduce(t0[:, 2:3], t0[:, 8:16], axis=AX.X, op=ALU.max)
            op_va = sd.add("vector", f_va, deps=[op_lg, op_ldx,
                                                 prev.get(("sc", par)),
                                                 prev.get(("dbg", par)),
                                                 prev.get(("tr", par))],
                           name=f"rt_va{st}")

            def f_sc(s, lg=lg, t0=t0):
                return s.activation(t0[:, 8:16], lg, AF.Exp, bias=t0[:, 1:2])
            op_sc = sd.add("scalar", f_sc, deps=[op_va], name=f"rt_sc{st}")

            def f_vb2(v, lg=lg, t0=t0, w_sb=w_sb):
                v.tensor_scalar(w_sb, lg, t0[:, 2:3], None, ALU.is_ge)
                v.tensor_tensor(w_sb, w_sb, t0[:, 8:16], ALU.mult)
                v.tensor_reduce(t0[:, 5:6], w_sb, axis=AX.X, op=ALU.add)
                v.reciprocal(t0[:, 6:7], t0[:, 5:6])
                return v.tensor_scalar(w_sb, w_sb, t0[:, 6:7], 2.0,
                                       ALU.mult, ALU.mult)
            op_vb = sd.add("vector", f_vb2, deps=[op_sc], name=f"rt_vb{st}")

            tr_ps = P3[0:8, par * 128:par * 128 + 128]

            def f_tr(t, w_sb=w_sb, tr_ps=tr_ps):
                return t.transpose(tr_ps, w_sb, ident_s[:])
            op_tr = sd.add("tensor", f_tr, deps=[op_vb, prev.get(("te", par))],
                           name=f"rt_tr{st}")

            def f_te(v, st=st, tr_ps=tr_ps):
                return v.tensor_copy(wTf[:, st * 128:(st + 1) * 128], tr_ps)
            op_te = sd.add("vector", f_te, deps=[op_tr], name=f"rt_te{st}")
            if DEBUG_L2:
                def f_dbg(v, st=st, lg=lg, t0=t0, w_sb=w_sb):
                    base = st * 32
                    v.tensor_copy(dbg_s[:, base:base + 8], lg)
                    v.tensor_copy(dbg_s[:, base + 8:base + 11], t0[:, 0:3])
                    v.tensor_copy(dbg_s[:, base + 13:base + 15], t0[:, 5:7])
                    v.tensor_copy(dbg_s[:, base + 16:base + 24], t0[:, 8:16])
                    return v.tensor_copy(dbg_s[:, base + 24:base + 32], w_sb)
                op_dbg = sd.add("vector", f_dbg, deps=[op_vb], name=f"rt_dbg{st}")
                prev[("dbg", par)] = op_dbg
                wT_ops.append(op_dbg)
            wT_ops.append(op_te)
            prev[("va", par)] = op_va
            prev[("sc", par)] = op_sc
            prev[("tr", par)] = op_tr
            prev[("te", par)] = op_te

        sax_ops = []
        prev_ax = {}
        for tt in range(NTT):
            par = tt % 2
            sl = slice(tt * 512, (tt + 1) * 512)
            ax_ps = (P4 if par == 0 else P5)

            def f_ax(t, sl=sl, ax_ps=ax_ps):
                last = None
                for kc in range(NCH):
                    last = t.matmul(ax_ps[:], As_t[kc], u2_t[kc][:, sl],
                                    start=(kc == 0), stop=(kc == NCH - 1))
                return last
            op_ax = sd.add("tensor", f_ax,
                           deps=[op_u2, op_ldw, prev_ax.get(("sax", par))],
                           name=f"ax{tt}")

            wa_ps = (P2 if par == 0 else P3)

            def f_wabc(t, sl=sl, wa_ps=wa_ps):
                return t.matmul(wa_ps[:], G_s[:], wTf[:, sl], start=True, stop=True)
            op_wabc = sd.add("tensor", f_wabc,
                             deps=wT_ops + [prev_ax.get(("wae", par))],
                             name=f"wabc{tt}")

            def f_wae(s, wa_ps=wa_ps, par=par):
                return s.copy(wa[:, par * 512:par * 512 + 512], wa_ps[:])
            op_wae = sd.add("scalar", f_wae, deps=[op_wabc], name=f"wae{tt}")

            def f_sax(v, sl=sl, ax_ps=ax_ps, par=par):
                return v.tensor_tensor(sax[:, sl], ax_ps[:],
                                       wa[:, par * 512:par * 512 + 512], ALU.mult)
            op_sax = sd.add("vector", f_sax, deps=[op_ax, op_wae], name=f"sax{tt}")
            sax_ops.append(op_sax)
            prev_ax[("sax", par)] = op_sax
            prev_ax[("wae", par)] = op_wae

        res_ops = []
        prev_res = {}
        for oc in range(8):
            for tt in range(NTT):
                par = (oc * NTT + tt) % 2
                sl = slice(tt * 512, (tt + 1) * 512)
                r_ps = (P0 if par == 0 else P1)

                def f_mm(t, oc=oc, sl=sl, r_ps=r_ps, tt=tt):
                    for kc in range(NCH):
                        t.matmul(r_ps[:], bw_s[oc][kc], u2_t[kc][:, sl],
                                 start=(kc == 0), stop=False)
                    return t.matmul(r_ps[:], Bs_t[oc], sax[:, sl],
                                    start=False, stop=True)
                op_mm = sd.add("tensor", f_mm,
                               deps=[op_u2, op_ldw, sax_ops[tt],
                                     prev_res.get(par)],
                               name=f"res{oc}_{tt}")

                def f_ev(s, oc=oc, sl=sl, r_ps=r_ps):
                    return s.add(f0_t[oc][:, sl], r_ps[:], bias_s[:, oc:oc + 1])
                op_ev = sd.add("scalar", f_ev, deps=[op_mm], name=f"rese{oc}_{tt}")
                res_ops.append(op_ev)
                prev_res[par] = op_ev

        op_st2 = emit_stats(sd, "st2", f0_t, rowf, consts, S0, S1, sq,
                            res_ops, NTT)
        rowsl = rowf
        prev_ap = None
        rl_ops = {}
        for tt in range(NTT):
            sl = slice(tt * 512, (tt + 1) * 512)
            for kc in range(NCH):
                def f_rl(sync, put, kc=kc, sl=sl):
                    put(sync.dma_start(out=xrl[:, kc * 512:kc * 512 + 512],
                                       in_=x1sc[kc * 128:(kc + 1) * 128, sl]))
                rl_ops[(tt, kc)] = sd.dma(
                    "sync", f_rl, ndma=1,
                    deps=[op_x1sp] + ([prev_ap] if prev_ap else []),
                    name=f"rl{tt}_{kc}", stream="rl")

            def f_bc2(t, sl=sl):
                t.matmul(P2[:], consts["ones1"], rowsl(0)[:, sl], start=True, stop=True)
                return t.matmul(P3[:], consts["ones1"], rowsl(1)[:, sl], start=True, stop=True)
            op_bc2 = sd.add("tensor", f_bc2,
                            deps=[op_st2] + ([prev_ap] if prev_ap else []),
                            name=f"x2bc{tt}")

            def f_ap2(v, sl=sl, tt=tt):
                last = None
                for kc in range(NCH):
                    v.tensor_tensor(scr[:], f0_t[kc][:, sl], P2[:], ALU.subtract)
                    v.tensor_tensor(scr[:], scr[:], P3[:], ALU.mult)
                    last = v.tensor_tensor(f0_t[kc][:, sl],
                                           xrl[:, kc * 512:kc * 512 + 512],
                                           scr[:], ALU.add)
                return last
            op_ap2 = sd.add("vector", f_ap2,
                            deps=[op_bc2] + [rl_ops[(tt, kc)] for kc in range(NCH)],
                            name=f"x2ap{tt}")
            prev_ap = op_ap2
        op_x2 = prev_ap

        op_st3 = emit_stats(sd, "st3", xs_t, rowf, consts, S0, S1, sq,
                            [op_x2], NTT)

        def st_out(sync, put):
            for ch in range(NCH):
                put(sync.dma_start(out=x2f[ch * 128:(ch + 1) * 128, :], in_=xs_t[ch]))
            put(sync.dma_start(out=stat3[0:1, :], in_=rows[0:1, 0:RPC]))
            put(sync.dma_start(out=stat3[1:2, :], in_=rows[0:1, RPC:2 * RPC]))
            if DEBUG_L2:
                put(sync.dma_start(out=dbg[:], in_=dbg_s[:]))
        op_out = sd.dma("sync", st_out, deps=[op_x2, op_st3], ndma=NCH + 2 + (1 if DEBUG_L2 else 0),
                        name="out", stream="out")

        def fin(g):
            return g.engine_nop()
        sd.add("gpsimd", fin, deps=[op_out], name="fin")

        sd.emit(block)
    return nc


def l0_ffn_inmaps(x0_fm_percore, oa_percore, ob_percore, inp):
    base_w = inp["L0_ffn_base_w"]
    bw_tiled = tile_lhsT(f32c(base_w.T))
    bias_c = f32c(np.asarray(inp["L0_ffn_base_b"]).reshape(8, 128).T)
    A_t = f32c(np.asarray(inp["L0_ffn_lora_A"]).transpose(1, 0, 2).reshape(D, 128))
    B2 = f32c(np.asarray(inp["L0_ffn_lora_B"]).reshape(128, D))
    B_t = tile_lhsT(B2)
    rwt = f32c(np.asarray(inp["L0_ffn_router_w"]).T)
    rbias = f32c(np.tile(np.asarray(inp["L0_ffn_router_b"])[None, :], (128, 1)))
    G = np.zeros((8, 128), np.float32)
    for e in range(8):
        G[e, e * 16:(e + 1) * 16] = 1.0
    ones1 = np.ones((1, 128), np.float32)
    cstm = np.zeros((128, 4), np.float32); cstm[:, 0] = 1e-5
    onescol = np.ones((128, 1), np.float32)
    ident = np.eye(128, dtype=np.float32)
    common = dict(bw_t=bw_tiled, bias_c=bias_c, A_t=A_t, B_t=B_t, rwt=rwt,
                  rbias=rbias, Gm=G, ones1=ones1, onescol=onescol, ident=ident,
                  cst=cstm, ones65=np.ones((65, 128), np.float32))
    return [dict(x0s=np.ascontiguousarray(x0_fm_percore[c]),
                 oa=np.ascontiguousarray(oa_percore[c]),
                 ob=np.ascontiguousarray(ob_percore[c]), **common)
            for c in range(8)]


if __name__ == "__main__":
    print("module only")


def build_l1_ffn():
    nc = bass.Bass("TRN2", target_bir_lowering=False)
    x2in = nc.dram_tensor("x2in", [D, RPC], F32, kind="ExternalInput").ap()
    yin = nc.dram_tensor("yin", [D, RPC], F32, kind="ExternalInput").ap()
    w12_t = nc.dram_tensor("w12_t", [32, 2 * D, 128], F32, kind="ExternalInput").ap()
    h1_t = nc.dram_tensor("h1_t", [32, 4096, 128], F32, kind="ExternalInput").ap()
    h2_t = nc.dram_tensor("h2_t", [32, 4096, 128], F32, kind="ExternalInput").ap()
    w3_t = nc.dram_tensor("w3_t", [8, 4096, 128], F32, kind="ExternalInput").ap()
    w1c = nc.dram_tensor("w1c", [128, 32], F32, kind="ExternalInput").ap()
    w2c = nc.dram_tensor("w2c", [128, 32], F32, kind="ExternalInput").ap()
    h1c = nc.dram_tensor("h1c", [128, 32], F32, kind="ExternalInput").ap()
    h2c = nc.dram_tensor("h2c", [128, 32], F32, kind="ExternalInput").ap()
    w3c = nc.dram_tensor("w3c", [128, 8], F32, kind="ExternalInput").ap()
    ones1 = nc.dram_tensor("ones1", [1, 128], F32, kind="ExternalInput").ap()
    ones65 = nc.dram_tensor("ones65", [65, 128], F32, kind="ExternalInput").ap()
    onescol = nc.dram_tensor("onescol", [128, 1], F32, kind="ExternalInput").ap()
    cst = nc.dram_tensor("cst", [128, 4], F32, kind="ExternalInput").ap()
    x4f = nc.dram_tensor("x4f", [D, RPC], F32, kind="ExternalOutput").ap()
    xsc = nc.dram_tensor("xsc", [D, RPC], F32, kind="Internal").ap()
    u4sc = nc.dram_tensor("u4sc", [D, RPC], BF16, kind="Internal").ap()

    HTOK = RPC // 2
    NT2 = HTOK // 512

    with ExitStack() as ST:
        E = ST.enter_context
        arA = E(nc.sbuf_tensor("arA", [128, NCH * RPC], F32))
        arB = E(nc.sbuf_tensor("arB", [128, NCH * RPC], F32))
        u4h = E(nc.sbuf_tensor("u4h", [128, NCH * 1024], BF16))
        slA = E(nc.sbuf_tensor("slA", [128, 2 * 16 * 128], BF16))
        slH = E(nc.sbuf_tensor("slH", [128, 2 * 32 * 128], BF16))
        sq = E(nc.sbuf_tensor("sq", [128, 4 * 512], F32))
        rows = E(nc.sbuf_tensor("rows", [1, 3 * RPC], F32))
        scr = E(nc.sbuf_tensor("scr", [128, 512], F32))
        sscr = E(nc.sbuf_tensor("sscr", [128, 512], F32))
        xrl = E(nc.sbuf_tensor("xrl", [128, 512], F32))
        colsb = E(nc.sbuf_tensor("colsb", [128, 32 * 4 + 8], F32))
        ones1_s = E(nc.sbuf_tensor("ones1_s", [1, 128], F32))
        ones65_s = E(nc.sbuf_tensor("ones65_s", [65, 128], F32))
        onescol_s = E(nc.sbuf_tensor("onescol_s", [128, 1], F32))
        cst_s = E(nc.sbuf_tensor("cst_s", [128, 4], F32))
        P = [E(nc.psum_tensor(f"P{i}", [128, 512], F32)) for i in range(6)]
        S0 = E(nc.psum_tensor("S0", [1, 512], F32))
        S1 = E(nc.psum_tensor("S1", [1, 512], F32))
        block = E(nc.Block())

        sd = Sched(nc)
        consts = {"ones1": ones1_s[:], "onescol": onescol_s[:],
                  "eps1": cst_s[0:1, 0:1],
                  "onesat": lambda i: ones65_s[32 * i:32 * i + 1, :]}
        x2_t = [arA[:, ch * RPC:(ch + 1) * RPC] for ch in range(NCH)]
        y_t = [arB[:, ch * RPC:(ch + 1) * RPC] for ch in range(NCH)]
        u4full = arB[:].bitcast(BF16)
        u4f_t = [u4full[:, ch * RPC:(ch + 1) * RPC] for ch in range(NCH)]
        u4_t = [u4h[:, ch * 1024:(ch + 1) * 1024] for ch in range(NCH)]
        hb = arA[:].bitcast(BF16)
        gb = arB[:].bitcast(BF16)
        f1v = arB
        w1col = colsb[:, 0:32]
        w2col = colsb[:, 32:64]
        h1col = colsb[:, 64:96]
        h2col = colsb[:, 96:128]
        w3col = colsb[:, 128:136]

        def rowf(i):
            return rows[0:1, i * RPC:(i + 1) * RPC]

        def ld_xy(sync, put):
            for ch in range(NCH):
                put(sync.dma_start(out=x2_t[ch], in_=x2in[ch * 128:(ch + 1) * 128, :]))
                put(sync.dma_start(out=y_t[ch], in_=yin[ch * 128:(ch + 1) * 128, :]))
            put(sync.dma_start(out=ones1_s[:], in_=ones1[:]))
            put(sync.dma_start(out=ones65_s[:], in_=ones65[:]))
            put(sync.dma_start(out=onescol_s[:], in_=onescol[:]))
            put(sync.dma_start(out=cst_s[:], in_=cst[:]))
            put(sync.dma_start(out=colsb[:, 0:32], in_=w1c[:]))
            put(sync.dma_start(out=colsb[:, 32:64], in_=w2c[:]))
            put(sync.dma_start(out=colsb[:, 64:96], in_=h1c[:]))
            put(sync.dma_start(out=colsb[:, 96:128], in_=h2c[:]))
            put(sync.dma_start(out=colsb[:, 128:136], in_=w3c[:]))
        op_ld = sd.dma("sync", ld_xy, ndma=2 * NCH + 9, name="ld_xy", stream="ldx")

        op_sty = emit_stats(sd, "sty", y_t, rowf, consts, S0, S1, sq, [op_ld], NTT)
        op_x3 = emit_apply(sd, "x3", y_t, x2_t, rowf, consts, P[0], P[1], scr,
                           [op_sty], NTT, residual_tiles=x2_t)

        op_st4 = emit_stats(sd, "st4", x2_t, rowf, consts, S0, S1, sq, [op_x3], NTT)
        op_u4 = emit_apply(sd, "u4", x2_t, u4f_t, rowf, consts, P[0], P[1], scr,
                           [op_st4], NTT)

        def sp(sync, put):
            for ch in range(NCH):
                put(sync.dma_start(out=xsc[ch * 128:(ch + 1) * 128, :], in_=x2_t[ch]))
            for ch in range(NCH):
                put(sync.dma_start(out=u4sc[ch * 128:(ch + 1) * 128, :], in_=u4f_t[ch]))
        op_sp = sd.dma("sync", sp, deps=[op_u4], ndma=2 * NCH, name="spill", stream="sp")

        out_ops = []
        chain_prev = op_sp
        for hf in range(2):
            hsl = slice(hf * HTOK, (hf + 1) * HTOK)

            def f_u4rl(sync, put, hsl=hsl):
                for ch in range(NCH):
                    put(sync.dma_start(out=u4_t[ch],
                                       in_=u4sc[ch * 128:(ch + 1) * 128, hsl]))
            op_u4rl = sd.dma("sync", f_u4rl, ndma=NCH, deps=[op_sp, chain_prev],
                             name=f"u4rl{hf}", stream="sp")

            def hview(oc, t2):
                return hb[:, oc * HTOK + t2 * 512:oc * HTOK + t2 * 512 + 512]

            def gview(oc, t2):
                return gb[:, oc * HTOK + t2 * 512:oc * HTOK + t2 * 512 + 512]

            hA_ops = []
            prevA = {}
            for oc in range(32):
                par = oc % 2

                def f_sl(g, put, oc=oc, par=par):
                    put(g.dma_start(
                        out=slA[:, par * 2048:(par + 1) * 2048]
                            .rearrange("p (kc o) -> p kc o", kc=16),
                        in_=w12_t[oc].rearrange("(kc p) o -> p kc o", p=128)))
                op_sl = sd.dma("gpsimd", f_sl, ndma=1,
                               deps=[prevA.get(("ev", par)), chain_prev],
                               name=f"hA_sl{hf}_{oc}", stream=f"slA{par}")

                for t2 in range(NT2):
                    pp = (2 * t2) % 4

                    def f_mm(t, oc=oc, t2=t2, par=par, pp=pp):
                        base = par * 2048
                        sl2 = slice(t2 * 512, t2 * 512 + 512)
                        for kc in range(NCH):
                            t.matmul(P[pp][:], slA[:, base + kc * 128:base + kc * 128 + 128],
                                     u4_t[kc][:, sl2], start=(kc == 0), stop=(kc == NCH - 1))
                        last = None
                        for kc in range(NCH):
                            last = t.matmul(P[pp + 1][:],
                                            slA[:, base + (8 + kc) * 128:base + (8 + kc) * 128 + 128],
                                            u4_t[kc][:, sl2], start=(kc == 0), stop=(kc == NCH - 1))
                        return last
                    op_mm = sd.add("tensor", f_mm,
                                   deps=[op_sl, op_u4rl, prevA.get(("h", (oc - 1, t2)))],
                                   name=f"hA_mm{hf}_{oc}_{t2}")

                    def f_si(s, oc=oc, pp=pp):
                        return s.activation(sscr[:], P[pp][:],
                                            AF.Silu, scale=w1col[:, oc:oc + 1])
                    op_si = sd.add("scalar", f_si,
                                   deps=[op_mm, op_ld, prevA.get("lasth")],
                                   name=f"hA_si{hf}_{oc}_{t2}")

                    def f_h(v, oc=oc, t2=t2, pp=pp):
                        return v.scalar_tensor_tensor(
                            hview(oc, t2), P[pp + 1][:], w2col[:, oc:oc + 1],
                            sscr[:], ALU.mult, ALU.mult)
                    op_h = sd.add("vector", f_h, deps=[op_mm, op_si],
                                  name=f"hA_h{hf}_{oc}_{t2}")
                    prevA[("h", (oc, t2))] = op_h
                    prevA["lasth"] = op_h
                    hA_ops.append(op_h)
                prevA[("ev", par)] = prevA[("h", (oc, NT2 - 1))]

            def big_phase(name, wt, col, src_view, dst_view, dst_is_f1, deps_all,
                          scol_base):
                ops = []
                prevB = {}
                noc = 8 if dst_is_f1 else 32
                for oc in range(noc):
                    par = oc % 2

                    def f_sl(g, put, oc=oc, par=par, wt=wt):
                        put(g.dma_start(
                            out=slH[:, par * 4096:(par + 1) * 4096]
                                .rearrange("p (kc o) -> p kc o", kc=32),
                            in_=wt[oc].rearrange("(kc p) o -> p kc o", p=128)))
                    op_sl = sd.dma("gpsimd", f_sl, ndma=1,
                                   deps=[prevB.get(("ev", par))] + deps_all,
                                   name=f"{name}_sl{oc}", stream=f"slH{par}")
                    for t2 in range(NT2):
                        pp = 4 + 0 if (oc * NT2 + t2) % 2 == 0 else 4 + 1

                        def f_mm(t, oc=oc, t2=t2, par=par, pp=pp):
                            last = None
                            for kc in range(32):
                                last = t.matmul(P[pp][:],
                                                slH[:, par * 4096 + kc * 128:par * 4096 + kc * 128 + 128],
                                                src_view(kc, t2), start=(kc == 0),
                                                stop=(kc == 31))
                            return last
                        op_mm = sd.add("tensor", f_mm,
                                       deps=[op_sl, prevB.get(("ev2", pp))] + deps_all,
                                       name=f"{name}_mm{oc}_{t2}")

                        if dst_is_f1:
                            def f_ev(s, oc=oc, t2=t2, pp=pp):
                                return s.mul(dst_view(oc, t2), P[pp][:],
                                             col[:, oc:oc + 1])
                        else:
                            def f_ev(s, oc=oc, t2=t2, pp=pp):
                                return s.activation(dst_view(oc, t2), P[pp][:],
                                                    AF.Gelu, scale=col[:, oc:oc + 1])
                        op_ev = sd.add("scalar", f_ev, deps=[op_mm, op_ld],
                                       name=f"{name}_ev{oc}_{t2}")
                        prevB[("ev2", pp)] = op_ev
                        ops.append(op_ev)
                    prevB[("ev", par)] = ops[-1]
                return ops

            hB = big_phase(f"hB{hf}", h1_t, h1col, hview, gview, False,
                           hA_ops + [chain_prev] if chain_prev is not None else hA_ops, 64)
            hC = big_phase(f"hC{hf}", h2_t, h2col, gview, hview, False, hB, 96)

            def f1view(oc, t2):
                return f1v[:, oc * HTOK + t2 * 512:oc * HTOK + t2 * 512 + 512]
            hD = big_phase(f"hD{hf}", w3_t, w3col, hview, f1view, True, hC, 128)

            f1_t = [f1v[:, ch * HTOK:(ch + 1) * HTOK] for ch in range(NCH)]

            def rowh(i):
                return rows[0:1, i * RPC:i * RPC + HTOK]
            op_stf = emit_stats(sd, f"stf{hf}", f1_t, rowh, consts, S0, S1, sq,
                                hD, NT2)
            prev_ap = None
            for t2 in range(NT2):
                def f_bc(t, t2=t2):
                    t.matmul(P[0][:], consts["ones1"], rowh(0)[:, t2 * 512:t2 * 512 + 512],
                             start=True, stop=True)
                    return t.matmul(P[1][:], consts["ones1"],
                                    rowh(1)[:, t2 * 512:t2 * 512 + 512],
                                    start=True, stop=True)
                op_bc = sd.add("tensor", f_bc,
                               deps=[op_stf] + ([prev_ap] if prev_ap else []),
                               name=f"x4bc{hf}_{t2}")
                for kc in range(NCH):
                    par = kc % 2

                    def f_rl(sync, put, kc=kc, t2=t2, hf=hf):
                        put(sync.dma_start(
                            out=xrl[:],
                            in_=xsc[kc * 128:(kc + 1) * 128,
                                    hf * HTOK + t2 * 512:hf * HTOK + t2 * 512 + 512]))
                    dep_rl = [op_sp, prevA.get("lastx4")]
                    if prev_ap is not None:
                        dep_rl.append(prev_ap)
                    op_rl = sd.dma("sync", f_rl, ndma=1,
                                   deps=dep_rl,
                                   name=f"x4rl{hf}_{t2}_{kc}", stream="rl")

                    def f_ap(v, kc=kc, t2=t2):
                        sl2 = slice(t2 * 512, t2 * 512 + 512)
                        v.tensor_tensor(scr[:], f1_t[kc][:, sl2], P[0][:], ALU.subtract)
                        v.tensor_tensor(scr[:], scr[:], P[1][:], ALU.mult)
                        return v.tensor_tensor(f1_t[kc][:, sl2],
                                               xrl[:],
                                               scr[:], ALU.add)
                    op_ap = sd.add("vector", f_ap, deps=[op_bc, op_rl],
                                   name=f"x4ap{hf}_{t2}_{kc}")
                    prevA[("x4", (t2, kc))] = op_ap
                    prevA["lastx4"] = op_ap

                    def f_out(sync, put, kc=kc, t2=t2, hf=hf):
                        put(sync.dma_start(
                            out=x4f[kc * 128:(kc + 1) * 128,
                                    hf * HTOK + t2 * 512:hf * HTOK + t2 * 512 + 512],
                            in_=f1_t[kc][:, t2 * 512:t2 * 512 + 512]))
                    op_o = sd.dma("sync", f_out, ndma=1, deps=[op_ap],
                                  name=f"x4o{hf}_{t2}_{kc}", stream="out")
                    out_ops.append(op_o)
                prev_ap = prevA[("x4", (t2, NCH - 1))]
            chain_prev = out_ops[-1]

        def fin(g):
            return g.engine_nop()
        sd.add("gpsimd", fin, deps=out_ops[-1:] + [out_ops[len(out_ops) // 2]],
               name="fin")
        sd.emit(block)
    return nc


def l1_ffn_inmaps(x2_fm_percore, y_fm_percore, inp):
    w12 = np.concatenate([np.asarray(inp["L1_ffn_w1_w"]).T,
                          np.asarray(inp["L1_ffn_w2_w"]).T], axis=0)
    w12_t = tile_lhsT(f32c(w12))
    h1_t = tile_lhsT(f32c(np.asarray(inp["L1_ffn_h1_w"]).T))
    h2_t = tile_lhsT(f32c(np.asarray(inp["L1_ffn_h2_w"]).T))
    w3_t = tile_lhsT(f32c(np.asarray(inp["L1_ffn_w3_w"]).T))

    def colize(s, n):
        return f32c(np.asarray(s).reshape(n, 128).T)
    common = dict(w12_t=w12_t, h1_t=h1_t, h2_t=h2_t, w3_t=w3_t,
                  w1c=colize(inp["L1_ffn_w1_s"], 32), w2c=colize(inp["L1_ffn_w2_s"], 32),
                  h1c=colize(inp["L1_ffn_h1_s"], 32), h2c=colize(inp["L1_ffn_h2_s"], 32),
                  w3c=colize(inp["L1_ffn_w3_s"], 8),
                  ones1=np.ones((1, 128), np.float32),
                  ones65=np.ones((65, 128), np.float32),
                  onescol=np.ones((128, 1), np.float32),
                  cst=np.concatenate([np.full((128, 1), 1e-5, np.float32),
                                      np.zeros((128, 3), np.float32)], axis=1))
    return [dict(x2in=np.ascontiguousarray(x2_fm_percore[c]),
                 yin=np.ascontiguousarray(y_fm_percore[c]), **common)
            for c in range(8)]


NBF = 2304
NFC = NBF // 128
CHH = 512


def build_l1_attn():
    nc = bass.Bass("TRN2", target_bir_lowering=False)
    x2tm = nc.dram_tensor("x2tm", [S, CHH], F32, kind="ExternalInput").ap()
    mu3c = nc.dram_tensor("mu3c", [128, 32], F32, kind="ExternalInput").ap()
    r3c = nc.dram_tensor("r3c", [128, 32], F32, kind="ExternalInput").ap()
    abT = nc.dram_tensor("abT", [64, 2 * CHH], F32, kind="ExternalInput").ap()
    h0row = nc.dram_tensor("h0row", [1, CHH], F32, kind="ExternalInput").ap()
    FWD = nc.dram_tensor("FWD", [1024, NBF], BF16, kind="ExternalInput").ap()
    IDM = nc.dram_tensor("IDM", [NBF, 1024], BF16, kind="ExternalInput").ap()
    COEFC = nc.dram_tensor("COEFC", [64, NBF], BF16, kind="ExternalInput").ap()
    ones1 = nc.dram_tensor("ones1", [1, 128], F32, kind="ExternalInput").ap()
    cst = nc.dram_tensor("cst", [128, 4], F32, kind="ExternalInput").ap()
    yout = nc.dram_tensor("yout", [S, CHH], F32, kind="ExternalOutput").ap()

    with ExitStack() as ST:
        E = ST.enter_context
        u3 = E(nc.sbuf_tensor("u3", [128, 16 * CHH], BF16))
        FWDs = E(nc.sbuf_tensor("FWDs", [128, 8 * NBF], BF16))
        IDMs = E(nc.sbuf_tensor("IDMs", [128, NFC * 1024], BF16))
        Kpk = E(nc.sbuf_tensor("Kpk", [128, NFC * CHH], BF16))
        Fb = E(nc.sbuf_tensor("Fb", [128, 2 * NFC * CHH], BF16))
        Xs = E(nc.sbuf_tensor("Xs", [128, NFC * CHH], BF16))
        COEFs = E(nc.sbuf_tensor("COEFs", [64, NBF], BF16))
        abs_ = E(nc.sbuf_tensor("abs_", [64, 2 * CHH], BF16))
        t14 = E(nc.sbuf_tensor("t14", [128, 4 * CHH], F32))
        cdiv = E(nc.sbuf_tensor("cdiv", [128, 6 * CHH], F32))
        xt = E(nc.sbuf_tensor("xt", [128, 2 * CHH], F32))
        ytile = E(nc.sbuf_tensor("ytile", [128, 2 * CHH], BF16))
        h0b = E(nc.sbuf_tensor("h0b", [128, CHH], F32))
        mucol = E(nc.sbuf_tensor("mucol", [128, 32], F32))
        rcol = E(nc.sbuf_tensor("rcol", [128, 32], F32))
        nmr = E(nc.sbuf_tensor("nmr", [128, 32], F32))
        ones1_s = E(nc.sbuf_tensor("ones1_s", [1, 128], F32))
        cst_s = E(nc.sbuf_tensor("cst_s", [128, 4], F32))
        P = [E(nc.psum_tensor(f"P{i}", [128, 512], F32)) for i in range(6)]
        block = E(nc.Block())

        sd = Sched(nc)
        Fbt = [[Fb[:, (par * NFC + fc) * CHH:(par * NFC + fc + 1) * CHH]
                for fc in range(NFC)] for par in range(2)]
        Xt = [Xs[:, fc * CHH:(fc + 1) * CHH] for fc in range(NFC)]
        Kt = [Kpk[:, fc * CHH:(fc + 1) * CHH] for fc in range(NFC)]
        u3t = lambda bpar, tc: u3[:, (bpar * 8 + tc) * CHH:(bpar * 8 + tc + 1) * CHH]
        FWDt = [[FWDs[:, (tc * NFC + fc) * 128:(tc * NFC + fc + 1) * 128]
                 for fc in range(NFC)] for tc in range(8)]
        IDMt = [[IDMs[:, (fc * 8 + tauc) * 128:(fc * 8 + tauc + 1) * 128]
                 for tauc in range(8)] for fc in range(NFC)]

        def ld_c(sync, put):
            put(sync.dma_start(out=mucol[:], in_=mu3c[:]))
            put(sync.dma_start(out=rcol[:], in_=r3c[:]))
            put(sync.dma_start(out=ones1_s[:], in_=ones1[:]))
            put(sync.dma_start(out=cst_s[:], in_=cst[:]))
            put(sync.dma_start(out=COEFs[:], in_=COEFC[:]))
            put(sync.dma_start(
                out=FWDs[:].rearrange("p (tc f) -> p tc f", tc=8),
                in_=FWD.rearrange("(tc p) f -> p tc f", p=128)))
            put(sync.dma_start(
                out=IDMs[:].rearrange("p (fc t) -> p fc t", fc=NFC),
                in_=IDM.rearrange("(fc p) t -> p fc t", p=128)))
        op_ldc = sd.dma("sync", ld_c, ndma=7, name="ld_c", stream="ldc")

        def ld_ab(g, put):
            put(g.dma_start(out=abs_[:], in_=abT[:]))
        op_ldab = sd.dma("gpsimd", ld_ab, ndma=1, name="ld_ab", stream="ldab")

        def f_nmr(v):
            return v.scalar_tensor_tensor(nmr[:], mucol[:], -1.0, rcol[:],
                                          ALU.mult, ALU.mult)
        op_nmr = sd.add("vector", f_nmr, deps=[op_ldc], name="nmr")

        prev_u3 = {}

        def emit_u3_block(blk, dep_extra):
            bpar = blk % 2
            ops = []
            for tc in range(8):
                rt = blk * 8 + tc
                par = rt % 2

                def f_ldx(sync, put, rt=rt, par=par):
                    put(sync.dma_start(out=xt[:, par * CHH:(par + 1) * CHH],
                                       in_=x2tm[rt * 128:(rt + 1) * 128, :]))
                op_ldx = sd.dma("sync", f_ldx, ndma=1,
                                deps=[prev_u3.get(par)], name=f"ldx{rt}",
                                stream=f"ldx{par}")

                def f_u3(v, rt=rt, par=par, bpar=bpar, tc=tc):
                    return v.tensor_scalar(u3t(bpar, tc),
                                           xt[:, par * CHH:(par + 1) * CHH],
                                           rcol[:, rt:rt + 1], nmr[:, rt:rt + 1],
                                           ALU.mult, ALU.add)
                op_u3 = sd.add("vector", f_u3,
                               deps=[op_ldx, op_nmr] + dep_extra,
                               name=f"u3{rt}")
                prev_u3[par] = op_u3
                ops.append(op_u3)
            return ops

        def f_h0l(sync, put):
            put(sync.dma_start(out=h0b[0:1, :], in_=h0row[:]))
        op_h0l = sd.dma("sync", f_h0l, ndma=1, name="h0l", stream="ldc")

        def f_h0(t):
            return t.matmul(P[5][:], ones1_s[:], h0b[0:1, :], start=True, stop=True)
        op_h0m = sd.add("tensor", f_h0, deps=[op_h0l], name="h0m")

        def f_h0e(v):
            return v.tensor_copy(h0b[:], P[5][:])
        op_h0e = sd.add("vector", f_h0e, deps=[op_h0m], name="h0e")

        kf_ops = []
        prev_kf = None
        for c in range(9):
            def f_cmm(t, c=c):
                t.matmul(P[0][:], COEFs[:, c * 128:(c + 1) * 128], abs_[:, 0:CHH],
                         start=True, stop=True)
                t.matmul(P[1][:], COEFs[:, (9 + c) * 128:(10 + c) * 128], abs_[:, 0:CHH],
                         start=True, stop=True)
                t.matmul(P[2][:], COEFs[:, c * 128:(c + 1) * 128], abs_[:, CHH:2 * CHH],
                         start=True, stop=True)
                return t.matmul(P[3][:], COEFs[:, (9 + c) * 128:(10 + c) * 128],
                                abs_[:, CHH:2 * CHH], start=True, stop=True)
            op_cmm = sd.add("tensor", f_cmm, deps=[op_ldab, op_ldc, prev_kf],
                            name=f"cmm{c}")

            def f_cev(s, c=c):
                s.add(cdiv[:, 0:CHH], P[0][:], cst_s[:, 1:2])
                s.copy(cdiv[:, CHH:2 * CHH], P[1][:])
                s.copy(cdiv[:, 2 * CHH:3 * CHH], P[2][:])
                return s.copy(cdiv[:, 3 * CHH:4 * CHH], P[3][:])
            op_cev = sd.add("scalar", f_cev, deps=[op_cmm, op_ldc], name=f"cev{c}")

            def f_cdv(v, c=c):
                Ar = cdiv[:, 0:CHH]; Ai = cdiv[:, CHH:2 * CHH]
                Br = cdiv[:, 2 * CHH:3 * CHH]; Bi = cdiv[:, 3 * CHH:4 * CHH]
                den = cdiv[:, 4 * CHH:5 * CHH]; rden = cdiv[:, 5 * CHH:6 * CHH]
                v.tensor_tensor(den, Ar, Ar, ALU.mult)
                v.tensor_tensor(t14[:, 0:CHH], Ai, Ai, ALU.mult)
                v.tensor_tensor(den, den, t14[:, 0:CHH], ALU.add)
                v.reciprocal(rden, den)
                v.tensor_tensor(t14[:, 0:CHH], Br, Ar, ALU.mult)
                v.tensor_tensor(t14[:, CHH:2 * CHH], Bi, Ai, ALU.mult)
                v.tensor_tensor(t14[:, 0:CHH], t14[:, 0:CHH], t14[:, CHH:2 * CHH], ALU.add)
                v.tensor_tensor(t14[:, 0:CHH], t14[:, 0:CHH], rden, ALU.mult)
                v.tensor_tensor(t14[:, 0:CHH], t14[:, 0:CHH], h0b[:], ALU.add)
                v.tensor_copy(Kt[c], t14[:, 0:CHH])
                v.tensor_tensor(t14[:, 2 * CHH:3 * CHH], Bi, Ar, ALU.mult)
                v.tensor_tensor(t14[:, 3 * CHH:4 * CHH], Br, Ai, ALU.mult)
                v.tensor_tensor(t14[:, 2 * CHH:3 * CHH], t14[:, 2 * CHH:3 * CHH],
                                t14[:, 3 * CHH:4 * CHH], ALU.subtract)
                v.tensor_tensor(t14[:, 2 * CHH:3 * CHH], t14[:, 2 * CHH:3 * CHH],
                                rden, ALU.mult)
                return v.tensor_copy(Kt[9 + c], t14[:, 2 * CHH:3 * CHH])
            op_cdv = sd.add("vector", f_cdv, deps=[op_cev, op_h0e], name=f"cdv{c}")
            prev_kf = op_cdv
            kf_ops.append(op_cdv)

        def f_kz2(v):
            v.tensor_copy(t14[0:1, 0:CHH], Kt[8][0:1, :])
            v.memset(Kt[8], 0.0)
            return v.tensor_copy(Kt[8][0:1, :], t14[0:1, 0:CHH])
        op_kz = sd.add("vector", f_kz2, deps=kf_ops, name="kz")
        def f_kz3(v):
            return v.memset(Kt[17], 0.0)
        op_kz3 = sd.add("vector", f_kz3, deps=kf_ops, name="kz3")
        kf_done = kf_ops + [op_kz, op_kz3]

        def f_fbz(v):
            return v.memset(Fb[:, 0:NFC * CHH], 0.0)
        op_fbz = sd.add("vector", f_fbz, name="fbz")

        prev_idft_evac = {}
        prev_x_use = None
        last_fmm_of_blk = {}
        out_ops = []
        for blk in range(4):
            cur = (blk + 1) % 2
            prv = blk % 2
            u3_blk_ops = emit_u3_block(
                blk, [last_fmm_of_blk.get(blk - 2)] if blk >= 2 else [])
            fwd_evs = []
            for fc in range(NFC):
                pp = fc % 2

                def f_fmm(t, blk=blk, fc=fc, pp=pp):
                    last = None
                    for tc in range(8):
                        last = t.matmul(P[pp][:], FWDt[tc][fc], u3t(blk % 2, tc),
                                        start=(tc == 0), stop=(tc == 7))
                    return last
                op_fmm = sd.add("tensor", f_fmm,
                                deps=u3_blk_ops + [op_ldc,
                                      prev_idft_evac.get(("fwd", fc, pp)),
                                      prev_x_use],
                                name=f"fmm{blk}_{fc}")
                last_fmm_of_blk[blk] = op_fmm

                def f_fev(s, blk=blk, fc=fc, pp=pp, cur=cur):
                    return s.copy(Fbt[cur][fc], P[pp][:])
                op_fev = sd.add("scalar", f_fev, deps=[op_fmm], name=f"fev{blk}_{fc}")
                prev_idft_evac[("fwd", fc, pp)] = op_fev
                fwd_evs.append(op_fev)

            def f_cmb(v, blk=blk, cur=cur, prv=prv):
                last = None
                for fc in range(NFC):
                    last = v.scalar_tensor_tensor(
                        Xt[fc], Fbt[cur][fc], cst_s[:, 2:3], Fbt[prv][fc],
                        ALU.mult, ALU.add)
                return last
            dep_cmb = fwd_evs + ([op_fbz] if blk == 0 else [])
            op_cmb = sd.add("vector", f_cmb, deps=dep_cmb + [prev_x_use],
                            name=f"cmb{blk}")

            def f_pw(v, blk=blk):
                last = None
                for c in range(9):
                    a = t14[:, 0:CHH]; b = t14[:, CHH:2 * CHH]
                    c2 = t14[:, 2 * CHH:3 * CHH]; d2 = t14[:, 3 * CHH:4 * CHH]
                    v.tensor_tensor(a, Xt[c], Kt[c], ALU.mult)
                    v.tensor_tensor(b, Xt[9 + c], Kt[9 + c], ALU.mult)
                    v.tensor_tensor(c2, Xt[c], Kt[9 + c], ALU.mult)
                    v.tensor_tensor(d2, Xt[9 + c], Kt[c], ALU.mult)
                    v.tensor_tensor(Xt[c], a, b, ALU.subtract)
                    last = v.tensor_tensor(Xt[9 + c], c2, d2, ALU.add)
                return last
            op_pw = sd.add("vector", f_pw, deps=[op_cmb] + kf_done, name=f"pw{blk}")
            prev_x_use = op_pw

            for tauc in range(8):
                pp = 4 + tauc % 2

                def f_imm(t, tauc=tauc, pp=pp):
                    last = None
                    for fc in range(NFC):
                        last = t.matmul(P[pp][:], IDMt[fc][tauc], Xt[fc],
                                        start=(fc == 0), stop=(fc == NFC - 1))
                    return last
                op_imm = sd.add("tensor", f_imm,
                                deps=[op_pw, op_ldc,
                                      prev_idft_evac.get(("inv", pp))],
                                name=f"imm{blk}_{tauc}")
                prev_x_use = op_pw

                def f_iev(s, tauc=tauc, pp=pp):
                    return s.copy(ytile[:, (tauc % 2) * CHH:(tauc % 2 + 1) * CHH],
                                  P[pp][:])
                dep_iev = [op_imm, prev_idft_evac.get(("yout", tauc % 2))]
                op_iev = sd.add("scalar", f_iev, deps=dep_iev, name=f"iev{blk}_{tauc}")
                prev_idft_evac[("inv", pp)] = op_iev

                def f_yout(g, put, blk=blk, tauc=tauc):
                    put(g.dma_start(
                        out=yout[blk * 1024 + tauc * 128:blk * 1024 + (tauc + 1) * 128, :],
                        in_=ytile[:, (tauc % 2) * CHH:(tauc % 2 + 1) * CHH]))
                op_yo = sd.dma("gpsimd", f_yout, ndma=1, deps=[op_iev],
                               name=f"yo{blk}_{tauc}", stream="yout")
                prev_idft_evac[("yout", tauc % 2)] = op_yo
                out_ops.append(op_yo)
            prev_x_use = op_imm

        def fin(s):
            return s.nop()
        sd.add("sync", fin, deps=[out_ops[-1]], name="fin")
        sd.emit(block)
    return nc


def dft_consts():
    NB = 2048
    fb = NB // 2 + 1
    iidx = np.arange(1, 65)
    FWD = np.zeros((1024, NBF), np.float64)
    t = np.arange(1024)
    for f in range(fb):
        ang = 2 * np.pi * f * t / NB
        FWD[:, f] = np.cos(ang)
        if 1 <= f <= 1023:
            FWD[:, 1152 + f] = -np.sin(ang)
    COEF = np.zeros((64, NBF), np.float64)
    for f in range(fb):
        ang = 2 * np.pi * f * iidx / NB
        COEF[:, f] = np.cos(ang)
        if 1 <= f <= 1023:
            COEF[:, 1152 + f] = -np.sin(ang)
    IDM = np.zeros((NBF, 1024), np.float64)
    tau = np.arange(1024) + 1024
    for f in range(fb):
        w = (1.0 if f in (0, fb - 1) else 2.0) / NB
        ang = 2 * np.pi * f * tau / NB
        IDM[f, :] = w * np.cos(ang)
        if 1 <= f <= 1023:
            IDM[1152 + f, :] = -w * np.sin(ang)
    import ml_dtypes
    bf = ml_dtypes.bfloat16
    return FWD.astype(bf), IDM.astype(bf), COEF.astype(bf)


def l1_attn_inmaps(x2_full, mu3, r3, inp):
    ab = np.asarray(inp["L1_attn_ab"])
    h0 = np.asarray(inp["L1_attn_h_0"])
    FWD, IDM, COEF = dft_consts()
    cstm = np.zeros((128, 4), np.float32)
    cstm[:, 0] = 1e-5
    cstm[:, 1] = 1.0
    cstm[:, 2] = ((-1.0) ** np.arange(128)).astype(np.float32)
    maps = []
    for c in range(8):
        b = c // 2
        ch0 = (c % 2) * CHH
        maps.append(dict(
            x2tm=np.ascontiguousarray(x2_full[b][:, ch0:ch0 + CHH]),
            mu3c=f32c(mu3[b * S:(b + 1) * S].reshape(32, 128).T),
            r3c=f32c(r3[b * S:(b + 1) * S].reshape(32, 128).T),
            abT=f32c(np.concatenate([ab.T[:, ch0:ch0 + CHH],
                                     ab.T[:, D + ch0:D + ch0 + CHH]], axis=1)),
            h0row=f32c(h0[None, ch0:ch0 + CHH]),
            FWD=FWD, IDM=IDM, COEFC=COEF,
            ones1=np.ones((1, 128), np.float32),
            cst=cstm,
        ))
    return maps


NKC = 34
KCOL = NKC * 128
NQT = 8


def build_l0_attn():
    nc = bass.Bass("TRN2", target_bir_lowering=False)
    x0f = nc.dram_tensor("x0f", [D, S], F32, kind="ExternalInput").ap()
    wq_t = nc.dram_tensor("wq_t", [4, D, 128], F32, kind="ExternalInput").ap()
    wk_t = nc.dram_tensor("wk_t", [4, D, 128], F32, kind="ExternalInput").ap()
    wv_t = nc.dram_tensor("wv_t", [4, D, 128], F32, kind="ExternalInput").ap()
    wo_t = nc.dram_tensor("wo_t", [8, 512, 128], F32, kind="ExternalInput").ap()
    gcol = nc.dram_tensor("gcol", [128, 8], F32, kind="ExternalInput").ap()
    nkT = nc.dram_tensor("nkT", [128, 8], F32, kind="ExternalInput").ap()
    nvT = nc.dram_tensor("nvT", [128, 8], F32, kind="ExternalInput").ap()
    qsrow = nc.dram_tensor("qsrow", [1, 64], F32, kind="ExternalInput").ap()
    ksrow = nc.dram_tensor("ksrow", [1, 64], F32, kind="ExternalInput").ap()
    qsd = nc.dram_tensor("qsd", [2, 128], F32, kind="ExternalInput").ap()
    ksd = nc.dram_tensor("ksd", [2, 128], F32, kind="ExternalInput").ap()
    bcols = nc.dram_tensor("bcols", [128, 8 * NKC], F32, kind="ExternalInput").ap()
    brow0 = nc.dram_tensor("brow0", [1, 8 * S], F32, kind="ExternalInput").ap()
    M1c = nc.dram_tensor("M1c", [128, 128], BF16, kind="ExternalInput").ap()
    M2v = nc.dram_tensor("M2v", [128, 1024], BF16, kind="ExternalInput").ap()
    onesbf = nc.dram_tensor("onesbf", [1, 128], BF16, kind="ExternalInput").ap()
    ones1 = nc.dram_tensor("ones1", [1, 128], F32, kind="ExternalInput").ap()
    onescol = nc.dram_tensor("onescol", [128, 1], F32, kind="ExternalInput").ap()
    identb = nc.dram_tensor("identb", [128, 128], BF16, kind="ExternalInput").ap()
    ones65d = nc.dram_tensor("ones65d", [65, 128], F32, kind="ExternalInput").ap()
    cst = nc.dram_tensor("cst", [128, 4], F32, kind="ExternalInput").ap()
    opart = nc.dram_tensor("opart", [D, S], F32, kind="ExternalOutput").ap()

    with ExitStack() as ST:
        E = ST.enter_context
        xn = E(nc.sbuf_tensor("xn", [128, NCH * S], BF16))
        ao = E(nc.sbuf_tensor("ao", [128, 4 * S], BF16))
        arena2 = E(nc.sbuf_tensor("arena2", [128, 4352], F32))
        xrl = arena2[:, 0:4096]
        kvb = arena2[:].bitcast(BF16)
        kpk = kvb[:, 0:KCOL]
        vfm = kvb[:, KCOL:2 * KCOL]
        vT = E(nc.sbuf_tensor("vT", [128, NKC * 130], BF16))
        qpk = E(nc.sbuf_tensor("qpk", [128, S], BF16))
        phat = E(nc.sbuf_tensor("phat", [128, 2 * 512], BF16))
        sq = E(nc.sbuf_tensor("sq", [128, 4 * 512], F32))
        arena = E(nc.sbuf_tensor("arena", [2, 3 * S], F32))
        nb = E(nc.sbuf_tensor("nb", [1, 512], BF16))
        nbf = E(nc.sbuf_tensor("nbf", [1, 512], F32))
        ou = E(nc.sbuf_tensor("ou", [64, 512], BF16))
        rsr = E(nc.sbuf_tensor("rsr", [65, 512], F32))
        scr = E(nc.sbuf_tensor("scr", [128, 2 * 512], F32))
        wsl = E(nc.sbuf_tensor("wsl", [128, 2 * 8 * 128], BF16))
        gc_s = E(nc.sbuf_tensor("gc_s", [128, 8], F32))
        nk_s = E(nc.sbuf_tensor("nk_s", [128, 8], F32))
        nv_s = E(nc.sbuf_tensor("nv_s", [128, 8], F32))
        qs_s = E(nc.sbuf_tensor("qs_s", [1, 64], F32))
        ks_s = E(nc.sbuf_tensor("ks_s", [1, 64], F32))
        qsd_s = E(nc.sbuf_tensor("qsd_s", [2, 128], F32))
        ksd_s = E(nc.sbuf_tensor("ksd_s", [2, 128], F32))
        bc_s = E(nc.sbuf_tensor("bc_s", [128, 8 * NKC], F32))
        M1_s = E(nc.sbuf_tensor("M1_s", [128, 128], BF16))
        M2_s = E(nc.sbuf_tensor("M2_s", [128, 1024], BF16))
        onesbf_s = E(nc.sbuf_tensor("onesbf_s", [1, 128], BF16))
        ones65_s = E(nc.sbuf_tensor("ones65_s", [65, 128], F32))
        onescol_s = E(nc.sbuf_tensor("onescol_s", [128, 1], F32))
        identb_s = E(nc.sbuf_tensor("identb_s", [128, 128], BF16))
        bd2_s = E(nc.sbuf_tensor("bd2_s", [128, 2], F32))
        cst_s = E(nc.sbuf_tensor("cst_s", [128, 4], F32))
        tiny = E(nc.sbuf_tensor("tiny", [1, 8], F32))
        P = [E(nc.psum_tensor(f"P{i}", [128, 512], F32)) for i in range(5)]
        PT = E(nc.psum_tensor("PT", [128, 128], BF16))
        PO = [E(nc.psum_tensor(f"PO{i}", [65, 512], F32)) for i in range(2)]
        block = E(nc.Block())

        sd = Sched(nc)
        xn_t = [xn[:, ch * S:(ch + 1) * S] for ch in range(NCH)]
        ao_t = [ao[:, ch * S:(ch + 1) * S] for ch in range(4)]

        def row(i):
            return arena[0:1, i * S:(i + 1) * S]
        row3 = xrl[0:1, 0:S]
        nrow = arena[0:2, 0:KCOL]
        qn2 = arena[0:2, S + 512:S + 512 + S]

        def ld_c(sync, put):
            for t, d_ in ((gc_s, gcol), (nk_s, nkT), (nv_s, nvT), (qs_s, qsrow),
                          (ks_s, ksrow), (qsd_s, qsd), (ksd_s, ksd), (bc_s, bcols),
                          (M1_s, M1c), (M2_s, M2v), (onesbf_s, onesbf),
                          (ones65_s, ones65d),
                          (onescol_s, onescol), (identb_s, identb), (cst_s, cst)):
                put(sync.dma_start(out=t[:], in_=d_[:]))
        op_ldc = sd.dma("sync", ld_c, ndma=15, name="ld_c", stream="ldc")

        def f_bd2(v):
            v.memset(bd2_s[:], 0.0)
            v.memset(bd2_s[0:64, 0:1], 1.0)
            return v.memset(bd2_s[64:128, 1:2], 1.0)
        op_bd2 = sd.add("vector", f_bd2, name="bd2")

        st_evs = []
        prev_ev = None
        for tt in range(NQT):
            sl = slice(tt * 512, (tt + 1) * 512)

            def f_rlA(sync, put, sl=sl):
                for kc in range(NCH):
                    put(sync.dma_start(out=xrl[:, kc * 512:(kc + 1) * 512],
                                       in_=x0f[kc * 128:(kc + 1) * 128, sl]))
            op_rlA = sd.dma("sync", f_rlA, ndma=NCH,
                            deps=[prev_ev], name=f"st_rl{tt}", stream="srl")

            def f_sqa(s):
                last = None
                for kc in range(4):
                    last = s.activation(sq[:, kc * 512:(kc + 1) * 512],
                                        xrl[:, kc * 512:(kc + 1) * 512], AF.Square)
                return last
            op_sqa = sd.add("scalar", f_sqa, deps=[op_rlA], name=f"st_sqa{tt}")

            def f_mma(t):
                for kc in range(NCH):
                    t.matmul(P[0][0:1, :], onescol_s[:], xrl[:, kc * 512:(kc + 1) * 512],
                             start=(kc == 0), stop=(kc == NCH - 1))
                last = None
                for kc in range(4):
                    last = t.matmul(P[1][0:1, :], onescol_s[:],
                                    sq[:, kc * 512:(kc + 1) * 512],
                                    start=(kc == 0), stop=False)
                return last
            op_mma = sd.add("tensor", f_mma, deps=[op_rlA, op_sqa, op_ldc, prev_ev],
                            name=f"st_mma{tt}")

            def f_sqb(s):
                last = None
                for kc in range(4):
                    last = s.activation(sq[:, kc * 512:(kc + 1) * 512],
                                        xrl[:, (4 + kc) * 512:(5 + kc) * 512], AF.Square)
                return last
            op_sqb = sd.add("scalar", f_sqb, deps=[op_mma], name=f"st_sqb{tt}")

            def f_mmb(t):
                last = None
                for kc in range(4):
                    last = t.matmul(P[1][0:1, :], onescol_s[:],
                                    sq[:, kc * 512:(kc + 1) * 512],
                                    start=False, stop=(kc == 3))
                return last
            op_mmb = sd.add("tensor", f_mmb, deps=[op_sqb], name=f"st_mmb{tt}")

            def f_ev(v, sl=sl):
                v.tensor_copy(row(0)[:, sl], P[0][0:1, :])
                return v.tensor_copy(row(1)[:, sl], P[1][0:1, :])
            prev_ev = sd.add("vector", f_ev, deps=[op_mmb], name=f"st_ev{tt}")
            st_evs.append(prev_ev)

        def f_rm1(v):
            v.tensor_scalar(row(0), row(0), 1.0 / D, None, ALU.mult)
            v.tensor_scalar(row(1), row(1), 1.0 / D, None, ALU.mult)
            v.tensor_tensor(row(2), row(0), row(0), ALU.mult)
            return v.tensor_tensor(row(1), row(1), row(2), ALU.subtract)
        op_rm1 = sd.add("vector", f_rm1, deps=st_evs, name="rm1")

        def f_rm2(s):
            return s.activation(row(2), row(1), AF.Sqrt, bias=cst_s[0:1, 0:1])
        op_rm2 = sd.add("scalar", f_rm2, deps=[op_rm1], name="rm2")

        def f_rm3(v):
            v.reciprocal(row(2), row(2))
            v.tensor_tensor(row3, row(1), row(2), ALU.mult)
            return v.tensor_tensor(row3, row3, row(2), ALU.mult)
        op_rm3 = sd.add("vector", f_rm3, deps=[op_rm2], name="rm3")

        def f_rm4(s):
            return s.activation(row3, row3, AF.Sqrt, bias=cst_s[0:1, 0:1])
        op_rm4 = sd.add("scalar", f_rm4, deps=[op_rm3], name="rm4")

        def f_rm5(v):
            v.reciprocal(row3, row3)
            return v.tensor_tensor(row(1), row(2), row3, ALU.mult)
        op_rm5 = sd.add("vector", f_rm5, deps=[op_rm4], name="rm5")

        prev_ap = None
        for tt in range(NQT):
            sl = slice(tt * 512, (tt + 1) * 512)

            def f_rlB(sync, put, sl=sl):
                for kc in range(NCH):
                    put(sync.dma_start(out=xrl[:, kc * 512:(kc + 1) * 512],
                                       in_=x0f[kc * 128:(kc + 1) * 128, sl]))
            op_rlB = sd.dma("sync", f_rlB, ndma=NCH,
                            deps=[op_rm5, prev_ap], name=f"ap_rl{tt}", stream="srl")

            def f_bc(t, sl=sl):
                t.matmul(P[2][:], ones65_s[0:1, :], row(0)[:, sl], start=True, stop=True)
                return t.matmul(P[3][:], ones65_s[0:1, :], row(1)[:, sl], start=True, stop=True)
            op_bc = sd.add("tensor", f_bc, deps=[op_rm5, prev_ap], name=f"ap_bc{tt}")

            def f_ap(v, sl=sl):
                last = None
                for kc in range(NCH):
                    v.tensor_tensor(scr[:, 0:512], xrl[:, kc * 512:(kc + 1) * 512],
                                    P[2][:], ALU.subtract)
                    v.tensor_tensor(scr[:, 0:512], scr[:, 0:512], P[3][:], ALU.mult)
                    last = v.tensor_scalar(xn_t[kc][:, sl], scr[:, 0:512],
                                           gc_s[:, kc:kc + 1], None, ALU.mult)
                return last
            op_ap = sd.add("vector", f_ap, deps=[op_rlB, op_bc, op_ldc], name=f"ap{tt}")
            prev_ap = op_ap
        op_xn = prev_ap

        def f_qk1(v):
            v.reduce_max(tiny[0:1, 0:1], qs_s[:], axis=AX.X, apply_absolute_value=True)
            v.reduce_max(tiny[0:1, 1:2], ks_s[:], axis=AX.X, apply_absolute_value=True)
            v.tensor_tensor(tiny[0:1, 2:3], tiny[0:1, 0:1], tiny[0:1, 1:2], ALU.mult)
            return v.tensor_scalar(tiny[0:1, 3:4], tiny[0:1, 2:3], -8.0, None, ALU.mult)
        op_qk = sd.add("vector", f_qk1, deps=[op_ldc], name="qk1")

        prev_hp = {}
        oproj_deps = []
        for hp in range(4):
            def f_wk(g, put, hp=hp):
                put(g.dma_start(
                    out=wsl[:, 0:1024].rearrange("p (kc o) -> p kc o", kc=NCH),
                    in_=wk_t[hp].rearrange("(kc p) o -> p kc o", p=128)))
                put(g.dma_start(
                    out=wsl[:, 1024:2048].rearrange("p (kc o) -> p kc o", kc=NCH),
                    in_=wv_t[hp].rearrange("(kc p) o -> p kc o", p=128)))
            op_wk = sd.dma("gpsimd", f_wk, ndma=2, deps=[prev_hp.get("wsl")],
                           name=f"wk{hp}", stream="wsl")

            kv_evs = []
            for tt in range(NQT):
                sl = slice(tt * 512, (tt + 1) * 512)
                pp = tt % 2

                def f_kmm(t, sl=sl, pp=pp):
                    for kc in range(NCH):
                        t.matmul(P[0 + pp][:], wsl[:, kc * 128:(kc + 1) * 128],
                                 xn_t[kc][:, sl], start=(kc == 0), stop=(kc == NCH - 1))
                    last = None
                    for kc in range(NCH):
                        last = t.matmul(P[2 + pp][:], wsl[:, 1024 + kc * 128:1024 + (kc + 1) * 128],
                                        xn_t[kc][:, sl], start=(kc == 0), stop=(kc == NCH - 1))
                    return last
                op_kmm = sd.add("tensor", f_kmm,
                                deps=[op_wk, op_xn, prev_hp.get(("kv", pp))],
                                name=f"kmm{hp}_{tt}")

                def f_kev(s, sl=sl, pp=pp):
                    s.copy(kpkh[0][0:64, sl], P[0 + pp][0:64, :])
                    s.copy(kpkh[1][0:64, sl], P[0 + pp][64:128, :])
                    return s.copy(vfm[:, sl], P[2 + pp][:])
                op_kev = sd.add("scalar", f_kev, deps=[op_kmm], name=f"kev{hp}_{tt}")
                prev_hp[("kv", pp)] = op_kev
                kv_evs.append(op_kev)

            def f_null(v, hp=hp):
                v.memset(kpkh[0][64:65, 0:KCOL], 1.0)
                v.memset(kpkh[1][64:65, 0:KCOL], 1.0)
                v.memset(kpkh[0][0:64, 4096:KCOL], 0.0)
                v.memset(kpkh[1][0:64, 4096:KCOL], 0.0)
                v.memset(vfm[:, 4096:KCOL], 0.0)
                v.tensor_copy(kpkh[0][0:64, 4096:4098], nk_s[0:64, hp * 2:hp * 2 + 2])
                v.tensor_copy(kpkh[1][0:64, 4096:4098], nk_s[64:128, hp * 2:hp * 2 + 2])
                return v.tensor_copy(vfm[:, 4096:4098], nv_s[:, hp * 2:hp * 2 + 2])
            op_null = sd.add("vector", f_null,
                             deps=[op_ldc, prev_hp.get("attn_done"),
                                   prev_hp.get("vT_done")],
                             name=f"null{hp}")

            nsq_ops = []
            for ti in range(9):
                w = 512 if ti < 8 else 256
                sl = slice(ti * 512, ti * 512 + w)
                pp = ti % 2

                def f_ksq(s, sl=sl, pp=pp, w=w):
                    s.activation(sq[0:64, pp * 512:pp * 512 + w], kpkh[0][0:64, sl],
                                 AF.Square)
                    return s.activation(sq[64:128, pp * 512:pp * 512 + w],
                                        kpkh[1][0:64, sl], AF.Square)
                op_ksq = sd.add("scalar", f_ksq,
                                deps=[op_null if ti >= 8 else kv_evs[ti],
                                      prev_hp.get("attn_done"),
                                      nsq_ops[-2] if ti >= 2 else None],
                                name=f"ksq{hp}_{ti}")

                def f_kns(t, sl=sl, pp=pp, w=w):
                    return t.matmul(P[4][0:2, 0:w], bd2_s[:], sq[:, pp * 512:pp * 512 + w],
                                    start=True, stop=True)
                op_kns = sd.add("tensor", f_kns, deps=[op_ksq, op_bd2,
                                                       nsq_ops[-1] if ti >= 1 else None],
                                name=f"kns{hp}_{ti}")

                def f_kne(v, sl=sl, w=w):
                    return v.tensor_copy(nrow[:, sl], P[4][0:2, 0:w])
                op_kne = sd.add("vector", f_kne, deps=[op_kns], name=f"kne{hp}_{ti}")
                nsq_ops.append(op_kne)

            def f_krn(v):
                return v.tensor_scalar_max(nrow[:], nrow[:], 1e-24)
            op_krn = sd.add("vector", f_krn, deps=nsq_ops, name=f"krn{hp}")

            def f_krs(s):
                return s.activation(nrow[:], nrow[:], AF.Sqrt)
            op_krs = sd.add("scalar", f_krs, deps=[op_krn], name=f"krs{hp}")

            def f_krr(v):
                return v.reciprocal(nrow[:], nrow[:])
            op_krr = sd.add("vector", f_krr, deps=[op_krs], name=f"krr{hp}")

            khat_ops = []
            for ti in range(9):
                w = 512 if ti < 8 else 256
                sl = slice(ti * 512, ti * 512 + w)
                pp = ti % 2

                def f_rnb(t, sl=sl, pp=pp, w=w):
                    return t.matmul(P[0 + pp][:, 0:w], ksd_s[:], nrow[:, sl],
                                    start=True, stop=True)
                op_rnb = sd.add("tensor", f_rnb,
                                deps=[op_krr, op_ldc,
                                      khat_ops[-2] if ti >= 2 else None],
                                name=f"rnb{hp}_{ti}")

                def f_kh(v, sl=sl, pp=pp, w=w):
                    return v.tensor_tensor(kpk[:, sl], kpk[:, sl], P[0 + pp][:, 0:w],
                                           ALU.mult)
                op_kh = sd.add("vector", f_kh, deps=[op_rnb], name=f"kh{hp}_{ti}")
                khat_ops.append(op_kh)

            def f_vone(v):
                vv = vT[:].rearrange("p (c k) -> p c k", k=130)
                v.memset(vv[:, :, 64:65], 1.0)
                return v.memset(vv[:, :, 129:130], 1.0)
            op_vone = sd.add("vector", f_vone, deps=[prev_hp.get("attn_done")],
                             name=f"vone{hp}")

            vt_evs = []
            prev_vt = None
            for tc in range(NKC):
                def f_vtr(t, tc=tc):
                    return t.transpose(PT[:], vfm[:, tc * 128:(tc + 1) * 128],
                                       identb_s[:])
                op_vtr = sd.add("tensor", f_vtr,
                                deps=[op_null, prev_vt] + kv_evs,
                                name=f"vtr{hp}_{tc}")

                def f_vte(v, tc=tc):
                    v.tensor_copy(vT[:, tc * 130:tc * 130 + 64], PT[:, 0:64])
                    return v.tensor_copy(vT[:, tc * 130 + 65:tc * 130 + 129],
                                         PT[:, 64:128])
                op_vte = sd.add("vector", f_vte, deps=[op_vtr, op_vone],
                                name=f"vte{hp}_{tc}")
                prev_vt = op_vte
                vt_evs.append(op_vte)
            prev_hp["vT_done"] = vt_evs[-1]

            def f_wq(g, put, hp=hp):
                put(g.dma_start(
                    out=wsl[:, 0:1024].rearrange("p (kc o) -> p kc o", kc=NCH),
                    in_=wq_t[hp].rearrange("(kc p) o -> p kc o", p=128)))
            op_wq = sd.dma("gpsimd", f_wq, ndma=1, deps=[kv_evs[-1]],
                           name=f"wq{hp}", stream="wsl2")

            q_evs = []
            for tt in range(NQT):
                sl = slice(tt * 512, (tt + 1) * 512)
                pp = tt % 2

                def f_qmm(t, sl=sl, pp=pp):
                    last = None
                    for kc in range(NCH):
                        last = t.matmul(P[0 + pp][:], wsl[:, kc * 128:(kc + 1) * 128],
                                        xn_t[kc][:, sl], start=(kc == 0), stop=(kc == NCH - 1))
                    return last
                op_qmm = sd.add("tensor", f_qmm,
                                deps=[op_wq, op_xn, prev_hp.get(("q", pp)),
                                      khat_ops[-1] if tt == 0 else None],
                                name=f"qmm{hp}_{tt}")

                def f_qsq(s, sl=sl, pp=pp):
                    return s.activation(sq[:, pp * 512:(pp + 1) * 512], P[0 + pp][:],
                                        AF.Square)
                op_qsq = sd.add("scalar", f_qsq, deps=[op_qmm,
                                                       prev_hp.get(("qs", pp))],
                                name=f"qsq{hp}_{tt}")

                qn_ps = P[4] if pp == 0 else P[3]

                def f_qns(t, sl=sl, pp=pp, qn_ps=qn_ps):
                    return t.matmul(qn_ps[0:2, :], bd2_s[:],
                                    sq[:, pp * 512:(pp + 1) * 512],
                                    start=True, stop=True)
                op_qns = sd.add("tensor", f_qns,
                                deps=[op_qsq, prev_hp.get(("qnev", pp)),
                                      prev_hp.get(("qh", pp))],
                                name=f"qns{hp}_{tt}")

                def f_qne(v, sl=sl, pp=pp, qn_ps=qn_ps):
                    return v.tensor_copy(qn2[:, sl], qn_ps[0:2, :])
                op_qne = sd.add("vector", f_qne, deps=[op_qns], name=f"qne{hp}_{tt}")
                prev_hp[("qnev", pp)] = op_qne

                def f_qev(s, sl=sl, pp=pp):
                    s.copy(scrq[:, (2 * pp) * 512:(2 * pp + 1) * 512],
                           P[0 + pp][0:64, :])
                    return s.copy(scrq[:, (2 * pp + 1) * 512:(2 * pp + 2) * 512],
                                  P[0 + pp][64:128, :])
                op_qev = sd.add("scalar", f_qev, deps=[op_qmm], name=f"qev{hp}_{tt}")

                def f_qrs(s, sl=sl):
                    return s.activation(qn2[:, sl], qn2[:, sl], AF.Sqrt)
                op_qrs = sd.add("scalar", f_qrs, deps=[op_qne], name=f"qrs{hp}_{tt}")

                def f_qrr(v, sl=sl):
                    return v.reciprocal(qn2[:, sl], qn2[:, sl])
                op_qrr = sd.add("vector", f_qrr, deps=[op_qrs], name=f"qrr{hp}_{tt}")

                def f_qrb(t, sl=sl, pp=pp):
                    t.matmul(P[2 + pp][0:64, :], qsd_s[:, 0:64], qn2[:, sl],
                             start=True, stop=True)
                    return t.matmul(P[5][0:64, :], qsd_s[:, 64:128], qn2[:, sl],
                                    start=True, stop=True)
                op_qrb = sd.add("tensor", f_qrb,
                                deps=[op_qrr, op_ldc, prev_hp.get(("qh", pp)),
                                      prev_hp.get("qhlast"), vt_evs[-1]],
                                name=f"qrb{hp}_{tt}")

                def f_qh(v, sl=sl, pp=pp):
                    v.tensor_tensor(qpkh[0][0:64, sl],
                                    scrq[:, (2 * pp) * 512:(2 * pp + 1) * 512],
                                    P[2 + pp][0:64, :], ALU.mult)
                    return v.tensor_tensor(qpkh[1][0:64, sl],
                                           scrq[:, (2 * pp + 1) * 512:(2 * pp + 2) * 512],
                                           P[5][0:64, :], ALU.mult)
                op_qh = sd.add("vector", f_qh, deps=[op_qrb, op_qev], name=f"qh{hp}_{tt}")
                prev_hp["qhlast"] = op_qh
                prev_hp[("q", pp)] = op_qh
                prev_hp[("qs", pp)] = op_qsq
                prev_hp[("qh", pp)] = op_qh
                q_evs.append(op_qh)

            attn_done_ops = []
            for hloc in range(2):
                h = hp * 2 + hloc
                rbase = hloc * 64
                for qt in range(NQT):
                    qsl = slice(qt * 512, (qt + 1) * 512)
                    par = (h * NQT + qt) % 2
                    kccs = list(range(0, min(4 * qt + 4, 32))) + [32]

                    def f_nb(sync, put, h=h, qt=qt):
                        put(sync.dma_start(out=nbf[0:1, :],
                                           in_=brow0[0:1, h * S + qt * 512:h * S + (qt + 1) * 512]))
                    op_nb = sd.dma("sync", f_nb, ndma=1,
                                   deps=[prev_hp.get("nbuse")],
                                   name=f"nb{h}_{qt}", stream="nb")

                    def f_nbc(v):
                        return v.tensor_scalar(nb[0:1, :], nbf[0:1, :],
                                               tiny[0:1, 3:4], None, ALU.add)
                    op_nbc = sd.add("vector", f_nbc, deps=[op_nb, op_qk,
                                                           prev_hp.get("nbuse")],
                                    name=f"nbc{h}_{qt}")

                    pv_last = None
                    step_ops = []
                    for si, kcc in enumerate(kccs):
                        spp = si % 2
                        is_diag = (4 * qt <= kcc <= 4 * qt + 3) and kcc < 32

                        def f_smm(t, kcc=kcc, qsl=qsl, spp=spp, is_diag=is_diag,
                                  qt=qt, rbase=rbase, par=par):
                            first = True
                            if is_diag:
                                dv = kcc - 4 * qt
                                off = 512 - dv * 128
                                t.matmul(P[0 + spp][:], M1_s[:],
                                         M2_s[:, off:off + 512],
                                         start=True, stop=False)
                                first = False
                            t.matmul(P[0 + spp][:],
                                     kpk[rbase:rbase + 64, kcc * 128:(kcc + 1) * 128],
                                     qpk[rbase:rbase + 64, qsl],
                                     start=first, stop=False)
                            return t.matmul(P[0 + spp][:], onesbf_s[:],
                                            nb[0:1, :],
                                            start=False, stop=True)
                        dep_smm = [op_nbc, khat_ops[-1], q_evs[-1],
                                   step_ops[-2] if si >= 2 else None]
                        op_smm = sd.add("tensor", f_smm, deps=dep_smm,
                                        name=f"smm{h}_{qt}_{kcc}")
                        prev_hp["nbuse"] = op_smm

                        def f_exp(s, kcc=kcc, spp=spp, h=h):
                            hloc2 = h % 8
                            return s.activation(phat[:, spp * 512:(spp + 1) * 512],
                                                P[0 + spp][:], AF.Exp,
                                                bias=bc_s[:, hloc2 * NKC + kcc:hloc2 * NKC + kcc + 1])
                        op_exp = sd.add("scalar", f_exp,
                                        deps=[op_smm, op_ldc,
                                              step_ops[-1] if si >= 1 else None],
                                        name=f"exp{h}_{qt}_{kcc}")

                        def f_pv(t, kcc=kcc, spp=spp, si=si, hloc=hloc, par=par):
                            return t.matmul(PO[par][:],
                                            vT[:, kcc * 130 + hloc * 65:kcc * 130 + hloc * 65 + 65],
                                            phat[:, spp * 512:(spp + 1) * 512],
                                            start=(si == 0), stop=(si == len(kccs) - 1))
                        op_pv = sd.add("tensor", f_pv,
                                       deps=[op_exp, vt_evs[-1],
                                             prev_hp.get(("fin", par))],
                                       name=f"pv{h}_{qt}_{kcc}")
                        step_ops.append(op_pv)
                        pv_last = op_pv

                    def f_rs(v, par=par):
                        return v.reciprocal(rsr[64:65, :], PO[par][64:65, :])
                    op_rs = sd.add("vector", f_rs, deps=[pv_last], name=f"rs{h}_{qt}")

                    def f_rsb(t, par=par):
                        return t.matmul(P[4][0:64, :], ones65_s[64:65, 0:64],
                                        rsr[64:65, :], start=True, stop=True)
                    op_rsb = sd.add("tensor", f_rsb,
                                    deps=[op_rs, prev_hp.get("rsb")],
                                    name=f"rsb{h}_{qt}")
                    prev_hp["rsb"] = op_rsb

                    def f_oev(s, par=par):
                        return s.copy(ou[:], PO[par][0:64, :])
                    op_oev = sd.add("scalar", f_oev,
                                    deps=[pv_last, prev_hp.get("oudone")],
                                    name=f"oev{h}_{qt}")

                    def f_fin(v, h=h, qt=qt, qsl=qsl, rbase=rbase):
                        return v.tensor_tensor(ao_t[h // 2][rbase:rbase + 64, qsl],
                                               ou[:],
                                               P[4][0:64, :], ALU.mult)
                    op_fin = sd.add("vector", f_fin, deps=[op_oev, op_rsb],
                                    name=f"fin{h}_{qt}")
                    prev_hp["oudone"] = op_fin
                    prev_hp[("fin", par)] = op_fin
                    attn_done_ops.append(op_fin)
            prev_hp["attn_done"] = attn_done_ops[-1]
            prev_hp["wsl"] = q_evs[-1]
            oproj_deps.extend(attn_done_ops)

        out_ops = []
        prev_o = {}
        for oc in range(8):
            def f_wo(g, put, oc=oc):
                put(g.dma_start(
                    out=wsl[:, (oc % 2) * 512:(oc % 2) * 512 + 512]
                        .rearrange("p (kc o) -> p kc o", kc=4),
                    in_=wo_t[oc].rearrange("(kc p) o -> p kc o", p=128)))
            op_wo = sd.dma("gpsimd", f_wo, ndma=1,
                           deps=oproj_deps[-1:] + [prev_o.get(("mm", oc % 2))],
                           name=f"wo{oc}", stream=f"wo{oc % 2}")
            for tt in range(NQT):
                sl = slice(tt * 512, (tt + 1) * 512)
                pp = tt % 2

                def f_omm(t, oc=oc, sl=sl, pp=pp):
                    last = None
                    for kc in range(4):
                        last = t.matmul(P[0 + pp][:],
                                        wsl[:, (oc % 2) * 512 + kc * 128:(oc % 2) * 512 + (kc + 1) * 128],
                                        ao_t[kc][:, sl], start=(kc == 0), stop=(kc == 3))
                    return last
                op_omm = sd.add("tensor", f_omm,
                                deps=[op_wo, prev_o.get(("ev", pp))] + oproj_deps[-1:],
                                name=f"omm{oc}_{tt}")
                prev_o[("mm", oc % 2)] = op_omm

                def f_oev2(s, pp=pp):
                    return s.copy(oscr[:, pp * 512:(pp + 1) * 512], P[0 + pp][:])
                op_oev2 = sd.add("scalar", f_oev2, deps=[op_omm], name=f"oev2{oc}_{tt}")

                def f_odma(sync, put, oc=oc, sl=sl, pp=pp):
                    put(sync.dma_start(out=opart[oc * 128:(oc + 1) * 128, sl],
                                       in_=scr[:, pp * 512:(pp + 1) * 512]))
                op_od = sd.dma("sync", f_odma, ndma=1, deps=[op_oev2],
                               name=f"od{oc}_{tt}", stream="out")
                prev_o[("ev", pp)] = op_od
                out_ops.append(op_od)

        def fin2(g):
            return g.engine_nop()
        sd.add("gpsimd", fin2, deps=[out_ops[-1]], name="fin2")
        sd.emit(block)
    return nc


def l0_attn_inmaps(inp):
    x0 = np.asarray(inp["L0_attn_x"])
    wq = np.asarray(inp["L0_attn_w_q"])
    wkv = np.asarray(inp["L0_attn_w_kv"])
    wo = np.asarray(inp["L0_attn_w_out"])
    nkv = np.asarray(inp["L0_attn_null_kv"]).reshape(HEADS, NNK, 2, DH)
    slopes = alibi_slopes()
    maps = []
    import ml_dtypes
    bf = ml_dtypes.bfloat16
    M1 = np.tril(np.ones((128, 128), np.float64)).T
    M2 = np.zeros((128, 1024), np.float64)
    for k in range(128):
        lim = min(1024, k + 512)
        M2[k, :lim] = -1e9
    M1b = M1.astype(bf)
    M2b = M2.astype(bf)
    cstm = np.zeros((128, 4), np.float32); cstm[:, 0] = 1e-5
    for c in range(8):
        b = c // 2
        H0 = (c % 2) * 8
        hsl = slice(H0 * DH, H0 * DH + 512)
        bcols = np.zeros((128, 8 * NKC), np.float32)
        brow0 = np.zeros((1, 8 * S), np.float32)
        q = np.arange(S)
        for hl in range(8):
            sl_ = float(slopes[H0 + hl])
            for kcc in range(NKC):
                p = np.arange(128)
                colv = np.full(128, -1e9, np.float32)
                if kcc < 32:
                    cglob = kcc * 128 + p
                    colv = (sl_ * (cglob + 2)).astype(np.float32)
                elif kcc == 32:
                    colv[:] = -1e9
                    colv[0] = 0.0
                    colv[1] = sl_ * 1.0
                bcols[:, hl * NKC + kcc] = colv
            brow0[0, hl * S:(hl + 1) * S] = -(sl_ * (q + 2) + 2.0)
        nkh = nkv[H0:H0 + 8, :, 0, :]
        nvh = nkv[H0:H0 + 8, :, 1, :]
        nkT_ = np.zeros((128, 8), np.float32)
        nvT_ = np.zeros((128, 8), np.float32)
        for hpi in range(4):
            for hl in range(2):
                h = hpi * 2 + hl
                nkT_[hl * 64:(hl + 1) * 64, hpi * 2:hpi * 2 + 2] = nkh[h].T
                nvT_[hl * 64:(hl + 1) * 64, hpi * 2:hpi * 2 + 2] = nvh[h].T
        qsd_ = np.zeros((2, 128), np.float32)
        qsd_[0, 0:64] = inp["L0_attn_q_scale"] * 8.0 if False else 0
        qsd_ = np.zeros((2, 128), np.float32)
        qsd_[0, 0:64] = np.asarray(inp["L0_attn_q_scale"], np.float64) * 8.0
        qsd_[1, 64:128] = np.asarray(inp["L0_attn_q_scale"], np.float64) * 8.0
        ksd_ = np.zeros((2, 128), np.float32)
        ksd_[0, 0:64] = inp["L0_attn_k_scale"]
        ksd_[1, 64:128] = inp["L0_attn_k_scale"]
        maps.append(dict(
            x0f=np.ascontiguousarray(x0[b].T),
            wq_t=tile_lhsT(f32c(wq[:, hsl])),
            wk_t=tile_lhsT(f32c(wkv[:, hsl])),
            wv_t=tile_lhsT(f32c(wkv[:, D + H0 * DH:D + H0 * DH + 512])),
            wo_t=tile_lhsT(f32c(wo[hsl.start:hsl.start + 512, :])),
            gcol=f32c(np.asarray(inp["L0_attn_gamma"]).reshape(8, 128).T),
            nkT=nkT_, nvT=nvT_,
            qsrow=f32c(np.asarray(inp["L0_attn_q_scale"])[None, :]),
            ksrow=f32c(np.asarray(inp["L0_attn_k_scale"])[None, :]),
            qsd=qsd_, ksd=ksd_,
            bcols=bcols, brow0=brow0,
            M1c=M1b, M2v=M2b,
            onesbf=np.ones((1, 128), bf),
            ones1=np.ones((1, 128), np.float32),
            ones65d=np.ones((65, 128), np.float32),
            onescol=np.ones((128, 1), np.float32),
            identb=np.eye(128).astype(bf),
            cst=cstm,
        ))
    return maps


def kernel_multilaunch(**inputs):
    inp = inputs
    x0 = np.asarray(inp["L0_attn_x"], dtype=np.float32)

    nc1 = build_l0_attn()
    r1 = run_bass_kernel_spmd(nc1, l0_attn_inmaps(inp), core_ids=list(range(8)))
    oparts = [np.asarray(r1.results[c]["opart"]) for c in range(8)]

    x0_fm = [np.ascontiguousarray(x0[c // 2].T[:, (c % 2) * RPC:(c % 2 + 1) * RPC])
             for c in range(8)]
    oa_pc = [oparts[2 * (c // 2)][:, (c % 2) * RPC:(c % 2 + 1) * RPC] for c in range(8)]
    ob_pc = [oparts[2 * (c // 2) + 1][:, (c % 2) * RPC:(c % 2 + 1) * RPC] for c in range(8)]
    nc2 = build_l0_ffn()
    r2 = run_bass_kernel_spmd(nc2, l0_ffn_inmaps(x0_fm, oa_pc, ob_pc, inp),
                              core_ids=list(range(8)))
    x2f = [np.asarray(r2.results[c]["x2f"]) for c in range(8)]
    st3 = [np.asarray(r2.results[c]["stat3"]) for c in range(8)]

    x2_full = np.stack([np.concatenate([x2f[2 * b].T, x2f[2 * b + 1].T], axis=0)
                        for b in range(B)])
    mu3 = np.concatenate([st3[c][0] for c in range(8)])
    r3 = np.concatenate([st3[c][1] for c in range(8)])
    nc3 = build_l1_attn()
    r3r = run_bass_kernel_spmd(nc3, l1_attn_inmaps(x2_full, mu3, r3, inp),
                               core_ids=list(range(8)))
    y = np.zeros((B, S, D), np.float32)
    for c in range(8):
        b, ch0 = c // 2, (c % 2) * CHH
        y[b][:, ch0:ch0 + CHH] = np.asarray(r3r.results[c]["yout"])

    y_fm = [np.ascontiguousarray(y[c // 2, (c % 2) * RPC:(c % 2 + 1) * RPC, :].T)
            for c in range(8)]
    nc4 = build_l1_ffn()
    r4 = run_bass_kernel_spmd(nc4, l1_ffn_inmaps(x2f, y_fm, inp),
                              core_ids=list(range(8)))
    out = np.stack([np.concatenate([np.asarray(r4.results[2 * b]["x4f"]).T,
                                    np.asarray(r4.results[2 * b + 1]["x4f"]).T], axis=0)
                    for b in range(B)])
    return np.ascontiguousarray(out.astype(np.float32))


class Arena:
    def __init__(self, ap):
        self.ap = ap
        self.NW = ap.shape[1]
        self.off = 0

    def reset(self):
        self.off = 0

    def alloc(self, p, n, dt, base=0):
        w32 = (n + 1) // 2 if dt == BF16 else n
        assert self.off + w32 <= self.NW, f"arena overflow {self.off}+{w32}"
        sl = self.ap[:][base:base + p, self.off:self.off + w32]
        self.off += w32
        return sl.bitcast(BF16)[:, 0:n] if dt == BF16 else sl


def build_fused():
    nc = bass.Bass("TRN2", target_bir_lowering=False)
    dm = {}
    def din(name, shape, dt):
        dm[name] = nc.dram_tensor(name, shape, dt, kind="ExternalInput").ap()
        return dm[name]
    def dint(name, shape, dt):
        dm[name] = nc.dram_tensor(name, shape, dt, kind="Internal").ap()
        return dm[name]

    x0f = din("x0f", [D, S], F32)
    wq_t = din("wq_t", [8, D, 128], F32)
    wk_t = din("wk_t", [8, D, 128], F32)
    wv_t = din("wv_t", [8, D, 128], F32)
    wo_t = din("wo_t", [8, D, 128], F32)
    gcol = din("gcol", [128, 8], F32)
    nkT = din("nkT", [128, 16], F32)
    nvT = din("nvT", [128, 16], F32)
    qsrow = din("qsrow", [1, 64], F32)
    ksrow = din("ksrow", [1, 64], F32)
    qsd = din("qsd", [2, 128], F32)
    ksd = din("ksd", [2, 128], F32)
    bcq = din("bcq", [128, 16 * 6 * NKC], F32)
    brow0 = din("brow0", [1, 16 * 3072], F32)
    M1c = din("M1c", [128, 128], BF16)
    M2Av = din("M2Av", [128, 1024], BF16)
    M2Bv = din("M2Bv", [128, 1024], BF16)
    onesbf = din("onesbf", [1, 128], BF16)
    ones1 = din("ones1", [1, 128], F32)
    ones65d = din("ones65d", [65, 128], F32)
    onescol = din("onescol", [128, 1], F32)
    onescolb = din("onescolb", [128, 1], BF16)
    identb = din("identb", [128, 128], BF16)
    idf32 = din("idf32", [128, 128], F32)
    hmask = din("hmask", [128, 1], F32)
    cst = din("cst", [128, 4], F32)
    x0q = din("x0q", [D, 3072], F32)
    bw_t = din("bw_t", [8, D, 128], F32)
    bias_c = din("bias_c", [128, 8], F32)
    A_t = din("A_t", [D, 128], F32)
    B_t = din("B_t", [8, 128, 128], F32)
    rwt = din("rwt", [D, 8], F32)
    rbias = din("rbias", [128, 8], F32)
    Gm = din("Gm", [8, 128], F32)
    abT = din("abT", [64, 2 * D], F32)
    h0row = din("h0row", [1, D], F32)
    FWD = din("FWD", [1024, NBF], BF16)
    IDM = din("IDM", [NBF, 1024], BF16)
    COEFC = din("COEFC", [64, NBF], BF16)
    w12_t = din("w12_t", [32, 2 * D, 128], F32)
    h1_t = din("h1_t", [32, 4096, 128], F32)
    h2_t = din("h2_t", [32, 4096, 128], F32)
    w3_t = din("w3_t", [8, 4096, 128], F32)
    w1c = din("w1c", [128, 32], F32)
    w2c = din("w2c", [128, 32], F32)
    h1c = din("h1c", [128, 32], F32)
    h2c = din("h2c", [128, 32], F32)
    w3c = din("w3c", [128, 8], F32)
    x1sc = dint("x1sc", [D, 3072], F32)
    x2sc = dint("x2sc", [D, RPC], F32)
    u3d = dint("u3d", [3072, D], BF16)
    ya = dint("ya", [D, RPC], BF16)
    osc = dint("osc", [D, 3072], F32)
    xsc = dint("xsc", [D, RPC], F32)
    u4sc = dint("u4sc", [D, RPC], BF16)
    x4f = nc.dram_tensor("x4f", [D, RPC], F32, kind="ExternalOutput").ap()
    x0fb = dint("x0fb", [D, S], BF16)
    x0qb = dint("x0qb", [D, 3072], BF16)

    PAIRS = [[0, 1], [2, 3], [4, 5], [6, 7]]

    with ExitStack() as ST:
        E = ST.enter_context
        ARW = 52856
        AR = E(nc.sbuf_tensor("AR", [128, ARW], F32))
        ones65_s = E(nc.sbuf_tensor("ones65_s", [65, 128], F32))
        onescol_s = E(nc.sbuf_tensor("onescol_s", [128, 1], F32))
        onescolb_s = E(nc.sbuf_tensor("onescolb_s", [128, 1], BF16))
        identb_s = E(nc.sbuf_tensor("identb_s", [128, 128], BF16))
        cst_s = E(nc.sbuf_tensor("cst_s", [128, 4], F32))
        colsb = E(nc.sbuf_tensor("colsb", [128, 32 * 4 + 8], F32))
        P = [E(nc.psum_tensor(f"P{i}", [128, 512], F32)) for i in range(6)]
        PO = [E(nc.psum_tensor(f"PO{i}", [65, 512], F32)) for i in range(2)]
        block = E(nc.Block())

        PT = P[5][:][:, 0:64].bitcast(BF16)
        S0 = P[4][:][0:1, :]
        S1 = P[5][:][0:1, :]
        ar = Arena(AR)
        sd = Sched(nc)

        def ld_consts(sync, put):
            for t, d_ in ((ones65_s, ones65d), (onescol_s, onescol),
                          (onescolb_s, onescolb), (identb_s, identb),
                          (cst_s, cst)):
                put(sync.dma_start(out=t[:], in_=d_[:]))
        op_ldc0 = sd.dma("sync", ld_consts, ndma=5, name="ldc0", stream="ldc")

        def f_pca(g, put):
            for kc in range(4):
                put(g.dma_start(out=x0fb[kc * 128:(kc + 1) * 128, :],
                                in_=x0f[kc * 128:(kc + 1) * 128, :]))
            for kc in range(4):
                put(g.dma_start(out=x0qb[kc * 128:(kc + 1) * 128, :],
                                in_=x0q[kc * 128:(kc + 1) * 128, :]))
        op_pca = sd.dma("gpsimd", f_pca, ndma=8, name="pca", stream="pc0")

        def f_pcb(g, put):
            for kc in range(4, 8):
                put(g.dma_start(out=x0fb[kc * 128:(kc + 1) * 128, :],
                                in_=x0f[kc * 128:(kc + 1) * 128, :]))
            for kc in range(4, 8):
                put(g.dma_start(out=x0qb[kc * 128:(kc + 1) * 128, :],
                                in_=x0q[kc * 128:(kc + 1) * 128, :]))
        op_pcb = sd.dma("gpsimd", f_pcb, ndma=8, name="pcb", stream="pc1")

        consts = {"ones1": ones65_s[0:1, :], "onescol": onescol_s[:],
                  "onescolb": onescolb_s[:], "eps1": cst_s[0:1, 0:1],
                  "onesat": None}

        ar.reset()
        gc_s = ar.alloc(128, 8, F32)
        nk_s = ar.alloc(128, 16, F32)
        nv_s = ar.alloc(128, 16, F32)
        qs_s = ar.alloc(1, 64, F32)
        ks_s = ar.alloc(1, 64, F32)
        qsd_s = ar.alloc(2, 128, F32)
        ksd_s = ar.alloc(2, 128, F32)
        bcq2 = ar.alloc(128, 2 * NKC, F32)
        M1_s = ar.alloc(128, 128, BF16)
        M2A_s = ar.alloc(128, 1024, BF16)
        M2B_s = ar.alloc(128, 1024, BF16)
        onesbf_s = ar.alloc(1, 128, BF16)
        bd2_s = ar.alloc(128, 2, F32)
        tiny = ar.alloc(1, 8, F32)
        hmask_s = ar.alloc(128, 1, F32)
        idf32_p1 = ar.alloc(1, 1, F32)

        def ld_p1c(sync, put):
            for t, d_ in ((gc_s, gcol), (nk_s, nkT), (nv_s, nvT), (qs_s, qsrow),
                          (ks_s, ksrow), (qsd_s, qsd), (ksd_s, ksd),
                          (M1_s, M1c), (M2A_s, M2Av), (M2B_s, M2Bv),
                          (onesbf_s, onesbf), (hmask_s, hmask),
                          (idf32_p1, idf32[0:1, 0:1])):
                put(sync.dma_start(out=t, in_=d_[:]))
        op_ldp1 = sd.dma("sync", ld_p1c, ndma=13, deps=[op_ldc0], name="ldp1",
                         stream="ldc")
        op_ldc = op_ldp1

        QTOK = 3072
        NQTL = 6
        ao = ar.alloc(128, NCH * QTOK, BF16)
        arena2 = ar.alloc(128, 4352, F32)
        xrl = arena2[:, 0:4096]
        kvb = arena2.bitcast(BF16)
        vfm = kvb[:, 0:KCOL]
        kpkh = [kvb[0:65, KCOL:2 * KCOL],
                ar.alloc(65, KCOL, BF16)]
        xrlb = ar.alloc(128, 2 * 8 * 512, BF16)
        vT = ar.alloc(128, NKC * 130, BF16)
        qpkh = [ar.alloc(65, QTOK, BF16),
                ar.alloc(65, QTOK, BF16)]
        phat = ar.alloc(128, 4 * 512, BF16)
        sq = ar.alloc(128, 8 * 512, F32)
        rowsg = ar.alloc(1, S, F32)
        rowsq = ar.alloc(1, QTOK, F32)
        rowx = ao[:].bitcast(F32)[0:1, 0:S]
        rcolk = ar.alloc(128, NKC, F32)
        crow = ar.alloc(1, 3 * 128, F32)
        nrow2 = ar.alloc(2, KCOL, F32)
        qn2 = nrow2[:, 0:QTOK]
        nbst = ar.alloc(1, QTOK, BF16)
        ou = ar.alloc(64, 512, BF16)
        rsr = ar.alloc(65, 512, F32)
        scrq = ar.alloc(64, 4 * 512, F32)
        wsl = ar.alloc(128, 3 * 8 * 128, BF16)

        ao_t = [ao[:, ch * QTOK:(ch + 1) * QTOK] for ch in range(NCH)]

        def f_bd2(v):
            v.memset(bd2_s[:], 0.0)
            v.memset(bd2_s[0:64, 0:1], 1.0)
            return v.memset(bd2_s[64:128, 1:2], 1.0)
        op_bd2 = sd.add("vector", f_bd2, name="bd2")

        xslot = [xrl, xrlb[:].bitcast(F32)[:, 0:4096]]
        st_state = {}

        def emit_dln_stats(tag, src_dram, ntok, murow, Rrow, scrrow, deps0):
            scrrow = scrrow[:, 0:ntok]
            n_tt = ntok // 512
            st_evs = []
            prev_ev = None
            for tt in range(n_tt):
                sl = slice(tt * 512, (tt + 1) * 512)
                sp2 = tt % 2
                xs = xslot[sp2]
                sqs = sq[:, sp2 * 2048:(sp2 + 1) * 2048]

                def f_rlA(sync, put, sl=sl, xs=xs):
                    for kc in range(4):
                        put(sync.dma_start(out=xs[:, kc * 512:(kc + 1) * 512],
                                           in_=src_dram[kc * 128:(kc + 1) * 128, sl]))
                op_rlA = sd.dma("sync", f_rlA, ndma=4,
                                deps=[st_state.get(("x", sp2))] + deps0,
                                name=f"{tag}rlA{tt}", stream=f"srl{sp2}")

                def f_rlB(g, put, sl=sl, xs=xs):
                    for kc in range(4, NCH):
                        put(g.dma_start(out=xs[:, kc * 512:(kc + 1) * 512],
                                        in_=src_dram[kc * 128:(kc + 1) * 128, sl]))
                op_rlB = sd.dma("gpsimd", f_rlB, ndma=4,
                                deps=[st_state.get(("x", sp2))] + deps0,
                                name=f"{tag}rlB{tt}", stream=f"srl2{sp2}")

                def f_sqa(s, xs=xs, sqs=sqs):
                    last = None
                    for kc in range(4):
                        last = s.activation(sqs[:, kc * 512:(kc + 1) * 512],
                                            xs[:, kc * 512:(kc + 1) * 512], AF.Square)
                    return last
                op_sqa = sd.add("scalar", f_sqa,
                                deps=[op_rlA, st_state.get(("sq", sp2))],
                                name=f"{tag}sqa{tt}")

                def f_mma(t, xs=xs, sqs=sqs):
                    for kc in range(NCH):
                        t.matmul(P[0][0:1, :], onescol_s[:], xs[:, kc * 512:(kc + 1) * 512],
                                 start=(kc == 0), stop=(kc == NCH - 1))
                    last = None
                    for kc in range(4):
                        last = t.matmul(P[1][0:1, :], onescol_s[:],
                                        sqs[:, kc * 512:(kc + 1) * 512],
                                        start=(kc == 0), stop=False)
                    return last
                op_mma = sd.add("tensor", f_mma,
                                deps=[op_rlA, op_rlB, op_sqa, op_ldc, prev_ev],
                                name=f"{tag}mma{tt}")

                def f_sqb(s, xs=xs, sqs=sqs):
                    last = None
                    for kc in range(4):
                        last = s.activation(sqs[:, kc * 512:(kc + 1) * 512],
                                            xs[:, (4 + kc) * 512:(5 + kc) * 512], AF.Square)
                    return last
                op_sqb = sd.add("scalar", f_sqb, deps=[op_mma], name=f"{tag}sqb{tt}")
                st_state[("x", sp2)] = op_sqb

                def f_mmb(t, sqs=sqs):
                    last = None
                    for kc in range(4):
                        last = t.matmul(P[1][0:1, :], onescol_s[:],
                                        sqs[:, kc * 512:(kc + 1) * 512],
                                        start=False, stop=(kc == 3))
                    return last
                op_mmb = sd.add("tensor", f_mmb, deps=[op_sqb], name=f"{tag}mmb{tt}")
                st_state[("sq", sp2)] = op_mmb

                def f_ev(v, sl=sl):
                    v.tensor_copy(murow[:, sl], P[0][0:1, :])
                    return v.tensor_copy(Rrow[:, sl], P[1][0:1, :])
                prev_ev = sd.add("vector", f_ev, deps=[op_mmb], name=f"{tag}ev{tt}")
                st_evs.append(prev_ev)

            def f_rm1(v):
                v.tensor_scalar(murow, murow, 1.0 / D, None, ALU.mult)
                v.tensor_scalar(Rrow, Rrow, 1.0 / D, None, ALU.mult)
                v.tensor_tensor(scrrow, murow, murow, ALU.mult)
                return v.tensor_tensor(Rrow, Rrow, scrrow, ALU.subtract)
            op_rm1 = sd.add("vector", f_rm1, deps=st_evs, name=f"{tag}rm1")

            def f_rm2(s):
                return s.activation(scrrow, Rrow, AF.Sqrt, bias=cst_s[0:1, 0:1])
            op_rm2 = sd.add("scalar", f_rm2, deps=[op_rm1], name=f"{tag}rm2")

            def f_rm3(v):
                v.reciprocal(scrrow, scrrow)
                v.tensor_tensor(Rrow, Rrow, scrrow, ALU.mult)
                return v.tensor_tensor(Rrow, Rrow, scrrow, ALU.mult)
            op_rm3 = sd.add("vector", f_rm3, deps=[op_rm2], name=f"{tag}rm3")

            def f_rm4(s):
                return s.activation(Rrow, Rrow, AF.Sqrt, bias=cst_s[0:1, 0:1])
            op_rm4 = sd.add("scalar", f_rm4, deps=[op_rm3], name=f"{tag}rm4")

            def f_rm5(v):
                v.reciprocal(Rrow, Rrow)
                v.tensor_tensor(Rrow, Rrow, scrrow, ALU.mult)
                return v.tensor_scalar(murow, murow, -1.0, None, ALU.mult)
            return sd.add("vector", f_rm5, deps=[op_rm4], name=f"{tag}rm5")

        mug = rowsg[0:1, 0:S]
        Rg = ao[:].bitcast(F32)[0:1, S:2 * S]
        op_stg = emit_dln_stats("sg", x0f, S, mug, Rg, rowx[0:1, :], [])
        muq = rowsq[0:1, 0:QTOK]
        Rq = ao[:].bitcast(F32)[0:1, 2 * S:2 * S + QTOK]
        op_stq = emit_dln_stats("sq", x0q, QTOK, muq, Rq, rowx[0:1, :], [op_stg])

        rc_ops = []
        prev_rc = None
        for tc in range(32):
            def f_rt(t, tc=tc):
                return t.transpose(P[2][:, 0:1], Rg[:, tc * 128:(tc + 1) * 128],
                                   idf32_p1[0:1, 0:1])
            op_rt = sd.add("tensor", f_rt, deps=[op_stg, prev_rc], name=f"rt{tc}")

            def f_rte(v, tc=tc):
                return v.tensor_copy(rcolk[:, tc:tc + 1], P[2][:, 0:1])
            op_rte = sd.add("vector", f_rte, deps=[op_rt], name=f"rte{tc}")
            prev_rc = op_rte
            rc_ops.append(op_rte)

        def f_rcn(v):
            return v.memset(rcolk[:, 32:NKC], 1.0)
        op_rcn = sd.add("vector", f_rcn, name="rcn")
        rc_ops.append(op_rcn)

        def f_qk1(v):
            v.reduce_max(tiny[0:1, 0:1], qs_s[:], axis=AX.X, apply_absolute_value=True)
            v.reduce_max(tiny[0:1, 1:2], ks_s[:], axis=AX.X, apply_absolute_value=True)
            v.tensor_tensor(tiny[0:1, 2:3], tiny[0:1, 0:1], tiny[0:1, 1:2], ALU.mult)
            return v.tensor_scalar(tiny[0:1, 3:4], tiny[0:1, 2:3], -8.0, None, ALU.mult)
        op_qk = sd.add("vector", f_qk1, deps=[op_ldc], name="qk1")

        prev_hp = {}
        oproj_deps = []
        for hp in range(8):
            def f_wk(g, put, hp=hp):
                put(g.dma_start(
                    out=wsl[:, 0:1024].rearrange("p (kc o) -> p kc o", kc=NCH),
                    in_=wk_t[hp].rearrange("(kc p) o -> p kc o", p=128)))
                put(g.dma_start(
                    out=wsl[:, 1024:2048].rearrange("p (kc o) -> p kc o", kc=NCH),
                    in_=wv_t[hp].rearrange("(kc p) o -> p kc o", p=128)))
            op_wk = sd.dma("gpsimd", f_wk, ndma=2, deps=[prev_hp.get("wsl")],
                           name=f"wk{hp}", stream="wsl")

            def f_wg(s, hp=hp):
                last = None
                for kc in range(NCH):
                    last = s.mul(wsl[:, kc * 128:(kc + 1) * 128],
                                 wsl[:, kc * 128:(kc + 1) * 128], gc_s[:, kc:kc + 1])
                for kc in range(NCH):
                    last = s.mul(wsl[:, 1024 + kc * 128:1024 + (kc + 1) * 128],
                                 wsl[:, 1024 + kc * 128:1024 + (kc + 1) * 128],
                                 gc_s[:, kc:kc + 1])
                return last
            op_wg = sd.add("scalar", f_wg, deps=[op_wk, op_ldc], name=f"wg{hp}")

            def f_cr(t, hp=hp):
                for kc in range(NCH):
                    t.matmul(P[2][0:1, 0:128], onescolb_s[:],
                             wsl[:, kc * 128:(kc + 1) * 128],
                             start=(kc == 0), stop=(kc == NCH - 1))
                last = None
                for kc in range(NCH):
                    last = t.matmul(P[3][0:1, 0:128], onescolb_s[:],
                                    wsl[:, 1024 + kc * 128:1024 + (kc + 1) * 128],
                                    start=(kc == 0), stop=(kc == NCH - 1))
                return last
            op_cr = sd.add("tensor", f_cr, deps=[op_wg, prev_hp.get("crow"), prev_rc],
                           name=f"cr{hp}")

            def f_cre(v):
                v.tensor_copy(crow[0:1, 0:128], P[2][0:1, 0:128])
                return v.tensor_copy(crow[0:1, 128:256], P[3][0:1, 0:128])
            op_cre = sd.add("vector", f_cre, deps=[op_cr], name=f"cre{hp}")
            prev_hp["crow"] = op_cre


            kv_evs = []
            for tt in range(NQT):
                sl = slice(tt * 512, (tt + 1) * 512)
                pp = tt % 2
                xsl = xrlb[:, pp * 4096:(pp + 1) * 4096]

                def f_xlA(sync, put, sl=sl, xsl=xsl):
                    for kc in range(4):
                        put(sync.dma_start(out=xsl[:, kc * 512:(kc + 1) * 512],
                                           in_=x0fb[kc * 128:(kc + 1) * 128, sl]))
                op_xlA = sd.dma("sync", f_xlA, ndma=4,
                                deps=[prev_hp.get(("kv", pp)), op_stq, op_pca, op_pcb,
                                      prev_hp.get(("kmmA", pp)),
                                      prev_hp.get(("qmmA", pp))],
                                name=f"kvxA{hp}_{tt}", stream=f"srlA{pp}")

                def f_xlB(g, put, sl=sl, xsl=xsl):
                    for kc in range(4, NCH):
                        put(g.dma_start(out=xsl[:, kc * 512:(kc + 1) * 512],
                                        in_=x0fb[kc * 128:(kc + 1) * 128, sl]))
                op_xlB = sd.dma("gpsimd", f_xlB, ndma=4,
                                deps=[prev_hp.get(("kv", pp)), op_stq, op_pca, op_pcb,
                                      prev_hp.get(("kmmB", pp)),
                                      prev_hp.get(("qmmB", pp))],
                                name=f"kvxB{hp}_{tt}", stream=f"srlB{pp}")

                def f_kmmA(t, sl=sl, pp=pp, xsl=xsl):
                    for kc in range(4):
                        t.matmul(P[0 + pp][:], wsl[:, kc * 128:(kc + 1) * 128],
                                 xsl[:, kc * 512:(kc + 1) * 512],
                                 start=(kc == 0), stop=False)
                    last = None
                    for kc in range(4):
                        last = t.matmul(P[2 + pp][:],
                                        wsl[:, 1024 + kc * 128:1024 + (kc + 1) * 128],
                                        xsl[:, kc * 512:(kc + 1) * 512],
                                        start=(kc == 0), stop=False)
                    return last
                op_kmmA = sd.add("tensor", f_kmmA,
                                 deps=[op_xlA, op_wg, op_cre, op_stg,
                                       prev_hp.get(("kv", pp))],
                                 name=f"kmmA{hp}_{tt}")

                def f_kmmB(t, sl=sl, pp=pp, xsl=xsl):
                    for kc in range(4, NCH):
                        t.matmul(P[0 + pp][:], wsl[:, kc * 128:(kc + 1) * 128],
                                 xsl[:, kc * 512:(kc + 1) * 512],
                                 start=False, stop=False)
                    t.matmul(P[0 + pp][:], crow[0:1, 0:128], mug[:, sl],
                             start=False, stop=True)
                    for kc in range(4, NCH):
                        t.matmul(P[2 + pp][:],
                                 wsl[:, 1024 + kc * 128:1024 + (kc + 1) * 128],
                                 xsl[:, kc * 512:(kc + 1) * 512],
                                 start=False, stop=False)
                    return t.matmul(P[2 + pp][:], crow[0:1, 128:256], mug[:, sl],
                                    start=False, stop=True)
                op_kmmB = sd.add("tensor", f_kmmB, deps=[op_xlB],
                                 name=f"kmmB{hp}_{tt}")

                def f_kev(s, sl=sl, pp=pp):
                    s.copy(kpkh[0][0:64, sl], P[0 + pp][0:64, :])
                    s.copy(kpkh[1][0:64, sl], P[0 + pp][64:128, :])
                    return s.copy(vfm[:, sl], P[2 + pp][:])
                op_kev = sd.add("scalar", f_kev, deps=[op_kmmB], name=f"kev{hp}_{tt}")
                prev_hp[("kv", pp)] = op_kev
                prev_hp[("kmmA", pp)] = op_kmmA
                prev_hp[("kmmB", pp)] = op_kmmB
                kv_evs.append(op_kev)

            def f_null(v, hp=hp):
                v.memset(kpkh[0][64:65, 0:KCOL], 1.0)
                v.memset(kpkh[1][64:65, 0:KCOL], 1.0)
                v.memset(kpkh[0][0:64, 4096:KCOL], 0.0)
                v.memset(kpkh[1][0:64, 4096:KCOL], 0.0)
                v.memset(vfm[:, 4096:KCOL], 0.0)
                v.tensor_copy(kpkh[0][0:64, 4096:4098], nk_s[0:64, hp * 2:hp * 2 + 2])
                v.tensor_copy(kpkh[1][0:64, 4096:4098], nk_s[64:128, hp * 2:hp * 2 + 2])
                return v.tensor_copy(vfm[:, 4096:4098], nv_s[:, hp * 2:hp * 2 + 2])
            op_null = sd.add("vector", f_null,
                             deps=[op_ldc, prev_hp.get("attn_done"),
                                   prev_hp.get("vT_done")],
                             name=f"null{hp}")

            def f_vone(v):
                vv = vT.rearrange("p (c k) -> p c k", k=130)
                v.memset(vv[:, :, 64:65], 1.0)
                return v.memset(vv[:, :, 129:130], 1.0)
            op_vone = sd.add("vector", f_vone, deps=[prev_hp.get("attn_done")],
                             name=f"vone{hp}")

            vt_evs = []
            prev_vt = None
            for tc in range(NKC):
                def f_vtr(t, tc=tc):
                    return t.transpose(PT, vfm[:, tc * 128:(tc + 1) * 128],
                                       identb_s[:])
                op_vtr = sd.add("tensor", f_vtr,
                                deps=[op_null if tc >= 32 else kv_evs[tc // 4],
                                      prev_vt],
                                name=f"vtr{hp}_{tc}")

                def f_vte(v, tc=tc):
                    v.tensor_scalar(vT[:, tc * 130:tc * 130 + 64], PT[:, 0:64],
                                    rcolk[:, tc:tc + 1], None, ALU.mult)
                    return v.tensor_scalar(vT[:, tc * 130 + 65:tc * 130 + 129],
                                           PT[:, 64:128],
                                           rcolk[:, tc:tc + 1], None, ALU.mult)
                op_vte = sd.add("vector", f_vte, deps=[op_vtr, op_vone] + rc_ops,
                                name=f"vte{hp}_{tc}")
                prev_vt = op_vte
                vt_evs.append(op_vte)
            prev_hp["vT_done"] = vt_evs[-1]

            nsq_ops = []
            khat_ops = []
            krr_ops = []
            for ti in range(9):
                w = 512 if ti < 8 else 256
                sl = slice(ti * 512, ti * 512 + w)
                pp = ti % 2
                nb_ps = (P[4] if pp == 0 else P[3])

                def f_ksq(s, sl=sl, pp=pp, w=w):
                    s.activation(sq[0:64, pp * 512:pp * 512 + w], kpkh[0][0:64, sl],
                                 AF.Square)
                    return s.activation(sq[64:128, pp * 512:pp * 512 + w],
                                        kpkh[1][0:64, sl], AF.Square)
                op_ksq = sd.add("scalar", f_ksq,
                                deps=[op_null if ti >= 8 else kv_evs[ti],
                                      prev_hp.get("attn_done"),
                                      nsq_ops[-2] if ti >= 2 else None],
                                name=f"ksq{hp}_{ti}")

                def f_kns(t, sl=sl, pp=pp, w=w, nb_ps=nb_ps):
                    return t.matmul(nb_ps[0:2, 0:w], bd2_s[:],
                                    sq[:, pp * 512:pp * 512 + w],
                                    start=True, stop=True)
                op_kns = sd.add("tensor", f_kns,
                                deps=[op_ksq, op_bd2,
                                      nsq_ops[-2] if ti >= 2 else None,
                                      khat_ops[-2] if ti >= 2 else None],
                                name=f"kns{hp}_{ti}")

                def f_kne(v, sl=sl, w=w, nb_ps=nb_ps):
                    v.tensor_copy(nrow2[:, sl], nb_ps[0:2, 0:w])
                    return v.tensor_scalar_max(nrow2[:, sl], nrow2[:, sl], 1e-24)
                op_kne = sd.add("vector", f_kne, deps=[op_kns], name=f"kne{hp}_{ti}")
                nsq_ops.append(op_kne)

                def f_krs(s, sl=sl):
                    return s.activation(nrow2[:, sl], nrow2[:, sl], AF.Sqrt)
                op_krs = sd.add("scalar", f_krs, deps=[op_kne], name=f"krs{hp}_{ti}")

                def f_krr(v, sl=sl):
                    return v.reciprocal(nrow2[:, sl], nrow2[:, sl])
                op_krr = sd.add("vector", f_krr, deps=[op_krs], name=f"krr{hp}_{ti}")
                krr_ops.append(op_krr)

                def f_rnb(t, sl=sl, pp=pp, w=w):
                    t.matmul(P[0 + pp][:, 0:w], ksd_s[:], nrow2[:, sl],
                             start=True, stop=True)
                    return t.matmul(P[2 + pp][0:64, 0:w], ksd_s[:, 64:128],
                                    nrow2[:, sl], start=True, stop=True)
                op_rnb = sd.add("tensor", f_rnb,
                                deps=[op_krr, op_ldc,
                                      khat_ops[-2] if ti >= 2 else None],
                                name=f"rnb{hp}_{ti}")

                def f_kh(v, sl=sl, pp=pp, w=w):
                    v.tensor_tensor(kpkh[0][0:64, sl], kpkh[0][0:64, sl],
                                    P[0 + pp][0:64, 0:w], ALU.mult)
                    return v.tensor_tensor(kpkh[1][0:64, sl], kpkh[1][0:64, sl],
                                           P[2 + pp][0:64, 0:w], ALU.mult)
                op_kh = sd.add("vector", f_kh, deps=[op_rnb],
                               name=f"kh{hp}_{ti}")
                khat_ops.append(op_kh)

            def f_wq(g, put, hp=hp):
                put(g.dma_start(
                    out=wsl[:, 2048:3072].rearrange("p (kc o) -> p kc o", kc=NCH),
                    in_=wq_t[hp].rearrange("(kc p) o -> p kc o", p=128)))
            op_wq = sd.dma("gpsimd", f_wq, ndma=1, deps=[prev_hp.get("wslq")],
                           name=f"wq{hp}", stream="wsl2")

            def f_wqg(s, hp=hp):
                last = None
                for kc in range(NCH):
                    last = s.mul(wsl[:, 2048 + kc * 128:2048 + (kc + 1) * 128],
                                 wsl[:, 2048 + kc * 128:2048 + (kc + 1) * 128],
                                 gc_s[:, kc:kc + 1])
                return last
            op_wqg = sd.add("scalar", f_wqg, deps=[op_wq, op_ldc], name=f"wqg{hp}")

            def f_crq(t, hp=hp):
                last = None
                for kc in range(NCH):
                    last = t.matmul(P[4][0:1, 0:128], onescolb_s[:],
                                    wsl[:, 2048 + kc * 128:2048 + (kc + 1) * 128],
                                    start=(kc == 0), stop=(kc == NCH - 1))
                return last
            op_crq = sd.add("tensor", f_crq, deps=[op_wqg, prev_hp.get("crowq")],
                            name=f"crq{hp}")

            def f_creq(v):
                return v.tensor_copy(crow[0:1, 256:384], P[4][0:1, 0:128])
            op_creq = sd.add("vector", f_creq, deps=[op_crq], name=f"creq{hp}")
            prev_hp["crowq"] = op_creq

            q_evs = []
            for tt in range(NQTL):
                sl = slice(tt * 512, (tt + 1) * 512)
                pp = tt % 2

                xsl = xrlb[:, pp * 4096:(pp + 1) * 4096]

                def f_xqA(sync, put, sl=sl, xsl=xsl):
                    for kc in range(4):
                        put(sync.dma_start(out=xsl[:, kc * 512:(kc + 1) * 512],
                                           in_=x0qb[kc * 128:(kc + 1) * 128, sl]))
                op_xqA = sd.dma("sync", f_xqA, ndma=4,
                                deps=[prev_hp.get(("q", pp)), op_pca, op_pcb,
                                      khat_ops[-1] if tt <= 1 else None,
                                      prev_hp.get(("kmmA", pp)) if tt <= 1 else None,
                                      prev_hp.get(("qmmA", pp))],
                                name=f"qxA{hp}_{tt}", stream=f"srlA{pp}")

                def f_xqB(g, put, sl=sl, xsl=xsl):
                    for kc in range(4, NCH):
                        put(g.dma_start(out=xsl[:, kc * 512:(kc + 1) * 512],
                                        in_=x0qb[kc * 128:(kc + 1) * 128, sl]))
                op_xqB = sd.dma("gpsimd", f_xqB, ndma=4,
                                deps=[prev_hp.get(("q", pp)), op_pca, op_pcb,
                                      khat_ops[-1] if tt <= 1 else None,
                                      prev_hp.get(("kmmB", pp)) if tt <= 1 else None,
                                      prev_hp.get(("qmmB", pp))],
                                name=f"qxB{hp}_{tt}", stream=f"srlB{pp}")

                def f_qmmA(t, sl=sl, pp=pp, xsl=xsl):
                    last = None
                    for kc in range(4):
                        last = t.matmul(P[0 + pp][:],
                                        wsl[:, 2048 + kc * 128:2048 + (kc + 1) * 128],
                                        xsl[:, kc * 512:(kc + 1) * 512],
                                        start=(kc == 0), stop=False)
                    return last
                op_qmmA = sd.add("tensor", f_qmmA,
                                 deps=[op_xqA, op_wqg, op_creq, op_stq,
                                       prev_hp.get(("q", pp)),
                                       khat_ops[-1] if tt == 0 else None],
                                 name=f"qmmA{hp}_{tt}")

                def f_qmm(t, sl=sl, pp=pp, xsl=xsl):
                    for kc in range(4, NCH):
                        t.matmul(P[0 + pp][:], wsl[:, 2048 + kc * 128:2048 + (kc + 1) * 128],
                                 xsl[:, kc * 512:(kc + 1) * 512],
                                 start=False, stop=False)
                    return t.matmul(P[0 + pp][:], crow[0:1, 256:384], muq[:, sl],
                                    start=False, stop=True)
                op_qmm = sd.add("tensor", f_qmm, deps=[op_xqB],
                                name=f"qmm{hp}_{tt}")
                prev_hp[("qmmA", pp)] = op_qmmA
                prev_hp[("qmmB", pp)] = op_qmm

                def f_qsq(s, sl=sl, pp=pp):
                    return s.activation(sq[:, pp * 512:(pp + 1) * 512], P[0 + pp][:],
                                        AF.Square)
                op_qsq = sd.add("scalar", f_qsq, deps=[op_qmm,
                                                       prev_hp.get(("qs", pp))],
                                name=f"qsq{hp}_{tt}")

                qn_ps = P[4] if pp == 0 else P[3]

                def f_qns(t, sl=sl, pp=pp, qn_ps=qn_ps):
                    return t.matmul(qn_ps[0:2, :], bd2_s[:],
                                    sq[:, pp * 512:(pp + 1) * 512],
                                    start=True, stop=True)
                op_qns = sd.add("tensor", f_qns,
                                deps=[op_qsq, prev_hp.get(("qnev", pp)),
                                      prev_hp.get(("qh", pp))],
                                name=f"qns{hp}_{tt}")

                def f_qne(v, sl=sl, pp=pp, qn_ps=qn_ps):
                    return v.tensor_copy(qn2[:, sl], qn_ps[0:2, :])
                op_qne = sd.add("vector", f_qne, deps=[op_qns], name=f"qne{hp}_{tt}")
                prev_hp[("qnev", pp)] = op_qne

                def f_qev(s, sl=sl, pp=pp):
                    s.copy(scrq[:, (2 * pp) * 512:(2 * pp + 1) * 512],
                           P[0 + pp][0:64, :])
                    return s.copy(scrq[:, (2 * pp + 1) * 512:(2 * pp + 2) * 512],
                                  P[0 + pp][64:128, :])
                op_qev = sd.add("scalar", f_qev, deps=[op_qmm], name=f"qev{hp}_{tt}")

                def f_qrs(s, sl=sl):
                    return s.activation(qn2[:, sl], qn2[:, sl], AF.Sqrt)
                op_qrs = sd.add("scalar", f_qrs, deps=[op_qne], name=f"qrs{hp}_{tt}")

                def f_qrr(v, sl=sl):
                    return v.reciprocal(qn2[:, sl], qn2[:, sl])
                op_qrr = sd.add("vector", f_qrr, deps=[op_qrs], name=f"qrr{hp}_{tt}")

                def f_qrb(t, sl=sl, pp=pp):
                    t.matmul(P[2 + pp][0:64, :], qsd_s[:, 0:64], qn2[:, sl],
                             start=True, stop=True)
                    return t.matmul(P[5][0:64, :], qsd_s[:, 64:128], qn2[:, sl],
                                    start=True, stop=True)
                op_qrb = sd.add("tensor", f_qrb,
                                deps=[op_qrr, op_ldc, prev_hp.get(("qh", pp)),
                                      prev_hp.get("qhlast"), vt_evs[-1]],
                                name=f"qrb{hp}_{tt}")

                def f_qh(v, sl=sl, pp=pp):
                    v.tensor_tensor(qpkh[0][0:64, sl],
                                    scrq[:, (2 * pp) * 512:(2 * pp + 1) * 512],
                                    P[2 + pp][0:64, :], ALU.mult)
                    return v.tensor_tensor(qpkh[1][0:64, sl],
                                           scrq[:, (2 * pp + 1) * 512:(2 * pp + 2) * 512],
                                           P[5][0:64, :], ALU.mult)
                op_qh = sd.add("vector", f_qh, deps=[op_qrb, op_qev], name=f"qh{hp}_{tt}")
                prev_hp["qhlast"] = op_qh
                prev_hp[("q", pp)] = op_qh
                prev_hp[("qs", pp)] = op_qsq
                prev_hp[("qh", pp)] = op_qh
                q_evs.append(op_qh)

            nb_ops = []
            prev_nbc = None
            for hl in range(2):
                def f_nbl(g, put, hp=hp, hl=hl):
                    put(g.dma_start(
                        out=nbst[0:1, :],
                        in_=brow0[0:1, (2 * hp + hl) * NQTL * 512:
                                  (2 * hp + hl) * NQTL * 512 + QTOK]))
                op_nbl = sd.dma("gpsimd", f_nbl, ndma=1,
                                deps=[prev_hp.get("attn_done"), prev_nbc,
                                      prev_hp.get("nbl")],
                                name=f"nbl{hp}_{hl}", stream="nbl")
                prev_hp["nbl"] = op_nbl

                def f_nba(v, hl=hl):
                    v.tensor_scalar(nbst[0:1, :], nbst[0:1, :],
                                    tiny[0:1, 3:4], None, ALU.add)
                    return v.tensor_copy(qpkh[hl][64:65, :], nbst[0:1, :])
                op_nba = sd.add("vector", f_nba,
                                deps=[op_nbl, op_qk, prev_hp.get("attn_done")],
                                name=f"nba{hp}_{hl}")
                prev_nbc = op_nba
                nb_ops.append(op_nba)

            attn_done_ops = []
            for hloc in range(2):
                h = hp * 2 + hloc
                rbase = hloc * 64
                for qt in range(NQTL):
                    qsl = slice(qt * 512, (qt + 1) * 512)
                    par = (h * NQTL + qt) % 2
                    gB = qt + 2
                    kccs = list(range(0, min(4 * gB + 4, 32))) + [32]
                    diagA = [4 * (qt - 2) + dv for dv in range(4)
                             if 0 <= 4 * (qt - 2) + dv < 32] if qt >= 2 else []
                    diagB = [4 * gB + dv for dv in range(4) if 4 * gB + dv < 32]

                    bpar = (h * NQTL + qt) % 2

                    def f_nb(sync, put, h=h, qt=qt, bpar=bpar):
                        put(sync.dma_start(
                            out=bcq2[:, bpar * NKC:(bpar + 1) * NKC],
                            in_=bcq[:, (h * NQTL + qt) * NKC:(h * NQTL + qt + 1) * NKC]))
                    op_nb = sd.dma("sync", f_nb, ndma=1,
                                   deps=[prev_hp.get(("bcuse", bpar)),
                                         prev_hp.get("nbld")],
                                   name=f"nb{h}_{qt}", stream="nb")
                    prev_hp["nbld"] = op_nb

                    pv_last = None
                    step_ops = []
                    pend_pv = []

                    def flush_pv():
                        fn, dep, nm, pspp = pend_pv.pop(0)
                        op = sd.add("tensor", fn, dep, nm)
                        step_ops.append(op)
                        prev_hp[("pvs", pspp)] = op
                        return op

                    for si, kcc in enumerate(kccs):
                        spp = si % 4
                        mA = kcc in diagA
                        mB = kcc in diagB

                        def f_smm(t, kcc=kcc, qsl=qsl, spp=spp, mA=mA, mB=mB,
                                  qt=qt, hloc=hloc):
                            first = True
                            if mA:
                                dv = kcc - 4 * (qt - 2)
                                off = 512 - dv * 128
                                t.matmul(P[spp][:], M1_s[:],
                                         M2A_s[:, off:off + 512],
                                         start=True, stop=False)
                                first = False
                            if mB:
                                dv = kcc - 4 * (qt + 2)
                                off = 512 - dv * 128
                                t.matmul(P[spp][:], M1_s[:],
                                         M2B_s[:, off:off + 512],
                                         start=first, stop=False)
                                first = False
                            return t.matmul(
                                P[spp][:],
                                kpkh[hloc][:, kcc * 128:(kcc + 1) * 128],
                                qpkh[hloc][:, qsl], start=first, stop=True)
                        dep_smm = [nb_ops[hloc], khat_ops[-1], q_evs[-1],
                                   prev_hp.get(("pvs", spp))]
                        op_smm = sd.add("tensor", f_smm, deps=dep_smm,
                                        name=f"smm{h}_{qt}_{kcc}")
                        if si == 1 and prev_hp.get("ptail"):
                            prev_hp.pop("ptail")()
                        if len(pend_pv) >= 2:
                            pv_last = flush_pv()

                        def f_exp(s, kcc=kcc, spp=spp, bpar=bpar):
                            return s.activation(phat[:, spp * 512:(spp + 1) * 512],
                                                P[spp][:], AF.Exp,
                                                bias=bcq2[:, bpar * NKC + kcc:bpar * NKC + kcc + 1])
                        op_exp = sd.add("scalar", f_exp,
                                        deps=[op_smm, op_ldc, op_nb,
                                              prev_hp.get(("pvs", spp))],
                                        name=f"exp{h}_{qt}_{kcc}", nodrain=True)
                        prev_hp[("bcuse", bpar)] = op_exp

                        def f_pv(t, kcc=kcc, spp=spp, si=si, hloc=hloc, par=par,
                                 nsteps=len(kccs)):
                            return t.matmul(PO[par][:],
                                            vT[:, kcc * 130 + hloc * 65:kcc * 130 + hloc * 65 + 65],
                                            phat[:, spp * 512:(spp + 1) * 512],
                                            start=(si == 0), stop=(si == nsteps - 1))
                        pend_pv.append([f_pv,
                                        [op_exp, vt_evs[-1],
                                         prev_hp.get(("fin", par))],
                                        f"pv{h}_{qt}_{kcc}", spp])
                    while pend_pv:
                        pv_last = flush_pv()

                    def emit_tail(pv_last=pv_last, par=par, h=h, qt=qt, qsl=qsl,
                                  rbase=rbase):
                        def f_rs(v, par=par):
                            return v.reciprocal(rsr[64:65, :], PO[par][64:65, :])
                        op_rs = sd.add("vector", f_rs, deps=[pv_last],
                                       name=f"rs{h}_{qt}", nodrain=True)

                        def f_rsb(t, par=par):
                            return t.matmul(P[4][0:64, :], ones65_s[64:65, 0:64],
                                            rsr[64:65, :], start=True, stop=True)
                        op_rsb = sd.add("tensor", f_rsb,
                                        deps=[op_rs, prev_hp.get("rsb")],
                                        name=f"rsb{h}_{qt}")
                        prev_hp["rsb"] = op_rsb

                        def f_oev(s, par=par):
                            return s.copy(ou[:], PO[par][0:64, :])
                        op_oev = sd.add("scalar", f_oev,
                                        deps=[pv_last, prev_hp.get("oudone")],
                                        name=f"oev{h}_{qt}")

                        def f_fin(v, h=h, qt=qt, qsl=qsl, rbase=rbase):
                            return v.tensor_tensor(ao_t[h // 2][rbase:rbase + 64, qsl],
                                                   ou[:],
                                                   P[4][0:64, :], ALU.mult)
                        op_fin = sd.add("vector", f_fin, deps=[op_oev, op_rsb],
                                        name=f"fin{h}_{qt}", nodrain=True)
                        prev_hp["oudone"] = op_fin
                        prev_hp[("fin", par)] = op_fin
                        attn_done_ops.append(op_fin)
                    prev_hp["ptail"] = emit_tail
            if prev_hp.get("ptail"):
                prev_hp.pop("ptail")()
            prev_hp["attn_done"] = attn_done_ops[-1]
            prev_hp["wsl"] = q_evs[-1]
            prev_hp["wslq"] = q_evs[-1]
            oproj_deps.extend(attn_done_ops)

        oscr = phat[:].bitcast(F32)

        out_ops1 = []
        prev_o = {}
        for oc in range(8):
            def f_wo(g, put, oc=oc):
                put(g.dma_start(
                    out=wsl[:, (oc % 2) * 1024:(oc % 2) * 1024 + 1024]
                        .rearrange("p (kc o) -> p kc o", kc=8),
                    in_=wo_t[oc].rearrange("(kc p) o -> p kc o", p=128)))
            op_wo = sd.dma("gpsimd", f_wo, ndma=1,
                           deps=oproj_deps[-1:] + [prev_o.get(("mm", oc % 2))],
                           name=f"wo{oc}", stream=f"wo{oc % 2}")
            for tt in range(NQTL):
                sl = slice(tt * 512, (tt + 1) * 512)
                pp = tt % 2

                def f_omm(t, oc=oc, sl=sl, pp=pp):
                    last = None
                    for kc in range(NCH):
                        last = t.matmul(P[0 + pp][:],
                                        wsl[:, (oc % 2) * 1024 + kc * 128:(oc % 2) * 1024 + (kc + 1) * 128],
                                        ao_t[kc][:, sl], start=(kc == 0), stop=(kc == 7))
                    return last
                op_omm = sd.add("tensor", f_omm,
                                deps=[op_wo, prev_o.get(("ev", pp))] + oproj_deps[-1:],
                                name=f"omm{oc}_{tt}")
                prev_o[("mm", oc % 2)] = op_omm

                def f_oev2(s, pp=pp):
                    return s.copy(oscr[:, pp * 512:(pp + 1) * 512], P[0 + pp][:])
                op_oev2 = sd.add("scalar", f_oev2, deps=[op_omm], name=f"oev2{oc}_{tt}")

                def f_odma(sync, put, oc=oc, sl=sl, pp=pp):
                    put(sync.dma_start(out=osc[oc * 128:(oc + 1) * 128, sl],
                                       in_=oscr[:, pp * 512:(pp + 1) * 512]))
                op_od = sd.dma("sync", f_odma, ndma=1,
                               deps=[op_oev2, prev_o.get("od")],
                               name=f"od{oc}_{tt}", stream="oout")
                prev_o[("ev", pp)] = op_od
                prev_o["od"] = op_od
                out_ops1.append(op_od)

        RG = 1536
        NTT2 = RG // 512
        ar.reset()
        bias_s = ar.alloc(128, 8, F32)
        G_s = ar.alloc(8, 128, F32)
        rbias_s = ar.alloc(128, 8, F32)
        idf32_s = ar.alloc(128, 128, F32)
        hm2_s = ar.alloc(128, 1, F32)

        def ld_p2c(sync, put):
            for t, d_ in ((bias_s, bias_c), (G_s, Gm), (rbias_s, rbias),
                          (idf32_s, idf32), (hm2_s, hmask)):
                put(sync.dma_start(out=t, in_=d_[:]))
        op_ldp2 = sd.dma("sync", ld_p2c, ndma=5, deps=[out_ops1[-1]], name="ldp2",
                         stream="p2ldx")

        xs = ar.alloc(128, NCH * RG, F32)
        u2 = ar.alloc(128, NCH * RG, BF16)
        bw = ar.alloc(128, 8 * NCH * 128, BF16)
        As = ar.alloc(128, NCH * 128, BF16)
        Bs = ar.alloc(128, 8 * 128, BF16)
        rw = ar.alloc(128, NCH * 8, BF16)
        sax = ar.alloc(128, RG, BF16)
        wTf = ar.alloc(8, RG, F32)
        sq2 = ar.alloc(128, 4 * 512, F32)
        scr2 = ar.alloc(128, 512, F32)
        wa = ar.alloc(128, 2 * 512, F32)
        rows2 = ar.alloc(1, 3 * RG, F32)
        lgs = ar.alloc(128, 2 * 8, F32)
        tp = ar.alloc(128, 2 * 16, F32)
        wsb = ar.alloc(128, 2 * 8, F32)
        xrl2 = ar.alloc(128, 8 * 512, F32)
        u3f = u2
        trb = ar.alloc(128, 3 * 128, BF16)

        xs_t = [xs[:, ch * RG:(ch + 1) * RG] for ch in range(NCH)]
        u2_t = [u2[:, ch * RG:(ch + 1) * RG] for ch in range(NCH)]
        f0_t = xs_t
        u3f_t = [u3f[:, ch * RG:(ch + 1) * RG] for ch in range(NCH)]
        bw_s = [[bw[:, (oc * NCH + kc) * 128:(oc * NCH + kc + 1) * 128]
                 for kc in range(NCH)] for oc in range(8)]
        As_t = [As[:, kc * 128:(kc + 1) * 128] for kc in range(NCH)]
        Bs_t = [Bs[:, oc * 128:(oc + 1) * 128] for oc in range(8)]
        rw_t = [rw[:, kc * 8:(kc + 1) * 8] for kc in range(NCH)]

        def rowf(i):
            return rows2[0:1, i * RG:(i + 1) * RG]

        def ld_w2(g, put):
            for oc in range(8):
                put(g.dma_start(
                    out=bw[:, oc * NCH * 128:(oc + 1) * NCH * 128]
                        .rearrange("p (kc o) -> p kc o", kc=NCH),
                    in_=bw_t[oc].rearrange("(kc p) o -> p kc o", p=128)))
            put(g.dma_start(
                out=As.rearrange("p (kc r) -> p kc r", kc=NCH),
                in_=A_t.rearrange("(kc p) r -> p kc r", p=128)))
            put(g.dma_start(
                out=Bs.rearrange("p (oc o) -> p oc o", oc=8),
                in_=B_t.rearrange("oc p o -> p oc o")))
            put(g.dma_start(
                out=rw.rearrange("p (kc e) -> p kc e", kc=NCH),
                in_=rwt.rearrange("(kc p) e -> p kc e", p=128)))
        op_ldw = sd.dma("gpsimd", ld_w2, ndma=11, deps=[out_ops1[-1]], name="p2_ldw",
                        stream="p2ldw")

        p2_exit = []
        prev_grp_exit = None
        prev_sp2 = None
        tr_carry = {}
        for rg in range(2):
            R0 = rg * RG
            gdep = [op_ldp2, op_ldw] + [x for x in (prev_grp_exit, prev_sp2) if x]

            def ld_o(sync, put, R0=R0):
                for ch in range(NCH):
                    put(sync.dma_start(out=xs_t[ch],
                                       in_=osc[ch * 128:(ch + 1) * 128, R0:R0 + RG]))
            op_ldx = sd.dma("sync", ld_o, ndma=NCH, deps=gdep + [out_ops1[-1]],
                            name=f"p2_ldx{rg}", stream="p2ldx")

            op_sto = emit_stats(sd, f"sto{rg}", xs_t, rowf, consts, S0, S1, sq2,
                                [op_ldx], NTT2, psS0b=P[2][0:1, :], psS1b=P[3][0:1, :])
            prev_x1 = None
            x1_ops = []
            for tt in range(NTT2):
                sl = slice(tt * 512, (tt + 1) * 512)

                def f_x0l(sync, put, sl=sl, R0=R0):
                    for kc in range(NCH):
                        put(sync.dma_start(out=xrl2[:, kc * 512:(kc + 1) * 512],
                                           in_=x0q[kc * 128:(kc + 1) * 128,
                                                   R0 + sl.start:R0 + sl.stop]))
                op_x0l = sd.dma("sync", f_x0l, ndma=NCH, deps=[op_sto, prev_x1],
                                name=f"x0l{rg}_{tt}", stream="p2rl")

                pb = 2 * (tt % 2)

                def f_x1bc(t, sl=sl, pb=pb):
                    t.matmul(P[pb][:], consts["ones1"], rowf(0)[:, sl], start=True, stop=True)
                    return t.matmul(P[pb + 1][:], consts["ones1"], rowf(1)[:, sl], start=True, stop=True)
                op_x1bc = sd.add("tensor", f_x1bc,
                                 deps=[op_sto, x1_ops[tt - 2] if tt >= 2 else None],
                                 name=f"x1bc{rg}_{tt}")

                def f_x1(v, sl=sl, pb=pb):
                    last = None
                    for kc in range(NCH):
                        v.tensor_tensor(scr2[:], xs_t[kc][:, sl], P[pb][:], ALU.subtract)
                        v.tensor_tensor(scr2[:], scr2[:], P[pb + 1][:], ALU.mult)
                        last = v.tensor_tensor(xs_t[kc][:, sl],
                                               xrl2[:, kc * 512:(kc + 1) * 512],
                                               scr2[:], ALU.add)
                    return last
                prev_x1 = sd.add("vector", f_x1, deps=[op_x0l, op_x1bc],
                                 name=f"x1_{rg}_{tt}")
                x1_ops.append(prev_x1)

            def f_x1sp(sync, put, R0=R0):
                for ch in range(NCH):
                    put(sync.dma_start(out=x1sc[ch * 128:(ch + 1) * 128, R0:R0 + RG],
                                       in_=xs_t[ch]))
            op_x1sp = sd.dma("sync", f_x1sp, ndma=NCH, deps=x1_ops, name=f"x1sp{rg}",
                             stream="p2sp")

            op_st1 = emit_stats(sd, f"st1_{rg}", xs_t, rowf, consts, S0, S1, sq2,
                                x1_ops, NTT2, psS0b=P[2][0:1, :], psS1b=P[3][0:1, :])
            op_u2 = emit_apply(sd, f"u2_{rg}", xs_t, u2_t, rowf, consts, P[0], P[1],
                               scr2, [op_st1], NTT2, psMb=P[2], psRb=P[3])

            NST2 = RG // 128
            wT_ops = []
            prevr = {}
            for st in range(NST2):
                par = st % 2
                sl = slice(st * 128, (st + 1) * 128)
                lg_ps = P[2][:, par * 8:par * 8 + 8]
                lg = lgs[:, par * 8:par * 8 + 8]
                t0 = tp[:, par * 16:par * 16 + 16]
                w_sb = wsb[:, par * 8:par * 8 + 8]

                def f_lg(t, sl=sl, lg_ps=lg_ps):
                    last = None
                    for kc in range(NCH):
                        last = t.matmul(lg_ps, u2_t[kc][:, sl], rw_t[kc],
                                        start=(kc == 0), stop=(kc == NCH - 1))
                    return last
                op_lg = sd.add("tensor", f_lg,
                               deps=[op_u2, op_ldw, prevr.get(("va", par))],
                               name=f"rt{rg}_lg{st}")

                def f_va(v, lg=lg, t0=t0, lg_ps=lg_ps):
                    v.tensor_tensor(lg, lg_ps, rbias_s[:], ALU.add)
                    v.tensor_reduce(t0[:, 0:1], lg, axis=AX.X, op=ALU.max)
                    v.tensor_scalar(t0[:, 1:2], t0[:, 0:1], -1.0, None, ALU.mult)
                    v.tensor_scalar(t0[:, 8:16], lg, t0[:, 0:1], None, ALU.is_equal)
                    v.scalar_tensor_tensor(t0[:, 8:16], t0[:, 8:16], -1e30, lg,
                                           ALU.mult, ALU.add)
                    return v.tensor_reduce(t0[:, 2:3], t0[:, 8:16], axis=AX.X, op=ALU.max)
                op_va = sd.add("vector", f_va, deps=[op_lg, op_ldp2,
                                                     prevr.get(("sc", par)),
                                                     prevr.get(("tr", par))],
                               name=f"rt{rg}_va{st}")

                def f_sc(s, lg=lg, t0=t0):
                    return s.activation(t0[:, 8:16], lg, AF.Exp, bias=t0[:, 1:2])
                op_sc = sd.add("scalar", f_sc, deps=[op_va], name=f"rt{rg}_sc{st}")

                def f_vb2(v, lg=lg, t0=t0, w_sb=w_sb):
                    v.tensor_scalar(w_sb, lg, t0[:, 2:3], None, ALU.is_ge)
                    v.tensor_tensor(w_sb, w_sb, t0[:, 8:16], ALU.mult)
                    v.tensor_reduce(t0[:, 5:6], w_sb, axis=AX.X, op=ALU.add)
                    v.reciprocal(t0[:, 6:7], t0[:, 5:6])
                    return v.tensor_scalar(w_sb, w_sb, t0[:, 6:7], 2.0,
                                           ALU.mult, ALU.mult)
                op_vb = sd.add("vector", f_vb2, deps=[op_sc], name=f"rt{rg}_vb{st}")

                tr_ps = P[3][0:8, par * 128:par * 128 + 128]

                def f_tr(t, w_sb=w_sb, tr_ps=tr_ps):
                    return t.transpose(tr_ps, w_sb, idf32_s)
                op_tr = sd.add("tensor", f_tr, deps=[op_vb, op_ldp2,
                                                     prevr.get(("te", par))],
                               name=f"rt{rg}_tr{st}")

                def f_te(v, st=st, tr_ps=tr_ps):
                    return v.tensor_copy(wTf[:, st * 128:(st + 1) * 128], tr_ps)
                op_te = sd.add("vector", f_te, deps=[op_tr], name=f"rt{rg}_te{st}")
                wT_ops.append(op_te)
                prevr[("va", par)] = op_va
                prevr[("sc", par)] = op_sc
                prevr[("tr", par)] = op_tr
                prevr[("te", par)] = op_te

            sax_ops = []
            prev_ax = {}
            for tt in range(NTT2):
                par = tt % 2
                sl = slice(tt * 512, (tt + 1) * 512)
                ax_ps = (P[4] if par == 0 else P[5])

                def f_ax(t, sl=sl, ax_ps=ax_ps):
                    last = None
                    for kc in range(NCH):
                        last = t.matmul(ax_ps[:], As_t[kc], u2_t[kc][:, sl],
                                        start=(kc == 0), stop=(kc == NCH - 1))
                    return last
                op_ax = sd.add("tensor", f_ax,
                               deps=[op_u2, op_ldw, prev_ax.get(("sax", par))],
                               name=f"ax{rg}_{tt}")

                wa_ps = (P[2] if par == 0 else P[3])

                def f_wabc(t, sl=sl, wa_ps=wa_ps):
                    return t.matmul(wa_ps[:], G_s[:], wTf[:, sl], start=True, stop=True)
                op_wabc = sd.add("tensor", f_wabc,
                                 deps=wT_ops + [op_ldp2, prev_ax.get(("wae", par))],
                                 name=f"wabc{rg}_{tt}")

                def f_wae(s, wa_ps=wa_ps, par=par):
                    return s.copy(wa[:, par * 512:par * 512 + 512], wa_ps[:])
                op_wae = sd.add("scalar", f_wae, deps=[op_wabc], name=f"wae{rg}_{tt}")

                def f_sax(v, sl=sl, ax_ps=ax_ps, par=par):
                    return v.tensor_tensor(sax[:, sl], ax_ps[:],
                                           wa[:, par * 512:par * 512 + 512], ALU.mult)
                op_sax = sd.add("vector", f_sax, deps=[op_ax, op_wae], name=f"sax{rg}_{tt}")
                sax_ops.append(op_sax)
                prev_ax[("sax", par)] = op_sax
                prev_ax[("wae", par)] = op_wae

            res_ops = []
            prev_res = {}
            for oc in range(8):
                for tt in range(NTT2):
                    par = (oc * NTT2 + tt) % 2
                    sl = slice(tt * 512, (tt + 1) * 512)
                    r_ps = (P[0] if par == 0 else P[1])

                    def f_mm(t, oc=oc, sl=sl, r_ps=r_ps, tt=tt):
                        for kc in range(NCH):
                            t.matmul(r_ps[:], bw_s[oc][kc], u2_t[kc][:, sl],
                                     start=(kc == 0), stop=False)
                        return t.matmul(r_ps[:], Bs_t[oc], sax[:, sl],
                                        start=False, stop=True)
                    op_mm = sd.add("tensor", f_mm,
                                   deps=[op_u2, op_ldw, sax_ops[tt],
                                         prev_res.get(par)],
                                   name=f"res{rg}_{oc}_{tt}")

                    def f_evr(s, oc=oc, sl=sl, r_ps=r_ps):
                        return s.add(f0_t[oc][:, sl], r_ps[:], bias_s[:, oc:oc + 1])
                    op_evr = sd.add("scalar", f_evr, deps=[op_mm, op_ldp2, op_x1sp],
                                    name=f"rese{rg}_{oc}_{tt}")
                    res_ops.append(op_evr)
                    prev_res[par] = op_evr

            op_st2 = emit_stats(sd, f"st2_{rg}", f0_t, rowf, consts, S0, S1, sq2,
                                res_ops, NTT2, psS0b=P[2][0:1, :], psS1b=P[3][0:1, :])
            prev_ap2 = None
            ap2s = []
            rl_ops = {}
            for tt in range(NTT2):
                sl = slice(tt * 512, (tt + 1) * 512)
                for kc in range(NCH):
                    def f_rl(sync, put, kc=kc, sl=sl, R0=R0):
                        put(sync.dma_start(out=xrl2[:, kc * 512:kc * 512 + 512],
                                           in_=x1sc[kc * 128:(kc + 1) * 128,
                                                    R0 + sl.start:R0 + sl.stop]))
                    rl_ops[(tt, kc)] = sd.dma(
                        "sync", f_rl, ndma=1,
                        deps=[op_x1sp, op_st2] + ([prev_ap2] if prev_ap2 else []),
                        name=f"p2rl{rg}_{tt}_{kc}", stream="p2rl")

                pb = 2 * (tt % 2)

                def f_bc2(t, sl=sl, pb=pb):
                    t.matmul(P[pb][:], consts["ones1"], rowf(0)[:, sl], start=True, stop=True)
                    return t.matmul(P[pb + 1][:], consts["ones1"], rowf(1)[:, sl], start=True, stop=True)
                op_bc2 = sd.add("tensor", f_bc2,
                                deps=[op_st2, ap2s[tt - 2] if tt >= 2 else None],
                                name=f"x2bc{rg}_{tt}")

                def f_ap2(v, sl=sl, tt=tt, pb=pb):
                    last = None
                    for kc in range(NCH):
                        v.tensor_tensor(scr2[:], f0_t[kc][:, sl], P[pb][:], ALU.subtract)
                        v.tensor_tensor(scr2[:], scr2[:], P[pb + 1][:], ALU.mult)
                        last = v.tensor_tensor(f0_t[kc][:, sl],
                                               xrl2[:, kc * 512:kc * 512 + 512],
                                               scr2[:], ALU.add)
                    return last
                op_ap2v = sd.add("vector", f_ap2,
                                 deps=[op_bc2] + [rl_ops[(tt, kc)] for kc in range(NCH)],
                                 name=f"x2ap{rg}_{tt}")
                prev_ap2 = op_ap2v
                ap2s.append(op_ap2v)
            op_x2 = prev_ap2

            def f_x2sp(sync, put, R0=R0, rg=rg):
                if rg == 0:
                    for ch in range(NCH):
                        put(sync.dma_start(out=x2sc[ch * 128:(ch + 1) * 128, 0:512],
                                           in_=xs_t[ch][:, 1024:1536]))
                else:
                    for ch in range(NCH):
                        put(sync.dma_start(out=x2sc[ch * 128:(ch + 1) * 128, 512:2048],
                                           in_=xs_t[ch][:, 0:1536]))
            op_x2sp = sd.dma("sync", f_x2sp, ndma=NCH, deps=[op_x2], name=f"x2sp{rg}",
                             stream="p2sp")
            prev_sp2 = op_x2sp

            op_st3 = emit_stats(sd, f"st3_{rg}", xs_t, rowf, consts, S0, S1, sq2,
                                [op_x2], NTT2, psS0b=P[2][0:1, :], psS1b=P[3][0:1, :])
            op_u3 = emit_apply(sd, f"u3_{rg}", xs_t, u3f_t, rowf, consts, P[0], P[1],
                               scr2, [op_st3], NTT2, psMb=P[2], psRb=P[3])

            if rg == 0:
                def f_hm(v):
                    last = None
                    for ch in range(NCH):
                        last = v.tensor_scalar(u3f_t[ch][:, 0:1024],
                                               u3f_t[ch][:, 0:1024],
                                               hm2_s[:, 0:1], None, ALU.mult)
                    return last
                op_u3 = sd.add("vector", f_hm, deps=[op_u3, op_ldp2], name="hm0")

            tr_ops = []
            prev_tr = dict(tr_carry)
            for tc in range(NST2):
                for cc in range(NCH):
                    par = (tc * NCH + cc) % 2

                    def f_t1(t, tc=tc, cc=cc):
                        return t.transpose(PT, u3f_t[cc][:, tc * 128:(tc + 1) * 128],
                                           identb_s[:])
                    op_t1 = sd.add("tensor", f_t1,
                                   deps=[op_u3, prev_tr.get("ev")],
                                   name=f"u3t{rg}_{tc}_{cc}")

                    def f_t2(v, par=par):
                        return v.tensor_copy(trb[:, par * 128:(par + 1) * 128], PT)
                    op_t2 = sd.add("vector", f_t2, deps=[op_t1, prev_tr.get(("dma", par))],
                                   name=f"u3te{rg}_{tc}_{cc}")
                    prev_tr["ev"] = op_t2

                    def f_t3(sync, put, tc=tc, cc=cc, par=par, R0=R0):
                        put(sync.dma_start(
                            out=u3d[R0 + tc * 128:R0 + (tc + 1) * 128,
                                    cc * 128:(cc + 1) * 128],
                            in_=trb[:, par * 128:(par + 1) * 128]))
                    op_t3 = sd.dma("sync", f_t3, ndma=1,
                                   deps=[op_t2],
                                   name=f"u3td{rg}_{tc}_{cc}", stream=f"u3d{par}")
                    prev_tr[("dma", par)] = op_t3
                    tr_ops.append(op_t3)
            prev_grp_exit = tr_ops[-1]
            tr_carry = dict(prev_tr)
            p2_exit.extend(tr_ops + [op_x2sp])

        op_u3done = tr_ops[-1]
        op_u3done2 = tr_ops[-2]
        op_cc2 = op_u3done

        ar.reset()
        u3b = ar.alloc(128, 24 * CHH, BF16)
        FWDs = ar.alloc(128, 8 * NBF, BF16)
        IDMs = ar.alloc(128, NFC * 1024, BF16)
        Kpk = ar.alloc(128, NFC * CHH, BF16)
        Fb = ar.alloc(128, 2 * NFC * CHH, BF16)
        Xs = ar.alloc(128, NFC * CHH, BF16)
        COEFs = ar.alloc(64, NBF, BF16)
        abs_ = ar.alloc(64, 2 * D, BF16)
        t14 = ar.alloc(128, 4 * CHH, F32)
        cdiv = ar.alloc(128, 4 * CHH, F32)
        ytile = ar.alloc(128, 2 * CHH, BF16)
        ytr = ar.alloc(128, 2 * 128, BF16)
        h0b = ar.alloc(128, CHH, F32)

        Fbt = [[Fb[:, (par2 * NFC + fc) * CHH:(par2 * NFC + fc + 1) * CHH]
                for fc in range(NFC)] for par2 in range(2)]
        Xt = [Xs[:, fc * CHH:(fc + 1) * CHH] for fc in range(NFC)]
        Kt = [Kpk[:, fc * CHH:(fc + 1) * CHH] for fc in range(NFC)]
        u3bt = lambda blk3, tc: u3b[:, (blk3 * 8 + tc) * CHH:(blk3 * 8 + tc + 1) * CHH]
        FWDt = [[FWDs[:, (tc * NFC + fc) * 128:(tc * NFC + fc + 1) * 128]
                 for fc in range(NFC)] for tc in range(8)]
        IDMt = [[IDMs[:, (fc * 8 + tauc) * 128:(fc * 8 + tauc + 1) * 128]
                 for tauc in range(8)] for fc in range(NFC)]

        def ld_c3(sync, put):
            put(sync.dma_start(out=COEFs, in_=COEFC[:]))
            put(sync.dma_start(
                out=FWDs.rearrange("p (tc f) -> p tc f", tc=8),
                in_=FWD.rearrange("(tc p) f -> p tc f", p=128)))
            put(sync.dma_start(
                out=IDMs.rearrange("p (fc t) -> p fc t", fc=NFC),
                in_=IDM.rearrange("(fc p) t -> p fc t", p=128)))
        op_ldc3 = sd.dma("sync", ld_c3, ndma=3, deps=[op_cc2, op_u3done2, op_x2sp], name="ldc3",
                         stream="p3ld")

        def ld_ab3(g, put):
            put(g.dma_start(out=abs_, in_=abT[:]))
        op_ldab = sd.dma("gpsimd", ld_ab3, ndma=1, deps=[op_cc2, op_u3done2], name="ldab",
                         stream="p3ab")

        out_ops3 = []
        prev_g = {}
        for grp in range(2):
            gof = grp * CHH

            def f_h0l(sync, put, gof=gof):
                put(sync.dma_start(out=h0b[0:1, :], in_=h0row[0:1, gof:gof + CHH]))
            op_h0l = sd.dma("sync", f_h0l, ndma=1,
                            deps=[prev_g.get("kf_done_last"), op_ldc3],
                            name=f"h0l{grp}", stream="p3ld")

            def f_h03(t, grp=grp):
                return t.matmul(P[4][:], ones65_s[0:1, :], h0b[0:1, :],
                                start=True, stop=True)
            op_h0m = sd.add("tensor", f_h03, deps=[op_h0l, prev_g.get("h0use")],
                            name=f"h0m{grp}")

            def f_h0e(v):
                return v.tensor_copy(h0b, P[4][:])
            op_h0e = sd.add("vector", f_h0e, deps=[op_h0m], name=f"h0e{grp}")

            kf_ops = []
            prev_kf = prev_g.get("kf_done_last")
            for c in range(9):
                def f_cmm(t, c=c, gof=gof):
                    t.matmul(P[0][:], COEFs[:, c * 128:(c + 1) * 128],
                             abs_[:, gof:gof + CHH], start=True, stop=True)
                    t.matmul(P[1][:], COEFs[:, (9 + c) * 128:(10 + c) * 128],
                             abs_[:, gof:gof + CHH], start=True, stop=True)
                    t.matmul(P[2][:], COEFs[:, c * 128:(c + 1) * 128],
                             abs_[:, D + gof:D + gof + CHH], start=True, stop=True)
                    return t.matmul(P[3][:], COEFs[:, (9 + c) * 128:(10 + c) * 128],
                                    abs_[:, D + gof:D + gof + CHH], start=True, stop=True)
                op_cmm = sd.add("tensor", f_cmm, deps=[op_ldab, op_ldc3, prev_kf],
                                name=f"cmm{grp}_{c}")

                def f_cev(s, c=c):
                    s.add(cdiv[:, 0:CHH], P[0][:], cst_s[:, 1:2])
                    s.copy(cdiv[:, CHH:2 * CHH], P[1][:])
                    s.copy(cdiv[:, 2 * CHH:3 * CHH], P[2][:])
                    return s.copy(cdiv[:, 3 * CHH:4 * CHH], P[3][:])
                op_cev = sd.add("scalar", f_cev, deps=[op_cmm, op_ldc3],
                                name=f"cev{grp}_{c}")

                def f_cdv(v, c=c):
                    Ar = cdiv[:, 0:CHH]; Ai = cdiv[:, CHH:2 * CHH]
                    Br = cdiv[:, 2 * CHH:3 * CHH]; Bi = cdiv[:, 3 * CHH:4 * CHH]
                    ta = t14[:, 0:CHH]; tb = t14[:, CHH:2 * CHH]
                    tc_ = t14[:, 2 * CHH:3 * CHH]; td = t14[:, 3 * CHH:4 * CHH]
                    v.tensor_tensor(ta, Br, Ar, ALU.mult)
                    v.tensor_tensor(tb, Bi, Ai, ALU.mult)
                    v.tensor_tensor(tc_, Bi, Ar, ALU.mult)
                    v.tensor_tensor(td, Br, Ai, ALU.mult)
                    v.tensor_tensor(Br, Ar, Ar, ALU.mult)
                    v.tensor_tensor(Bi, Ai, Ai, ALU.mult)
                    v.tensor_tensor(Br, Br, Bi, ALU.add)
                    v.reciprocal(Br, Br)
                    v.tensor_tensor(ta, ta, tb, ALU.add)
                    v.tensor_tensor(ta, ta, Br, ALU.mult)
                    v.tensor_tensor(ta, ta, h0b, ALU.add)
                    v.tensor_copy(Kt[c], ta)
                    v.tensor_tensor(tc_, tc_, td, ALU.subtract)
                    v.tensor_tensor(tc_, tc_, Br, ALU.mult)
                    return v.tensor_copy(Kt[9 + c], tc_)
                op_cdv = sd.add("vector", f_cdv, deps=[op_cev, op_h0e],
                                name=f"cdv{grp}_{c}")
                prev_kf = op_cdv
                kf_ops.append(op_cdv)

            def f_kz2(v):
                v.tensor_copy(t14[0:1, 0:CHH], Kt[8][0:1, :])
                v.memset(Kt[8], 0.0)
                return v.tensor_copy(Kt[8][0:1, :], t14[0:1, 0:CHH])
            op_kz = sd.add("vector", f_kz2, deps=kf_ops, name=f"kz{grp}")

            def f_kz3(v):
                return v.memset(Kt[17], 0.0)
            op_kz3 = sd.add("vector", f_kz3, deps=kf_ops, name=f"kz3{grp}")
            kf_done = kf_ops + [op_kz, op_kz3]
            prev_g["kf_done_last"] = op_kz3
            prev_g["h0use"] = kf_ops[-1]

            ld_u3_ops = []
            for blk3 in range(3):
                for tc in range(8):
                    def f_lu(sync, put, blk3=blk3, tc=tc, gof=gof):
                        rt = blk3 * 8 + tc
                        put(sync.dma_start(
                            out=u3bt(blk3, tc),
                            in_=u3d[rt * 128:(rt + 1) * 128, gof:gof + CHH]))
                    op_lu = sd.dma("sync", f_lu, ndma=1,
                                   deps=[op_cc2, op_u3done2, prev_g.get("u3use")],
                                   name=f"p3lu{grp}_{blk3}_{tc}",
                                   stream=f"p3x{blk3}")
                    ld_u3_ops.append(op_lu)

            fb3 = []
            fwd_store = {}
            last_imm = None
            for blk3 in range(3):
                par2 = blk3 % 2
                evs = []
                for fc in range(NFC):
                    pp = fc % 2

                    def f_fmm(t, blk3=blk3, fc=fc, pp=pp):
                        last = None
                        for tc in range(8):
                            last = t.matmul(P[pp][:], FWDt[tc][fc], u3bt(blk3, tc),
                                            start=(tc == 0), stop=(tc == 7))
                        return last
                    op_fmm = sd.add("tensor", f_fmm,
                                    deps=ld_u3_ops[blk3 * 8:blk3 * 8 + 8] +
                                         [op_ldc3, fwd_store.get((fc, pp)),
                                          fwd_store.get(("last", pp)),
                                          prev_g.get("xuse"), op_kz3],
                                    name=f"fmm{grp}_{blk3}_{fc}")

                    def f_fev(s, fc=fc, pp=pp, par2=par2):
                        return s.copy(Fbt[par2][fc], P[pp][:])
                    op_fev = sd.add("scalar", f_fev,
                                    deps=[op_fmm, fwd_store.get(("cmb", par2))],
                                    name=f"fev{grp}_{blk3}_{fc}")
                    fwd_store[(fc, pp)] = op_fev
                    fwd_store[("last", pp)] = op_fev
                    evs.append(op_fev)
                fb3.append(evs)

                if blk3 >= 1:
                    seg_i = blk3 - 1
                    cur = blk3 % 2
                    prv = (blk3 - 1) % 2

                    def f_cmb(v, cur=cur, prv=prv):
                        last = None
                        for fc in range(NFC):
                            last = v.scalar_tensor_tensor(
                                Xt[fc], Fbt[cur][fc], cst_s[:, 2:3], Fbt[prv][fc],
                                ALU.mult, ALU.add)
                        return last
                    op_cmb = sd.add("vector", f_cmb,
                                    deps=fb3[blk3] + fb3[blk3 - 1] +
                                         [prev_g.get("xuse")],
                                    name=f"cmb{grp}_{seg_i}")
                    fwd_store[("cmb", prv)] = op_cmb

                    def f_pw(v, seg_i=seg_i):
                        last = None
                        for c in range(9):
                            a2 = t14[:, 0:CHH]; b2 = t14[:, CHH:2 * CHH]
                            c2 = t14[:, 2 * CHH:3 * CHH]; d2 = t14[:, 3 * CHH:4 * CHH]
                            v.tensor_tensor(a2, Xt[c], Kt[c], ALU.mult)
                            v.tensor_tensor(b2, Xt[9 + c], Kt[9 + c], ALU.mult)
                            v.tensor_tensor(c2, Xt[c], Kt[9 + c], ALU.mult)
                            v.tensor_tensor(d2, Xt[9 + c], Kt[c], ALU.mult)
                            v.tensor_tensor(Xt[c], a2, b2, ALU.subtract)
                            last = v.tensor_tensor(Xt[9 + c], c2, d2, ALU.add)
                        return last
                    op_pw = sd.add("vector", f_pw, deps=[op_cmb] + kf_done,
                                   name=f"pw{grp}_{seg_i}")

                    for tauc in range(8):
                        pp3 = 2 + tauc % 2

                        def f_imm(t, tauc=tauc, pp3=pp3):
                            last = None
                            for fc in range(NFC):
                                last = t.matmul(P[pp3][:], IDMt[fc][tauc], Xt[fc],
                                                start=(fc == 0), stop=(fc == NFC - 1))
                            return last
                        op_imm = sd.add("tensor", f_imm,
                                        deps=[op_pw, op_ldc3,
                                              prev_g.get(("inv", pp3))],
                                        name=f"imm{grp}_{seg_i}_{tauc}")
                        last_imm = op_imm

                        def f_iev(s, tauc=tauc, pp3=pp3):
                            return s.copy(ytile[:, (tauc % 2) * CHH:(tauc % 2 + 1) * CHH],
                                          P[pp3][:])
                        dep_iev = [op_imm, prev_g.get(("yev", tauc % 2))]
                        op_iev = sd.add("scalar", f_iev, deps=dep_iev,
                                        name=f"iev{grp}_{seg_i}_{tauc}")
                        prev_g[("inv", pp3)] = op_iev

                        for sub in range(4):
                            tpar = sub % 2

                            def f_ytr(t, tauc=tauc, sub=sub):
                                return t.transpose(
                                    PT, ytile[:, (tauc % 2) * CHH + sub * 128:
                                              (tauc % 2) * CHH + (sub + 1) * 128],
                                    identb_s[:])
                            op_ytr = sd.add("tensor", f_ytr,
                                            deps=[op_iev, prev_g.get("ytrev")],
                                            name=f"ytr{grp}_{seg_i}_{tauc}_{sub}")

                            def f_yte(v, tpar=tpar):
                                return v.tensor_copy(ytr[:, tpar * 128:(tpar + 1) * 128], PT)
                            op_yte = sd.add("vector", f_yte,
                                            deps=[op_ytr, prev_g.get(("ytd", tpar))],
                                            name=f"yte{grp}_{seg_i}_{tauc}_{sub}")
                            prev_g["ytrev"] = op_yte

                            def f_ytd(sync, put, seg_i=seg_i, tauc=tauc, sub=sub,
                                      gof=gof, tpar=tpar):
                                put(sync.dma_start(
                                    out=ya[gof + sub * 128:gof + (sub + 1) * 128,
                                           seg_i * 1024 + tauc * 128:
                                           seg_i * 1024 + (tauc + 1) * 128],
                                    in_=ytr[:, tpar * 128:(tpar + 1) * 128]))
                            op_ytd = sd.dma("sync", f_ytd, ndma=1, deps=[op_yte],
                                            name=f"ytd{grp}_{seg_i}_{tauc}_{sub}",
                                            stream=f"p3y{tpar}")
                            prev_g[("ytd", tpar)] = op_ytd
                            out_ops3.append(op_ytd)
                        prev_g[("yev", tauc % 2)] = op_ytd
                    prev_g["xuse"] = last_imm
            prev_g["u3use"] = prev_g["xuse"]

        ar.reset()
        HTOK = RPC // 2
        NT2 = HTOK // 512
        arA = ar.alloc(128, NCH * RPC, F32)
        arB = ar.alloc(128, NCH * RPC, F32)
        slA = ar.alloc(128, 2 * 16 * 128, BF16)
        slH = ar.alloc(128, 2 * 32 * 128, BF16)
        u4h = slH[:, 0:NCH * 1024]
        sq4 = ar.alloc(128, 4 * 512, F32)
        rows4 = ar.alloc(1, 3 * RPC, F32)
        scr4 = ar.alloc(128, 512, F32)
        sscr = ar.alloc(128, 512, F32)
        xrl4 = ar.alloc(128, 512, F32)
        yfm = arB.bitcast(BF16)[:, 0:NCH * RPC]

        x2_t = [arA[:, ch * RPC:(ch + 1) * RPC] for ch in range(NCH)]
        y_t = [yfm[:, ch * RPC:(ch + 1) * RPC] for ch in range(NCH)]
        u4f_t = [yfm[:, ch * RPC:(ch + 1) * RPC] for ch in range(NCH)]
        u4_t = [u4h[:, ch * 1024:(ch + 1) * 1024] for ch in range(NCH)]
        hb = arA.bitcast(BF16)
        gb = arB.bitcast(BF16)
        f1v = arB

        def rowf4(i):
            return rows4[0:1, i * RPC:(i + 1) * RPC]

        def ld_x24(sync, put):
            for ch in range(NCH):
                put(sync.dma_start(out=x2_t[ch], in_=x2sc[ch * 128:(ch + 1) * 128, :]))
            for ch in range(NCH):
                put(sync.dma_start(out=y_t[ch], in_=ya[ch * 128:(ch + 1) * 128, :]))
            for t, d_ in ((colsb[:, 0:32], w1c), (colsb[:, 32:64], w2c),
                          (colsb[:, 64:96], h1c), (colsb[:, 96:128], h2c),
                          (colsb[:, 128:136], w3c)):
                put(sync.dma_start(out=t, in_=d_[:]))
        op_ld4 = sd.dma("sync", ld_x24, ndma=2 * NCH + 5, deps=out_ops3,
                        name="p4ld", stream="p4ld")

        op_sty = emit_stats(sd, "sty", y_t, rowf4, consts, S0, S1, sq4,
                            [op_ld4], NTT, onescol_key="onescolb")
        op_x3 = emit_apply(sd, "x3", y_t, x2_t, rowf4, consts, P[0], P[1], scr4,
                           [op_sty], NTT, residual_tiles=x2_t)

        op_st4 = emit_stats(sd, "st4", x2_t, rowf4, consts, S0, S1, sq4, [op_x3], NTT)
        op_u4 = emit_apply(sd, "u4", x2_t, u4f_t, rowf4, consts, P[0], P[1], scr4,
                           [op_st4], NTT)

        def sp4(sync, put):
            for ch in range(NCH):
                put(sync.dma_start(out=xsc[ch * 128:(ch + 1) * 128, :], in_=x2_t[ch]))
            for ch in range(NCH):
                put(sync.dma_start(out=u4sc[ch * 128:(ch + 1) * 128, :], in_=u4f_t[ch]))
        op_sp4 = sd.dma("sync", sp4, deps=[op_u4], ndma=2 * NCH, name="p4spill",
                        stream="p4sp")

        out_ops4 = []
        chain_prev = op_sp4
        prevA4 = {}
        for hf in range(2):
            hsl = slice(hf * HTOK, (hf + 1) * HTOK)

            def f_u4rl(sync, put, hsl=hsl):
                for ch in range(NCH):
                    put(sync.dma_start(out=u4_t[ch],
                                       in_=u4sc[ch * 128:(ch + 1) * 128, hsl]))
            op_u4rl = sd.dma("sync", f_u4rl, ndma=NCH, deps=[op_sp4, chain_prev],
                             name=f"u4rl{hf}", stream="p4sp")

            def hview(oc, t2):
                return hb[:, oc * HTOK + t2 * 512:oc * HTOK + t2 * 512 + 512]

            def gview(oc, t2):
                return gb[:, oc * HTOK + t2 * 512:oc * HTOK + t2 * 512 + 512]

            hA_ops = []
            for oc in range(32):
                par = oc % 2

                def f_sl4(g, put, oc=oc, par=par):
                    put(g.dma_start(
                        out=slA[:, par * 2048:(par + 1) * 2048]
                            .rearrange("p (kc o) -> p kc o", kc=16),
                        in_=w12_t[oc].rearrange("(kc p) o -> p kc o", p=128)))
                op_sl = sd.dma("gpsimd", f_sl4, ndma=1,
                               deps=[prevA4.get(("ev", par)), chain_prev],
                               name=f"hA_sl{hf}_{oc}", stream=f"slA{par}")

                for t2 in range(NT2):
                    pp = (2 * t2) % 4

                    def f_mm4(t, oc=oc, t2=t2, par=par, pp=pp):
                        base = par * 2048
                        sl2 = slice(t2 * 512, t2 * 512 + 512)
                        for kc in range(NCH):
                            t.matmul(P[pp][:], slA[:, base + kc * 128:base + kc * 128 + 128],
                                     u4_t[kc][:, sl2], start=(kc == 0), stop=(kc == NCH - 1))
                        last = None
                        for kc in range(NCH):
                            last = t.matmul(P[pp + 1][:],
                                            slA[:, base + (8 + kc) * 128:base + (8 + kc) * 128 + 128],
                                            u4_t[kc][:, sl2], start=(kc == 0), stop=(kc == NCH - 1))
                        return last
                    op_mm4 = sd.add("tensor", f_mm4,
                                    deps=[op_sl, op_u4rl, prevA4.get(("h", (oc - 1, t2)))],
                                    name=f"hA_mm{hf}_{oc}_{t2}")

                    def f_si4(s, oc=oc, pp=pp):
                        return s.activation(sscr[:], P[pp][:],
                                            AF.Silu, scale=colsb[:, 0:32][:, oc:oc + 1])
                    op_si = sd.add("scalar", f_si4,
                                   deps=[op_mm4, op_ld4, prevA4.get("lasth")],
                                   name=f"hA_si{hf}_{oc}_{t2}")

                    def f_h4(v, oc=oc, t2=t2, pp=pp):
                        return v.scalar_tensor_tensor(
                            hview(oc, t2), P[pp + 1][:], colsb[:, 32:64][:, oc:oc + 1],
                            sscr[:], ALU.mult, ALU.mult)
                    op_h = sd.add("vector", f_h4, deps=[op_mm4, op_si],
                                  name=f"hA_h{hf}_{oc}_{t2}")
                    prevA4[("h", (oc, t2))] = op_h
                    prevA4["lasth"] = op_h
                    hA_ops.append(op_h)
                prevA4[("ev", par)] = prevA4[("h", (oc, NT2 - 1))]

            def big_phase4(name, wt, col, src_view, dst_view, dst_is_f1, deps_all):
                ops = []
                prevB = {}
                noc = 8 if dst_is_f1 else 32
                for oc in range(noc):
                    par = oc % 2

                    def f_sl(g, put, oc=oc, par=par, wt=wt):
                        put(g.dma_start(
                            out=slH[:, par * 4096:(par + 1) * 4096]
                                .rearrange("p (kc o) -> p kc o", kc=32),
                            in_=wt[oc].rearrange("(kc p) o -> p kc o", p=128)))
                    op_sl = sd.dma("gpsimd", f_sl, ndma=1,
                                   deps=[prevB.get(("ev", par))] + deps_all,
                                   name=f"{name}_sl{oc}", stream=f"slH{par}")
                    for t2 in range(NT2):
                        pp = 4 if (oc * NT2 + t2) % 2 == 0 else 5

                        def f_mm(t, oc=oc, t2=t2, par=par, pp=pp):
                            last = None
                            for kc in range(32):
                                last = t.matmul(P[pp][:],
                                                slH[:, par * 4096 + kc * 128:par * 4096 + kc * 128 + 128],
                                                src_view(kc, t2), start=(kc == 0),
                                                stop=(kc == 31))
                            return last
                        op_mm = sd.add("tensor", f_mm,
                                       deps=[op_sl, prevB.get(("ev2", pp))] + deps_all,
                                       name=f"{name}_mm{oc}_{t2}")

                        if dst_is_f1:
                            def f_ev(s, oc=oc, t2=t2, pp=pp):
                                return s.mul(dst_view(oc, t2), P[pp][:],
                                             col[:, oc:oc + 1])
                        else:
                            def f_ev(s, oc=oc, t2=t2, pp=pp):
                                return s.activation(dst_view(oc, t2), P[pp][:],
                                                    AF.Gelu, scale=col[:, oc:oc + 1])
                        op_ev = sd.add("scalar", f_ev, deps=[op_mm, op_ld4],
                                       name=f"{name}_ev{oc}_{t2}")
                        prevB[("ev2", pp)] = op_ev
                        ops.append(op_ev)
                    prevB[("ev", par)] = ops[-1]
                return ops

            hB = big_phase4(f"hB{hf}", h1_t, colsb[:, 64:96], hview, gview, False,
                            hA_ops + ([chain_prev] if chain_prev is not None else []))
            hC = big_phase4(f"hC{hf}", h2_t, colsb[:, 96:128], gview, hview, False, hB)

            def f1view(oc, t2):
                return f1v[:, oc * HTOK + t2 * 512:oc * HTOK + t2 * 512 + 512]
            hD = big_phase4(f"hD{hf}", w3_t, colsb[:, 128:136], hview, f1view, True, hC)

            f1_t = [f1v[:, ch * HTOK:(ch + 1) * HTOK] for ch in range(NCH)]

            def rowh(i):
                return rows4[0:1, i * RPC:i * RPC + HTOK]
            op_stf = emit_stats(sd, f"stf{hf}", f1_t, rowh, consts, S0, S1, sq4,
                                hD, NT2)
            prev_apf = None
            for t2 in range(NT2):
                def f_bcf(t, t2=t2):
                    t.matmul(P[0][:], consts["ones1"], rowh(0)[:, t2 * 512:t2 * 512 + 512],
                             start=True, stop=True)
                    return t.matmul(P[1][:], consts["ones1"],
                                    rowh(1)[:, t2 * 512:t2 * 512 + 512],
                                    start=True, stop=True)
                op_bcf = sd.add("tensor", f_bcf,
                                deps=[op_stf] + ([prev_apf] if prev_apf else []),
                                name=f"x4bc{hf}_{t2}")
                for kc in range(NCH):
                    def f_rlf(sync, put, kc=kc, t2=t2, hf=hf):
                        put(sync.dma_start(
                            out=xrl4,
                            in_=xsc[kc * 128:(kc + 1) * 128,
                                    hf * HTOK + t2 * 512:hf * HTOK + t2 * 512 + 512]))
                    dep_rl = [op_sp4, prevA4.get("lastx4")]
                    if prev_apf is not None:
                        dep_rl.append(prev_apf)
                    op_rlf = sd.dma("sync", f_rlf, ndma=1,
                                    deps=dep_rl,
                                    name=f"x4rl{hf}_{t2}_{kc}", stream="p4rl")

                    def f_apf(v, kc=kc, t2=t2):
                        sl2 = slice(t2 * 512, t2 * 512 + 512)
                        v.tensor_tensor(scr4, f1_t[kc][:, sl2], P[0][:], ALU.subtract)
                        v.tensor_tensor(scr4, scr4, P[1][:], ALU.mult)
                        return v.tensor_tensor(f1_t[kc][:, sl2],
                                               xrl4,
                                               scr4, ALU.add)
                    op_apf = sd.add("vector", f_apf, deps=[op_bcf, op_rlf],
                                    name=f"x4ap{hf}_{t2}_{kc}")
                    prevA4["lastx4"] = op_apf

                    def f_outf(sync, put, kc=kc, t2=t2, hf=hf):
                        put(sync.dma_start(
                            out=x4f[kc * 128:(kc + 1) * 128,
                                    hf * HTOK + t2 * 512:hf * HTOK + t2 * 512 + 512],
                            in_=f1_t[kc][:, t2 * 512:t2 * 512 + 512]))
                    op_of = sd.dma("sync", f_outf, ndma=1, deps=[op_apf],
                                   name=f"x4o{hf}_{t2}_{kc}", stream="out")
                    out_ops4.append(op_of)
                prev_apf = prevA4["lastx4"]
            chain_prev = out_ops4[-1]

        def fin4(g):
            return g.engine_nop()
        sd.add("gpsimd", fin4, deps=[out_ops4[-1]], name="fin4")
        sd.emit(block)
    return nc


def fused_inmaps(inp):
    import ml_dtypes
    bf = ml_dtypes.bfloat16
    x0 = np.asarray(inp["L0_attn_x"], dtype=np.float32)
    wq = np.asarray(inp["L0_attn_w_q"])
    wkv = np.asarray(inp["L0_attn_w_kv"])
    wo = np.asarray(inp["L0_attn_w_out"])
    nkv = np.asarray(inp["L0_attn_null_kv"]).reshape(HEADS, NNK, 2, DH)
    slopes = alibi_slopes()
    qs = np.asarray(inp["L0_attn_q_scale"])
    ks = np.asarray(inp["L0_attn_k_scale"])

    M1 = np.tril(np.ones((128, 128), np.float64)).T
    BIGM = np.zeros((128, 1024), np.float64)
    for k in range(128):
        lim = min(1024, k + 512)
        BIGM[k, :lim] = -1e9
    M1b = M1.astype(bf)
    BIGMb = BIGM.astype(bf)
    Z128 = np.zeros((128, 1024), bf)
    cstm = np.zeros((128, 4), np.float32)
    cstm[:, 0] = 1e-5
    cstm[:, 1] = 1.0
    cstm[:, 2] = ((-1.0) ** np.arange(128)).astype(np.float32)

    nkT_ = np.zeros((128, 16), np.float32)
    nvT_ = np.zeros((128, 16), np.float32)
    for hpi in range(8):
        for hl in range(2):
            h = hpi * 2 + hl
            nkT_[hl * 64:(hl + 1) * 64, hpi * 2:hpi * 2 + 2] = nkv[h, :, 0, :].T
            nvT_[hl * 64:(hl + 1) * 64, hpi * 2:hpi * 2 + 2] = nkv[h, :, 1, :].T
    qsd_ = np.zeros((2, 128), np.float32)
    qsd_[0, 0:64] = np.asarray(qs, np.float64) * 8.0
    qsd_[1, 64:128] = np.asarray(qs, np.float64) * 8.0
    ksd_ = np.zeros((2, 128), np.float32)
    ksd_[0, 0:64] = ks
    ksd_[1, 64:128] = ks

    common2 = dict(
        bw_t=tile_lhsT(f32c(np.asarray(inp["L0_ffn_base_w"]).T)),
        bias_c=f32c(np.asarray(inp["L0_ffn_base_b"]).reshape(8, 128).T),
        A_t=f32c(np.asarray(inp["L0_ffn_lora_A"]).transpose(1, 0, 2).reshape(D, 128)),
        B_t=tile_lhsT(f32c(np.asarray(inp["L0_ffn_lora_B"]).reshape(128, D))),
        rwt=f32c(np.asarray(inp["L0_ffn_router_w"]).T),
        rbias=f32c(np.tile(np.asarray(inp["L0_ffn_router_b"])[None, :], (128, 1))),
        idf32=np.eye(128, dtype=np.float32),
    )
    G = np.zeros((8, 128), np.float32)
    for e in range(8):
        G[e, e * 16:(e + 1) * 16] = 1.0
    common2["Gm"] = G
    FWDm, IDMm, COEFm = dft_consts()
    conv_common = dict(abT=f32c(np.asarray(inp["L1_attn_ab"]).T),
                       h0row=f32c(np.asarray(inp["L1_attn_h_0"])[None, :]),
                       FWD=FWDm, IDM=IDMm, COEFC=COEFm)
    w12 = np.concatenate([np.asarray(inp["L1_ffn_w1_w"]).T,
                          np.asarray(inp["L1_ffn_w2_w"]).T], axis=0)

    def colize(s, n):
        return f32c(np.asarray(s).reshape(n, 128).T)
    common4 = dict(w12_t=tile_lhsT(f32c(w12)),
                   h1_t=tile_lhsT(f32c(np.asarray(inp["L1_ffn_h1_w"]).T)),
                   h2_t=tile_lhsT(f32c(np.asarray(inp["L1_ffn_h2_w"]).T)),
                   w3_t=tile_lhsT(f32c(np.asarray(inp["L1_ffn_w3_w"]).T)),
                   w1c=colize(inp["L1_ffn_w1_s"], 32), w2c=colize(inp["L1_ffn_w2_s"], 32),
                   h1c=colize(inp["L1_ffn_h1_s"], 32), h2c=colize(inp["L1_ffn_h2_s"], 32),
                   w3c=colize(inp["L1_ffn_w3_s"], 8))
    common = dict(
        wq_t=tile_lhsT(f32c(wq)),
        wk_t=tile_lhsT(f32c(wkv[:, :D])),
        wv_t=tile_lhsT(f32c(wkv[:, D:])),
        wo_t=tile_lhsT(f32c(wo)),
        gcol=f32c(np.asarray(inp["L0_attn_gamma"]).reshape(8, 128).T),
        nkT=nkT_, nvT=nvT_,
        qsrow=f32c(qs[None, :]), ksrow=f32c(ks[None, :]),
        qsd=qsd_, ksd=ksd_,
        M1c=M1b,
        onesbf=np.ones((1, 128), bf),
        ones1=np.ones((1, 128), np.float32),
        ones65d=np.ones((65, 128), np.float32),
        onescol=np.ones((128, 1), np.float32),
        onescolb=np.ones((128, 1), bf),
        identb=np.eye(128).astype(bf),
        cst=cstm,
    )
    common.update(common2)
    common.update(conv_common)
    common.update(common4)

    NQTL = 6
    maps = []
    for c in range(8):
        b, p = c // 2, c % 2
        r0 = p * RPC - 1024
        bcq_ = np.zeros((128, 16 * NQTL * NKC), np.float32)
        brow0_ = np.zeros((1, 16 * NQTL * 512), np.float32)
        pp_ = np.arange(128)
        for h in range(16):
            sl_ = float(slopes[h])
            for qt in range(NQTL):
                g0 = r0 + qt * 512
                dummy = g0 < 0
                cmax = -1 if dummy else 4 * (g0 // 512) + 3
                for kcc in range(NKC):
                    colv = np.full(128, -1e9, np.float32)
                    if kcc < 32 and kcc <= cmax:
                        colv = (sl_ * (kcc * 128 + pp_ + 2)).astype(np.float32)
                    elif kcc == 32:
                        colv[0] = 0.0
                        colv[1] = sl_ * 1.0
                    bcq_[:, (h * NQTL + qt) * NKC + kcc] = colv
                qrow = np.arange(g0, g0 + 512)
                brow0_[0, (h * NQTL + qt) * 512:(h * NQTL + qt + 1) * 512] = \
                    0.0 if dummy else -(sl_ * (qrow + 2) + 2.0)
        x0q_ = np.zeros((D, 3072), np.float32)
        lo = max(0, r0)
        x0q_[:, (lo - r0):3072] = x0[b].T[:, lo:r0 + 3072]
        if r0 < 0:
            x0q_[:, 0:-r0] = x0[b].T[:, 0:-r0]
        maps.append(dict(
            x0f=np.ascontiguousarray(x0[b].T),
            x0q=x0q_,
            bcq=bcq_, brow0=brow0_,
            M2Av=(BIGMb if p == 0 else Z128),
            M2Bv=(BIGMb if p == 1 else Z128),
            hmask=np.full((128, 1), 0.0 if p == 0 else 1.0, np.float32),
            **common,
        ))
    return maps


def ml_bf16():
    import ml_dtypes
    return ml_dtypes.bfloat16


def kernel(**inputs):
    nc = build_fused()
    maps = fused_inmaps(inputs)
    r = run_bass_kernel_spmd(nc, maps, core_ids=list(range(8)))
    out = np.stack([np.concatenate([np.asarray(r.results[2 * b]["x4f"]).T,
                                    np.asarray(r.results[2 * b + 1]["x4f"]).T], axis=0)
                    for b in range(B)])
    return np.ascontiguousarray(out.astype(np.float32))


# version 16
# speedup vs baseline: 1.0049x; 1.0049x over previous
import sys
sys.path.insert(0, "/opt/trn_rl_repo")
import math
from contextlib import ExitStack
import numpy as np
import concourse.bass as bass
import concourse.mybir as mybir
from concourse.bass_utils import run_bass_kernel_spmd

F32 = mybir.dt.float32
BF16 = mybir.dt.bfloat16
AF = mybir.ActivationFunctionType
ALU = mybir.AluOpType
AX = mybir.AxisListType

DEBUG_L2 = False
B, S, D = 4, 4096, 1024
NTOK = B * S
RPC = NTOK // 8
NCH = D // 128
NTT = RPC // 512
EPS = 1e-5
HEADS, DH, NNK = 16, 64, 2

ENGINES = ("tensor", "vector", "scalar", "gpsimd", "sync")


class DrainingEngine:
    _PASS = {"wait_ge", "drain", "engine_nop", "nop"}

    def __init__(self, eng):
        self._e = eng
        self._n = 0
        self._last_dma = False

    def __getattr__(self, k):
        f = getattr(self._e, k)
        if k in DrainingEngine._PASS or not callable(f):
            return f
        is_dma = k == "dma_start"

        def wrapped(*a, **kw):
            if self._n and not (is_dma and self._last_dma):
                self._e.drain()
            self._n += 1
            self._last_dma = is_dma
            return f(*a, **kw)
        return wrapped


class Op:
    __slots__ = ("engine", "fn", "deps", "ndma", "count_after", "name", "skey",
                 "nodrain")

    def __init__(self, engine, fn, deps, ndma, name, skey, nodrain=False):
        self.engine = engine
        self.fn = fn
        self.deps = deps
        self.ndma = ndma
        self.count_after = None
        self.name = name
        self.skey = skey
        self.nodrain = nodrain


class Sched:

    def __init__(self, nc):
        self.nc = nc
        self.ops = []

    def add(self, engine, fn, deps=(), name="", nodrain=False):
        op = Op(engine, fn, [d for d in deps if d is not None], 0, name, engine,
                nodrain)
        self.ops.append(op)
        return op

    def dma(self, engine, fn, deps=(), ndma=1, name="", stream="main"):
        op = Op(engine, fn, [d for d in deps if d is not None], ndma, name,
                "dma_" + stream)
        self.ops.append(op)
        return op

    def emit(self, block):
        skeys = []
        for op in self.ops:
            if op.skey not in skeys:
                skeys.append(op.skey)
        sems = {}
        for k in skeys:
            h = self.nc.semaphore(f"sem_{k}")
            sems[k] = h.__enter__()
        cnt = {k: 0 for k in skeys}
        for op in self.ops:
            cnt[op.skey] += 16 * op.ndma if op.ndma else 1
            op.count_after = cnt[op.skey]
        for e in ENGINES:
            ops_e = [o for o in self.ops if o.engine == e]
            if not ops_e:
                continue
            sec = getattr(block, e)

            def make(ops_e=ops_e, e=e):
                def body(eng):
                    if e in ("vector", "scalar", "gpsimd"):
                        eng = DrainingEngine(eng)
                    done = {}
                    for op in ops_e:
                        waits = {}
                        for d in op.deps:
                            if d.engine == e and d.ndma == 0 and op.ndma == 0:
                                continue
                            if d.engine == e and d.ndma and op.ndma == 0:
                                pass
                            waits[d.skey] = max(waits.get(d.skey, 0), d.count_after)
                        for dk, c in waits.items():
                            if c > done.get(dk, -1):
                                eng.wait_ge(sems[dk], c)
                                done[dk] = c
                        if op.nodrain and hasattr(eng, "_n"):
                            eng._n = 0
                        if op.ndma:
                            n = [0]

                            def put(instr, n=n, op=op):
                                instr.then_inc(sems[op.skey], 16)
                                n[0] += 1
                            op.fn(eng, put)
                            assert n[0] == op.ndma, (op.name, n[0], op.ndma)
                        else:
                            last = op.fn(eng)
                            assert last is not None, f"op {op.name} returned None"
                            last.then_inc(sems[op.skey], 1)
                return body
            sec(make())
        return sems


def tile_lhsT(wT):
    K, M = wT.shape
    return np.ascontiguousarray(wT.reshape(K, M // 128, 128).transpose(1, 0, 2))


def f32c(a):
    return np.ascontiguousarray(np.asarray(a, dtype=np.float32))


def alibi_slopes():
    start = 2.0 ** (-2.0 ** (-(math.log2(HEADS) - 3.0)))
    return np.array([start * start ** i for i in range(HEADS)], dtype=np.float64)


def emit_stats(sd, name, src_tiles, row, consts, psS0, psS1, sqbuf,
               deps, n_tt, width=512, onescol_key="onescol",
               psS0b=None, psS1b=None):
    W = width
    evacs = []
    prev_v = None
    prev_mmb = None
    for tt in range(n_tt):
        sl = slice(tt * W, (tt + 1) * W)
        if psS0b is not None:
            pS0, pS1 = (psS0, psS1) if tt % 2 == 0 else (psS0b, psS1b)
            pair_ev = evacs[tt - 2] if tt >= 2 else None
            sq_dep = prev_mmb
        else:
            pS0, pS1 = psS0, psS1
            pair_ev = prev_v
            sq_dep = prev_v

        def f_sqa(s, sl=sl):
            last = None
            for kc in range(4):
                last = s.activation(sqbuf[:, kc * W:(kc + 1) * W],
                                    src_tiles[kc][:, sl], AF.Square)
            return last
        op_sqa = sd.add("scalar", f_sqa,
                        deps=deps + [x for x in (sq_dep,) if x],
                        name=f"{name}_sqa{tt}")

        def f_mma(t, sl=sl, pS0=pS0, pS1=pS1):
            for kc in range(NCH):
                t.matmul(pS0[0:1, :W], consts[onescol_key], src_tiles[kc][:, sl],
                         start=(kc == 0), stop=(kc == NCH - 1))
            last = None
            for kc in range(4):
                last = t.matmul(pS1[0:1, :W], consts["onescol"],
                                sqbuf[:, kc * W:(kc + 1) * W],
                                start=(kc == 0), stop=False)
            return last
        op_mma = sd.add("tensor", f_mma,
                        deps=deps + [op_sqa] + [x for x in (pair_ev,) if x],
                        name=f"{name}_mma{tt}")

        def f_sqb(s, sl=sl):
            last = None
            for kc in range(4):
                last = s.activation(sqbuf[:, kc * W:(kc + 1) * W],
                                    src_tiles[4 + kc][:, sl], AF.Square)
            return last
        op_sqb = sd.add("scalar", f_sqb, deps=[op_mma], name=f"{name}_sqb{tt}")

        def f_mmb(t, sl=sl, pS1=pS1):
            last = None
            for kc in range(4):
                last = t.matmul(pS1[0:1, :W], consts["onescol"],
                                sqbuf[:, kc * W:(kc + 1) * W],
                                start=False, stop=(kc == 3))
            return last
        op_mmb = sd.add("tensor", f_mmb, deps=[op_sqb], name=f"{name}_mmb{tt}")
        prev_mmb = op_mmb

        def f_ev(v, sl=sl, pS0=pS0, pS1=pS1):
            v.tensor_copy(row(0)[:, sl], pS0[0:1, :W])
            return v.tensor_copy(row(2)[:, sl], pS1[0:1, :W])
        prev_v = sd.add("vector", f_ev, deps=[op_mmb], name=f"{name}_ev{tt}",
                        nodrain=True)
        evacs.append(prev_v)

    def f_r1(v):
        v.tensor_scalar(row(0), row(0), 1.0 / D, None, ALU.mult)
        v.tensor_scalar(row(2), row(2), 1.0 / D, None, ALU.mult)
        v.tensor_tensor(row(1), row(0), row(0), ALU.mult)
        return v.tensor_tensor(row(2), row(2), row(1), ALU.subtract)
    op_r1 = sd.add("vector", f_r1, deps=evacs, name=f"{name}_r1")

    def f_r2(s):
        return s.activation(row(2), row(2), AF.Sqrt, bias=consts["eps1"])
    op_r2 = sd.add("scalar", f_r2, deps=[op_r1], name=f"{name}_r2")

    def f_r3(v):
        return v.reciprocal(row(1), row(2))
    return sd.add("vector", f_r3, deps=[op_r2], name=f"{name}_r3")


def emit_apply(sd, name, src_tiles, dst_tiles, row, consts,
               psM, psR, scr, deps, n_tt, residual_tiles=None, width=512,
               psMb=None, psRb=None):
    W = width
    prev = None
    last_op = None
    aps = []
    for tt in range(n_tt):
        sl = slice(tt * W, (tt + 1) * W)
        if psMb is not None:
            pM, pR = (psM, psR) if tt % 2 == 0 else (psMb, psRb)
            pair_ap = aps[tt - 2] if tt >= 2 else None
        else:
            pM, pR = psM, psR
            pair_ap = prev

        def f_bc(t, sl=sl, pM=pM, pR=pR):
            t.matmul(pM[:, :W], consts["ones1"], row(0)[:, sl], start=True, stop=True)
            return t.matmul(pR[:, :W], consts["ones1"], row(1)[:, sl], start=True, stop=True)
        op_bc = sd.add("tensor", f_bc,
                       deps=deps + [x for x in (pair_ap,) if x],
                       name=f"{name}_bc{tt}")

        def f_ap(v, sl=sl, pM=pM, pR=pR):
            last = None
            for kc in range(NCH):
                v.tensor_tensor(scr[:, :W], src_tiles[kc][:, sl], pM[:, :W], ALU.subtract)
                if residual_tiles is None:
                    last = v.tensor_tensor(dst_tiles[kc][:, sl], scr[:, :W],
                                           pR[:, :W], ALU.mult)
                else:
                    v.tensor_tensor(scr[:, :W], scr[:, :W], pR[:, :W], ALU.mult)
                    last = v.tensor_tensor(dst_tiles[kc][:, sl],
                                           residual_tiles[kc][:, sl],
                                           scr[:, :W], ALU.add)
            return last
        prev = sd.add("vector", f_ap, deps=[op_bc] + deps, name=f"{name}_ap{tt}")
        aps.append(prev)
        last_op = prev
    return last_op


def build_l0_ffn():
    nc = bass.Bass("TRN2", target_bir_lowering=False)
    x0s = nc.dram_tensor("x0s", [D, RPC], F32, kind="ExternalInput").ap()
    oa = nc.dram_tensor("oa", [D, RPC], F32, kind="ExternalInput").ap()
    ob = nc.dram_tensor("ob", [D, RPC], F32, kind="ExternalInput").ap()
    x1sc = nc.dram_tensor("x1sc", [D, RPC], F32, kind="Internal").ap()
    bw_t = nc.dram_tensor("bw_t", [8, D, 128], F32, kind="ExternalInput").ap()
    bias_c = nc.dram_tensor("bias_c", [128, 8], F32, kind="ExternalInput").ap()
    A_t = nc.dram_tensor("A_t", [D, 128], F32, kind="ExternalInput").ap()
    B_t = nc.dram_tensor("B_t", [8, 128, 128], F32, kind="ExternalInput").ap()
    rwt = nc.dram_tensor("rwt", [D, 8], F32, kind="ExternalInput").ap()
    rbias = nc.dram_tensor("rbias", [128, 8], F32, kind="ExternalInput").ap()
    Gm = nc.dram_tensor("Gm", [8, 128], F32, kind="ExternalInput").ap()
    ones1 = nc.dram_tensor("ones1", [1, 128], F32, kind="ExternalInput").ap()
    ones65 = nc.dram_tensor("ones65", [65, 128], F32, kind="ExternalInput").ap()
    cst = nc.dram_tensor("cst", [128, 4], F32, kind="ExternalInput").ap()
    onescol = nc.dram_tensor("onescol", [128, 1], F32, kind="ExternalInput").ap()
    ident = nc.dram_tensor("ident", [128, 128], F32, kind="ExternalInput").ap()
    x2f = nc.dram_tensor("x2f", [D, RPC], F32, kind="ExternalOutput").ap()
    if DEBUG_L2:
        dbg = nc.dram_tensor("dbg", [128, (RPC // 128) * 32], F32, kind="ExternalOutput").ap()
    stat3 = nc.dram_tensor("stat3", [2, RPC], F32, kind="ExternalOutput").ap()

    with ExitStack() as ST:
        E = ST.enter_context
        xs = E(nc.sbuf_tensor("xs", [128, NCH * RPC], F32))
        u2 = E(nc.sbuf_tensor("u2", [128, NCH * RPC], BF16))
        bw = E(nc.sbuf_tensor("bw", [128, 8 * NCH * 128], BF16))
        As = E(nc.sbuf_tensor("As", [128, NCH * 128], BF16))
        Bs = E(nc.sbuf_tensor("Bs", [128, 8 * 128], BF16))
        rw = E(nc.sbuf_tensor("rw", [128, NCH * 8], BF16))
        sax = E(nc.sbuf_tensor("sax", [128, RPC], BF16))
        wTf = E(nc.sbuf_tensor("wTf", [8, RPC], F32))
        sq = E(nc.sbuf_tensor("sq", [128, 4 * 512], F32))
        scr = E(nc.sbuf_tensor("scr", [128, 512], F32))
        wa = E(nc.sbuf_tensor("wa", [128, 2 * 512], F32))
        rows = E(nc.sbuf_tensor("rows", [1, 3 * RPC], F32))
        bias_s = E(nc.sbuf_tensor("bias_s", [128, 8], F32))
        ones1_s = E(nc.sbuf_tensor("ones1_s", [1, 128], F32))
        ones65_s = E(nc.sbuf_tensor("ones65_s", [65, 128], F32))
        cst_s = E(nc.sbuf_tensor("cst_s", [128, 4], F32))
        onescol_s = E(nc.sbuf_tensor("onescol_s", [128, 1], F32))
        ident_s = E(nc.sbuf_tensor("ident_s", [128, 128], F32))
        G_s = E(nc.sbuf_tensor("G_s", [8, 128], F32))
        rbias_s = E(nc.sbuf_tensor("rbias_s", [128, 8], F32))
        lgs = E(nc.sbuf_tensor("lgs", [128, 2 * 8], F32))
        tp = E(nc.sbuf_tensor("tp", [128, 2 * 16], F32))
        wsb = E(nc.sbuf_tensor("wsb", [128, 2 * 8], F32))
        dbg_s = E(nc.sbuf_tensor("dbg_s", [128, (RPC // 128) * 32], F32)) if DEBUG_L2 else None
        xrl = E(nc.sbuf_tensor("xrl", [128, 8 * 512], F32))
        P0 = E(nc.psum_tensor("P0", [128, 512], F32))
        P1 = E(nc.psum_tensor("P1", [128, 512], F32))
        P2 = E(nc.psum_tensor("P2", [128, 512], F32))
        P3 = E(nc.psum_tensor("P3", [128, 512], F32))
        P4 = E(nc.psum_tensor("P4", [128, 512], F32))
        P5 = E(nc.psum_tensor("P5", [128, 512], F32))
        S0 = E(nc.psum_tensor("S0", [1, 512], F32))
        S1 = E(nc.psum_tensor("S1", [1, 512], F32))
        block = E(nc.Block())
        sd = Sched(nc)
        consts = {"ones1": ones1_s[:], "onescol": onescol_s[:],
                  "eps1": cst_s[0:1, 0:1], "epsc": cst_s[:, 0:1],
                  "onesat": lambda i: ones65_s[32 * i:32 * i + 1, :]}
        xs_t = [xs[:, ch * RPC:(ch + 1) * RPC] for ch in range(NCH)]
        u2_t = [u2[:, ch * RPC:(ch + 1) * RPC] for ch in range(NCH)]
        f0_t = xs_t
        bw_s = [[bw[:, (oc * NCH + kc) * 128:(oc * NCH + kc + 1) * 128]
                 for kc in range(NCH)] for oc in range(8)]
        As_t = [As[:, kc * 128:(kc + 1) * 128] for kc in range(NCH)]
        Bs_t = [Bs[:, oc * 128:(oc + 1) * 128] for oc in range(8)]
        rw_t = [rw[:, kc * 8:(kc + 1) * 8] for kc in range(NCH)]

        def ld_x(sync, put):
            for ch in range(NCH):
                put(sync.dma_start(out=xs_t[ch], in_=oa[ch * 128:(ch + 1) * 128, :]))
            put(sync.dma_start(out=bias_s[:], in_=bias_c[:]))
            put(sync.dma_start(out=ones1_s[:], in_=ones1[:]))
            put(sync.dma_start(out=ones65_s[:], in_=ones65[:]))
            put(sync.dma_start(out=cst_s[:], in_=cst[:]))
            put(sync.dma_start(out=onescol_s[:], in_=onescol[:]))
            put(sync.dma_start(out=ident_s[:], in_=ident[:]))
            put(sync.dma_start(out=G_s[:], in_=Gm[:]))
            put(sync.dma_start(out=rbias_s[:], in_=rbias[:]))
        op_ldx = sd.dma("sync", ld_x, ndma=NCH + 8, name="ld_x", stream="ldx")

        def ld_w(g, put):
            for oc in range(8):
                put(g.dma_start(
                    out=bw[:, oc * NCH * 128:(oc + 1) * NCH * 128]
                        .rearrange("p (kc o) -> p kc o", kc=NCH),
                    in_=bw_t[oc].rearrange("(kc p) o -> p kc o", p=128)))
            put(g.dma_start(
                out=As[:].rearrange("p (kc r) -> p kc r", kc=NCH),
                in_=A_t.rearrange("(kc p) r -> p kc r", p=128)))
            put(g.dma_start(
                out=Bs[:].rearrange("p (oc o) -> p oc o", oc=8),
                in_=B_t.rearrange("oc p o -> p oc o")))
            put(g.dma_start(
                out=rw[:].rearrange("p (kc e) -> p kc e", kc=NCH),
                in_=rwt.rearrange("(kc p) e -> p kc e", p=128)))
        op_ldw = sd.dma("gpsimd", ld_w, ndma=11, name="ld_w", stream="ldw")

        if DEBUG_L2:
            def f_z(v):
                return v.memset(dbg_s[:], 0.0)
            sd.add("vector", f_z, name="dbg_z")

        def rowf(i):
            return rows[0:1, i * RPC:(i + 1) * RPC]

        osum_ops = []
        prev_os = None
        for tt in range(NTT):
            sl = slice(tt * 512, (tt + 1) * 512)

            def f_obl(sync, put, sl=sl):
                for kc in range(NCH):
                    put(sync.dma_start(out=xrl[:, kc * 512:(kc + 1) * 512],
                                       in_=ob[kc * 128:(kc + 1) * 128, sl]))
            op_obl = sd.dma("sync", f_obl, ndma=NCH, deps=[prev_os],
                            name=f"obl{tt}", stream="rl")

            def f_osum(v, sl=sl):
                last = None
                for kc in range(NCH):
                    last = v.tensor_tensor(xs_t[kc][:, sl], xs_t[kc][:, sl],
                                           xrl[:, kc * 512:(kc + 1) * 512], ALU.add)
                return last
            prev_os = sd.add("vector", f_osum, deps=[op_obl, op_ldx],
                             name=f"osum{tt}")
            osum_ops.append(prev_os)

        op_sto = emit_stats(sd, "sto", xs_t, rowf, consts, S0, S1, sq,
                            osum_ops, NTT)
        prev_x1 = None
        x1_ops = []
        for tt in range(NTT):
            sl = slice(tt * 512, (tt + 1) * 512)

            def f_x0l(sync, put, sl=sl):
                for kc in range(NCH):
                    put(sync.dma_start(out=xrl[:, kc * 512:(kc + 1) * 512],
                                       in_=x0s[kc * 128:(kc + 1) * 128, sl]))
            op_x0l = sd.dma("sync", f_x0l, ndma=NCH, deps=[op_sto, prev_x1],
                            name=f"x0l{tt}", stream="rl")

            def f_x1bc(t, sl=sl):
                t.matmul(P0[:], consts["ones1"], rowf(0)[:, sl], start=True, stop=True)
                return t.matmul(P1[:], consts["ones1"], rowf(1)[:, sl], start=True, stop=True)
            op_x1bc = sd.add("tensor", f_x1bc, deps=[op_sto, prev_x1],
                             name=f"x1bc{tt}")

            def f_x1(v, sl=sl):
                last = None
                for kc in range(NCH):
                    v.tensor_tensor(scr[:], xs_t[kc][:, sl], P0[:], ALU.subtract)
                    v.tensor_tensor(scr[:], scr[:], P1[:], ALU.mult)
                    last = v.tensor_tensor(xs_t[kc][:, sl],
                                           xrl[:, kc * 512:(kc + 1) * 512],
                                           scr[:], ALU.add)
                return last
            prev_x1 = sd.add("vector", f_x1, deps=[op_x0l, op_x1bc], name=f"x1_{tt}")
            x1_ops.append(prev_x1)

        def f_x1sp(sync, put):
            for ch in range(NCH):
                put(sync.dma_start(out=x1sc[ch * 128:(ch + 1) * 128, :], in_=xs_t[ch]))
        op_x1sp = sd.dma("sync", f_x1sp, ndma=NCH, deps=x1_ops, name="x1sp",
                         stream="sp")

        op_st1 = emit_stats(sd, "st1", xs_t, rowf, consts, S0, S1, sq,
                            x1_ops, NTT)
        op_u2 = emit_apply(sd, "u2", xs_t, u2_t, rowf, consts, P0, P1, scr,
                           [op_st1], NTT)

        NST = RPC // 128
        wT_ops = []
        prev = {}
        for st in range(NST):
            par = st % 2
            sl = slice(st * 128, (st + 1) * 128)
            lg_ps = P2[:, par * 8:par * 8 + 8]
            lg = lgs[:, par * 8:par * 8 + 8]
            t0 = tp[:, par * 16:par * 16 + 16]
            w_sb = wsb[:, par * 8:par * 8 + 8]

            def f_lg(t, sl=sl, lg_ps=lg_ps):
                last = None
                for kc in range(NCH):
                    last = t.matmul(lg_ps, u2_t[kc][:, sl], rw_t[kc],
                                    start=(kc == 0), stop=(kc == NCH - 1))
                return last
            op_lg = sd.add("tensor", f_lg,
                           deps=[op_u2, op_ldw, prev.get(("va", par))],
                           name=f"rt_lg{st}")

            def f_va(v, lg=lg, t0=t0, lg_ps=lg_ps):
                v.tensor_tensor(lg, lg_ps, rbias_s[:], ALU.add)
                v.tensor_reduce(t0[:, 0:1], lg, axis=AX.X, op=ALU.max)
                v.tensor_scalar(t0[:, 1:2], t0[:, 0:1], -1.0, None, ALU.mult)
                v.tensor_scalar(t0[:, 8:16], lg, t0[:, 0:1], None, ALU.is_equal)
                v.scalar_tensor_tensor(t0[:, 8:16], t0[:, 8:16], -1e30, lg,
                                       ALU.mult, ALU.add)
                return v.tensor_reduce(t0[:, 2:3], t0[:, 8:16], axis=AX.X, op=ALU.max)
            op_va = sd.add("vector", f_va, deps=[op_lg, op_ldx,
                                                 prev.get(("sc", par)),
                                                 prev.get(("dbg", par)),
                                                 prev.get(("tr", par))],
                           name=f"rt_va{st}")

            def f_sc(s, lg=lg, t0=t0):
                return s.activation(t0[:, 8:16], lg, AF.Exp, bias=t0[:, 1:2])
            op_sc = sd.add("scalar", f_sc, deps=[op_va], name=f"rt_sc{st}")

            def f_vb2(v, lg=lg, t0=t0, w_sb=w_sb):
                v.tensor_scalar(w_sb, lg, t0[:, 2:3], None, ALU.is_ge)
                v.tensor_tensor(w_sb, w_sb, t0[:, 8:16], ALU.mult)
                v.tensor_reduce(t0[:, 5:6], w_sb, axis=AX.X, op=ALU.add)
                v.reciprocal(t0[:, 6:7], t0[:, 5:6])
                return v.tensor_scalar(w_sb, w_sb, t0[:, 6:7], 2.0,
                                       ALU.mult, ALU.mult)
            op_vb = sd.add("vector", f_vb2, deps=[op_sc], name=f"rt_vb{st}")

            tr_ps = P3[0:8, par * 128:par * 128 + 128]

            def f_tr(t, w_sb=w_sb, tr_ps=tr_ps):
                return t.transpose(tr_ps, w_sb, ident_s[:])
            op_tr = sd.add("tensor", f_tr, deps=[op_vb, prev.get(("te", par))],
                           name=f"rt_tr{st}")

            def f_te(v, st=st, tr_ps=tr_ps):
                return v.tensor_copy(wTf[:, st * 128:(st + 1) * 128], tr_ps)
            op_te = sd.add("vector", f_te, deps=[op_tr], name=f"rt_te{st}")
            if DEBUG_L2:
                def f_dbg(v, st=st, lg=lg, t0=t0, w_sb=w_sb):
                    base = st * 32
                    v.tensor_copy(dbg_s[:, base:base + 8], lg)
                    v.tensor_copy(dbg_s[:, base + 8:base + 11], t0[:, 0:3])
                    v.tensor_copy(dbg_s[:, base + 13:base + 15], t0[:, 5:7])
                    v.tensor_copy(dbg_s[:, base + 16:base + 24], t0[:, 8:16])
                    return v.tensor_copy(dbg_s[:, base + 24:base + 32], w_sb)
                op_dbg = sd.add("vector", f_dbg, deps=[op_vb], name=f"rt_dbg{st}")
                prev[("dbg", par)] = op_dbg
                wT_ops.append(op_dbg)
            wT_ops.append(op_te)
            prev[("va", par)] = op_va
            prev[("sc", par)] = op_sc
            prev[("tr", par)] = op_tr
            prev[("te", par)] = op_te

        sax_ops = []
        prev_ax = {}
        for tt in range(NTT):
            par = tt % 2
            sl = slice(tt * 512, (tt + 1) * 512)
            ax_ps = (P4 if par == 0 else P5)

            def f_ax(t, sl=sl, ax_ps=ax_ps):
                last = None
                for kc in range(NCH):
                    last = t.matmul(ax_ps[:], As_t[kc], u2_t[kc][:, sl],
                                    start=(kc == 0), stop=(kc == NCH - 1))
                return last
            op_ax = sd.add("tensor", f_ax,
                           deps=[op_u2, op_ldw, prev_ax.get(("sax", par))],
                           name=f"ax{tt}")

            wa_ps = (P2 if par == 0 else P3)

            def f_wabc(t, sl=sl, wa_ps=wa_ps):
                return t.matmul(wa_ps[:], G_s[:], wTf[:, sl], start=True, stop=True)
            op_wabc = sd.add("tensor", f_wabc,
                             deps=wT_ops + [prev_ax.get(("wae", par))],
                             name=f"wabc{tt}")

            def f_wae(s, wa_ps=wa_ps, par=par):
                return s.copy(wa[:, par * 512:par * 512 + 512], wa_ps[:])
            op_wae = sd.add("scalar", f_wae, deps=[op_wabc], name=f"wae{tt}")

            def f_sax(v, sl=sl, ax_ps=ax_ps, par=par):
                return v.tensor_tensor(sax[:, sl], ax_ps[:],
                                       wa[:, par * 512:par * 512 + 512], ALU.mult)
            op_sax = sd.add("vector", f_sax, deps=[op_ax, op_wae], name=f"sax{tt}")
            sax_ops.append(op_sax)
            prev_ax[("sax", par)] = op_sax
            prev_ax[("wae", par)] = op_wae

        res_ops = []
        prev_res = {}
        for oc in range(8):
            for tt in range(NTT):
                par = (oc * NTT + tt) % 2
                sl = slice(tt * 512, (tt + 1) * 512)
                r_ps = (P0 if par == 0 else P1)

                def f_mm(t, oc=oc, sl=sl, r_ps=r_ps, tt=tt):
                    for kc in range(NCH):
                        t.matmul(r_ps[:], bw_s[oc][kc], u2_t[kc][:, sl],
                                 start=(kc == 0), stop=False)
                    return t.matmul(r_ps[:], Bs_t[oc], sax[:, sl],
                                    start=False, stop=True)
                op_mm = sd.add("tensor", f_mm,
                               deps=[op_u2, op_ldw, sax_ops[tt],
                                     prev_res.get(par)],
                               name=f"res{oc}_{tt}")

                def f_ev(s, oc=oc, sl=sl, r_ps=r_ps):
                    return s.add(f0_t[oc][:, sl], r_ps[:], bias_s[:, oc:oc + 1])
                op_ev = sd.add("scalar", f_ev, deps=[op_mm], name=f"rese{oc}_{tt}")
                res_ops.append(op_ev)
                prev_res[par] = op_ev

        op_st2 = emit_stats(sd, "st2", f0_t, rowf, consts, S0, S1, sq,
                            res_ops, NTT)
        rowsl = rowf
        prev_ap = None
        rl_ops = {}
        for tt in range(NTT):
            sl = slice(tt * 512, (tt + 1) * 512)
            for kc in range(NCH):
                def f_rl(sync, put, kc=kc, sl=sl):
                    put(sync.dma_start(out=xrl[:, kc * 512:kc * 512 + 512],
                                       in_=x1sc[kc * 128:(kc + 1) * 128, sl]))
                rl_ops[(tt, kc)] = sd.dma(
                    "sync", f_rl, ndma=1,
                    deps=[op_x1sp] + ([prev_ap] if prev_ap else []),
                    name=f"rl{tt}_{kc}", stream="rl")

            def f_bc2(t, sl=sl):
                t.matmul(P2[:], consts["ones1"], rowsl(0)[:, sl], start=True, stop=True)
                return t.matmul(P3[:], consts["ones1"], rowsl(1)[:, sl], start=True, stop=True)
            op_bc2 = sd.add("tensor", f_bc2,
                            deps=[op_st2] + ([prev_ap] if prev_ap else []),
                            name=f"x2bc{tt}")

            def f_ap2(v, sl=sl, tt=tt):
                last = None
                for kc in range(NCH):
                    v.tensor_tensor(scr[:], f0_t[kc][:, sl], P2[:], ALU.subtract)
                    v.tensor_tensor(scr[:], scr[:], P3[:], ALU.mult)
                    last = v.tensor_tensor(f0_t[kc][:, sl],
                                           xrl[:, kc * 512:kc * 512 + 512],
                                           scr[:], ALU.add)
                return last
            op_ap2 = sd.add("vector", f_ap2,
                            deps=[op_bc2] + [rl_ops[(tt, kc)] for kc in range(NCH)],
                            name=f"x2ap{tt}")
            prev_ap = op_ap2
        op_x2 = prev_ap

        op_st3 = emit_stats(sd, "st3", xs_t, rowf, consts, S0, S1, sq,
                            [op_x2], NTT)

        def st_out(sync, put):
            for ch in range(NCH):
                put(sync.dma_start(out=x2f[ch * 128:(ch + 1) * 128, :], in_=xs_t[ch]))
            put(sync.dma_start(out=stat3[0:1, :], in_=rows[0:1, 0:RPC]))
            put(sync.dma_start(out=stat3[1:2, :], in_=rows[0:1, RPC:2 * RPC]))
            if DEBUG_L2:
                put(sync.dma_start(out=dbg[:], in_=dbg_s[:]))
        op_out = sd.dma("sync", st_out, deps=[op_x2, op_st3], ndma=NCH + 2 + (1 if DEBUG_L2 else 0),
                        name="out", stream="out")

        def fin(g):
            return g.engine_nop()
        sd.add("gpsimd", fin, deps=[op_out], name="fin")

        sd.emit(block)
    return nc


def l0_ffn_inmaps(x0_fm_percore, oa_percore, ob_percore, inp):
    base_w = inp["L0_ffn_base_w"]
    bw_tiled = tile_lhsT(f32c(base_w.T))
    bias_c = f32c(np.asarray(inp["L0_ffn_base_b"]).reshape(8, 128).T)
    A_t = f32c(np.asarray(inp["L0_ffn_lora_A"]).transpose(1, 0, 2).reshape(D, 128))
    B2 = f32c(np.asarray(inp["L0_ffn_lora_B"]).reshape(128, D))
    B_t = tile_lhsT(B2)
    rwt = f32c(np.asarray(inp["L0_ffn_router_w"]).T)
    rbias = f32c(np.tile(np.asarray(inp["L0_ffn_router_b"])[None, :], (128, 1)))
    G = np.zeros((8, 128), np.float32)
    for e in range(8):
        G[e, e * 16:(e + 1) * 16] = 1.0
    ones1 = np.ones((1, 128), np.float32)
    cstm = np.zeros((128, 4), np.float32); cstm[:, 0] = 1e-5
    onescol = np.ones((128, 1), np.float32)
    ident = np.eye(128, dtype=np.float32)
    common = dict(bw_t=bw_tiled, bias_c=bias_c, A_t=A_t, B_t=B_t, rwt=rwt,
                  rbias=rbias, Gm=G, ones1=ones1, onescol=onescol, ident=ident,
                  cst=cstm, ones65=np.ones((65, 128), np.float32))
    return [dict(x0s=np.ascontiguousarray(x0_fm_percore[c]),
                 oa=np.ascontiguousarray(oa_percore[c]),
                 ob=np.ascontiguousarray(ob_percore[c]), **common)
            for c in range(8)]


if __name__ == "__main__":
    print("module only")


def build_l1_ffn():
    nc = bass.Bass("TRN2", target_bir_lowering=False)
    x2in = nc.dram_tensor("x2in", [D, RPC], F32, kind="ExternalInput").ap()
    yin = nc.dram_tensor("yin", [D, RPC], F32, kind="ExternalInput").ap()
    w12_t = nc.dram_tensor("w12_t", [32, 2 * D, 128], F32, kind="ExternalInput").ap()
    h1_t = nc.dram_tensor("h1_t", [32, 4096, 128], F32, kind="ExternalInput").ap()
    h2_t = nc.dram_tensor("h2_t", [32, 4096, 128], F32, kind="ExternalInput").ap()
    w3_t = nc.dram_tensor("w3_t", [8, 4096, 128], F32, kind="ExternalInput").ap()
    w1c = nc.dram_tensor("w1c", [128, 32], F32, kind="ExternalInput").ap()
    w2c = nc.dram_tensor("w2c", [128, 32], F32, kind="ExternalInput").ap()
    h1c = nc.dram_tensor("h1c", [128, 32], F32, kind="ExternalInput").ap()
    h2c = nc.dram_tensor("h2c", [128, 32], F32, kind="ExternalInput").ap()
    w3c = nc.dram_tensor("w3c", [128, 8], F32, kind="ExternalInput").ap()
    ones1 = nc.dram_tensor("ones1", [1, 128], F32, kind="ExternalInput").ap()
    ones65 = nc.dram_tensor("ones65", [65, 128], F32, kind="ExternalInput").ap()
    onescol = nc.dram_tensor("onescol", [128, 1], F32, kind="ExternalInput").ap()
    cst = nc.dram_tensor("cst", [128, 4], F32, kind="ExternalInput").ap()
    x4f = nc.dram_tensor("x4f", [D, RPC], F32, kind="ExternalOutput").ap()
    xsc = nc.dram_tensor("xsc", [D, RPC], F32, kind="Internal").ap()
    u4sc = nc.dram_tensor("u4sc", [D, RPC], BF16, kind="Internal").ap()

    HTOK = RPC // 2
    NT2 = HTOK // 512

    with ExitStack() as ST:
        E = ST.enter_context
        arA = E(nc.sbuf_tensor("arA", [128, NCH * RPC], F32))
        arB = E(nc.sbuf_tensor("arB", [128, NCH * RPC], F32))
        u4h = E(nc.sbuf_tensor("u4h", [128, NCH * 1024], BF16))
        slA = E(nc.sbuf_tensor("slA", [128, 2 * 16 * 128], BF16))
        slH = E(nc.sbuf_tensor("slH", [128, 2 * 32 * 128], BF16))
        sq = E(nc.sbuf_tensor("sq", [128, 4 * 512], F32))
        rows = E(nc.sbuf_tensor("rows", [1, 3 * RPC], F32))
        scr = E(nc.sbuf_tensor("scr", [128, 512], F32))
        sscr = E(nc.sbuf_tensor("sscr", [128, 512], F32))
        xrl = E(nc.sbuf_tensor("xrl", [128, 512], F32))
        colsb = E(nc.sbuf_tensor("colsb", [128, 32 * 4 + 8], F32))
        ones1_s = E(nc.sbuf_tensor("ones1_s", [1, 128], F32))
        ones65_s = E(nc.sbuf_tensor("ones65_s", [65, 128], F32))
        onescol_s = E(nc.sbuf_tensor("onescol_s", [128, 1], F32))
        cst_s = E(nc.sbuf_tensor("cst_s", [128, 4], F32))
        P = [E(nc.psum_tensor(f"P{i}", [128, 512], F32)) for i in range(6)]
        S0 = E(nc.psum_tensor("S0", [1, 512], F32))
        S1 = E(nc.psum_tensor("S1", [1, 512], F32))
        block = E(nc.Block())

        sd = Sched(nc)
        consts = {"ones1": ones1_s[:], "onescol": onescol_s[:],
                  "eps1": cst_s[0:1, 0:1],
                  "onesat": lambda i: ones65_s[32 * i:32 * i + 1, :]}
        x2_t = [arA[:, ch * RPC:(ch + 1) * RPC] for ch in range(NCH)]
        y_t = [arB[:, ch * RPC:(ch + 1) * RPC] for ch in range(NCH)]
        u4full = arB[:].bitcast(BF16)
        u4f_t = [u4full[:, ch * RPC:(ch + 1) * RPC] for ch in range(NCH)]
        u4_t = [u4h[:, ch * 1024:(ch + 1) * 1024] for ch in range(NCH)]
        hb = arA[:].bitcast(BF16)
        gb = arB[:].bitcast(BF16)
        f1v = arB
        w1col = colsb[:, 0:32]
        w2col = colsb[:, 32:64]
        h1col = colsb[:, 64:96]
        h2col = colsb[:, 96:128]
        w3col = colsb[:, 128:136]

        def rowf(i):
            return rows[0:1, i * RPC:(i + 1) * RPC]

        def ld_xy(sync, put):
            for ch in range(NCH):
                put(sync.dma_start(out=x2_t[ch], in_=x2in[ch * 128:(ch + 1) * 128, :]))
                put(sync.dma_start(out=y_t[ch], in_=yin[ch * 128:(ch + 1) * 128, :]))
            put(sync.dma_start(out=ones1_s[:], in_=ones1[:]))
            put(sync.dma_start(out=ones65_s[:], in_=ones65[:]))
            put(sync.dma_start(out=onescol_s[:], in_=onescol[:]))
            put(sync.dma_start(out=cst_s[:], in_=cst[:]))
            put(sync.dma_start(out=colsb[:, 0:32], in_=w1c[:]))
            put(sync.dma_start(out=colsb[:, 32:64], in_=w2c[:]))
            put(sync.dma_start(out=colsb[:, 64:96], in_=h1c[:]))
            put(sync.dma_start(out=colsb[:, 96:128], in_=h2c[:]))
            put(sync.dma_start(out=colsb[:, 128:136], in_=w3c[:]))
        op_ld = sd.dma("sync", ld_xy, ndma=2 * NCH + 9, name="ld_xy", stream="ldx")

        op_sty = emit_stats(sd, "sty", y_t, rowf, consts, S0, S1, sq, [op_ld], NTT)
        op_x3 = emit_apply(sd, "x3", y_t, x2_t, rowf, consts, P[0], P[1], scr,
                           [op_sty], NTT, residual_tiles=x2_t)

        op_st4 = emit_stats(sd, "st4", x2_t, rowf, consts, S0, S1, sq, [op_x3], NTT)
        op_u4 = emit_apply(sd, "u4", x2_t, u4f_t, rowf, consts, P[0], P[1], scr,
                           [op_st4], NTT)

        def sp(sync, put):
            for ch in range(NCH):
                put(sync.dma_start(out=xsc[ch * 128:(ch + 1) * 128, :], in_=x2_t[ch]))
            for ch in range(NCH):
                put(sync.dma_start(out=u4sc[ch * 128:(ch + 1) * 128, :], in_=u4f_t[ch]))
        op_sp = sd.dma("sync", sp, deps=[op_u4], ndma=2 * NCH, name="spill", stream="sp")

        out_ops = []
        chain_prev = op_sp
        for hf in range(2):
            hsl = slice(hf * HTOK, (hf + 1) * HTOK)

            def f_u4rl(sync, put, hsl=hsl):
                for ch in range(NCH):
                    put(sync.dma_start(out=u4_t[ch],
                                       in_=u4sc[ch * 128:(ch + 1) * 128, hsl]))
            op_u4rl = sd.dma("sync", f_u4rl, ndma=NCH, deps=[op_sp, chain_prev],
                             name=f"u4rl{hf}", stream="sp")

            def hview(oc, t2):
                return hb[:, oc * HTOK + t2 * 512:oc * HTOK + t2 * 512 + 512]

            def gview(oc, t2):
                return gb[:, oc * HTOK + t2 * 512:oc * HTOK + t2 * 512 + 512]

            hA_ops = []
            prevA = {}
            for oc in range(32):
                par = oc % 2

                def f_sl(g, put, oc=oc, par=par):
                    put(g.dma_start(
                        out=slA[:, par * 2048:(par + 1) * 2048]
                            .rearrange("p (kc o) -> p kc o", kc=16),
                        in_=w12_t[oc].rearrange("(kc p) o -> p kc o", p=128)))
                op_sl = sd.dma("gpsimd", f_sl, ndma=1,
                               deps=[prevA.get(("ev", par)), chain_prev],
                               name=f"hA_sl{hf}_{oc}", stream=f"slA{par}")

                for t2 in range(NT2):
                    pp = (2 * t2) % 4

                    def f_mm(t, oc=oc, t2=t2, par=par, pp=pp):
                        base = par * 2048
                        sl2 = slice(t2 * 512, t2 * 512 + 512)
                        for kc in range(NCH):
                            t.matmul(P[pp][:], slA[:, base + kc * 128:base + kc * 128 + 128],
                                     u4_t[kc][:, sl2], start=(kc == 0), stop=(kc == NCH - 1))
                        last = None
                        for kc in range(NCH):
                            last = t.matmul(P[pp + 1][:],
                                            slA[:, base + (8 + kc) * 128:base + (8 + kc) * 128 + 128],
                                            u4_t[kc][:, sl2], start=(kc == 0), stop=(kc == NCH - 1))
                        return last
                    op_mm = sd.add("tensor", f_mm,
                                   deps=[op_sl, op_u4rl, prevA.get(("h", (oc - 1, t2)))],
                                   name=f"hA_mm{hf}_{oc}_{t2}")

                    def f_si(s, oc=oc, pp=pp):
                        return s.activation(sscr[:], P[pp][:],
                                            AF.Silu, scale=w1col[:, oc:oc + 1])
                    op_si = sd.add("scalar", f_si,
                                   deps=[op_mm, op_ld, prevA.get("lasth")],
                                   name=f"hA_si{hf}_{oc}_{t2}")

                    def f_h(v, oc=oc, t2=t2, pp=pp):
                        return v.scalar_tensor_tensor(
                            hview(oc, t2), P[pp + 1][:], w2col[:, oc:oc + 1],
                            sscr[:], ALU.mult, ALU.mult)
                    op_h = sd.add("vector", f_h, deps=[op_mm, op_si],
                                  name=f"hA_h{hf}_{oc}_{t2}")
                    prevA[("h", (oc, t2))] = op_h
                    prevA["lasth"] = op_h
                    hA_ops.append(op_h)
                prevA[("ev", par)] = prevA[("h", (oc, NT2 - 1))]

            def big_phase(name, wt, col, src_view, dst_view, dst_is_f1, deps_all,
                          scol_base):
                ops = []
                prevB = {}
                noc = 8 if dst_is_f1 else 32
                for oc in range(noc):
                    par = oc % 2

                    def f_sl(g, put, oc=oc, par=par, wt=wt):
                        put(g.dma_start(
                            out=slH[:, par * 4096:(par + 1) * 4096]
                                .rearrange("p (kc o) -> p kc o", kc=32),
                            in_=wt[oc].rearrange("(kc p) o -> p kc o", p=128)))
                    op_sl = sd.dma("gpsimd", f_sl, ndma=1,
                                   deps=[prevB.get(("ev", par))] + deps_all,
                                   name=f"{name}_sl{oc}", stream=f"slH{par}")
                    for t2 in range(NT2):
                        pp = 4 + 0 if (oc * NT2 + t2) % 2 == 0 else 4 + 1

                        def f_mm(t, oc=oc, t2=t2, par=par, pp=pp):
                            last = None
                            for kc in range(32):
                                last = t.matmul(P[pp][:],
                                                slH[:, par * 4096 + kc * 128:par * 4096 + kc * 128 + 128],
                                                src_view(kc, t2), start=(kc == 0),
                                                stop=(kc == 31))
                            return last
                        op_mm = sd.add("tensor", f_mm,
                                       deps=[op_sl, prevB.get(("ev2", pp))] + deps_all,
                                       name=f"{name}_mm{oc}_{t2}")

                        if dst_is_f1:
                            def f_ev(s, oc=oc, t2=t2, pp=pp):
                                return s.mul(dst_view(oc, t2), P[pp][:],
                                             col[:, oc:oc + 1])
                        else:
                            def f_ev(s, oc=oc, t2=t2, pp=pp):
                                return s.activation(dst_view(oc, t2), P[pp][:],
                                                    AF.Gelu, scale=col[:, oc:oc + 1])
                        op_ev = sd.add("scalar", f_ev, deps=[op_mm, op_ld],
                                       name=f"{name}_ev{oc}_{t2}")
                        prevB[("ev2", pp)] = op_ev
                        ops.append(op_ev)
                    prevB[("ev", par)] = ops[-1]
                return ops

            hB = big_phase(f"hB{hf}", h1_t, h1col, hview, gview, False,
                           hA_ops + [chain_prev] if chain_prev is not None else hA_ops, 64)
            hC = big_phase(f"hC{hf}", h2_t, h2col, gview, hview, False, hB, 96)

            def f1view(oc, t2):
                return f1v[:, oc * HTOK + t2 * 512:oc * HTOK + t2 * 512 + 512]
            hD = big_phase(f"hD{hf}", w3_t, w3col, hview, f1view, True, hC, 128)

            f1_t = [f1v[:, ch * HTOK:(ch + 1) * HTOK] for ch in range(NCH)]

            def rowh(i):
                return rows[0:1, i * RPC:i * RPC + HTOK]
            op_stf = emit_stats(sd, f"stf{hf}", f1_t, rowh, consts, S0, S1, sq,
                                hD, NT2)
            prev_ap = None
            for t2 in range(NT2):
                def f_bc(t, t2=t2):
                    t.matmul(P[0][:], consts["ones1"], rowh(0)[:, t2 * 512:t2 * 512 + 512],
                             start=True, stop=True)
                    return t.matmul(P[1][:], consts["ones1"],
                                    rowh(1)[:, t2 * 512:t2 * 512 + 512],
                                    start=True, stop=True)
                op_bc = sd.add("tensor", f_bc,
                               deps=[op_stf] + ([prev_ap] if prev_ap else []),
                               name=f"x4bc{hf}_{t2}")
                for kc in range(NCH):
                    par = kc % 2

                    def f_rl(sync, put, kc=kc, t2=t2, hf=hf):
                        put(sync.dma_start(
                            out=xrl[:],
                            in_=xsc[kc * 128:(kc + 1) * 128,
                                    hf * HTOK + t2 * 512:hf * HTOK + t2 * 512 + 512]))
                    dep_rl = [op_sp, prevA.get("lastx4")]
                    if prev_ap is not None:
                        dep_rl.append(prev_ap)
                    op_rl = sd.dma("sync", f_rl, ndma=1,
                                   deps=dep_rl,
                                   name=f"x4rl{hf}_{t2}_{kc}", stream="rl")

                    def f_ap(v, kc=kc, t2=t2):
                        sl2 = slice(t2 * 512, t2 * 512 + 512)
                        v.tensor_tensor(scr[:], f1_t[kc][:, sl2], P[0][:], ALU.subtract)
                        v.tensor_tensor(scr[:], scr[:], P[1][:], ALU.mult)
                        return v.tensor_tensor(f1_t[kc][:, sl2],
                                               xrl[:],
                                               scr[:], ALU.add)
                    op_ap = sd.add("vector", f_ap, deps=[op_bc, op_rl],
                                   name=f"x4ap{hf}_{t2}_{kc}")
                    prevA[("x4", (t2, kc))] = op_ap
                    prevA["lastx4"] = op_ap

                    def f_out(sync, put, kc=kc, t2=t2, hf=hf):
                        put(sync.dma_start(
                            out=x4f[kc * 128:(kc + 1) * 128,
                                    hf * HTOK + t2 * 512:hf * HTOK + t2 * 512 + 512],
                            in_=f1_t[kc][:, t2 * 512:t2 * 512 + 512]))
                    op_o = sd.dma("sync", f_out, ndma=1, deps=[op_ap],
                                  name=f"x4o{hf}_{t2}_{kc}", stream="out")
                    out_ops.append(op_o)
                prev_ap = prevA[("x4", (t2, NCH - 1))]
            chain_prev = out_ops[-1]

        def fin(g):
            return g.engine_nop()
        sd.add("gpsimd", fin, deps=out_ops[-1:] + [out_ops[len(out_ops) // 2]],
               name="fin")
        sd.emit(block)
    return nc


def l1_ffn_inmaps(x2_fm_percore, y_fm_percore, inp):
    w12 = np.concatenate([np.asarray(inp["L1_ffn_w1_w"]).T,
                          np.asarray(inp["L1_ffn_w2_w"]).T], axis=0)
    w12_t = tile_lhsT(f32c(w12))
    h1_t = tile_lhsT(f32c(np.asarray(inp["L1_ffn_h1_w"]).T))
    h2_t = tile_lhsT(f32c(np.asarray(inp["L1_ffn_h2_w"]).T))
    w3_t = tile_lhsT(f32c(np.asarray(inp["L1_ffn_w3_w"]).T))

    def colize(s, n):
        return f32c(np.asarray(s).reshape(n, 128).T)
    common = dict(w12_t=w12_t, h1_t=h1_t, h2_t=h2_t, w3_t=w3_t,
                  w1c=colize(inp["L1_ffn_w1_s"], 32), w2c=colize(inp["L1_ffn_w2_s"], 32),
                  h1c=colize(inp["L1_ffn_h1_s"], 32), h2c=colize(inp["L1_ffn_h2_s"], 32),
                  w3c=colize(inp["L1_ffn_w3_s"], 8),
                  ones1=np.ones((1, 128), np.float32),
                  ones65=np.ones((65, 128), np.float32),
                  onescol=np.ones((128, 1), np.float32),
                  cst=np.concatenate([np.full((128, 1), 1e-5, np.float32),
                                      np.zeros((128, 3), np.float32)], axis=1))
    return [dict(x2in=np.ascontiguousarray(x2_fm_percore[c]),
                 yin=np.ascontiguousarray(y_fm_percore[c]), **common)
            for c in range(8)]


NBF = 2304
NFC = NBF // 128
CHH = 512


def build_l1_attn():
    nc = bass.Bass("TRN2", target_bir_lowering=False)
    x2tm = nc.dram_tensor("x2tm", [S, CHH], F32, kind="ExternalInput").ap()
    mu3c = nc.dram_tensor("mu3c", [128, 32], F32, kind="ExternalInput").ap()
    r3c = nc.dram_tensor("r3c", [128, 32], F32, kind="ExternalInput").ap()
    abT = nc.dram_tensor("abT", [64, 2 * CHH], F32, kind="ExternalInput").ap()
    h0row = nc.dram_tensor("h0row", [1, CHH], F32, kind="ExternalInput").ap()
    FWD = nc.dram_tensor("FWD", [1024, NBF], BF16, kind="ExternalInput").ap()
    IDM = nc.dram_tensor("IDM", [NBF, 1024], BF16, kind="ExternalInput").ap()
    COEFC = nc.dram_tensor("COEFC", [64, NBF], BF16, kind="ExternalInput").ap()
    ones1 = nc.dram_tensor("ones1", [1, 128], F32, kind="ExternalInput").ap()
    cst = nc.dram_tensor("cst", [128, 4], F32, kind="ExternalInput").ap()
    yout = nc.dram_tensor("yout", [S, CHH], F32, kind="ExternalOutput").ap()

    with ExitStack() as ST:
        E = ST.enter_context
        u3 = E(nc.sbuf_tensor("u3", [128, 16 * CHH], BF16))
        FWDs = E(nc.sbuf_tensor("FWDs", [128, 8 * NBF], BF16))
        IDMs = E(nc.sbuf_tensor("IDMs", [128, NFC * 1024], BF16))
        Kpk = E(nc.sbuf_tensor("Kpk", [128, NFC * CHH], BF16))
        Fb = E(nc.sbuf_tensor("Fb", [128, 2 * NFC * CHH], BF16))
        Xs = E(nc.sbuf_tensor("Xs", [128, NFC * CHH], BF16))
        COEFs = E(nc.sbuf_tensor("COEFs", [64, NBF], BF16))
        abs_ = E(nc.sbuf_tensor("abs_", [64, 2 * CHH], BF16))
        t14 = E(nc.sbuf_tensor("t14", [128, 4 * CHH], F32))
        cdiv = E(nc.sbuf_tensor("cdiv", [128, 6 * CHH], F32))
        xt = E(nc.sbuf_tensor("xt", [128, 2 * CHH], F32))
        ytile = E(nc.sbuf_tensor("ytile", [128, 2 * CHH], BF16))
        h0b = E(nc.sbuf_tensor("h0b", [128, CHH], F32))
        mucol = E(nc.sbuf_tensor("mucol", [128, 32], F32))
        rcol = E(nc.sbuf_tensor("rcol", [128, 32], F32))
        nmr = E(nc.sbuf_tensor("nmr", [128, 32], F32))
        ones1_s = E(nc.sbuf_tensor("ones1_s", [1, 128], F32))
        cst_s = E(nc.sbuf_tensor("cst_s", [128, 4], F32))
        P = [E(nc.psum_tensor(f"P{i}", [128, 512], F32)) for i in range(6)]
        block = E(nc.Block())

        sd = Sched(nc)
        Fbt = [[Fb[:, (par * NFC + fc) * CHH:(par * NFC + fc + 1) * CHH]
                for fc in range(NFC)] for par in range(2)]
        Xt = [Xs[:, fc * CHH:(fc + 1) * CHH] for fc in range(NFC)]
        Kt = [Kpk[:, fc * CHH:(fc + 1) * CHH] for fc in range(NFC)]
        u3t = lambda bpar, tc: u3[:, (bpar * 8 + tc) * CHH:(bpar * 8 + tc + 1) * CHH]
        FWDt = [[FWDs[:, (tc * NFC + fc) * 128:(tc * NFC + fc + 1) * 128]
                 for fc in range(NFC)] for tc in range(8)]
        IDMt = [[IDMs[:, (fc * 8 + tauc) * 128:(fc * 8 + tauc + 1) * 128]
                 for tauc in range(8)] for fc in range(NFC)]

        def ld_c(sync, put):
            put(sync.dma_start(out=mucol[:], in_=mu3c[:]))
            put(sync.dma_start(out=rcol[:], in_=r3c[:]))
            put(sync.dma_start(out=ones1_s[:], in_=ones1[:]))
            put(sync.dma_start(out=cst_s[:], in_=cst[:]))
            put(sync.dma_start(out=COEFs[:], in_=COEFC[:]))
            put(sync.dma_start(
                out=FWDs[:].rearrange("p (tc f) -> p tc f", tc=8),
                in_=FWD.rearrange("(tc p) f -> p tc f", p=128)))
            put(sync.dma_start(
                out=IDMs[:].rearrange("p (fc t) -> p fc t", fc=NFC),
                in_=IDM.rearrange("(fc p) t -> p fc t", p=128)))
        op_ldc = sd.dma("sync", ld_c, ndma=7, name="ld_c", stream="ldc")

        def ld_ab(g, put):
            put(g.dma_start(out=abs_[:], in_=abT[:]))
        op_ldab = sd.dma("gpsimd", ld_ab, ndma=1, name="ld_ab", stream="ldab")

        def f_nmr(v):
            return v.scalar_tensor_tensor(nmr[:], mucol[:], -1.0, rcol[:],
                                          ALU.mult, ALU.mult)
        op_nmr = sd.add("vector", f_nmr, deps=[op_ldc], name="nmr")

        prev_u3 = {}

        def emit_u3_block(blk, dep_extra):
            bpar = blk % 2
            ops = []
            for tc in range(8):
                rt = blk * 8 + tc
                par = rt % 2

                def f_ldx(sync, put, rt=rt, par=par):
                    put(sync.dma_start(out=xt[:, par * CHH:(par + 1) * CHH],
                                       in_=x2tm[rt * 128:(rt + 1) * 128, :]))
                op_ldx = sd.dma("sync", f_ldx, ndma=1,
                                deps=[prev_u3.get(par)], name=f"ldx{rt}",
                                stream=f"ldx{par}")

                def f_u3(v, rt=rt, par=par, bpar=bpar, tc=tc):
                    return v.tensor_scalar(u3t(bpar, tc),
                                           xt[:, par * CHH:(par + 1) * CHH],
                                           rcol[:, rt:rt + 1], nmr[:, rt:rt + 1],
                                           ALU.mult, ALU.add)
                op_u3 = sd.add("vector", f_u3,
                               deps=[op_ldx, op_nmr] + dep_extra,
                               name=f"u3{rt}")
                prev_u3[par] = op_u3
                ops.append(op_u3)
            return ops

        def f_h0l(sync, put):
            put(sync.dma_start(out=h0b[0:1, :], in_=h0row[:]))
        op_h0l = sd.dma("sync", f_h0l, ndma=1, name="h0l", stream="ldc")

        def f_h0(t):
            return t.matmul(P[5][:], ones1_s[:], h0b[0:1, :], start=True, stop=True)
        op_h0m = sd.add("tensor", f_h0, deps=[op_h0l], name="h0m")

        def f_h0e(v):
            return v.tensor_copy(h0b[:], P[5][:])
        op_h0e = sd.add("vector", f_h0e, deps=[op_h0m], name="h0e")

        kf_ops = []
        prev_kf = None
        for c in range(9):
            def f_cmm(t, c=c):
                t.matmul(P[0][:], COEFs[:, c * 128:(c + 1) * 128], abs_[:, 0:CHH],
                         start=True, stop=True)
                t.matmul(P[1][:], COEFs[:, (9 + c) * 128:(10 + c) * 128], abs_[:, 0:CHH],
                         start=True, stop=True)
                t.matmul(P[2][:], COEFs[:, c * 128:(c + 1) * 128], abs_[:, CHH:2 * CHH],
                         start=True, stop=True)
                return t.matmul(P[3][:], COEFs[:, (9 + c) * 128:(10 + c) * 128],
                                abs_[:, CHH:2 * CHH], start=True, stop=True)
            op_cmm = sd.add("tensor", f_cmm, deps=[op_ldab, op_ldc, prev_kf],
                            name=f"cmm{c}")

            def f_cev(s, c=c):
                s.add(cdiv[:, 0:CHH], P[0][:], cst_s[:, 1:2])
                s.copy(cdiv[:, CHH:2 * CHH], P[1][:])
                s.copy(cdiv[:, 2 * CHH:3 * CHH], P[2][:])
                return s.copy(cdiv[:, 3 * CHH:4 * CHH], P[3][:])
            op_cev = sd.add("scalar", f_cev, deps=[op_cmm, op_ldc], name=f"cev{c}")

            def f_cdv(v, c=c):
                Ar = cdiv[:, 0:CHH]; Ai = cdiv[:, CHH:2 * CHH]
                Br = cdiv[:, 2 * CHH:3 * CHH]; Bi = cdiv[:, 3 * CHH:4 * CHH]
                den = cdiv[:, 4 * CHH:5 * CHH]; rden = cdiv[:, 5 * CHH:6 * CHH]
                v.tensor_tensor(den, Ar, Ar, ALU.mult)
                v.tensor_tensor(t14[:, 0:CHH], Ai, Ai, ALU.mult)
                v.tensor_tensor(den, den, t14[:, 0:CHH], ALU.add)
                v.reciprocal(rden, den)
                v.tensor_tensor(t14[:, 0:CHH], Br, Ar, ALU.mult)
                v.tensor_tensor(t14[:, CHH:2 * CHH], Bi, Ai, ALU.mult)
                v.tensor_tensor(t14[:, 0:CHH], t14[:, 0:CHH], t14[:, CHH:2 * CHH], ALU.add)
                v.tensor_tensor(t14[:, 0:CHH], t14[:, 0:CHH], rden, ALU.mult)
                v.tensor_tensor(t14[:, 0:CHH], t14[:, 0:CHH], h0b[:], ALU.add)
                v.tensor_copy(Kt[c], t14[:, 0:CHH])
                v.tensor_tensor(t14[:, 2 * CHH:3 * CHH], Bi, Ar, ALU.mult)
                v.tensor_tensor(t14[:, 3 * CHH:4 * CHH], Br, Ai, ALU.mult)
                v.tensor_tensor(t14[:, 2 * CHH:3 * CHH], t14[:, 2 * CHH:3 * CHH],
                                t14[:, 3 * CHH:4 * CHH], ALU.subtract)
                v.tensor_tensor(t14[:, 2 * CHH:3 * CHH], t14[:, 2 * CHH:3 * CHH],
                                rden, ALU.mult)
                return v.tensor_copy(Kt[9 + c], t14[:, 2 * CHH:3 * CHH])
            op_cdv = sd.add("vector", f_cdv, deps=[op_cev, op_h0e], name=f"cdv{c}")
            prev_kf = op_cdv
            kf_ops.append(op_cdv)

        def f_kz2(v):
            v.tensor_copy(t14[0:1, 0:CHH], Kt[8][0:1, :])
            v.memset(Kt[8], 0.0)
            return v.tensor_copy(Kt[8][0:1, :], t14[0:1, 0:CHH])
        op_kz = sd.add("vector", f_kz2, deps=kf_ops, name="kz")
        def f_kz3(v):
            return v.memset(Kt[17], 0.0)
        op_kz3 = sd.add("vector", f_kz3, deps=kf_ops, name="kz3")
        kf_done = kf_ops + [op_kz, op_kz3]

        def f_fbz(v):
            return v.memset(Fb[:, 0:NFC * CHH], 0.0)
        op_fbz = sd.add("vector", f_fbz, name="fbz")

        prev_idft_evac = {}
        prev_x_use = None
        last_fmm_of_blk = {}
        out_ops = []
        for blk in range(4):
            cur = (blk + 1) % 2
            prv = blk % 2
            u3_blk_ops = emit_u3_block(
                blk, [last_fmm_of_blk.get(blk - 2)] if blk >= 2 else [])
            fwd_evs = []
            for fc in range(NFC):
                pp = fc % 2

                def f_fmm(t, blk=blk, fc=fc, pp=pp):
                    last = None
                    for tc in range(8):
                        last = t.matmul(P[pp][:], FWDt[tc][fc], u3t(blk % 2, tc),
                                        start=(tc == 0), stop=(tc == 7))
                    return last
                op_fmm = sd.add("tensor", f_fmm,
                                deps=u3_blk_ops + [op_ldc,
                                      prev_idft_evac.get(("fwd", fc, pp)),
                                      prev_x_use],
                                name=f"fmm{blk}_{fc}")
                last_fmm_of_blk[blk] = op_fmm

                def f_fev(s, blk=blk, fc=fc, pp=pp, cur=cur):
                    return s.copy(Fbt[cur][fc], P[pp][:])
                op_fev = sd.add("scalar", f_fev, deps=[op_fmm], name=f"fev{blk}_{fc}")
                prev_idft_evac[("fwd", fc, pp)] = op_fev
                fwd_evs.append(op_fev)

            def f_cmb(v, blk=blk, cur=cur, prv=prv):
                last = None
                for fc in range(NFC):
                    last = v.scalar_tensor_tensor(
                        Xt[fc], Fbt[cur][fc], cst_s[:, 2:3], Fbt[prv][fc],
                        ALU.mult, ALU.add)
                return last
            dep_cmb = fwd_evs + ([op_fbz] if blk == 0 else [])
            op_cmb = sd.add("vector", f_cmb, deps=dep_cmb + [prev_x_use],
                            name=f"cmb{blk}")

            def f_pw(v, blk=blk):
                last = None
                for c in range(9):
                    a = t14[:, 0:CHH]; b = t14[:, CHH:2 * CHH]
                    c2 = t14[:, 2 * CHH:3 * CHH]; d2 = t14[:, 3 * CHH:4 * CHH]
                    v.tensor_tensor(a, Xt[c], Kt[c], ALU.mult)
                    v.tensor_tensor(b, Xt[9 + c], Kt[9 + c], ALU.mult)
                    v.tensor_tensor(c2, Xt[c], Kt[9 + c], ALU.mult)
                    v.tensor_tensor(d2, Xt[9 + c], Kt[c], ALU.mult)
                    v.tensor_tensor(Xt[c], a, b, ALU.subtract)
                    last = v.tensor_tensor(Xt[9 + c], c2, d2, ALU.add)
                return last
            op_pw = sd.add("vector", f_pw, deps=[op_cmb] + kf_done, name=f"pw{blk}")
            prev_x_use = op_pw

            for tauc in range(8):
                pp = 4 + tauc % 2

                def f_imm(t, tauc=tauc, pp=pp):
                    last = None
                    for fc in range(NFC):
                        last = t.matmul(P[pp][:], IDMt[fc][tauc], Xt[fc],
                                        start=(fc == 0), stop=(fc == NFC - 1))
                    return last
                op_imm = sd.add("tensor", f_imm,
                                deps=[op_pw, op_ldc,
                                      prev_idft_evac.get(("inv", pp))],
                                name=f"imm{blk}_{tauc}")
                prev_x_use = op_pw

                def f_iev(s, tauc=tauc, pp=pp):
                    return s.copy(ytile[:, (tauc % 2) * CHH:(tauc % 2 + 1) * CHH],
                                  P[pp][:])
                dep_iev = [op_imm, prev_idft_evac.get(("yout", tauc % 2))]
                op_iev = sd.add("scalar", f_iev, deps=dep_iev, name=f"iev{blk}_{tauc}")
                prev_idft_evac[("inv", pp)] = op_iev

                def f_yout(g, put, blk=blk, tauc=tauc):
                    put(g.dma_start(
                        out=yout[blk * 1024 + tauc * 128:blk * 1024 + (tauc + 1) * 128, :],
                        in_=ytile[:, (tauc % 2) * CHH:(tauc % 2 + 1) * CHH]))
                op_yo = sd.dma("gpsimd", f_yout, ndma=1, deps=[op_iev],
                               name=f"yo{blk}_{tauc}", stream="yout")
                prev_idft_evac[("yout", tauc % 2)] = op_yo
                out_ops.append(op_yo)
            prev_x_use = op_imm

        def fin(s):
            return s.nop()
        sd.add("sync", fin, deps=[out_ops[-1]], name="fin")
        sd.emit(block)
    return nc


def dft_consts():
    NB = 2048
    fb = NB // 2 + 1
    iidx = np.arange(1, 65)
    FWD = np.zeros((1024, NBF), np.float64)
    t = np.arange(1024)
    for f in range(fb):
        ang = 2 * np.pi * f * t / NB
        FWD[:, f] = np.cos(ang)
        if 1 <= f <= 1023:
            FWD[:, 1152 + f] = -np.sin(ang)
    COEF = np.zeros((64, NBF), np.float64)
    for f in range(fb):
        ang = 2 * np.pi * f * iidx / NB
        COEF[:, f] = np.cos(ang)
        if 1 <= f <= 1023:
            COEF[:, 1152 + f] = -np.sin(ang)
    IDM = np.zeros((NBF, 1024), np.float64)
    tau = np.arange(1024) + 1024
    for f in range(fb):
        w = (1.0 if f in (0, fb - 1) else 2.0) / NB
        ang = 2 * np.pi * f * tau / NB
        IDM[f, :] = w * np.cos(ang)
        if 1 <= f <= 1023:
            IDM[1152 + f, :] = -w * np.sin(ang)
    import ml_dtypes
    bf = ml_dtypes.bfloat16
    return FWD.astype(bf), IDM.astype(bf), COEF.astype(bf)


def l1_attn_inmaps(x2_full, mu3, r3, inp):
    ab = np.asarray(inp["L1_attn_ab"])
    h0 = np.asarray(inp["L1_attn_h_0"])
    FWD, IDM, COEF = dft_consts()
    cstm = np.zeros((128, 4), np.float32)
    cstm[:, 0] = 1e-5
    cstm[:, 1] = 1.0
    cstm[:, 2] = ((-1.0) ** np.arange(128)).astype(np.float32)
    maps = []
    for c in range(8):
        b = c // 2
        ch0 = (c % 2) * CHH
        maps.append(dict(
            x2tm=np.ascontiguousarray(x2_full[b][:, ch0:ch0 + CHH]),
            mu3c=f32c(mu3[b * S:(b + 1) * S].reshape(32, 128).T),
            r3c=f32c(r3[b * S:(b + 1) * S].reshape(32, 128).T),
            abT=f32c(np.concatenate([ab.T[:, ch0:ch0 + CHH],
                                     ab.T[:, D + ch0:D + ch0 + CHH]], axis=1)),
            h0row=f32c(h0[None, ch0:ch0 + CHH]),
            FWD=FWD, IDM=IDM, COEFC=COEF,
            ones1=np.ones((1, 128), np.float32),
            cst=cstm,
        ))
    return maps


NKC = 34
KCOL = NKC * 128
NQT = 8


def build_l0_attn():
    nc = bass.Bass("TRN2", target_bir_lowering=False)
    x0f = nc.dram_tensor("x0f", [D, S], F32, kind="ExternalInput").ap()
    wq_t = nc.dram_tensor("wq_t", [4, D, 128], F32, kind="ExternalInput").ap()
    wk_t = nc.dram_tensor("wk_t", [4, D, 128], F32, kind="ExternalInput").ap()
    wv_t = nc.dram_tensor("wv_t", [4, D, 128], F32, kind="ExternalInput").ap()
    wo_t = nc.dram_tensor("wo_t", [8, 512, 128], F32, kind="ExternalInput").ap()
    gcol = nc.dram_tensor("gcol", [128, 8], F32, kind="ExternalInput").ap()
    nkT = nc.dram_tensor("nkT", [128, 8], F32, kind="ExternalInput").ap()
    nvT = nc.dram_tensor("nvT", [128, 8], F32, kind="ExternalInput").ap()
    qsrow = nc.dram_tensor("qsrow", [1, 64], F32, kind="ExternalInput").ap()
    ksrow = nc.dram_tensor("ksrow", [1, 64], F32, kind="ExternalInput").ap()
    qsd = nc.dram_tensor("qsd", [2, 128], F32, kind="ExternalInput").ap()
    ksd = nc.dram_tensor("ksd", [2, 128], F32, kind="ExternalInput").ap()
    bcols = nc.dram_tensor("bcols", [128, 8 * NKC], F32, kind="ExternalInput").ap()
    brow0 = nc.dram_tensor("brow0", [1, 8 * S], F32, kind="ExternalInput").ap()
    M1c = nc.dram_tensor("M1c", [128, 128], BF16, kind="ExternalInput").ap()
    M2v = nc.dram_tensor("M2v", [128, 1024], BF16, kind="ExternalInput").ap()
    onesbf = nc.dram_tensor("onesbf", [1, 128], BF16, kind="ExternalInput").ap()
    ones1 = nc.dram_tensor("ones1", [1, 128], F32, kind="ExternalInput").ap()
    onescol = nc.dram_tensor("onescol", [128, 1], F32, kind="ExternalInput").ap()
    identb = nc.dram_tensor("identb", [128, 128], BF16, kind="ExternalInput").ap()
    ones65d = nc.dram_tensor("ones65d", [65, 128], F32, kind="ExternalInput").ap()
    cst = nc.dram_tensor("cst", [128, 4], F32, kind="ExternalInput").ap()
    opart = nc.dram_tensor("opart", [D, S], F32, kind="ExternalOutput").ap()

    with ExitStack() as ST:
        E = ST.enter_context
        xn = E(nc.sbuf_tensor("xn", [128, NCH * S], BF16))
        ao = E(nc.sbuf_tensor("ao", [128, 4 * S], BF16))
        arena2 = E(nc.sbuf_tensor("arena2", [128, 4352], F32))
        xrl = arena2[:, 0:4096]
        kvb = arena2[:].bitcast(BF16)
        kpk = kvb[:, 0:KCOL]
        vfm = kvb[:, KCOL:2 * KCOL]
        vT = E(nc.sbuf_tensor("vT", [128, NKC * 130], BF16))
        qpk = E(nc.sbuf_tensor("qpk", [128, S], BF16))
        phat = E(nc.sbuf_tensor("phat", [128, 2 * 512], BF16))
        sq = E(nc.sbuf_tensor("sq", [128, 4 * 512], F32))
        arena = E(nc.sbuf_tensor("arena", [2, 3 * S], F32))
        nb = E(nc.sbuf_tensor("nb", [1, 512], BF16))
        nbf = E(nc.sbuf_tensor("nbf", [1, 512], F32))
        ou = E(nc.sbuf_tensor("ou", [64, 512], BF16))
        rsr = E(nc.sbuf_tensor("rsr", [65, 512], F32))
        scr = E(nc.sbuf_tensor("scr", [128, 2 * 512], F32))
        wsl = E(nc.sbuf_tensor("wsl", [128, 2 * 8 * 128], BF16))
        gc_s = E(nc.sbuf_tensor("gc_s", [128, 8], F32))
        nk_s = E(nc.sbuf_tensor("nk_s", [128, 8], F32))
        nv_s = E(nc.sbuf_tensor("nv_s", [128, 8], F32))
        qs_s = E(nc.sbuf_tensor("qs_s", [1, 64], F32))
        ks_s = E(nc.sbuf_tensor("ks_s", [1, 64], F32))
        qsd_s = E(nc.sbuf_tensor("qsd_s", [2, 128], F32))
        ksd_s = E(nc.sbuf_tensor("ksd_s", [2, 128], F32))
        bc_s = E(nc.sbuf_tensor("bc_s", [128, 8 * NKC], F32))
        M1_s = E(nc.sbuf_tensor("M1_s", [128, 128], BF16))
        M2_s = E(nc.sbuf_tensor("M2_s", [128, 1024], BF16))
        onesbf_s = E(nc.sbuf_tensor("onesbf_s", [1, 128], BF16))
        ones65_s = E(nc.sbuf_tensor("ones65_s", [65, 128], F32))
        onescol_s = E(nc.sbuf_tensor("onescol_s", [128, 1], F32))
        identb_s = E(nc.sbuf_tensor("identb_s", [128, 128], BF16))
        bd2_s = E(nc.sbuf_tensor("bd2_s", [128, 2], F32))
        cst_s = E(nc.sbuf_tensor("cst_s", [128, 4], F32))
        tiny = E(nc.sbuf_tensor("tiny", [1, 8], F32))
        P = [E(nc.psum_tensor(f"P{i}", [128, 512], F32)) for i in range(5)]
        PT = E(nc.psum_tensor("PT", [128, 128], BF16))
        PO = [E(nc.psum_tensor(f"PO{i}", [65, 512], F32)) for i in range(2)]
        block = E(nc.Block())

        sd = Sched(nc)
        xn_t = [xn[:, ch * S:(ch + 1) * S] for ch in range(NCH)]
        ao_t = [ao[:, ch * S:(ch + 1) * S] for ch in range(4)]

        def row(i):
            return arena[0:1, i * S:(i + 1) * S]
        row3 = xrl[0:1, 0:S]
        nrow = arena[0:2, 0:KCOL]
        qn2 = arena[0:2, S + 512:S + 512 + S]

        def ld_c(sync, put):
            for t, d_ in ((gc_s, gcol), (nk_s, nkT), (nv_s, nvT), (qs_s, qsrow),
                          (ks_s, ksrow), (qsd_s, qsd), (ksd_s, ksd), (bc_s, bcols),
                          (M1_s, M1c), (M2_s, M2v), (onesbf_s, onesbf),
                          (ones65_s, ones65d),
                          (onescol_s, onescol), (identb_s, identb), (cst_s, cst)):
                put(sync.dma_start(out=t[:], in_=d_[:]))
        op_ldc = sd.dma("sync", ld_c, ndma=15, name="ld_c", stream="ldc")

        def f_bd2(v):
            v.memset(bd2_s[:], 0.0)
            v.memset(bd2_s[0:64, 0:1], 1.0)
            return v.memset(bd2_s[64:128, 1:2], 1.0)
        op_bd2 = sd.add("vector", f_bd2, name="bd2")

        st_evs = []
        prev_ev = None
        for tt in range(NQT):
            sl = slice(tt * 512, (tt + 1) * 512)

            def f_rlA(sync, put, sl=sl):
                for kc in range(NCH):
                    put(sync.dma_start(out=xrl[:, kc * 512:(kc + 1) * 512],
                                       in_=x0f[kc * 128:(kc + 1) * 128, sl]))
            op_rlA = sd.dma("sync", f_rlA, ndma=NCH,
                            deps=[prev_ev], name=f"st_rl{tt}", stream="srl")

            def f_sqa(s):
                last = None
                for kc in range(4):
                    last = s.activation(sq[:, kc * 512:(kc + 1) * 512],
                                        xrl[:, kc * 512:(kc + 1) * 512], AF.Square)
                return last
            op_sqa = sd.add("scalar", f_sqa, deps=[op_rlA], name=f"st_sqa{tt}")

            def f_mma(t):
                for kc in range(NCH):
                    t.matmul(P[0][0:1, :], onescol_s[:], xrl[:, kc * 512:(kc + 1) * 512],
                             start=(kc == 0), stop=(kc == NCH - 1))
                last = None
                for kc in range(4):
                    last = t.matmul(P[1][0:1, :], onescol_s[:],
                                    sq[:, kc * 512:(kc + 1) * 512],
                                    start=(kc == 0), stop=False)
                return last
            op_mma = sd.add("tensor", f_mma, deps=[op_rlA, op_sqa, op_ldc, prev_ev],
                            name=f"st_mma{tt}")

            def f_sqb(s):
                last = None
                for kc in range(4):
                    last = s.activation(sq[:, kc * 512:(kc + 1) * 512],
                                        xrl[:, (4 + kc) * 512:(5 + kc) * 512], AF.Square)
                return last
            op_sqb = sd.add("scalar", f_sqb, deps=[op_mma], name=f"st_sqb{tt}")

            def f_mmb(t):
                last = None
                for kc in range(4):
                    last = t.matmul(P[1][0:1, :], onescol_s[:],
                                    sq[:, kc * 512:(kc + 1) * 512],
                                    start=False, stop=(kc == 3))
                return last
            op_mmb = sd.add("tensor", f_mmb, deps=[op_sqb], name=f"st_mmb{tt}")

            def f_ev(v, sl=sl):
                v.tensor_copy(row(0)[:, sl], P[0][0:1, :])
                return v.tensor_copy(row(1)[:, sl], P[1][0:1, :])
            prev_ev = sd.add("vector", f_ev, deps=[op_mmb], name=f"st_ev{tt}")
            st_evs.append(prev_ev)

        def f_rm1(v):
            v.tensor_scalar(row(0), row(0), 1.0 / D, None, ALU.mult)
            v.tensor_scalar(row(1), row(1), 1.0 / D, None, ALU.mult)
            v.tensor_tensor(row(2), row(0), row(0), ALU.mult)
            return v.tensor_tensor(row(1), row(1), row(2), ALU.subtract)
        op_rm1 = sd.add("vector", f_rm1, deps=st_evs, name="rm1")

        def f_rm2(s):
            return s.activation(row(2), row(1), AF.Sqrt, bias=cst_s[0:1, 0:1])
        op_rm2 = sd.add("scalar", f_rm2, deps=[op_rm1], name="rm2")

        def f_rm3(v):
            v.reciprocal(row(2), row(2))
            v.tensor_tensor(row3, row(1), row(2), ALU.mult)
            return v.tensor_tensor(row3, row3, row(2), ALU.mult)
        op_rm3 = sd.add("vector", f_rm3, deps=[op_rm2], name="rm3")

        def f_rm4(s):
            return s.activation(row3, row3, AF.Sqrt, bias=cst_s[0:1, 0:1])
        op_rm4 = sd.add("scalar", f_rm4, deps=[op_rm3], name="rm4")

        def f_rm5(v):
            v.reciprocal(row3, row3)
            return v.tensor_tensor(row(1), row(2), row3, ALU.mult)
        op_rm5 = sd.add("vector", f_rm5, deps=[op_rm4], name="rm5")

        prev_ap = None
        for tt in range(NQT):
            sl = slice(tt * 512, (tt + 1) * 512)

            def f_rlB(sync, put, sl=sl):
                for kc in range(NCH):
                    put(sync.dma_start(out=xrl[:, kc * 512:(kc + 1) * 512],
                                       in_=x0f[kc * 128:(kc + 1) * 128, sl]))
            op_rlB = sd.dma("sync", f_rlB, ndma=NCH,
                            deps=[op_rm5, prev_ap], name=f"ap_rl{tt}", stream="srl")

            def f_bc(t, sl=sl):
                t.matmul(P[2][:], ones65_s[0:1, :], row(0)[:, sl], start=True, stop=True)
                return t.matmul(P[3][:], ones65_s[0:1, :], row(1)[:, sl], start=True, stop=True)
            op_bc = sd.add("tensor", f_bc, deps=[op_rm5, prev_ap], name=f"ap_bc{tt}")

            def f_ap(v, sl=sl):
                last = None
                for kc in range(NCH):
                    v.tensor_tensor(scr[:, 0:512], xrl[:, kc * 512:(kc + 1) * 512],
                                    P[2][:], ALU.subtract)
                    v.tensor_tensor(scr[:, 0:512], scr[:, 0:512], P[3][:], ALU.mult)
                    last = v.tensor_scalar(xn_t[kc][:, sl], scr[:, 0:512],
                                           gc_s[:, kc:kc + 1], None, ALU.mult)
                return last
            op_ap = sd.add("vector", f_ap, deps=[op_rlB, op_bc, op_ldc], name=f"ap{tt}")
            prev_ap = op_ap
        op_xn = prev_ap

        def f_qk1(v):
            v.reduce_max(tiny[0:1, 0:1], qs_s[:], axis=AX.X, apply_absolute_value=True)
            v.reduce_max(tiny[0:1, 1:2], ks_s[:], axis=AX.X, apply_absolute_value=True)
            v.tensor_tensor(tiny[0:1, 2:3], tiny[0:1, 0:1], tiny[0:1, 1:2], ALU.mult)
            return v.tensor_scalar(tiny[0:1, 3:4], tiny[0:1, 2:3], -8.0, None, ALU.mult)
        op_qk = sd.add("vector", f_qk1, deps=[op_ldc], name="qk1")

        prev_hp = {}
        oproj_deps = []
        for hp in range(4):
            def f_wk(g, put, hp=hp):
                put(g.dma_start(
                    out=wsl[:, 0:1024].rearrange("p (kc o) -> p kc o", kc=NCH),
                    in_=wk_t[hp].rearrange("(kc p) o -> p kc o", p=128)))
                put(g.dma_start(
                    out=wsl[:, 1024:2048].rearrange("p (kc o) -> p kc o", kc=NCH),
                    in_=wv_t[hp].rearrange("(kc p) o -> p kc o", p=128)))
            op_wk = sd.dma("gpsimd", f_wk, ndma=2, deps=[prev_hp.get("wsl")],
                           name=f"wk{hp}", stream="wsl")

            kv_evs = []
            for tt in range(NQT):
                sl = slice(tt * 512, (tt + 1) * 512)
                pp = tt % 2

                def f_kmm(t, sl=sl, pp=pp):
                    for kc in range(NCH):
                        t.matmul(P[0 + pp][:], wsl[:, kc * 128:(kc + 1) * 128],
                                 xn_t[kc][:, sl], start=(kc == 0), stop=(kc == NCH - 1))
                    last = None
                    for kc in range(NCH):
                        last = t.matmul(P[2 + pp][:], wsl[:, 1024 + kc * 128:1024 + (kc + 1) * 128],
                                        xn_t[kc][:, sl], start=(kc == 0), stop=(kc == NCH - 1))
                    return last
                op_kmm = sd.add("tensor", f_kmm,
                                deps=[op_wk, op_xn, prev_hp.get(("kv", pp))],
                                name=f"kmm{hp}_{tt}")

                def f_kev(s, sl=sl, pp=pp):
                    s.copy(kpkh[0][0:64, sl], P[0 + pp][0:64, :])
                    s.copy(kpkh[1][0:64, sl], P[0 + pp][64:128, :])
                    return s.copy(vfm[:, sl], P[2 + pp][:])
                op_kev = sd.add("scalar", f_kev, deps=[op_kmm], name=f"kev{hp}_{tt}")
                prev_hp[("kv", pp)] = op_kev
                kv_evs.append(op_kev)

            def f_null(v, hp=hp):
                v.memset(kpkh[0][64:65, 0:KCOL], 1.0)
                v.memset(kpkh[1][64:65, 0:KCOL], 1.0)
                v.memset(kpkh[0][0:64, 4096:KCOL], 0.0)
                v.memset(kpkh[1][0:64, 4096:KCOL], 0.0)
                v.memset(vfm[:, 4096:KCOL], 0.0)
                v.tensor_copy(kpkh[0][0:64, 4096:4098], nk_s[0:64, hp * 2:hp * 2 + 2])
                v.tensor_copy(kpkh[1][0:64, 4096:4098], nk_s[64:128, hp * 2:hp * 2 + 2])
                return v.tensor_copy(vfm[:, 4096:4098], nv_s[:, hp * 2:hp * 2 + 2])
            op_null = sd.add("vector", f_null,
                             deps=[op_ldc, prev_hp.get("attn_done"),
                                   prev_hp.get("vT_done")],
                             name=f"null{hp}")

            nsq_ops = []
            for ti in range(9):
                w = 512 if ti < 8 else 256
                sl = slice(ti * 512, ti * 512 + w)
                pp = ti % 2

                def f_ksq(s, sl=sl, pp=pp, w=w):
                    s.activation(sq[0:64, pp * 512:pp * 512 + w], kpkh[0][0:64, sl],
                                 AF.Square)
                    return s.activation(sq[64:128, pp * 512:pp * 512 + w],
                                        kpkh[1][0:64, sl], AF.Square)
                op_ksq = sd.add("scalar", f_ksq,
                                deps=[op_null if ti >= 8 else kv_evs[ti],
                                      prev_hp.get("attn_done"),
                                      nsq_ops[-2] if ti >= 2 else None],
                                name=f"ksq{hp}_{ti}")

                def f_kns(t, sl=sl, pp=pp, w=w):
                    return t.matmul(P[4][0:2, 0:w], bd2_s[:], sq[:, pp * 512:pp * 512 + w],
                                    start=True, stop=True)
                op_kns = sd.add("tensor", f_kns, deps=[op_ksq, op_bd2,
                                                       nsq_ops[-1] if ti >= 1 else None],
                                name=f"kns{hp}_{ti}")

                def f_kne(v, sl=sl, w=w):
                    return v.tensor_copy(nrow[:, sl], P[4][0:2, 0:w])
                op_kne = sd.add("vector", f_kne, deps=[op_kns], name=f"kne{hp}_{ti}",
                                nodrain=True)
                nsq_ops.append(op_kne)

            def f_krn(v):
                return v.tensor_scalar_max(nrow[:], nrow[:], 1e-24)
            op_krn = sd.add("vector", f_krn, deps=nsq_ops, name=f"krn{hp}")

            def f_krs(s):
                return s.activation(nrow[:], nrow[:], AF.Sqrt)
            op_krs = sd.add("scalar", f_krs, deps=[op_krn], name=f"krs{hp}")

            def f_krr(v):
                return v.reciprocal(nrow[:], nrow[:])
            op_krr = sd.add("vector", f_krr, deps=[op_krs], name=f"krr{hp}")

            khat_ops = []
            for ti in range(9):
                w = 512 if ti < 8 else 256
                sl = slice(ti * 512, ti * 512 + w)
                pp = ti % 2

                def f_rnb(t, sl=sl, pp=pp, w=w):
                    return t.matmul(P[0 + pp][:, 0:w], ksd_s[:], nrow[:, sl],
                                    start=True, stop=True)
                op_rnb = sd.add("tensor", f_rnb,
                                deps=[op_krr, op_ldc,
                                      khat_ops[-2] if ti >= 2 else None],
                                name=f"rnb{hp}_{ti}")

                def f_kh(v, sl=sl, pp=pp, w=w):
                    return v.tensor_tensor(kpk[:, sl], kpk[:, sl], P[0 + pp][:, 0:w],
                                           ALU.mult)
                op_kh = sd.add("vector", f_kh, deps=[op_rnb], name=f"kh{hp}_{ti}")
                khat_ops.append(op_kh)

            def f_vone(v):
                vv = vT[:].rearrange("p (c k) -> p c k", k=130)
                v.memset(vv[:, :, 64:65], 1.0)
                return v.memset(vv[:, :, 129:130], 1.0)
            op_vone = sd.add("vector", f_vone, deps=[prev_hp.get("attn_done")],
                             name=f"vone{hp}")

            vt_evs = []
            prev_vt = None
            for tc in range(NKC):
                def f_vtr(t, tc=tc):
                    return t.transpose(PT[:], vfm[:, tc * 128:(tc + 1) * 128],
                                       identb_s[:])
                op_vtr = sd.add("tensor", f_vtr,
                                deps=[op_null, prev_vt] + kv_evs,
                                name=f"vtr{hp}_{tc}")

                def f_vte(v, tc=tc):
                    v.tensor_copy(vT[:, tc * 130:tc * 130 + 64], PT[:, 0:64])
                    return v.tensor_copy(vT[:, tc * 130 + 65:tc * 130 + 129],
                                         PT[:, 64:128])
                op_vte = sd.add("vector", f_vte, deps=[op_vtr, op_vone],
                                name=f"vte{hp}_{tc}")
                prev_vt = op_vte
                vt_evs.append(op_vte)
            prev_hp["vT_done"] = vt_evs[-1]

            def f_wq(g, put, hp=hp):
                put(g.dma_start(
                    out=wsl[:, 0:1024].rearrange("p (kc o) -> p kc o", kc=NCH),
                    in_=wq_t[hp].rearrange("(kc p) o -> p kc o", p=128)))
            op_wq = sd.dma("gpsimd", f_wq, ndma=1, deps=[kv_evs[-1]],
                           name=f"wq{hp}", stream="wsl2")

            q_evs = []
            for tt in range(NQT):
                sl = slice(tt * 512, (tt + 1) * 512)
                pp = tt % 2

                def f_qmm(t, sl=sl, pp=pp):
                    last = None
                    for kc in range(NCH):
                        last = t.matmul(P[0 + pp][:], wsl[:, kc * 128:(kc + 1) * 128],
                                        xn_t[kc][:, sl], start=(kc == 0), stop=(kc == NCH - 1))
                    return last
                op_qmm = sd.add("tensor", f_qmm,
                                deps=[op_wq, op_xn, prev_hp.get(("q", pp)),
                                      khat_ops[-1] if tt == 0 else None],
                                name=f"qmm{hp}_{tt}")

                def f_qsq(s, sl=sl, pp=pp):
                    return s.activation(sq[:, pp * 512:(pp + 1) * 512], P[0 + pp][:],
                                        AF.Square)
                op_qsq = sd.add("scalar", f_qsq, deps=[op_qmm,
                                                       prev_hp.get(("qs", pp))],
                                name=f"qsq{hp}_{tt}")

                qn_ps = P[4] if pp == 0 else P[3]

                def f_qns(t, sl=sl, pp=pp, qn_ps=qn_ps):
                    return t.matmul(qn_ps[0:2, :], bd2_s[:],
                                    sq[:, pp * 512:(pp + 1) * 512],
                                    start=True, stop=True)
                op_qns = sd.add("tensor", f_qns,
                                deps=[op_qsq, prev_hp.get(("qnev", pp)),
                                      prev_hp.get(("qh", pp))],
                                name=f"qns{hp}_{tt}")

                def f_qne(v, sl=sl, pp=pp, qn_ps=qn_ps):
                    return v.tensor_copy(qn2[:, sl], qn_ps[0:2, :])
                op_qne = sd.add("vector", f_qne, deps=[op_qns], name=f"qne{hp}_{tt}",
                                nodrain=True)
                prev_hp[("qnev", pp)] = op_qne

                def f_qev(s, sl=sl, pp=pp):
                    s.copy(scrq[:, (2 * pp) * 512:(2 * pp + 1) * 512],
                           P[0 + pp][0:64, :])
                    return s.copy(scrq[:, (2 * pp + 1) * 512:(2 * pp + 2) * 512],
                                  P[0 + pp][64:128, :])
                op_qev = sd.add("scalar", f_qev, deps=[op_qmm], name=f"qev{hp}_{tt}")

                def f_qrs(s, sl=sl):
                    return s.activation(qn2[:, sl], qn2[:, sl], AF.Sqrt)
                op_qrs = sd.add("scalar", f_qrs, deps=[op_qne], name=f"qrs{hp}_{tt}")

                def f_qrr(v, sl=sl):
                    return v.reciprocal(qn2[:, sl], qn2[:, sl])
                op_qrr = sd.add("vector", f_qrr, deps=[op_qrs], name=f"qrr{hp}_{tt}",
                                nodrain=True)

                def f_qrb(t, sl=sl, pp=pp):
                    t.matmul(P[2 + pp][0:64, :], qsd_s[:, 0:64], qn2[:, sl],
                             start=True, stop=True)
                    return t.matmul(P[5][0:64, :], qsd_s[:, 64:128], qn2[:, sl],
                                    start=True, stop=True)
                op_qrb = sd.add("tensor", f_qrb,
                                deps=[op_qrr, op_ldc, prev_hp.get(("qh", pp)),
                                      prev_hp.get("qhlast"), vt_evs[-1]],
                                name=f"qrb{hp}_{tt}")

                def f_qh(v, sl=sl, pp=pp):
                    v.tensor_tensor(qpkh[0][0:64, sl],
                                    scrq[:, (2 * pp) * 512:(2 * pp + 1) * 512],
                                    P[2 + pp][0:64, :], ALU.mult)
                    return v.tensor_tensor(qpkh[1][0:64, sl],
                                           scrq[:, (2 * pp + 1) * 512:(2 * pp + 2) * 512],
                                           P[5][0:64, :], ALU.mult)
                op_qh = sd.add("vector", f_qh, deps=[op_qrb, op_qev], name=f"qh{hp}_{tt}")
                prev_hp["qhlast"] = op_qh
                prev_hp[("q", pp)] = op_qh
                prev_hp[("qs", pp)] = op_qsq
                prev_hp[("qh", pp)] = op_qh
                q_evs.append(op_qh)

            attn_done_ops = []
            for hloc in range(2):
                h = hp * 2 + hloc
                rbase = hloc * 64
                for qt in range(NQT):
                    qsl = slice(qt * 512, (qt + 1) * 512)
                    par = (h * NQT + qt) % 2
                    kccs = list(range(0, min(4 * qt + 4, 32))) + [32]

                    def f_nb(sync, put, h=h, qt=qt):
                        put(sync.dma_start(out=nbf[0:1, :],
                                           in_=brow0[0:1, h * S + qt * 512:h * S + (qt + 1) * 512]))
                    op_nb = sd.dma("sync", f_nb, ndma=1,
                                   deps=[prev_hp.get("nbuse")],
                                   name=f"nb{h}_{qt}", stream="nb")

                    def f_nbc(v):
                        return v.tensor_scalar(nb[0:1, :], nbf[0:1, :],
                                               tiny[0:1, 3:4], None, ALU.add)
                    op_nbc = sd.add("vector", f_nbc, deps=[op_nb, op_qk,
                                                           prev_hp.get("nbuse")],
                                    name=f"nbc{h}_{qt}")

                    pv_last = None
                    step_ops = []
                    for si, kcc in enumerate(kccs):
                        spp = si % 2
                        is_diag = (4 * qt <= kcc <= 4 * qt + 3) and kcc < 32

                        def f_smm(t, kcc=kcc, qsl=qsl, spp=spp, is_diag=is_diag,
                                  qt=qt, rbase=rbase, par=par):
                            first = True
                            if is_diag:
                                dv = kcc - 4 * qt
                                off = 512 - dv * 128
                                t.matmul(P[0 + spp][:], M1_s[:],
                                         M2_s[:, off:off + 512],
                                         start=True, stop=False)
                                first = False
                            t.matmul(P[0 + spp][:],
                                     kpk[rbase:rbase + 64, kcc * 128:(kcc + 1) * 128],
                                     qpk[rbase:rbase + 64, qsl],
                                     start=first, stop=False)
                            return t.matmul(P[0 + spp][:], onesbf_s[:],
                                            nb[0:1, :],
                                            start=False, stop=True)
                        dep_smm = [op_nbc, khat_ops[-1], q_evs[-1],
                                   step_ops[-2] if si >= 2 else None]
                        op_smm = sd.add("tensor", f_smm, deps=dep_smm,
                                        name=f"smm{h}_{qt}_{kcc}")
                        prev_hp["nbuse"] = op_smm

                        def f_exp(s, kcc=kcc, spp=spp, h=h):
                            hloc2 = h % 8
                            return s.activation(phat[:, spp * 512:(spp + 1) * 512],
                                                P[0 + spp][:], AF.Exp,
                                                bias=bc_s[:, hloc2 * NKC + kcc:hloc2 * NKC + kcc + 1])
                        op_exp = sd.add("scalar", f_exp,
                                        deps=[op_smm, op_ldc,
                                              step_ops[-1] if si >= 1 else None],
                                        name=f"exp{h}_{qt}_{kcc}")

                        def f_pv(t, kcc=kcc, spp=spp, si=si, hloc=hloc, par=par):
                            return t.matmul(PO[par][:],
                                            vT[:, kcc * 130 + hloc * 65:kcc * 130 + hloc * 65 + 65],
                                            phat[:, spp * 512:(spp + 1) * 512],
                                            start=(si == 0), stop=(si == len(kccs) - 1))
                        op_pv = sd.add("tensor", f_pv,
                                       deps=[op_exp, vt_evs[-1],
                                             prev_hp.get(("fin", par))],
                                       name=f"pv{h}_{qt}_{kcc}")
                        step_ops.append(op_pv)
                        pv_last = op_pv

                    def f_rs(v, par=par):
                        return v.reciprocal(rsr[64:65, :], PO[par][64:65, :])
                    op_rs = sd.add("vector", f_rs, deps=[pv_last], name=f"rs{h}_{qt}")

                    def f_rsb(t, par=par):
                        return t.matmul(P[4][0:64, :], ones65_s[64:65, 0:64],
                                        rsr[64:65, :], start=True, stop=True)
                    op_rsb = sd.add("tensor", f_rsb,
                                    deps=[op_rs, prev_hp.get("rsb")],
                                    name=f"rsb{h}_{qt}")
                    prev_hp["rsb"] = op_rsb

                    def f_oev(s, par=par):
                        return s.copy(ou[:], PO[par][0:64, :])
                    op_oev = sd.add("scalar", f_oev,
                                    deps=[pv_last, prev_hp.get("oudone")],
                                    name=f"oev{h}_{qt}")

                    def f_fin(v, h=h, qt=qt, qsl=qsl, rbase=rbase):
                        return v.tensor_tensor(ao_t[h // 2][rbase:rbase + 64, qsl],
                                               ou[:],
                                               P[4][0:64, :], ALU.mult)
                    op_fin = sd.add("vector", f_fin, deps=[op_oev, op_rsb],
                                    name=f"fin{h}_{qt}")
                    prev_hp["oudone"] = op_fin
                    prev_hp[("fin", par)] = op_fin
                    attn_done_ops.append(op_fin)
            prev_hp["attn_done"] = attn_done_ops[-1]
            prev_hp["wsl"] = q_evs[-1]
            oproj_deps.extend(attn_done_ops)

        out_ops = []
        prev_o = {}
        for oc in range(8):
            def f_wo(g, put, oc=oc):
                put(g.dma_start(
                    out=wsl[:, (oc % 2) * 512:(oc % 2) * 512 + 512]
                        .rearrange("p (kc o) -> p kc o", kc=4),
                    in_=wo_t[oc].rearrange("(kc p) o -> p kc o", p=128)))
            op_wo = sd.dma("gpsimd", f_wo, ndma=1,
                           deps=oproj_deps[-1:] + [prev_o.get(("mm", oc % 2))],
                           name=f"wo{oc}", stream=f"wo{oc % 2}")
            for tt in range(NQT):
                sl = slice(tt * 512, (tt + 1) * 512)
                pp = tt % 2

                def f_omm(t, oc=oc, sl=sl, pp=pp):
                    last = None
                    for kc in range(4):
                        last = t.matmul(P[0 + pp][:],
                                        wsl[:, (oc % 2) * 512 + kc * 128:(oc % 2) * 512 + (kc + 1) * 128],
                                        ao_t[kc][:, sl], start=(kc == 0), stop=(kc == 3))
                    return last
                op_omm = sd.add("tensor", f_omm,
                                deps=[op_wo, prev_o.get(("ev", pp))] + oproj_deps[-1:],
                                name=f"omm{oc}_{tt}")
                prev_o[("mm", oc % 2)] = op_omm

                def f_oev2(s, pp=pp):
                    return s.copy(oscr[:, pp * 512:(pp + 1) * 512], P[0 + pp][:])
                op_oev2 = sd.add("scalar", f_oev2, deps=[op_omm], name=f"oev2{oc}_{tt}")

                def f_odma(sync, put, oc=oc, sl=sl, pp=pp):
                    put(sync.dma_start(out=opart[oc * 128:(oc + 1) * 128, sl],
                                       in_=scr[:, pp * 512:(pp + 1) * 512]))
                op_od = sd.dma("sync", f_odma, ndma=1, deps=[op_oev2],
                               name=f"od{oc}_{tt}", stream="out")
                prev_o[("ev", pp)] = op_od
                out_ops.append(op_od)

        def fin2(g):
            return g.engine_nop()
        sd.add("gpsimd", fin2, deps=[out_ops[-1]], name="fin2")
        sd.emit(block)
    return nc


def l0_attn_inmaps(inp):
    x0 = np.asarray(inp["L0_attn_x"])
    wq = np.asarray(inp["L0_attn_w_q"])
    wkv = np.asarray(inp["L0_attn_w_kv"])
    wo = np.asarray(inp["L0_attn_w_out"])
    nkv = np.asarray(inp["L0_attn_null_kv"]).reshape(HEADS, NNK, 2, DH)
    slopes = alibi_slopes()
    maps = []
    import ml_dtypes
    bf = ml_dtypes.bfloat16
    M1 = np.tril(np.ones((128, 128), np.float64)).T
    M2 = np.zeros((128, 1024), np.float64)
    for k in range(128):
        lim = min(1024, k + 512)
        M2[k, :lim] = -1e9
    M1b = M1.astype(bf)
    M2b = M2.astype(bf)
    cstm = np.zeros((128, 4), np.float32); cstm[:, 0] = 1e-5
    for c in range(8):
        b = c // 2
        H0 = (c % 2) * 8
        hsl = slice(H0 * DH, H0 * DH + 512)
        bcols = np.zeros((128, 8 * NKC), np.float32)
        brow0 = np.zeros((1, 8 * S), np.float32)
        q = np.arange(S)
        for hl in range(8):
            sl_ = float(slopes[H0 + hl])
            for kcc in range(NKC):
                p = np.arange(128)
                colv = np.full(128, -1e9, np.float32)
                if kcc < 32:
                    cglob = kcc * 128 + p
                    colv = (sl_ * (cglob + 2)).astype(np.float32)
                elif kcc == 32:
                    colv[:] = -1e9
                    colv[0] = 0.0
                    colv[1] = sl_ * 1.0
                bcols[:, hl * NKC + kcc] = colv
            brow0[0, hl * S:(hl + 1) * S] = -(sl_ * (q + 2) + 2.0)
        nkh = nkv[H0:H0 + 8, :, 0, :]
        nvh = nkv[H0:H0 + 8, :, 1, :]
        nkT_ = np.zeros((128, 8), np.float32)
        nvT_ = np.zeros((128, 8), np.float32)
        for hpi in range(4):
            for hl in range(2):
                h = hpi * 2 + hl
                nkT_[hl * 64:(hl + 1) * 64, hpi * 2:hpi * 2 + 2] = nkh[h].T
                nvT_[hl * 64:(hl + 1) * 64, hpi * 2:hpi * 2 + 2] = nvh[h].T
        qsd_ = np.zeros((2, 128), np.float32)
        qsd_[0, 0:64] = inp["L0_attn_q_scale"] * 8.0 if False else 0
        qsd_ = np.zeros((2, 128), np.float32)
        qsd_[0, 0:64] = np.asarray(inp["L0_attn_q_scale"], np.float64) * 8.0
        qsd_[1, 64:128] = np.asarray(inp["L0_attn_q_scale"], np.float64) * 8.0
        ksd_ = np.zeros((2, 128), np.float32)
        ksd_[0, 0:64] = inp["L0_attn_k_scale"]
        ksd_[1, 64:128] = inp["L0_attn_k_scale"]
        maps.append(dict(
            x0f=np.ascontiguousarray(x0[b].T),
            wq_t=tile_lhsT(f32c(wq[:, hsl])),
            wk_t=tile_lhsT(f32c(wkv[:, hsl])),
            wv_t=tile_lhsT(f32c(wkv[:, D + H0 * DH:D + H0 * DH + 512])),
            wo_t=tile_lhsT(f32c(wo[hsl.start:hsl.start + 512, :])),
            gcol=f32c(np.asarray(inp["L0_attn_gamma"]).reshape(8, 128).T),
            nkT=nkT_, nvT=nvT_,
            qsrow=f32c(np.asarray(inp["L0_attn_q_scale"])[None, :]),
            ksrow=f32c(np.asarray(inp["L0_attn_k_scale"])[None, :]),
            qsd=qsd_, ksd=ksd_,
            bcols=bcols, brow0=brow0,
            M1c=M1b, M2v=M2b,
            onesbf=np.ones((1, 128), bf),
            ones1=np.ones((1, 128), np.float32),
            ones65d=np.ones((65, 128), np.float32),
            onescol=np.ones((128, 1), np.float32),
            identb=np.eye(128).astype(bf),
            cst=cstm,
        ))
    return maps


def kernel_multilaunch(**inputs):
    inp = inputs
    x0 = np.asarray(inp["L0_attn_x"], dtype=np.float32)

    nc1 = build_l0_attn()
    r1 = run_bass_kernel_spmd(nc1, l0_attn_inmaps(inp), core_ids=list(range(8)))
    oparts = [np.asarray(r1.results[c]["opart"]) for c in range(8)]

    x0_fm = [np.ascontiguousarray(x0[c // 2].T[:, (c % 2) * RPC:(c % 2 + 1) * RPC])
             for c in range(8)]
    oa_pc = [oparts[2 * (c // 2)][:, (c % 2) * RPC:(c % 2 + 1) * RPC] for c in range(8)]
    ob_pc = [oparts[2 * (c // 2) + 1][:, (c % 2) * RPC:(c % 2 + 1) * RPC] for c in range(8)]
    nc2 = build_l0_ffn()
    r2 = run_bass_kernel_spmd(nc2, l0_ffn_inmaps(x0_fm, oa_pc, ob_pc, inp),
                              core_ids=list(range(8)))
    x2f = [np.asarray(r2.results[c]["x2f"]) for c in range(8)]
    st3 = [np.asarray(r2.results[c]["stat3"]) for c in range(8)]

    x2_full = np.stack([np.concatenate([x2f[2 * b].T, x2f[2 * b + 1].T], axis=0)
                        for b in range(B)])
    mu3 = np.concatenate([st3[c][0] for c in range(8)])
    r3 = np.concatenate([st3[c][1] for c in range(8)])
    nc3 = build_l1_attn()
    r3r = run_bass_kernel_spmd(nc3, l1_attn_inmaps(x2_full, mu3, r3, inp),
                               core_ids=list(range(8)))
    y = np.zeros((B, S, D), np.float32)
    for c in range(8):
        b, ch0 = c // 2, (c % 2) * CHH
        y[b][:, ch0:ch0 + CHH] = np.asarray(r3r.results[c]["yout"])

    y_fm = [np.ascontiguousarray(y[c // 2, (c % 2) * RPC:(c % 2 + 1) * RPC, :].T)
            for c in range(8)]
    nc4 = build_l1_ffn()
    r4 = run_bass_kernel_spmd(nc4, l1_ffn_inmaps(x2f, y_fm, inp),
                              core_ids=list(range(8)))
    out = np.stack([np.concatenate([np.asarray(r4.results[2 * b]["x4f"]).T,
                                    np.asarray(r4.results[2 * b + 1]["x4f"]).T], axis=0)
                    for b in range(B)])
    return np.ascontiguousarray(out.astype(np.float32))


class Arena:
    def __init__(self, ap):
        self.ap = ap
        self.NW = ap.shape[1]
        self.off = 0

    def reset(self):
        self.off = 0

    def alloc(self, p, n, dt, base=0):
        w32 = (n + 1) // 2 if dt == BF16 else n
        assert self.off + w32 <= self.NW, f"arena overflow {self.off}+{w32}"
        sl = self.ap[:][base:base + p, self.off:self.off + w32]
        self.off += w32
        return sl.bitcast(BF16)[:, 0:n] if dt == BF16 else sl


def build_fused():
    nc = bass.Bass("TRN2", target_bir_lowering=False)
    dm = {}
    def din(name, shape, dt):
        dm[name] = nc.dram_tensor(name, shape, dt, kind="ExternalInput").ap()
        return dm[name]
    def dint(name, shape, dt):
        dm[name] = nc.dram_tensor(name, shape, dt, kind="Internal").ap()
        return dm[name]

    x0f = din("x0f", [D, S], F32)
    wq_t = din("wq_t", [8, D, 128], F32)
    wk_t = din("wk_t", [8, D, 128], F32)
    wv_t = din("wv_t", [8, D, 128], F32)
    wo_t = din("wo_t", [8, D, 128], F32)
    gcol = din("gcol", [128, 8], F32)
    nkT = din("nkT", [128, 16], F32)
    nvT = din("nvT", [128, 16], F32)
    qsrow = din("qsrow", [1, 64], F32)
    ksrow = din("ksrow", [1, 64], F32)
    qsd = din("qsd", [2, 128], F32)
    ksd = din("ksd", [2, 128], F32)
    bcq = din("bcq", [128, 16 * 6 * NKC], F32)
    brow0 = din("brow0", [1, 16 * 3072], F32)
    M1c = din("M1c", [128, 128], BF16)
    M2Av = din("M2Av", [128, 1024], BF16)
    M2Bv = din("M2Bv", [128, 1024], BF16)
    onesbf = din("onesbf", [1, 128], BF16)
    ones1 = din("ones1", [1, 128], F32)
    ones65d = din("ones65d", [65, 128], F32)
    onescol = din("onescol", [128, 1], F32)
    onescolb = din("onescolb", [128, 1], BF16)
    identb = din("identb", [128, 128], BF16)
    idf32 = din("idf32", [128, 128], F32)
    hmask = din("hmask", [128, 1], F32)
    cst = din("cst", [128, 4], F32)
    x0q = din("x0q", [D, 3072], F32)
    bw_t = din("bw_t", [8, D, 128], F32)
    bias_c = din("bias_c", [128, 8], F32)
    A_t = din("A_t", [D, 128], F32)
    B_t = din("B_t", [8, 128, 128], F32)
    rwt = din("rwt", [D, 8], F32)
    rbias = din("rbias", [128, 8], F32)
    Gm = din("Gm", [8, 128], F32)
    abT = din("abT", [64, 2 * D], F32)
    h0row = din("h0row", [1, D], F32)
    FWD = din("FWD", [1024, NBF], BF16)
    IDM = din("IDM", [NBF, 1024], BF16)
    COEFC = din("COEFC", [64, NBF], BF16)
    w12_t = din("w12_t", [32, 2 * D, 128], F32)
    h1_t = din("h1_t", [32, 4096, 128], F32)
    h2_t = din("h2_t", [32, 4096, 128], F32)
    w3_t = din("w3_t", [8, 4096, 128], F32)
    w1c = din("w1c", [128, 32], F32)
    w2c = din("w2c", [128, 32], F32)
    h1c = din("h1c", [128, 32], F32)
    h2c = din("h2c", [128, 32], F32)
    w3c = din("w3c", [128, 8], F32)
    x1sc = dint("x1sc", [D, 3072], F32)
    x2sc = dint("x2sc", [D, RPC], F32)
    u3d = dint("u3d", [3072, D], BF16)
    ya = dint("ya", [D, RPC], BF16)
    osc = dint("osc", [D, 3072], F32)
    xsc = dint("xsc", [D, RPC], F32)
    u4sc = dint("u4sc", [D, RPC], BF16)
    x4f = nc.dram_tensor("x4f", [D, RPC], F32, kind="ExternalOutput").ap()
    x0fb = dint("x0fb", [D, S], BF16)
    x0qb = dint("x0qb", [D, 3072], BF16)

    PAIRS = [[0, 1], [2, 3], [4, 5], [6, 7]]

    with ExitStack() as ST:
        E = ST.enter_context
        ARW = 52856
        AR = E(nc.sbuf_tensor("AR", [128, ARW], F32))
        ones65_s = E(nc.sbuf_tensor("ones65_s", [65, 128], F32))
        onescol_s = E(nc.sbuf_tensor("onescol_s", [128, 1], F32))
        onescolb_s = E(nc.sbuf_tensor("onescolb_s", [128, 1], BF16))
        identb_s = E(nc.sbuf_tensor("identb_s", [128, 128], BF16))
        cst_s = E(nc.sbuf_tensor("cst_s", [128, 4], F32))
        colsb = E(nc.sbuf_tensor("colsb", [128, 32 * 4 + 8], F32))
        P = [E(nc.psum_tensor(f"P{i}", [128, 512], F32)) for i in range(6)]
        PO = [E(nc.psum_tensor(f"PO{i}", [65, 512], F32)) for i in range(2)]
        block = E(nc.Block())

        PT = P[5][:][:, 0:64].bitcast(BF16)
        S0 = P[4][:][0:1, :]
        S1 = P[5][:][0:1, :]
        ar = Arena(AR)
        sd = Sched(nc)

        def ld_consts(sync, put):
            for t, d_ in ((ones65_s, ones65d), (onescol_s, onescol),
                          (onescolb_s, onescolb), (identb_s, identb),
                          (cst_s, cst)):
                put(sync.dma_start(out=t[:], in_=d_[:]))
        op_ldc0 = sd.dma("sync", ld_consts, ndma=5, name="ldc0", stream="ldc")

        def f_pca(g, put):
            for kc in range(4):
                put(g.dma_start(out=x0fb[kc * 128:(kc + 1) * 128, :],
                                in_=x0f[kc * 128:(kc + 1) * 128, :]))
            for kc in range(4):
                put(g.dma_start(out=x0qb[kc * 128:(kc + 1) * 128, :],
                                in_=x0q[kc * 128:(kc + 1) * 128, :]))
        op_pca = sd.dma("gpsimd", f_pca, ndma=8, name="pca", stream="pc0")

        def f_pcb(g, put):
            for kc in range(4, 8):
                put(g.dma_start(out=x0fb[kc * 128:(kc + 1) * 128, :],
                                in_=x0f[kc * 128:(kc + 1) * 128, :]))
            for kc in range(4, 8):
                put(g.dma_start(out=x0qb[kc * 128:(kc + 1) * 128, :],
                                in_=x0q[kc * 128:(kc + 1) * 128, :]))
        op_pcb = sd.dma("gpsimd", f_pcb, ndma=8, name="pcb", stream="pc1")

        consts = {"ones1": ones65_s[0:1, :], "onescol": onescol_s[:],
                  "onescolb": onescolb_s[:], "eps1": cst_s[0:1, 0:1],
                  "onesat": None}

        ar.reset()
        gc_s = ar.alloc(128, 8, F32)
        nk_s = ar.alloc(128, 16, F32)
        nv_s = ar.alloc(128, 16, F32)
        qs_s = ar.alloc(1, 64, F32)
        ks_s = ar.alloc(1, 64, F32)
        qsd_s = ar.alloc(2, 128, F32)
        ksd_s = ar.alloc(2, 128, F32)
        bcq2 = ar.alloc(128, 2 * NKC, F32)
        M1_s = ar.alloc(128, 128, BF16)
        M2A_s = ar.alloc(128, 1024, BF16)
        M2B_s = ar.alloc(128, 1024, BF16)
        onesbf_s = ar.alloc(1, 128, BF16)
        bd2_s = ar.alloc(128, 2, F32)
        tiny = ar.alloc(1, 8, F32)
        hmask_s = ar.alloc(128, 1, F32)
        idf32_p1 = ar.alloc(1, 1, F32)

        def ld_p1c(sync, put):
            for t, d_ in ((gc_s, gcol), (nk_s, nkT), (nv_s, nvT), (qs_s, qsrow),
                          (ks_s, ksrow), (qsd_s, qsd), (ksd_s, ksd),
                          (M1_s, M1c), (M2A_s, M2Av), (M2B_s, M2Bv),
                          (onesbf_s, onesbf), (hmask_s, hmask),
                          (idf32_p1, idf32[0:1, 0:1])):
                put(sync.dma_start(out=t, in_=d_[:]))
        op_ldp1 = sd.dma("sync", ld_p1c, ndma=13, deps=[op_ldc0], name="ldp1",
                         stream="ldc")
        op_ldc = op_ldp1

        QTOK = 3072
        NQTL = 6
        ao = ar.alloc(128, NCH * QTOK, BF16)
        arena2 = ar.alloc(128, 4352, F32)
        xrl = arena2[:, 0:4096]
        kvb = arena2.bitcast(BF16)
        vfm = kvb[:, 0:KCOL]
        kpkh = [kvb[0:65, KCOL:2 * KCOL],
                ar.alloc(65, KCOL, BF16)]
        xrlb = ar.alloc(128, 2 * 8 * 512, BF16)
        vT = ar.alloc(128, NKC * 130, BF16)
        qpkh = [ar.alloc(65, QTOK, BF16),
                ar.alloc(65, QTOK, BF16)]
        phat = ar.alloc(128, 4 * 512, BF16)
        sq = ar.alloc(128, 8 * 512, F32)
        rowsg = ar.alloc(1, S, F32)
        rowsq = ar.alloc(1, QTOK, F32)
        rowx = ao[:].bitcast(F32)[0:1, 0:S]
        rcolk = ar.alloc(128, NKC, F32)
        crow = ar.alloc(1, 3 * 128, F32)
        nrow2 = ar.alloc(2, KCOL, F32)
        qn2 = nrow2[:, 0:QTOK]
        nbst = ar.alloc(1, QTOK, BF16)
        ou = ar.alloc(64, 512, BF16)
        rsr = ar.alloc(65, 512, F32)
        scrq = ar.alloc(64, 4 * 512, F32)
        wsl = ar.alloc(128, 3 * 8 * 128, BF16)

        ao_t = [ao[:, ch * QTOK:(ch + 1) * QTOK] for ch in range(NCH)]

        def f_bd2(v):
            v.memset(bd2_s[:], 0.0)
            v.memset(bd2_s[0:64, 0:1], 1.0)
            return v.memset(bd2_s[64:128, 1:2], 1.0)
        op_bd2 = sd.add("vector", f_bd2, name="bd2")

        xslot = [xrl, xrlb[:].bitcast(F32)[:, 0:4096]]
        st_state = {}

        def emit_dln_stats(tag, src_dram, ntok, murow, Rrow, scrrow, deps0):
            scrrow = scrrow[:, 0:ntok]
            n_tt = ntok // 512
            st_evs = []
            prev_ev = None
            for tt in range(n_tt):
                sl = slice(tt * 512, (tt + 1) * 512)
                sp2 = tt % 2
                xs = xslot[sp2]
                sqs = sq[:, sp2 * 2048:(sp2 + 1) * 2048]

                def f_rlA(sync, put, sl=sl, xs=xs):
                    for kc in range(4):
                        put(sync.dma_start(out=xs[:, kc * 512:(kc + 1) * 512],
                                           in_=src_dram[kc * 128:(kc + 1) * 128, sl]))
                op_rlA = sd.dma("sync", f_rlA, ndma=4,
                                deps=[st_state.get(("x", sp2))] + deps0,
                                name=f"{tag}rlA{tt}", stream=f"srl{sp2}")

                def f_rlB(g, put, sl=sl, xs=xs):
                    for kc in range(4, NCH):
                        put(g.dma_start(out=xs[:, kc * 512:(kc + 1) * 512],
                                        in_=src_dram[kc * 128:(kc + 1) * 128, sl]))
                op_rlB = sd.dma("gpsimd", f_rlB, ndma=4,
                                deps=[st_state.get(("x", sp2))] + deps0,
                                name=f"{tag}rlB{tt}", stream=f"srl2{sp2}")

                def f_sqa(s, xs=xs, sqs=sqs):
                    last = None
                    for kc in range(4):
                        last = s.activation(sqs[:, kc * 512:(kc + 1) * 512],
                                            xs[:, kc * 512:(kc + 1) * 512], AF.Square)
                    return last
                op_sqa = sd.add("scalar", f_sqa,
                                deps=[op_rlA, st_state.get(("sq", sp2))],
                                name=f"{tag}sqa{tt}")

                def f_mma(t, xs=xs, sqs=sqs):
                    for kc in range(NCH):
                        t.matmul(P[0][0:1, :], onescol_s[:], xs[:, kc * 512:(kc + 1) * 512],
                                 start=(kc == 0), stop=(kc == NCH - 1))
                    last = None
                    for kc in range(4):
                        last = t.matmul(P[1][0:1, :], onescol_s[:],
                                        sqs[:, kc * 512:(kc + 1) * 512],
                                        start=(kc == 0), stop=False)
                    return last
                op_mma = sd.add("tensor", f_mma,
                                deps=[op_rlA, op_rlB, op_sqa, op_ldc, prev_ev],
                                name=f"{tag}mma{tt}")

                def f_sqb(s, xs=xs, sqs=sqs):
                    last = None
                    for kc in range(4):
                        last = s.activation(sqs[:, kc * 512:(kc + 1) * 512],
                                            xs[:, (4 + kc) * 512:(5 + kc) * 512], AF.Square)
                    return last
                op_sqb = sd.add("scalar", f_sqb, deps=[op_mma], name=f"{tag}sqb{tt}")
                st_state[("x", sp2)] = op_sqb

                def f_mmb(t, sqs=sqs):
                    last = None
                    for kc in range(4):
                        last = t.matmul(P[1][0:1, :], onescol_s[:],
                                        sqs[:, kc * 512:(kc + 1) * 512],
                                        start=False, stop=(kc == 3))
                    return last
                op_mmb = sd.add("tensor", f_mmb, deps=[op_sqb], name=f"{tag}mmb{tt}")
                st_state[("sq", sp2)] = op_mmb

                def f_ev(v, sl=sl):
                    v.tensor_copy(murow[:, sl], P[0][0:1, :])
                    return v.tensor_copy(Rrow[:, sl], P[1][0:1, :])
                prev_ev = sd.add("vector", f_ev, deps=[op_mmb], name=f"{tag}ev{tt}",
                                 nodrain=True)
                st_evs.append(prev_ev)

            def f_rm1(v):
                v.tensor_scalar(murow, murow, 1.0 / D, None, ALU.mult)
                v.tensor_scalar(Rrow, Rrow, 1.0 / D, None, ALU.mult)
                v.tensor_tensor(scrrow, murow, murow, ALU.mult)
                return v.tensor_tensor(Rrow, Rrow, scrrow, ALU.subtract)
            op_rm1 = sd.add("vector", f_rm1, deps=st_evs, name=f"{tag}rm1")

            def f_rm2(s):
                return s.activation(scrrow, Rrow, AF.Sqrt, bias=cst_s[0:1, 0:1])
            op_rm2 = sd.add("scalar", f_rm2, deps=[op_rm1], name=f"{tag}rm2")

            def f_rm3(v):
                v.reciprocal(scrrow, scrrow)
                v.tensor_tensor(Rrow, Rrow, scrrow, ALU.mult)
                return v.tensor_tensor(Rrow, Rrow, scrrow, ALU.mult)
            op_rm3 = sd.add("vector", f_rm3, deps=[op_rm2], name=f"{tag}rm3")

            def f_rm4(s):
                return s.activation(Rrow, Rrow, AF.Sqrt, bias=cst_s[0:1, 0:1])
            op_rm4 = sd.add("scalar", f_rm4, deps=[op_rm3], name=f"{tag}rm4")

            def f_rm5(v):
                v.reciprocal(Rrow, Rrow)
                v.tensor_tensor(Rrow, Rrow, scrrow, ALU.mult)
                return v.tensor_scalar(murow, murow, -1.0, None, ALU.mult)
            return sd.add("vector", f_rm5, deps=[op_rm4], name=f"{tag}rm5")

        mug = rowsg[0:1, 0:S]
        Rg = ao[:].bitcast(F32)[0:1, S:2 * S]
        op_stg = emit_dln_stats("sg", x0f, S, mug, Rg, rowx[0:1, :], [])
        muq = rowsq[0:1, 0:QTOK]
        Rq = ao[:].bitcast(F32)[0:1, 2 * S:2 * S + QTOK]
        op_stq = emit_dln_stats("sq", x0q, QTOK, muq, Rq, rowx[0:1, :], [op_stg])

        rc_ops = []
        prev_rc = None
        for tc in range(32):
            def f_rt(t, tc=tc):
                return t.transpose(P[2][:, 0:1], Rg[:, tc * 128:(tc + 1) * 128],
                                   idf32_p1[0:1, 0:1])
            op_rt = sd.add("tensor", f_rt, deps=[op_stg, prev_rc], name=f"rt{tc}")

            def f_rte(v, tc=tc):
                return v.tensor_copy(rcolk[:, tc:tc + 1], P[2][:, 0:1])
            op_rte = sd.add("vector", f_rte, deps=[op_rt], name=f"rte{tc}",
                            nodrain=True)
            prev_rc = op_rte
            rc_ops.append(op_rte)

        def f_rcn(v):
            return v.memset(rcolk[:, 32:NKC], 1.0)
        op_rcn = sd.add("vector", f_rcn, name="rcn")
        rc_ops.append(op_rcn)

        def f_qk1(v):
            v.reduce_max(tiny[0:1, 0:1], qs_s[:], axis=AX.X, apply_absolute_value=True)
            v.reduce_max(tiny[0:1, 1:2], ks_s[:], axis=AX.X, apply_absolute_value=True)
            v.tensor_tensor(tiny[0:1, 2:3], tiny[0:1, 0:1], tiny[0:1, 1:2], ALU.mult)
            return v.tensor_scalar(tiny[0:1, 3:4], tiny[0:1, 2:3], -8.0, None, ALU.mult)
        op_qk = sd.add("vector", f_qk1, deps=[op_ldc], name="qk1")

        prev_hp = {}
        oproj_deps = []
        for hp in range(8):
            def f_wk(g, put, hp=hp):
                put(g.dma_start(
                    out=wsl[:, 0:1024].rearrange("p (kc o) -> p kc o", kc=NCH),
                    in_=wk_t[hp].rearrange("(kc p) o -> p kc o", p=128)))
                put(g.dma_start(
                    out=wsl[:, 1024:2048].rearrange("p (kc o) -> p kc o", kc=NCH),
                    in_=wv_t[hp].rearrange("(kc p) o -> p kc o", p=128)))
            op_wk = sd.dma("gpsimd", f_wk, ndma=2, deps=[prev_hp.get("wsl")],
                           name=f"wk{hp}", stream="wsl")

            def f_wg(s, hp=hp):
                last = None
                for kc in range(NCH):
                    last = s.mul(wsl[:, kc * 128:(kc + 1) * 128],
                                 wsl[:, kc * 128:(kc + 1) * 128], gc_s[:, kc:kc + 1])
                for kc in range(NCH):
                    last = s.mul(wsl[:, 1024 + kc * 128:1024 + (kc + 1) * 128],
                                 wsl[:, 1024 + kc * 128:1024 + (kc + 1) * 128],
                                 gc_s[:, kc:kc + 1])
                return last
            op_wg = sd.add("scalar", f_wg, deps=[op_wk, op_ldc], name=f"wg{hp}")

            def f_cr(t, hp=hp):
                for kc in range(NCH):
                    t.matmul(P[2][0:1, 0:128], onescolb_s[:],
                             wsl[:, kc * 128:(kc + 1) * 128],
                             start=(kc == 0), stop=(kc == NCH - 1))
                last = None
                for kc in range(NCH):
                    last = t.matmul(P[3][0:1, 0:128], onescolb_s[:],
                                    wsl[:, 1024 + kc * 128:1024 + (kc + 1) * 128],
                                    start=(kc == 0), stop=(kc == NCH - 1))
                return last
            op_cr = sd.add("tensor", f_cr, deps=[op_wg, prev_hp.get("crow"), prev_rc],
                           name=f"cr{hp}")

            def f_cre(v):
                v.tensor_copy(crow[0:1, 0:128], P[2][0:1, 0:128])
                return v.tensor_copy(crow[0:1, 128:256], P[3][0:1, 0:128])
            op_cre = sd.add("vector", f_cre, deps=[op_cr], name=f"cre{hp}")
            prev_hp["crow"] = op_cre


            kv_evs = []
            for tt in range(NQT):
                sl = slice(tt * 512, (tt + 1) * 512)
                pp = tt % 2
                xsl = xrlb[:, pp * 4096:(pp + 1) * 4096]

                def f_xlA(sync, put, sl=sl, xsl=xsl):
                    for kc in range(4):
                        put(sync.dma_start(out=xsl[:, kc * 512:(kc + 1) * 512],
                                           in_=x0fb[kc * 128:(kc + 1) * 128, sl]))
                op_xlA = sd.dma("sync", f_xlA, ndma=4,
                                deps=[prev_hp.get(("kv", pp)), op_stq, op_pca, op_pcb,
                                      prev_hp.get(("kmmA", pp)),
                                      prev_hp.get(("qmmA", pp))],
                                name=f"kvxA{hp}_{tt}", stream=f"srlA{pp}")

                def f_xlB(g, put, sl=sl, xsl=xsl):
                    for kc in range(4, NCH):
                        put(g.dma_start(out=xsl[:, kc * 512:(kc + 1) * 512],
                                        in_=x0fb[kc * 128:(kc + 1) * 128, sl]))
                op_xlB = sd.dma("gpsimd", f_xlB, ndma=4,
                                deps=[prev_hp.get(("kv", pp)), op_stq, op_pca, op_pcb,
                                      prev_hp.get(("kmmB", pp)),
                                      prev_hp.get(("qmmB", pp))],
                                name=f"kvxB{hp}_{tt}", stream=f"srlB{pp}")

                def f_kmmA(t, sl=sl, pp=pp, xsl=xsl):
                    for kc in range(4):
                        t.matmul(P[0 + pp][:], wsl[:, kc * 128:(kc + 1) * 128],
                                 xsl[:, kc * 512:(kc + 1) * 512],
                                 start=(kc == 0), stop=False)
                    last = None
                    for kc in range(4):
                        last = t.matmul(P[2 + pp][:],
                                        wsl[:, 1024 + kc * 128:1024 + (kc + 1) * 128],
                                        xsl[:, kc * 512:(kc + 1) * 512],
                                        start=(kc == 0), stop=False)
                    return last
                op_kmmA = sd.add("tensor", f_kmmA,
                                 deps=[op_xlA, op_wg, op_cre, op_stg,
                                       prev_hp.get(("kv", pp))],
                                 name=f"kmmA{hp}_{tt}")

                def f_kmmB(t, sl=sl, pp=pp, xsl=xsl):
                    for kc in range(4, NCH):
                        t.matmul(P[0 + pp][:], wsl[:, kc * 128:(kc + 1) * 128],
                                 xsl[:, kc * 512:(kc + 1) * 512],
                                 start=False, stop=False)
                    t.matmul(P[0 + pp][:], crow[0:1, 0:128], mug[:, sl],
                             start=False, stop=True)
                    for kc in range(4, NCH):
                        t.matmul(P[2 + pp][:],
                                 wsl[:, 1024 + kc * 128:1024 + (kc + 1) * 128],
                                 xsl[:, kc * 512:(kc + 1) * 512],
                                 start=False, stop=False)
                    return t.matmul(P[2 + pp][:], crow[0:1, 128:256], mug[:, sl],
                                    start=False, stop=True)
                op_kmmB = sd.add("tensor", f_kmmB, deps=[op_xlB],
                                 name=f"kmmB{hp}_{tt}")

                def f_kev(s, sl=sl, pp=pp):
                    s.copy(kpkh[0][0:64, sl], P[0 + pp][0:64, :])
                    s.copy(kpkh[1][0:64, sl], P[0 + pp][64:128, :])
                    return s.copy(vfm[:, sl], P[2 + pp][:])
                op_kev = sd.add("scalar", f_kev, deps=[op_kmmB], name=f"kev{hp}_{tt}",
                                nodrain=True)
                prev_hp[("kv", pp)] = op_kev
                prev_hp[("kmmA", pp)] = op_kmmA
                prev_hp[("kmmB", pp)] = op_kmmB
                kv_evs.append(op_kev)

            def f_null(v, hp=hp):
                v.memset(kpkh[0][64:65, 0:KCOL], 1.0)
                v.memset(kpkh[1][64:65, 0:KCOL], 1.0)
                v.memset(kpkh[0][0:64, 4096:KCOL], 0.0)
                v.memset(kpkh[1][0:64, 4096:KCOL], 0.0)
                v.memset(vfm[:, 4096:KCOL], 0.0)
                v.tensor_copy(kpkh[0][0:64, 4096:4098], nk_s[0:64, hp * 2:hp * 2 + 2])
                v.tensor_copy(kpkh[1][0:64, 4096:4098], nk_s[64:128, hp * 2:hp * 2 + 2])
                return v.tensor_copy(vfm[:, 4096:4098], nv_s[:, hp * 2:hp * 2 + 2])
            op_null = sd.add("vector", f_null,
                             deps=[op_ldc, prev_hp.get("attn_done"),
                                   prev_hp.get("vT_done")],
                             name=f"null{hp}")

            def f_vone(v):
                vv = vT.rearrange("p (c k) -> p c k", k=130)
                v.memset(vv[:, :, 64:65], 1.0)
                return v.memset(vv[:, :, 129:130], 1.0)
            op_vone = sd.add("vector", f_vone, deps=[prev_hp.get("attn_done")],
                             name=f"vone{hp}")

            vt_evs = []
            prev_vt = None
            for tc in range(NKC):
                def f_vtr(t, tc=tc):
                    return t.transpose(PT, vfm[:, tc * 128:(tc + 1) * 128],
                                       identb_s[:])
                op_vtr = sd.add("tensor", f_vtr,
                                deps=[op_null if tc >= 32 else kv_evs[tc // 4],
                                      prev_vt],
                                name=f"vtr{hp}_{tc}")

                def f_vte(v, tc=tc):
                    v.tensor_scalar(vT[:, tc * 130:tc * 130 + 64], PT[:, 0:64],
                                    rcolk[:, tc:tc + 1], None, ALU.mult)
                    return v.tensor_scalar(vT[:, tc * 130 + 65:tc * 130 + 129],
                                           PT[:, 64:128],
                                           rcolk[:, tc:tc + 1], None, ALU.mult)
                op_vte = sd.add("vector", f_vte, deps=[op_vtr, op_vone] + rc_ops,
                                name=f"vte{hp}_{tc}")
                prev_vt = op_vte
                vt_evs.append(op_vte)
            prev_hp["vT_done"] = vt_evs[-1]

            nsq_ops = []
            khat_ops = []
            krr_ops = []
            for ti in range(9):
                w = 512 if ti < 8 else 256
                sl = slice(ti * 512, ti * 512 + w)
                pp = ti % 2
                nb_ps = (P[4] if pp == 0 else P[3])

                def f_ksq(s, sl=sl, pp=pp, w=w):
                    s.activation(sq[0:64, pp * 512:pp * 512 + w], kpkh[0][0:64, sl],
                                 AF.Square)
                    return s.activation(sq[64:128, pp * 512:pp * 512 + w],
                                        kpkh[1][0:64, sl], AF.Square)
                op_ksq = sd.add("scalar", f_ksq,
                                deps=[op_null if ti >= 8 else kv_evs[ti],
                                      prev_hp.get("attn_done"),
                                      nsq_ops[-2] if ti >= 2 else None],
                                name=f"ksq{hp}_{ti}")

                def f_kns(t, sl=sl, pp=pp, w=w, nb_ps=nb_ps):
                    return t.matmul(nb_ps[0:2, 0:w], bd2_s[:],
                                    sq[:, pp * 512:pp * 512 + w],
                                    start=True, stop=True)
                op_kns = sd.add("tensor", f_kns,
                                deps=[op_ksq, op_bd2,
                                      nsq_ops[-2] if ti >= 2 else None,
                                      khat_ops[-2] if ti >= 2 else None],
                                name=f"kns{hp}_{ti}")

                def f_kne(v, sl=sl, w=w, nb_ps=nb_ps):
                    v.tensor_copy(nrow2[:, sl], nb_ps[0:2, 0:w])
                    return v.tensor_scalar_max(nrow2[:, sl], nrow2[:, sl], 1e-24)
                op_kne = sd.add("vector", f_kne, deps=[op_kns], name=f"kne{hp}_{ti}",
                                nodrain=True)
                nsq_ops.append(op_kne)

                def f_krs(s, sl=sl):
                    return s.activation(nrow2[:, sl], nrow2[:, sl], AF.Sqrt)
                op_krs = sd.add("scalar", f_krs, deps=[op_kne], name=f"krs{hp}_{ti}",
                                nodrain=True)

                def f_krr(v, sl=sl):
                    return v.reciprocal(nrow2[:, sl], nrow2[:, sl])
                op_krr = sd.add("vector", f_krr, deps=[op_krs], name=f"krr{hp}_{ti}",
                                nodrain=True)
                krr_ops.append(op_krr)

                def f_rnb(t, sl=sl, pp=pp, w=w):
                    t.matmul(P[0 + pp][:, 0:w], ksd_s[:], nrow2[:, sl],
                             start=True, stop=True)
                    return t.matmul(P[2 + pp][0:64, 0:w], ksd_s[:, 64:128],
                                    nrow2[:, sl], start=True, stop=True)
                op_rnb = sd.add("tensor", f_rnb,
                                deps=[op_krr, op_ldc,
                                      khat_ops[-2] if ti >= 2 else None],
                                name=f"rnb{hp}_{ti}")

                def f_kh(v, sl=sl, pp=pp, w=w):
                    v.tensor_tensor(kpkh[0][0:64, sl], kpkh[0][0:64, sl],
                                    P[0 + pp][0:64, 0:w], ALU.mult)
                    return v.tensor_tensor(kpkh[1][0:64, sl], kpkh[1][0:64, sl],
                                           P[2 + pp][0:64, 0:w], ALU.mult)
                op_kh = sd.add("vector", f_kh, deps=[op_rnb],
                               name=f"kh{hp}_{ti}")
                khat_ops.append(op_kh)

            def f_wq(g, put, hp=hp):
                put(g.dma_start(
                    out=wsl[:, 2048:3072].rearrange("p (kc o) -> p kc o", kc=NCH),
                    in_=wq_t[hp].rearrange("(kc p) o -> p kc o", p=128)))
            op_wq = sd.dma("gpsimd", f_wq, ndma=1, deps=[prev_hp.get("wslq")],
                           name=f"wq{hp}", stream="wsl2")

            def f_wqg(s, hp=hp):
                last = None
                for kc in range(NCH):
                    last = s.mul(wsl[:, 2048 + kc * 128:2048 + (kc + 1) * 128],
                                 wsl[:, 2048 + kc * 128:2048 + (kc + 1) * 128],
                                 gc_s[:, kc:kc + 1])
                return last
            op_wqg = sd.add("scalar", f_wqg, deps=[op_wq, op_ldc], name=f"wqg{hp}")

            def f_crq(t, hp=hp):
                last = None
                for kc in range(NCH):
                    last = t.matmul(P[4][0:1, 0:128], onescolb_s[:],
                                    wsl[:, 2048 + kc * 128:2048 + (kc + 1) * 128],
                                    start=(kc == 0), stop=(kc == NCH - 1))
                return last
            op_crq = sd.add("tensor", f_crq, deps=[op_wqg, prev_hp.get("crowq")],
                            name=f"crq{hp}")

            def f_creq(v):
                return v.tensor_copy(crow[0:1, 256:384], P[4][0:1, 0:128])
            op_creq = sd.add("vector", f_creq, deps=[op_crq], name=f"creq{hp}")
            prev_hp["crowq"] = op_creq

            q_evs = []
            for tt in range(NQTL):
                sl = slice(tt * 512, (tt + 1) * 512)
                pp = tt % 2

                xsl = xrlb[:, pp * 4096:(pp + 1) * 4096]

                def f_xqA(sync, put, sl=sl, xsl=xsl):
                    for kc in range(4):
                        put(sync.dma_start(out=xsl[:, kc * 512:(kc + 1) * 512],
                                           in_=x0qb[kc * 128:(kc + 1) * 128, sl]))
                op_xqA = sd.dma("sync", f_xqA, ndma=4,
                                deps=[prev_hp.get(("q", pp)), op_pca, op_pcb,
                                      khat_ops[-1] if tt <= 1 else None,
                                      prev_hp.get(("kmmA", pp)) if tt <= 1 else None,
                                      prev_hp.get(("qmmA", pp))],
                                name=f"qxA{hp}_{tt}", stream=f"srlA{pp}")

                def f_xqB(g, put, sl=sl, xsl=xsl):
                    for kc in range(4, NCH):
                        put(g.dma_start(out=xsl[:, kc * 512:(kc + 1) * 512],
                                        in_=x0qb[kc * 128:(kc + 1) * 128, sl]))
                op_xqB = sd.dma("gpsimd", f_xqB, ndma=4,
                                deps=[prev_hp.get(("q", pp)), op_pca, op_pcb,
                                      khat_ops[-1] if tt <= 1 else None,
                                      prev_hp.get(("kmmB", pp)) if tt <= 1 else None,
                                      prev_hp.get(("qmmB", pp))],
                                name=f"qxB{hp}_{tt}", stream=f"srlB{pp}")

                def f_qmmA(t, sl=sl, pp=pp, xsl=xsl):
                    last = None
                    for kc in range(4):
                        last = t.matmul(P[0 + pp][:],
                                        wsl[:, 2048 + kc * 128:2048 + (kc + 1) * 128],
                                        xsl[:, kc * 512:(kc + 1) * 512],
                                        start=(kc == 0), stop=False)
                    return last
                op_qmmA = sd.add("tensor", f_qmmA,
                                 deps=[op_xqA, op_wqg, op_creq, op_stq,
                                       prev_hp.get(("q", pp)),
                                       khat_ops[-1] if tt == 0 else None],
                                 name=f"qmmA{hp}_{tt}")

                def f_qmm(t, sl=sl, pp=pp, xsl=xsl):
                    for kc in range(4, NCH):
                        t.matmul(P[0 + pp][:], wsl[:, 2048 + kc * 128:2048 + (kc + 1) * 128],
                                 xsl[:, kc * 512:(kc + 1) * 512],
                                 start=False, stop=False)
                    return t.matmul(P[0 + pp][:], crow[0:1, 256:384], muq[:, sl],
                                    start=False, stop=True)
                op_qmm = sd.add("tensor", f_qmm, deps=[op_xqB],
                                name=f"qmm{hp}_{tt}")
                prev_hp[("qmmA", pp)] = op_qmmA
                prev_hp[("qmmB", pp)] = op_qmm

                def f_qsq(s, sl=sl, pp=pp):
                    return s.activation(sq[:, pp * 512:(pp + 1) * 512], P[0 + pp][:],
                                        AF.Square)
                op_qsq = sd.add("scalar", f_qsq, deps=[op_qmm,
                                                       prev_hp.get(("qs", pp))],
                                name=f"qsq{hp}_{tt}")

                qn_ps = P[4] if pp == 0 else P[3]

                def f_qns(t, sl=sl, pp=pp, qn_ps=qn_ps):
                    return t.matmul(qn_ps[0:2, :], bd2_s[:],
                                    sq[:, pp * 512:(pp + 1) * 512],
                                    start=True, stop=True)
                op_qns = sd.add("tensor", f_qns,
                                deps=[op_qsq, prev_hp.get(("qnev", pp)),
                                      prev_hp.get(("qh", pp))],
                                name=f"qns{hp}_{tt}")

                def f_qne(v, sl=sl, pp=pp, qn_ps=qn_ps):
                    return v.tensor_copy(qn2[:, sl], qn_ps[0:2, :])
                op_qne = sd.add("vector", f_qne, deps=[op_qns], name=f"qne{hp}_{tt}",
                                nodrain=True)
                prev_hp[("qnev", pp)] = op_qne

                def f_qev(s, sl=sl, pp=pp):
                    s.copy(scrq[:, (2 * pp) * 512:(2 * pp + 1) * 512],
                           P[0 + pp][0:64, :])
                    return s.copy(scrq[:, (2 * pp + 1) * 512:(2 * pp + 2) * 512],
                                  P[0 + pp][64:128, :])
                op_qev = sd.add("scalar", f_qev, deps=[op_qmm], name=f"qev{hp}_{tt}")

                def f_qrs(s, sl=sl):
                    return s.activation(qn2[:, sl], qn2[:, sl], AF.Sqrt)
                op_qrs = sd.add("scalar", f_qrs, deps=[op_qne], name=f"qrs{hp}_{tt}")

                def f_qrr(v, sl=sl):
                    return v.reciprocal(qn2[:, sl], qn2[:, sl])
                op_qrr = sd.add("vector", f_qrr, deps=[op_qrs], name=f"qrr{hp}_{tt}",
                                nodrain=True)

                def f_qrb(t, sl=sl, pp=pp):
                    t.matmul(P[2 + pp][0:64, :], qsd_s[:, 0:64], qn2[:, sl],
                             start=True, stop=True)
                    return t.matmul(P[5][0:64, :], qsd_s[:, 64:128], qn2[:, sl],
                                    start=True, stop=True)
                op_qrb = sd.add("tensor", f_qrb,
                                deps=[op_qrr, op_ldc, prev_hp.get(("qh", pp)),
                                      prev_hp.get("qhlast"), vt_evs[-1]],
                                name=f"qrb{hp}_{tt}")

                def f_qh(v, sl=sl, pp=pp):
                    v.tensor_tensor(qpkh[0][0:64, sl],
                                    scrq[:, (2 * pp) * 512:(2 * pp + 1) * 512],
                                    P[2 + pp][0:64, :], ALU.mult)
                    return v.tensor_tensor(qpkh[1][0:64, sl],
                                           scrq[:, (2 * pp + 1) * 512:(2 * pp + 2) * 512],
                                           P[5][0:64, :], ALU.mult)
                op_qh = sd.add("vector", f_qh, deps=[op_qrb, op_qev], name=f"qh{hp}_{tt}")
                prev_hp["qhlast"] = op_qh
                prev_hp[("q", pp)] = op_qh
                prev_hp[("qs", pp)] = op_qsq
                prev_hp[("qh", pp)] = op_qh
                q_evs.append(op_qh)

            nb_ops = []
            prev_nbc = None
            for hl in range(2):
                def f_nbl(g, put, hp=hp, hl=hl):
                    put(g.dma_start(
                        out=nbst[0:1, :],
                        in_=brow0[0:1, (2 * hp + hl) * NQTL * 512:
                                  (2 * hp + hl) * NQTL * 512 + QTOK]))
                op_nbl = sd.dma("gpsimd", f_nbl, ndma=1,
                                deps=[prev_hp.get("attn_done"), prev_nbc,
                                      prev_hp.get("nbl")],
                                name=f"nbl{hp}_{hl}", stream="nbl")
                prev_hp["nbl"] = op_nbl

                def f_nba(v, hl=hl):
                    v.tensor_scalar(nbst[0:1, :], nbst[0:1, :],
                                    tiny[0:1, 3:4], None, ALU.add)
                    return v.tensor_copy(qpkh[hl][64:65, :], nbst[0:1, :])
                op_nba = sd.add("vector", f_nba,
                                deps=[op_nbl, op_qk, prev_hp.get("attn_done")],
                                name=f"nba{hp}_{hl}")
                prev_nbc = op_nba
                nb_ops.append(op_nba)

            attn_done_ops = []
            for hloc in range(2):
                h = hp * 2 + hloc
                rbase = hloc * 64
                for qt in range(NQTL):
                    qsl = slice(qt * 512, (qt + 1) * 512)
                    par = (h * NQTL + qt) % 2
                    gB = qt + 2
                    kccs = list(range(0, min(4 * gB + 4, 32))) + [32]
                    diagA = [4 * (qt - 2) + dv for dv in range(4)
                             if 0 <= 4 * (qt - 2) + dv < 32] if qt >= 2 else []
                    diagB = [4 * gB + dv for dv in range(4) if 4 * gB + dv < 32]

                    bpar = (h * NQTL + qt) % 2

                    def f_nb(sync, put, h=h, qt=qt, bpar=bpar):
                        put(sync.dma_start(
                            out=bcq2[:, bpar * NKC:(bpar + 1) * NKC],
                            in_=bcq[:, (h * NQTL + qt) * NKC:(h * NQTL + qt + 1) * NKC]))
                    op_nb = sd.dma("sync", f_nb, ndma=1,
                                   deps=[prev_hp.get(("bcuse", bpar)),
                                         prev_hp.get("nbld")],
                                   name=f"nb{h}_{qt}", stream="nb")
                    prev_hp["nbld"] = op_nb

                    pv_last = None
                    step_ops = []
                    pend_pv = []

                    def flush_pv():
                        fn, dep, nm, pspp = pend_pv.pop(0)
                        op = sd.add("tensor", fn, dep, nm)
                        step_ops.append(op)
                        prev_hp[("pvs", pspp)] = op
                        return op

                    for si, kcc in enumerate(kccs):
                        spp = si % 4
                        mA = kcc in diagA
                        mB = kcc in diagB

                        def f_smm(t, kcc=kcc, qsl=qsl, spp=spp, mA=mA, mB=mB,
                                  qt=qt, hloc=hloc):
                            first = True
                            if mA:
                                dv = kcc - 4 * (qt - 2)
                                off = 512 - dv * 128
                                t.matmul(P[spp][:], M1_s[:],
                                         M2A_s[:, off:off + 512],
                                         start=True, stop=False)
                                first = False
                            if mB:
                                dv = kcc - 4 * (qt + 2)
                                off = 512 - dv * 128
                                t.matmul(P[spp][:], M1_s[:],
                                         M2B_s[:, off:off + 512],
                                         start=first, stop=False)
                                first = False
                            return t.matmul(
                                P[spp][:],
                                kpkh[hloc][:, kcc * 128:(kcc + 1) * 128],
                                qpkh[hloc][:, qsl], start=first, stop=True)
                        dep_smm = [nb_ops[hloc], khat_ops[-1], q_evs[-1],
                                   prev_hp.get(("pvs", spp))]
                        op_smm = sd.add("tensor", f_smm, deps=dep_smm,
                                        name=f"smm{h}_{qt}_{kcc}")
                        if si == 1 and prev_hp.get("ptail"):
                            prev_hp.pop("ptail")()
                        if len(pend_pv) >= 2:
                            pv_last = flush_pv()

                        def f_exp(s, kcc=kcc, spp=spp, bpar=bpar):
                            return s.activation(phat[:, spp * 512:(spp + 1) * 512],
                                                P[spp][:], AF.Exp,
                                                bias=bcq2[:, bpar * NKC + kcc:bpar * NKC + kcc + 1])
                        op_exp = sd.add("scalar", f_exp,
                                        deps=[op_smm, op_ldc, op_nb,
                                              prev_hp.get(("pvs", spp))],
                                        name=f"exp{h}_{qt}_{kcc}", nodrain=True)
                        prev_hp[("bcuse", bpar)] = op_exp

                        def f_pv(t, kcc=kcc, spp=spp, si=si, hloc=hloc, par=par,
                                 nsteps=len(kccs)):
                            return t.matmul(PO[par][:],
                                            vT[:, kcc * 130 + hloc * 65:kcc * 130 + hloc * 65 + 65],
                                            phat[:, spp * 512:(spp + 1) * 512],
                                            start=(si == 0), stop=(si == nsteps - 1))
                        pend_pv.append([f_pv,
                                        [op_exp, vt_evs[-1],
                                         prev_hp.get(("fin", par))],
                                        f"pv{h}_{qt}_{kcc}", spp])
                    while pend_pv:
                        pv_last = flush_pv()

                    def emit_tail(pv_last=pv_last, par=par, h=h, qt=qt, qsl=qsl,
                                  rbase=rbase):
                        def f_rs(v, par=par):
                            return v.reciprocal(rsr[64:65, :], PO[par][64:65, :])
                        op_rs = sd.add("vector", f_rs, deps=[pv_last],
                                       name=f"rs{h}_{qt}", nodrain=True)

                        def f_rsb(t, par=par):
                            return t.matmul(P[4][0:64, :], ones65_s[64:65, 0:64],
                                            rsr[64:65, :], start=True, stop=True)
                        op_rsb = sd.add("tensor", f_rsb,
                                        deps=[op_rs, prev_hp.get("rsb")],
                                        name=f"rsb{h}_{qt}")
                        prev_hp["rsb"] = op_rsb

                        def f_oev(s, par=par):
                            return s.copy(ou[:], PO[par][0:64, :])
                        op_oev = sd.add("scalar", f_oev,
                                        deps=[pv_last, prev_hp.get("oudone")],
                                        name=f"oev{h}_{qt}")

                        def f_fin(v, h=h, qt=qt, qsl=qsl, rbase=rbase):
                            return v.tensor_tensor(ao_t[h // 2][rbase:rbase + 64, qsl],
                                                   ou[:],
                                                   P[4][0:64, :], ALU.mult)
                        op_fin = sd.add("vector", f_fin, deps=[op_oev, op_rsb],
                                        name=f"fin{h}_{qt}", nodrain=True)
                        prev_hp["oudone"] = op_fin
                        prev_hp[("fin", par)] = op_fin
                        attn_done_ops.append(op_fin)
                    prev_hp["ptail"] = emit_tail
            if prev_hp.get("ptail"):
                prev_hp.pop("ptail")()
            prev_hp["attn_done"] = attn_done_ops[-1]
            prev_hp["wsl"] = q_evs[-1]
            prev_hp["wslq"] = q_evs[-1]
            oproj_deps.extend(attn_done_ops)

        oscr = phat[:].bitcast(F32)

        out_ops1 = []
        prev_o = {}
        for oc in range(8):
            def f_wo(g, put, oc=oc):
                put(g.dma_start(
                    out=wsl[:, (oc % 2) * 1024:(oc % 2) * 1024 + 1024]
                        .rearrange("p (kc o) -> p kc o", kc=8),
                    in_=wo_t[oc].rearrange("(kc p) o -> p kc o", p=128)))
            op_wo = sd.dma("gpsimd", f_wo, ndma=1,
                           deps=oproj_deps[-1:] + [prev_o.get(("mm", oc % 2))],
                           name=f"wo{oc}", stream=f"wo{oc % 2}")
            for tt in range(NQTL):
                sl = slice(tt * 512, (tt + 1) * 512)
                pp = tt % 2

                def f_omm(t, oc=oc, sl=sl, pp=pp):
                    last = None
                    for kc in range(NCH):
                        last = t.matmul(P[0 + pp][:],
                                        wsl[:, (oc % 2) * 1024 + kc * 128:(oc % 2) * 1024 + (kc + 1) * 128],
                                        ao_t[kc][:, sl], start=(kc == 0), stop=(kc == 7))
                    return last
                op_omm = sd.add("tensor", f_omm,
                                deps=[op_wo, prev_o.get(("ev", pp))] + oproj_deps[-1:],
                                name=f"omm{oc}_{tt}")
                prev_o[("mm", oc % 2)] = op_omm

                def f_oev2(s, pp=pp):
                    return s.copy(oscr[:, pp * 512:(pp + 1) * 512], P[0 + pp][:])
                op_oev2 = sd.add("scalar", f_oev2, deps=[op_omm], name=f"oev2{oc}_{tt}")

                def f_odma(sync, put, oc=oc, sl=sl, pp=pp):
                    put(sync.dma_start(out=osc[oc * 128:(oc + 1) * 128, sl],
                                       in_=oscr[:, pp * 512:(pp + 1) * 512]))
                op_od = sd.dma("sync", f_odma, ndma=1,
                               deps=[op_oev2, prev_o.get("od")],
                               name=f"od{oc}_{tt}", stream="oout")
                prev_o[("ev", pp)] = op_od
                prev_o["od"] = op_od
                out_ops1.append(op_od)

        RG = 1536
        NTT2 = RG // 512
        ar.reset()
        bias_s = ar.alloc(128, 8, F32)
        G_s = ar.alloc(8, 128, F32)
        rbias_s = ar.alloc(128, 8, F32)
        idf32_s = ar.alloc(128, 128, F32)
        hm2_s = ar.alloc(128, 1, F32)

        def ld_p2c(sync, put):
            for t, d_ in ((bias_s, bias_c), (G_s, Gm), (rbias_s, rbias),
                          (idf32_s, idf32), (hm2_s, hmask)):
                put(sync.dma_start(out=t, in_=d_[:]))
        op_ldp2 = sd.dma("sync", ld_p2c, ndma=5, deps=[out_ops1[-1]], name="ldp2",
                         stream="p2ldx")

        xs = ar.alloc(128, NCH * RG, F32)
        u2 = ar.alloc(128, NCH * RG, BF16)
        bw = ar.alloc(128, 8 * NCH * 128, BF16)
        As = ar.alloc(128, NCH * 128, BF16)
        Bs = ar.alloc(128, 8 * 128, BF16)
        rw = ar.alloc(128, NCH * 8, BF16)
        sax = ar.alloc(128, RG, BF16)
        wTf = ar.alloc(8, RG, F32)
        sq2 = ar.alloc(128, 4 * 512, F32)
        scr2 = ar.alloc(128, 512, F32)
        wa = ar.alloc(128, 2 * 512, F32)
        rows2 = ar.alloc(1, 3 * RG, F32)
        lgs = ar.alloc(128, 2 * 8, F32)
        tp = ar.alloc(128, 2 * 16, F32)
        wsb = ar.alloc(128, 2 * 8, F32)
        xrl2 = ar.alloc(128, 8 * 512, F32)
        u3f = u2
        trb = ar.alloc(128, 3 * 128, BF16)

        xs_t = [xs[:, ch * RG:(ch + 1) * RG] for ch in range(NCH)]
        u2_t = [u2[:, ch * RG:(ch + 1) * RG] for ch in range(NCH)]
        f0_t = xs_t
        u3f_t = [u3f[:, ch * RG:(ch + 1) * RG] for ch in range(NCH)]
        bw_s = [[bw[:, (oc * NCH + kc) * 128:(oc * NCH + kc + 1) * 128]
                 for kc in range(NCH)] for oc in range(8)]
        As_t = [As[:, kc * 128:(kc + 1) * 128] for kc in range(NCH)]
        Bs_t = [Bs[:, oc * 128:(oc + 1) * 128] for oc in range(8)]
        rw_t = [rw[:, kc * 8:(kc + 1) * 8] for kc in range(NCH)]

        def rowf(i):
            return rows2[0:1, i * RG:(i + 1) * RG]

        def ld_w2(g, put):
            for oc in range(8):
                put(g.dma_start(
                    out=bw[:, oc * NCH * 128:(oc + 1) * NCH * 128]
                        .rearrange("p (kc o) -> p kc o", kc=NCH),
                    in_=bw_t[oc].rearrange("(kc p) o -> p kc o", p=128)))
            put(g.dma_start(
                out=As.rearrange("p (kc r) -> p kc r", kc=NCH),
                in_=A_t.rearrange("(kc p) r -> p kc r", p=128)))
            put(g.dma_start(
                out=Bs.rearrange("p (oc o) -> p oc o", oc=8),
                in_=B_t.rearrange("oc p o -> p oc o")))
            put(g.dma_start(
                out=rw.rearrange("p (kc e) -> p kc e", kc=NCH),
                in_=rwt.rearrange("(kc p) e -> p kc e", p=128)))
        op_ldw = sd.dma("gpsimd", ld_w2, ndma=11, deps=[out_ops1[-1]], name="p2_ldw",
                        stream="p2ldw")

        p2_exit = []
        prev_grp_exit = None
        prev_sp2 = None
        tr_carry = {}
        for rg in range(2):
            R0 = rg * RG
            gdep = [op_ldp2, op_ldw] + [x for x in (prev_grp_exit, prev_sp2) if x]

            def ld_o(sync, put, R0=R0):
                for ch in range(NCH):
                    put(sync.dma_start(out=xs_t[ch],
                                       in_=osc[ch * 128:(ch + 1) * 128, R0:R0 + RG]))
            op_ldx = sd.dma("sync", ld_o, ndma=NCH, deps=gdep + [out_ops1[-1]],
                            name=f"p2_ldx{rg}", stream="p2ldx")

            op_sto = emit_stats(sd, f"sto{rg}", xs_t, rowf, consts, S0, S1, sq2,
                                [op_ldx], NTT2, psS0b=P[2][0:1, :], psS1b=P[3][0:1, :])
            prev_x1 = None
            x1_ops = []
            for tt in range(NTT2):
                sl = slice(tt * 512, (tt + 1) * 512)

                def f_x0l(sync, put, sl=sl, R0=R0):
                    for kc in range(NCH):
                        put(sync.dma_start(out=xrl2[:, kc * 512:(kc + 1) * 512],
                                           in_=x0q[kc * 128:(kc + 1) * 128,
                                                   R0 + sl.start:R0 + sl.stop]))
                op_x0l = sd.dma("sync", f_x0l, ndma=NCH, deps=[op_sto, prev_x1],
                                name=f"x0l{rg}_{tt}", stream="p2rl")

                pb = 2 * (tt % 2)

                def f_x1bc(t, sl=sl, pb=pb):
                    t.matmul(P[pb][:], consts["ones1"], rowf(0)[:, sl], start=True, stop=True)
                    return t.matmul(P[pb + 1][:], consts["ones1"], rowf(1)[:, sl], start=True, stop=True)
                op_x1bc = sd.add("tensor", f_x1bc,
                                 deps=[op_sto, x1_ops[tt - 2] if tt >= 2 else None],
                                 name=f"x1bc{rg}_{tt}")

                def f_x1(v, sl=sl, pb=pb):
                    last = None
                    for kc in range(NCH):
                        v.tensor_tensor(scr2[:], xs_t[kc][:, sl], P[pb][:], ALU.subtract)
                        v.tensor_tensor(scr2[:], scr2[:], P[pb + 1][:], ALU.mult)
                        last = v.tensor_tensor(xs_t[kc][:, sl],
                                               xrl2[:, kc * 512:(kc + 1) * 512],
                                               scr2[:], ALU.add)
                    return last
                prev_x1 = sd.add("vector", f_x1, deps=[op_x0l, op_x1bc],
                                 name=f"x1_{rg}_{tt}")
                x1_ops.append(prev_x1)

            def f_x1sp(sync, put, R0=R0):
                for ch in range(NCH):
                    put(sync.dma_start(out=x1sc[ch * 128:(ch + 1) * 128, R0:R0 + RG],
                                       in_=xs_t[ch]))
            op_x1sp = sd.dma("sync", f_x1sp, ndma=NCH, deps=x1_ops, name=f"x1sp{rg}",
                             stream="p2sp")

            op_st1 = emit_stats(sd, f"st1_{rg}", xs_t, rowf, consts, S0, S1, sq2,
                                x1_ops, NTT2, psS0b=P[2][0:1, :], psS1b=P[3][0:1, :])
            op_u2 = emit_apply(sd, f"u2_{rg}", xs_t, u2_t, rowf, consts, P[0], P[1],
                               scr2, [op_st1], NTT2, psMb=P[2], psRb=P[3])

            NST2 = RG // 128
            wT_ops = []
            prevr = {}
            for st in range(NST2):
                par = st % 2
                sl = slice(st * 128, (st + 1) * 128)
                lg_ps = P[2][:, par * 8:par * 8 + 8]
                lg = lgs[:, par * 8:par * 8 + 8]
                t0 = tp[:, par * 16:par * 16 + 16]
                w_sb = wsb[:, par * 8:par * 8 + 8]

                def f_lg(t, sl=sl, lg_ps=lg_ps):
                    last = None
                    for kc in range(NCH):
                        last = t.matmul(lg_ps, u2_t[kc][:, sl], rw_t[kc],
                                        start=(kc == 0), stop=(kc == NCH - 1))
                    return last
                op_lg = sd.add("tensor", f_lg,
                               deps=[op_u2, op_ldw, prevr.get(("va", par))],
                               name=f"rt{rg}_lg{st}")

                def f_va(v, lg=lg, t0=t0, lg_ps=lg_ps):
                    v.tensor_tensor(lg, lg_ps, rbias_s[:], ALU.add)
                    v.tensor_reduce(t0[:, 0:1], lg, axis=AX.X, op=ALU.max)
                    v.tensor_scalar(t0[:, 1:2], t0[:, 0:1], -1.0, None, ALU.mult)
                    v.tensor_scalar(t0[:, 8:16], lg, t0[:, 0:1], None, ALU.is_equal)
                    v.scalar_tensor_tensor(t0[:, 8:16], t0[:, 8:16], -1e30, lg,
                                           ALU.mult, ALU.add)
                    return v.tensor_reduce(t0[:, 2:3], t0[:, 8:16], axis=AX.X, op=ALU.max)
                op_va = sd.add("vector", f_va, deps=[op_lg, op_ldp2,
                                                     prevr.get(("sc", par)),
                                                     prevr.get(("tr", par))],
                               name=f"rt{rg}_va{st}")

                def f_sc(s, lg=lg, t0=t0):
                    return s.activation(t0[:, 8:16], lg, AF.Exp, bias=t0[:, 1:2])
                op_sc = sd.add("scalar", f_sc, deps=[op_va], name=f"rt{rg}_sc{st}")

                def f_vb2(v, lg=lg, t0=t0, w_sb=w_sb):
                    v.tensor_scalar(w_sb, lg, t0[:, 2:3], None, ALU.is_ge)
                    v.tensor_tensor(w_sb, w_sb, t0[:, 8:16], ALU.mult)
                    v.tensor_reduce(t0[:, 5:6], w_sb, axis=AX.X, op=ALU.add)
                    v.reciprocal(t0[:, 6:7], t0[:, 5:6])
                    return v.tensor_scalar(w_sb, w_sb, t0[:, 6:7], 2.0,
                                           ALU.mult, ALU.mult)
                op_vb = sd.add("vector", f_vb2, deps=[op_sc], name=f"rt{rg}_vb{st}")

                tr_ps = P[3][0:8, par * 128:par * 128 + 128]

                def f_tr(t, w_sb=w_sb, tr_ps=tr_ps):
                    return t.transpose(tr_ps, w_sb, idf32_s)
                op_tr = sd.add("tensor", f_tr, deps=[op_vb, op_ldp2,
                                                     prevr.get(("te", par))],
                               name=f"rt{rg}_tr{st}")

                def f_te(v, st=st, tr_ps=tr_ps):
                    return v.tensor_copy(wTf[:, st * 128:(st + 1) * 128], tr_ps)
                op_te = sd.add("vector", f_te, deps=[op_tr], name=f"rt{rg}_te{st}")
                wT_ops.append(op_te)
                prevr[("va", par)] = op_va
                prevr[("sc", par)] = op_sc
                prevr[("tr", par)] = op_tr
                prevr[("te", par)] = op_te

            sax_ops = []
            prev_ax = {}
            for tt in range(NTT2):
                par = tt % 2
                sl = slice(tt * 512, (tt + 1) * 512)
                ax_ps = (P[4] if par == 0 else P[5])

                def f_ax(t, sl=sl, ax_ps=ax_ps):
                    last = None
                    for kc in range(NCH):
                        last = t.matmul(ax_ps[:], As_t[kc], u2_t[kc][:, sl],
                                        start=(kc == 0), stop=(kc == NCH - 1))
                    return last
                op_ax = sd.add("tensor", f_ax,
                               deps=[op_u2, op_ldw, prev_ax.get(("sax", par))],
                               name=f"ax{rg}_{tt}")

                wa_ps = (P[2] if par == 0 else P[3])

                def f_wabc(t, sl=sl, wa_ps=wa_ps):
                    return t.matmul(wa_ps[:], G_s[:], wTf[:, sl], start=True, stop=True)
                op_wabc = sd.add("tensor", f_wabc,
                                 deps=wT_ops + [op_ldp2, prev_ax.get(("wae", par))],
                                 name=f"wabc{rg}_{tt}")

                def f_wae(s, wa_ps=wa_ps, par=par):
                    return s.copy(wa[:, par * 512:par * 512 + 512], wa_ps[:])
                op_wae = sd.add("scalar", f_wae, deps=[op_wabc], name=f"wae{rg}_{tt}")

                def f_sax(v, sl=sl, ax_ps=ax_ps, par=par):
                    return v.tensor_tensor(sax[:, sl], ax_ps[:],
                                           wa[:, par * 512:par * 512 + 512], ALU.mult)
                op_sax = sd.add("vector", f_sax, deps=[op_ax, op_wae], name=f"sax{rg}_{tt}")
                sax_ops.append(op_sax)
                prev_ax[("sax", par)] = op_sax
                prev_ax[("wae", par)] = op_wae

            res_ops = []
            prev_res = {}
            for oc in range(8):
                for tt in range(NTT2):
                    par = (oc * NTT2 + tt) % 2
                    sl = slice(tt * 512, (tt + 1) * 512)
                    r_ps = (P[0] if par == 0 else P[1])

                    def f_mm(t, oc=oc, sl=sl, r_ps=r_ps, tt=tt):
                        for kc in range(NCH):
                            t.matmul(r_ps[:], bw_s[oc][kc], u2_t[kc][:, sl],
                                     start=(kc == 0), stop=False)
                        return t.matmul(r_ps[:], Bs_t[oc], sax[:, sl],
                                        start=False, stop=True)
                    op_mm = sd.add("tensor", f_mm,
                                   deps=[op_u2, op_ldw, sax_ops[tt],
                                         prev_res.get(par)],
                                   name=f"res{rg}_{oc}_{tt}")

                    def f_evr(s, oc=oc, sl=sl, r_ps=r_ps):
                        return s.add(f0_t[oc][:, sl], r_ps[:], bias_s[:, oc:oc + 1])
                    op_evr = sd.add("scalar", f_evr, deps=[op_mm, op_ldp2, op_x1sp],
                                    name=f"rese{rg}_{oc}_{tt}")
                    res_ops.append(op_evr)
                    prev_res[par] = op_evr

            op_st2 = emit_stats(sd, f"st2_{rg}", f0_t, rowf, consts, S0, S1, sq2,
                                res_ops, NTT2, psS0b=P[2][0:1, :], psS1b=P[3][0:1, :])
            prev_ap2 = None
            ap2s = []
            rl_ops = {}
            for tt in range(NTT2):
                sl = slice(tt * 512, (tt + 1) * 512)
                for kc in range(NCH):
                    def f_rl(sync, put, kc=kc, sl=sl, R0=R0):
                        put(sync.dma_start(out=xrl2[:, kc * 512:kc * 512 + 512],
                                           in_=x1sc[kc * 128:(kc + 1) * 128,
                                                    R0 + sl.start:R0 + sl.stop]))
                    rl_ops[(tt, kc)] = sd.dma(
                        "sync", f_rl, ndma=1,
                        deps=[op_x1sp, op_st2] + ([prev_ap2] if prev_ap2 else []),
                        name=f"p2rl{rg}_{tt}_{kc}", stream="p2rl")

                pb = 2 * (tt % 2)

                def f_bc2(t, sl=sl, pb=pb):
                    t.matmul(P[pb][:], consts["ones1"], rowf(0)[:, sl], start=True, stop=True)
                    return t.matmul(P[pb + 1][:], consts["ones1"], rowf(1)[:, sl], start=True, stop=True)
                op_bc2 = sd.add("tensor", f_bc2,
                                deps=[op_st2, ap2s[tt - 2] if tt >= 2 else None],
                                name=f"x2bc{rg}_{tt}")

                def f_ap2(v, sl=sl, tt=tt, pb=pb):
                    last = None
                    for kc in range(NCH):
                        v.tensor_tensor(scr2[:], f0_t[kc][:, sl], P[pb][:], ALU.subtract)
                        v.tensor_tensor(scr2[:], scr2[:], P[pb + 1][:], ALU.mult)
                        last = v.tensor_tensor(f0_t[kc][:, sl],
                                               xrl2[:, kc * 512:kc * 512 + 512],
                                               scr2[:], ALU.add)
                    return last
                op_ap2v = sd.add("vector", f_ap2,
                                 deps=[op_bc2] + [rl_ops[(tt, kc)] for kc in range(NCH)],
                                 name=f"x2ap{rg}_{tt}")
                prev_ap2 = op_ap2v
                ap2s.append(op_ap2v)
            op_x2 = prev_ap2

            def f_x2sp(sync, put, R0=R0, rg=rg):
                if rg == 0:
                    for ch in range(NCH):
                        put(sync.dma_start(out=x2sc[ch * 128:(ch + 1) * 128, 0:512],
                                           in_=xs_t[ch][:, 1024:1536]))
                else:
                    for ch in range(NCH):
                        put(sync.dma_start(out=x2sc[ch * 128:(ch + 1) * 128, 512:2048],
                                           in_=xs_t[ch][:, 0:1536]))
            op_x2sp = sd.dma("sync", f_x2sp, ndma=NCH, deps=[op_x2], name=f"x2sp{rg}",
                             stream="p2sp")
            prev_sp2 = op_x2sp

            op_st3 = emit_stats(sd, f"st3_{rg}", xs_t, rowf, consts, S0, S1, sq2,
                                [op_x2], NTT2, psS0b=P[2][0:1, :], psS1b=P[3][0:1, :])
            op_u3 = emit_apply(sd, f"u3_{rg}", xs_t, u3f_t, rowf, consts, P[0], P[1],
                               scr2, [op_st3], NTT2, psMb=P[2], psRb=P[3])

            if rg == 0:
                def f_hm(v):
                    last = None
                    for ch in range(NCH):
                        last = v.tensor_scalar(u3f_t[ch][:, 0:1024],
                                               u3f_t[ch][:, 0:1024],
                                               hm2_s[:, 0:1], None, ALU.mult)
                    return last
                op_u3 = sd.add("vector", f_hm, deps=[op_u3, op_ldp2], name="hm0")

            tr_ops = []
            prev_tr = dict(tr_carry)
            for tc in range(NST2):
                for cc in range(NCH):
                    par = (tc * NCH + cc) % 2

                    def f_t1(t, tc=tc, cc=cc):
                        return t.transpose(PT, u3f_t[cc][:, tc * 128:(tc + 1) * 128],
                                           identb_s[:])
                    op_t1 = sd.add("tensor", f_t1,
                                   deps=[op_u3, prev_tr.get("ev")],
                                   name=f"u3t{rg}_{tc}_{cc}")

                    def f_t2(v, par=par):
                        return v.tensor_copy(trb[:, par * 128:(par + 1) * 128], PT)
                    op_t2 = sd.add("vector", f_t2, deps=[op_t1, prev_tr.get(("dma", par))],
                                   name=f"u3te{rg}_{tc}_{cc}", nodrain=True)
                    prev_tr["ev"] = op_t2

                    def f_t3(sync, put, tc=tc, cc=cc, par=par, R0=R0):
                        put(sync.dma_start(
                            out=u3d[R0 + tc * 128:R0 + (tc + 1) * 128,
                                    cc * 128:(cc + 1) * 128],
                            in_=trb[:, par * 128:(par + 1) * 128]))
                    op_t3 = sd.dma("sync", f_t3, ndma=1,
                                   deps=[op_t2],
                                   name=f"u3td{rg}_{tc}_{cc}", stream=f"u3d{par}")
                    prev_tr[("dma", par)] = op_t3
                    tr_ops.append(op_t3)
            prev_grp_exit = tr_ops[-1]
            tr_carry = dict(prev_tr)
            p2_exit.extend(tr_ops + [op_x2sp])

        op_u3done = tr_ops[-1]
        op_u3done2 = tr_ops[-2]
        op_cc2 = op_u3done

        ar.reset()
        u3b = ar.alloc(128, 24 * CHH, BF16)
        FWDs = ar.alloc(128, 8 * NBF, BF16)
        IDMs = ar.alloc(128, NFC * 1024, BF16)
        Kpk = ar.alloc(128, NFC * CHH, BF16)
        Fb = ar.alloc(128, 2 * NFC * CHH, BF16)
        Xs = ar.alloc(128, NFC * CHH, BF16)
        COEFs = ar.alloc(64, NBF, BF16)
        abs_ = ar.alloc(64, 2 * D, BF16)
        t14 = ar.alloc(128, 4 * CHH, F32)
        cdiv = ar.alloc(128, 4 * CHH, F32)
        ytile = ar.alloc(128, 2 * CHH, BF16)
        ytr = ar.alloc(128, 2 * 128, BF16)
        h0b = ar.alloc(128, CHH, F32)

        Fbt = [[Fb[:, (par2 * NFC + fc) * CHH:(par2 * NFC + fc + 1) * CHH]
                for fc in range(NFC)] for par2 in range(2)]
        Xt = [Xs[:, fc * CHH:(fc + 1) * CHH] for fc in range(NFC)]
        Kt = [Kpk[:, fc * CHH:(fc + 1) * CHH] for fc in range(NFC)]
        u3bt = lambda blk3, tc: u3b[:, (blk3 * 8 + tc) * CHH:(blk3 * 8 + tc + 1) * CHH]
        FWDt = [[FWDs[:, (tc * NFC + fc) * 128:(tc * NFC + fc + 1) * 128]
                 for fc in range(NFC)] for tc in range(8)]
        IDMt = [[IDMs[:, (fc * 8 + tauc) * 128:(fc * 8 + tauc + 1) * 128]
                 for tauc in range(8)] for fc in range(NFC)]

        def ld_c3(sync, put):
            put(sync.dma_start(out=COEFs, in_=COEFC[:]))
            put(sync.dma_start(
                out=FWDs.rearrange("p (tc f) -> p tc f", tc=8),
                in_=FWD.rearrange("(tc p) f -> p tc f", p=128)))
            put(sync.dma_start(
                out=IDMs.rearrange("p (fc t) -> p fc t", fc=NFC),
                in_=IDM.rearrange("(fc p) t -> p fc t", p=128)))
        op_ldc3 = sd.dma("sync", ld_c3, ndma=3, deps=[op_cc2, op_u3done2, op_x2sp], name="ldc3",
                         stream="p3ld")

        def ld_ab3(g, put):
            put(g.dma_start(out=abs_, in_=abT[:]))
        op_ldab = sd.dma("gpsimd", ld_ab3, ndma=1, deps=[op_cc2, op_u3done2], name="ldab",
                         stream="p3ab")

        out_ops3 = []
        prev_g = {}
        for grp in range(2):
            gof = grp * CHH

            def f_h0l(sync, put, gof=gof):
                put(sync.dma_start(out=h0b[0:1, :], in_=h0row[0:1, gof:gof + CHH]))
            op_h0l = sd.dma("sync", f_h0l, ndma=1,
                            deps=[prev_g.get("kf_done_last"), op_ldc3],
                            name=f"h0l{grp}", stream="p3ld")

            def f_h03(t, grp=grp):
                return t.matmul(P[4][:], ones65_s[0:1, :], h0b[0:1, :],
                                start=True, stop=True)
            op_h0m = sd.add("tensor", f_h03, deps=[op_h0l, prev_g.get("h0use")],
                            name=f"h0m{grp}")

            def f_h0e(v):
                return v.tensor_copy(h0b, P[4][:])
            op_h0e = sd.add("vector", f_h0e, deps=[op_h0m], name=f"h0e{grp}")

            kf_ops = []
            prev_kf = prev_g.get("kf_done_last")
            for c in range(9):
                def f_cmm(t, c=c, gof=gof):
                    t.matmul(P[0][:], COEFs[:, c * 128:(c + 1) * 128],
                             abs_[:, gof:gof + CHH], start=True, stop=True)
                    t.matmul(P[1][:], COEFs[:, (9 + c) * 128:(10 + c) * 128],
                             abs_[:, gof:gof + CHH], start=True, stop=True)
                    t.matmul(P[2][:], COEFs[:, c * 128:(c + 1) * 128],
                             abs_[:, D + gof:D + gof + CHH], start=True, stop=True)
                    return t.matmul(P[3][:], COEFs[:, (9 + c) * 128:(10 + c) * 128],
                                    abs_[:, D + gof:D + gof + CHH], start=True, stop=True)
                op_cmm = sd.add("tensor", f_cmm, deps=[op_ldab, op_ldc3, prev_kf],
                                name=f"cmm{grp}_{c}")

                def f_cev(s, c=c):
                    s.add(cdiv[:, 0:CHH], P[0][:], cst_s[:, 1:2])
                    s.copy(cdiv[:, CHH:2 * CHH], P[1][:])
                    s.copy(cdiv[:, 2 * CHH:3 * CHH], P[2][:])
                    return s.copy(cdiv[:, 3 * CHH:4 * CHH], P[3][:])
                op_cev = sd.add("scalar", f_cev, deps=[op_cmm, op_ldc3],
                                name=f"cev{grp}_{c}")

                def f_cdv(v, c=c):
                    Ar = cdiv[:, 0:CHH]; Ai = cdiv[:, CHH:2 * CHH]
                    Br = cdiv[:, 2 * CHH:3 * CHH]; Bi = cdiv[:, 3 * CHH:4 * CHH]
                    ta = t14[:, 0:CHH]; tb = t14[:, CHH:2 * CHH]
                    tc_ = t14[:, 2 * CHH:3 * CHH]; td = t14[:, 3 * CHH:4 * CHH]
                    v.tensor_tensor(ta, Br, Ar, ALU.mult)
                    v.tensor_tensor(tb, Bi, Ai, ALU.mult)
                    v.tensor_tensor(tc_, Bi, Ar, ALU.mult)
                    v.tensor_tensor(td, Br, Ai, ALU.mult)
                    v.tensor_tensor(Br, Ar, Ar, ALU.mult)
                    v.tensor_tensor(Bi, Ai, Ai, ALU.mult)
                    v.tensor_tensor(Br, Br, Bi, ALU.add)
                    v.reciprocal(Br, Br)
                    v.tensor_tensor(ta, ta, tb, ALU.add)
                    v.tensor_tensor(ta, ta, Br, ALU.mult)
                    v.tensor_tensor(ta, ta, h0b, ALU.add)
                    v.tensor_copy(Kt[c], ta)
                    v.tensor_tensor(tc_, tc_, td, ALU.subtract)
                    v.tensor_tensor(tc_, tc_, Br, ALU.mult)
                    return v.tensor_copy(Kt[9 + c], tc_)
                op_cdv = sd.add("vector", f_cdv, deps=[op_cev, op_h0e],
                                name=f"cdv{grp}_{c}")
                prev_kf = op_cdv
                kf_ops.append(op_cdv)

            def f_kz2(v):
                v.tensor_copy(t14[0:1, 0:CHH], Kt[8][0:1, :])
                v.memset(Kt[8], 0.0)
                return v.tensor_copy(Kt[8][0:1, :], t14[0:1, 0:CHH])
            op_kz = sd.add("vector", f_kz2, deps=kf_ops, name=f"kz{grp}")

            def f_kz3(v):
                return v.memset(Kt[17], 0.0)
            op_kz3 = sd.add("vector", f_kz3, deps=kf_ops, name=f"kz3{grp}")
            kf_done = kf_ops + [op_kz, op_kz3]
            prev_g["kf_done_last"] = op_kz3
            prev_g["h0use"] = kf_ops[-1]

            ld_u3_ops = []
            for blk3 in range(3):
                for tc in range(8):
                    def f_lu(sync, put, blk3=blk3, tc=tc, gof=gof):
                        rt = blk3 * 8 + tc
                        put(sync.dma_start(
                            out=u3bt(blk3, tc),
                            in_=u3d[rt * 128:(rt + 1) * 128, gof:gof + CHH]))
                    op_lu = sd.dma("sync", f_lu, ndma=1,
                                   deps=[op_cc2, op_u3done2, prev_g.get("u3use")],
                                   name=f"p3lu{grp}_{blk3}_{tc}",
                                   stream=f"p3x{blk3}")
                    ld_u3_ops.append(op_lu)

            fb3 = []
            fwd_store = {}
            last_imm = None
            for blk3 in range(3):
                par2 = blk3 % 2
                evs = []
                for fc in range(NFC):
                    pp = fc % 2

                    def f_fmm(t, blk3=blk3, fc=fc, pp=pp):
                        last = None
                        for tc in range(8):
                            last = t.matmul(P[pp][:], FWDt[tc][fc], u3bt(blk3, tc),
                                            start=(tc == 0), stop=(tc == 7))
                        return last
                    op_fmm = sd.add("tensor", f_fmm,
                                    deps=ld_u3_ops[blk3 * 8:blk3 * 8 + 8] +
                                         [op_ldc3, fwd_store.get((fc, pp)),
                                          fwd_store.get(("last", pp)),
                                          prev_g.get("xuse"), op_kz3],
                                    name=f"fmm{grp}_{blk3}_{fc}")

                    def f_fev(s, fc=fc, pp=pp, par2=par2):
                        return s.copy(Fbt[par2][fc], P[pp][:])
                    op_fev = sd.add("scalar", f_fev,
                                    deps=[op_fmm, fwd_store.get(("cmb", par2))],
                                    name=f"fev{grp}_{blk3}_{fc}", nodrain=True)
                    fwd_store[(fc, pp)] = op_fev
                    fwd_store[("last", pp)] = op_fev
                    evs.append(op_fev)
                fb3.append(evs)

                if blk3 >= 1:
                    seg_i = blk3 - 1
                    cur = blk3 % 2
                    prv = (blk3 - 1) % 2

                    def f_cmb(v, cur=cur, prv=prv):
                        last = None
                        for fc in range(NFC):
                            last = v.scalar_tensor_tensor(
                                Xt[fc], Fbt[cur][fc], cst_s[:, 2:3], Fbt[prv][fc],
                                ALU.mult, ALU.add)
                        return last
                    op_cmb = sd.add("vector", f_cmb,
                                    deps=fb3[blk3] + fb3[blk3 - 1] +
                                         [prev_g.get("xuse")],
                                    name=f"cmb{grp}_{seg_i}")
                    fwd_store[("cmb", prv)] = op_cmb

                    def f_pw(v, seg_i=seg_i):
                        last = None
                        for c in range(9):
                            a2 = t14[:, 0:CHH]; b2 = t14[:, CHH:2 * CHH]
                            c2 = t14[:, 2 * CHH:3 * CHH]; d2 = t14[:, 3 * CHH:4 * CHH]
                            v.tensor_tensor(a2, Xt[c], Kt[c], ALU.mult)
                            v.tensor_tensor(b2, Xt[9 + c], Kt[9 + c], ALU.mult)
                            v.tensor_tensor(c2, Xt[c], Kt[9 + c], ALU.mult)
                            v.tensor_tensor(d2, Xt[9 + c], Kt[c], ALU.mult)
                            v.tensor_tensor(Xt[c], a2, b2, ALU.subtract)
                            last = v.tensor_tensor(Xt[9 + c], c2, d2, ALU.add)
                        return last
                    op_pw = sd.add("vector", f_pw, deps=[op_cmb] + kf_done,
                                   name=f"pw{grp}_{seg_i}")

                    for tauc in range(8):
                        pp3 = 2 + tauc % 2

                        def f_imm(t, tauc=tauc, pp3=pp3):
                            last = None
                            for fc in range(NFC):
                                last = t.matmul(P[pp3][:], IDMt[fc][tauc], Xt[fc],
                                                start=(fc == 0), stop=(fc == NFC - 1))
                            return last
                        op_imm = sd.add("tensor", f_imm,
                                        deps=[op_pw, op_ldc3,
                                              prev_g.get(("inv", pp3))],
                                        name=f"imm{grp}_{seg_i}_{tauc}")
                        last_imm = op_imm

                        def f_iev(s, tauc=tauc, pp3=pp3):
                            return s.copy(ytile[:, (tauc % 2) * CHH:(tauc % 2 + 1) * CHH],
                                          P[pp3][:])
                        dep_iev = [op_imm, prev_g.get(("yev", tauc % 2))]
                        op_iev = sd.add("scalar", f_iev, deps=dep_iev,
                                        name=f"iev{grp}_{seg_i}_{tauc}")
                        prev_g[("inv", pp3)] = op_iev

                        for sub in range(4):
                            tpar = sub % 2

                            def f_ytr(t, tauc=tauc, sub=sub):
                                return t.transpose(
                                    PT, ytile[:, (tauc % 2) * CHH + sub * 128:
                                              (tauc % 2) * CHH + (sub + 1) * 128],
                                    identb_s[:])
                            op_ytr = sd.add("tensor", f_ytr,
                                            deps=[op_iev, prev_g.get("ytrev")],
                                            name=f"ytr{grp}_{seg_i}_{tauc}_{sub}")

                            def f_yte(v, tpar=tpar):
                                return v.tensor_copy(ytr[:, tpar * 128:(tpar + 1) * 128], PT)
                            op_yte = sd.add("vector", f_yte,
                                            deps=[op_ytr, prev_g.get(("ytd", tpar))],
                                            name=f"yte{grp}_{seg_i}_{tauc}_{sub}")
                            prev_g["ytrev"] = op_yte

                            def f_ytd(sync, put, seg_i=seg_i, tauc=tauc, sub=sub,
                                      gof=gof, tpar=tpar):
                                put(sync.dma_start(
                                    out=ya[gof + sub * 128:gof + (sub + 1) * 128,
                                           seg_i * 1024 + tauc * 128:
                                           seg_i * 1024 + (tauc + 1) * 128],
                                    in_=ytr[:, tpar * 128:(tpar + 1) * 128]))
                            op_ytd = sd.dma("sync", f_ytd, ndma=1, deps=[op_yte],
                                            name=f"ytd{grp}_{seg_i}_{tauc}_{sub}",
                                            stream=f"p3y{tpar}")
                            prev_g[("ytd", tpar)] = op_ytd
                            out_ops3.append(op_ytd)
                        prev_g[("yev", tauc % 2)] = op_ytd
                    prev_g["xuse"] = last_imm
            prev_g["u3use"] = prev_g["xuse"]

        ar.reset()
        HTOK = RPC // 2
        NT2 = HTOK // 512
        arA = ar.alloc(128, NCH * RPC, F32)
        arB = ar.alloc(128, NCH * RPC, F32)
        slA = ar.alloc(128, 2 * 16 * 128, BF16)
        slH = ar.alloc(128, 2 * 32 * 128, BF16)
        u4h = slH[:, 0:NCH * 1024]
        sq4 = ar.alloc(128, 4 * 512, F32)
        rows4 = ar.alloc(1, 3 * RPC, F32)
        scr4 = ar.alloc(128, 512, F32)
        sscr = ar.alloc(128, 512, F32)
        xrl4 = ar.alloc(128, 512, F32)
        yfm = arB.bitcast(BF16)[:, 0:NCH * RPC]

        x2_t = [arA[:, ch * RPC:(ch + 1) * RPC] for ch in range(NCH)]
        y_t = [yfm[:, ch * RPC:(ch + 1) * RPC] for ch in range(NCH)]
        u4f_t = [yfm[:, ch * RPC:(ch + 1) * RPC] for ch in range(NCH)]
        u4_t = [u4h[:, ch * 1024:(ch + 1) * 1024] for ch in range(NCH)]
        hb = arA.bitcast(BF16)
        gb = arB.bitcast(BF16)
        f1v = arB

        def rowf4(i):
            return rows4[0:1, i * RPC:(i + 1) * RPC]

        def ld_x24(sync, put):
            for ch in range(NCH):
                put(sync.dma_start(out=x2_t[ch], in_=x2sc[ch * 128:(ch + 1) * 128, :]))
            for ch in range(NCH):
                put(sync.dma_start(out=y_t[ch], in_=ya[ch * 128:(ch + 1) * 128, :]))
            for t, d_ in ((colsb[:, 0:32], w1c), (colsb[:, 32:64], w2c),
                          (colsb[:, 64:96], h1c), (colsb[:, 96:128], h2c),
                          (colsb[:, 128:136], w3c)):
                put(sync.dma_start(out=t, in_=d_[:]))
        op_ld4 = sd.dma("sync", ld_x24, ndma=2 * NCH + 5, deps=out_ops3,
                        name="p4ld", stream="p4ld")

        op_sty = emit_stats(sd, "sty", y_t, rowf4, consts, S0, S1, sq4,
                            [op_ld4], NTT, onescol_key="onescolb")
        op_x3 = emit_apply(sd, "x3", y_t, x2_t, rowf4, consts, P[0], P[1], scr4,
                           [op_sty], NTT, residual_tiles=x2_t)

        op_st4 = emit_stats(sd, "st4", x2_t, rowf4, consts, S0, S1, sq4, [op_x3], NTT)
        op_u4 = emit_apply(sd, "u4", x2_t, u4f_t, rowf4, consts, P[0], P[1], scr4,
                           [op_st4], NTT)

        def sp4(sync, put):
            for ch in range(NCH):
                put(sync.dma_start(out=xsc[ch * 128:(ch + 1) * 128, :], in_=x2_t[ch]))
            for ch in range(NCH):
                put(sync.dma_start(out=u4sc[ch * 128:(ch + 1) * 128, :], in_=u4f_t[ch]))
        op_sp4 = sd.dma("sync", sp4, deps=[op_u4], ndma=2 * NCH, name="p4spill",
                        stream="p4sp")

        out_ops4 = []
        chain_prev = op_sp4
        prevA4 = {}
        for hf in range(2):
            hsl = slice(hf * HTOK, (hf + 1) * HTOK)

            def f_u4rl(sync, put, hsl=hsl):
                for ch in range(NCH):
                    put(sync.dma_start(out=u4_t[ch],
                                       in_=u4sc[ch * 128:(ch + 1) * 128, hsl]))
            op_u4rl = sd.dma("sync", f_u4rl, ndma=NCH, deps=[op_sp4, chain_prev],
                             name=f"u4rl{hf}", stream="p4sp")

            def hview(oc, t2):
                return hb[:, oc * HTOK + t2 * 512:oc * HTOK + t2 * 512 + 512]

            def gview(oc, t2):
                return gb[:, oc * HTOK + t2 * 512:oc * HTOK + t2 * 512 + 512]

            hA_ops = []
            for oc in range(32):
                par = oc % 2

                def f_sl4(g, put, oc=oc, par=par):
                    put(g.dma_start(
                        out=slA[:, par * 2048:(par + 1) * 2048]
                            .rearrange("p (kc o) -> p kc o", kc=16),
                        in_=w12_t[oc].rearrange("(kc p) o -> p kc o", p=128)))
                op_sl = sd.dma("gpsimd", f_sl4, ndma=1,
                               deps=[prevA4.get(("ev", par)), chain_prev],
                               name=f"hA_sl{hf}_{oc}", stream=f"slA{par}")

                for t2 in range(NT2):
                    pp = (2 * t2) % 4

                    def f_mm4(t, oc=oc, t2=t2, par=par, pp=pp):
                        base = par * 2048
                        sl2 = slice(t2 * 512, t2 * 512 + 512)
                        for kc in range(NCH):
                            t.matmul(P[pp][:], slA[:, base + kc * 128:base + kc * 128 + 128],
                                     u4_t[kc][:, sl2], start=(kc == 0), stop=(kc == NCH - 1))
                        last = None
                        for kc in range(NCH):
                            last = t.matmul(P[pp + 1][:],
                                            slA[:, base + (8 + kc) * 128:base + (8 + kc) * 128 + 128],
                                            u4_t[kc][:, sl2], start=(kc == 0), stop=(kc == NCH - 1))
                        return last
                    op_mm4 = sd.add("tensor", f_mm4,
                                    deps=[op_sl, op_u4rl, prevA4.get(("h", (oc - 1, t2)))],
                                    name=f"hA_mm{hf}_{oc}_{t2}")

                    def f_si4(s, oc=oc, pp=pp):
                        return s.activation(sscr[:], P[pp][:],
                                            AF.Silu, scale=colsb[:, 0:32][:, oc:oc + 1])
                    op_si = sd.add("scalar", f_si4,
                                   deps=[op_mm4, op_ld4, prevA4.get("lasth")],
                                   name=f"hA_si{hf}_{oc}_{t2}")

                    def f_h4(v, oc=oc, t2=t2, pp=pp):
                        return v.scalar_tensor_tensor(
                            hview(oc, t2), P[pp + 1][:], colsb[:, 32:64][:, oc:oc + 1],
                            sscr[:], ALU.mult, ALU.mult)
                    op_h = sd.add("vector", f_h4, deps=[op_mm4, op_si],
                                  name=f"hA_h{hf}_{oc}_{t2}")
                    prevA4[("h", (oc, t2))] = op_h
                    prevA4["lasth"] = op_h
                    hA_ops.append(op_h)
                prevA4[("ev", par)] = prevA4[("h", (oc, NT2 - 1))]

            def big_phase4(name, wt, col, src_view, dst_view, dst_is_f1, deps_all):
                ops = []
                prevB = {}
                noc = 8 if dst_is_f1 else 32
                for oc in range(noc):
                    par = oc % 2

                    def f_sl(g, put, oc=oc, par=par, wt=wt):
                        put(g.dma_start(
                            out=slH[:, par * 4096:(par + 1) * 4096]
                                .rearrange("p (kc o) -> p kc o", kc=32),
                            in_=wt[oc].rearrange("(kc p) o -> p kc o", p=128)))
                    op_sl = sd.dma("gpsimd", f_sl, ndma=1,
                                   deps=[prevB.get(("ev", par))] + deps_all,
                                   name=f"{name}_sl{oc}", stream=f"slH{par}")
                    for t2 in range(NT2):
                        pp = 4 if (oc * NT2 + t2) % 2 == 0 else 5

                        def f_mm(t, oc=oc, t2=t2, par=par, pp=pp):
                            last = None
                            for kc in range(32):
                                last = t.matmul(P[pp][:],
                                                slH[:, par * 4096 + kc * 128:par * 4096 + kc * 128 + 128],
                                                src_view(kc, t2), start=(kc == 0),
                                                stop=(kc == 31))
                            return last
                        op_mm = sd.add("tensor", f_mm,
                                       deps=[op_sl, prevB.get(("ev2", pp))] + deps_all,
                                       name=f"{name}_mm{oc}_{t2}")

                        if dst_is_f1:
                            def f_ev(s, oc=oc, t2=t2, pp=pp):
                                return s.mul(dst_view(oc, t2), P[pp][:],
                                             col[:, oc:oc + 1])
                        else:
                            def f_ev(s, oc=oc, t2=t2, pp=pp):
                                return s.activation(dst_view(oc, t2), P[pp][:],
                                                    AF.Gelu, scale=col[:, oc:oc + 1])
                        op_ev = sd.add("scalar", f_ev, deps=[op_mm, op_ld4],
                                       name=f"{name}_ev{oc}_{t2}")
                        prevB[("ev2", pp)] = op_ev
                        ops.append(op_ev)
                    prevB[("ev", par)] = ops[-1]
                return ops

            hB = big_phase4(f"hB{hf}", h1_t, colsb[:, 64:96], hview, gview, False,
                            hA_ops + ([chain_prev] if chain_prev is not None else []))
            hC = big_phase4(f"hC{hf}", h2_t, colsb[:, 96:128], gview, hview, False, hB)

            def f1view(oc, t2):
                return f1v[:, oc * HTOK + t2 * 512:oc * HTOK + t2 * 512 + 512]
            hD = big_phase4(f"hD{hf}", w3_t, colsb[:, 128:136], hview, f1view, True, hC)

            f1_t = [f1v[:, ch * HTOK:(ch + 1) * HTOK] for ch in range(NCH)]

            def rowh(i):
                return rows4[0:1, i * RPC:i * RPC + HTOK]
            op_stf = emit_stats(sd, f"stf{hf}", f1_t, rowh, consts, S0, S1, sq4,
                                hD, NT2)
            prev_apf = None
            for t2 in range(NT2):
                def f_bcf(t, t2=t2):
                    t.matmul(P[0][:], consts["ones1"], rowh(0)[:, t2 * 512:t2 * 512 + 512],
                             start=True, stop=True)
                    return t.matmul(P[1][:], consts["ones1"],
                                    rowh(1)[:, t2 * 512:t2 * 512 + 512],
                                    start=True, stop=True)
                op_bcf = sd.add("tensor", f_bcf,
                                deps=[op_stf] + ([prev_apf] if prev_apf else []),
                                name=f"x4bc{hf}_{t2}")
                for kc in range(NCH):
                    def f_rlf(sync, put, kc=kc, t2=t2, hf=hf):
                        put(sync.dma_start(
                            out=xrl4,
                            in_=xsc[kc * 128:(kc + 1) * 128,
                                    hf * HTOK + t2 * 512:hf * HTOK + t2 * 512 + 512]))
                    dep_rl = [op_sp4, prevA4.get("lastx4")]
                    if prev_apf is not None:
                        dep_rl.append(prev_apf)
                    op_rlf = sd.dma("sync", f_rlf, ndma=1,
                                    deps=dep_rl,
                                    name=f"x4rl{hf}_{t2}_{kc}", stream="p4rl")

                    def f_apf(v, kc=kc, t2=t2):
                        sl2 = slice(t2 * 512, t2 * 512 + 512)
                        v.tensor_tensor(scr4, f1_t[kc][:, sl2], P[0][:], ALU.subtract)
                        v.tensor_tensor(scr4, scr4, P[1][:], ALU.mult)
                        return v.tensor_tensor(f1_t[kc][:, sl2],
                                               xrl4,
                                               scr4, ALU.add)
                    op_apf = sd.add("vector", f_apf, deps=[op_bcf, op_rlf],
                                    name=f"x4ap{hf}_{t2}_{kc}")
                    prevA4["lastx4"] = op_apf

                    def f_outf(sync, put, kc=kc, t2=t2, hf=hf):
                        put(sync.dma_start(
                            out=x4f[kc * 128:(kc + 1) * 128,
                                    hf * HTOK + t2 * 512:hf * HTOK + t2 * 512 + 512],
                            in_=f1_t[kc][:, t2 * 512:t2 * 512 + 512]))
                    op_of = sd.dma("sync", f_outf, ndma=1, deps=[op_apf],
                                   name=f"x4o{hf}_{t2}_{kc}", stream="out")
                    out_ops4.append(op_of)
                prev_apf = prevA4["lastx4"]
            chain_prev = out_ops4[-1]

        def fin4(g):
            return g.engine_nop()
        sd.add("gpsimd", fin4, deps=[out_ops4[-1]], name="fin4")
        sd.emit(block)
    return nc


def fused_inmaps(inp):
    import ml_dtypes
    bf = ml_dtypes.bfloat16
    x0 = np.asarray(inp["L0_attn_x"], dtype=np.float32)
    wq = np.asarray(inp["L0_attn_w_q"])
    wkv = np.asarray(inp["L0_attn_w_kv"])
    wo = np.asarray(inp["L0_attn_w_out"])
    nkv = np.asarray(inp["L0_attn_null_kv"]).reshape(HEADS, NNK, 2, DH)
    slopes = alibi_slopes()
    qs = np.asarray(inp["L0_attn_q_scale"])
    ks = np.asarray(inp["L0_attn_k_scale"])

    M1 = np.tril(np.ones((128, 128), np.float64)).T
    BIGM = np.zeros((128, 1024), np.float64)
    for k in range(128):
        lim = min(1024, k + 512)
        BIGM[k, :lim] = -1e9
    M1b = M1.astype(bf)
    BIGMb = BIGM.astype(bf)
    Z128 = np.zeros((128, 1024), bf)
    cstm = np.zeros((128, 4), np.float32)
    cstm[:, 0] = 1e-5
    cstm[:, 1] = 1.0
    cstm[:, 2] = ((-1.0) ** np.arange(128)).astype(np.float32)

    nkT_ = np.zeros((128, 16), np.float32)
    nvT_ = np.zeros((128, 16), np.float32)
    for hpi in range(8):
        for hl in range(2):
            h = hpi * 2 + hl
            nkT_[hl * 64:(hl + 1) * 64, hpi * 2:hpi * 2 + 2] = nkv[h, :, 0, :].T
            nvT_[hl * 64:(hl + 1) * 64, hpi * 2:hpi * 2 + 2] = nkv[h, :, 1, :].T
    qsd_ = np.zeros((2, 128), np.float32)
    qsd_[0, 0:64] = np.asarray(qs, np.float64) * 8.0
    qsd_[1, 64:128] = np.asarray(qs, np.float64) * 8.0
    ksd_ = np.zeros((2, 128), np.float32)
    ksd_[0, 0:64] = ks
    ksd_[1, 64:128] = ks

    common2 = dict(
        bw_t=tile_lhsT(f32c(np.asarray(inp["L0_ffn_base_w"]).T)),
        bias_c=f32c(np.asarray(inp["L0_ffn_base_b"]).reshape(8, 128).T),
        A_t=f32c(np.asarray(inp["L0_ffn_lora_A"]).transpose(1, 0, 2).reshape(D, 128)),
        B_t=tile_lhsT(f32c(np.asarray(inp["L0_ffn_lora_B"]).reshape(128, D))),
        rwt=f32c(np.asarray(inp["L0_ffn_router_w"]).T),
        rbias=f32c(np.tile(np.asarray(inp["L0_ffn_router_b"])[None, :], (128, 1))),
        idf32=np.eye(128, dtype=np.float32),
    )
    G = np.zeros((8, 128), np.float32)
    for e in range(8):
        G[e, e * 16:(e + 1) * 16] = 1.0
    common2["Gm"] = G
    FWDm, IDMm, COEFm = dft_consts()
    conv_common = dict(abT=f32c(np.asarray(inp["L1_attn_ab"]).T),
                       h0row=f32c(np.asarray(inp["L1_attn_h_0"])[None, :]),
                       FWD=FWDm, IDM=IDMm, COEFC=COEFm)
    w12 = np.concatenate([np.asarray(inp["L1_ffn_w1_w"]).T,
                          np.asarray(inp["L1_ffn_w2_w"]).T], axis=0)

    def colize(s, n):
        return f32c(np.asarray(s).reshape(n, 128).T)
    common4 = dict(w12_t=tile_lhsT(f32c(w12)),
                   h1_t=tile_lhsT(f32c(np.asarray(inp["L1_ffn_h1_w"]).T)),
                   h2_t=tile_lhsT(f32c(np.asarray(inp["L1_ffn_h2_w"]).T)),
                   w3_t=tile_lhsT(f32c(np.asarray(inp["L1_ffn_w3_w"]).T)),
                   w1c=colize(inp["L1_ffn_w1_s"], 32), w2c=colize(inp["L1_ffn_w2_s"], 32),
                   h1c=colize(inp["L1_ffn_h1_s"], 32), h2c=colize(inp["L1_ffn_h2_s"], 32),
                   w3c=colize(inp["L1_ffn_w3_s"], 8))
    common = dict(
        wq_t=tile_lhsT(f32c(wq)),
        wk_t=tile_lhsT(f32c(wkv[:, :D])),
        wv_t=tile_lhsT(f32c(wkv[:, D:])),
        wo_t=tile_lhsT(f32c(wo)),
        gcol=f32c(np.asarray(inp["L0_attn_gamma"]).reshape(8, 128).T),
        nkT=nkT_, nvT=nvT_,
        qsrow=f32c(qs[None, :]), ksrow=f32c(ks[None, :]),
        qsd=qsd_, ksd=ksd_,
        M1c=M1b,
        onesbf=np.ones((1, 128), bf),
        ones1=np.ones((1, 128), np.float32),
        ones65d=np.ones((65, 128), np.float32),
        onescol=np.ones((128, 1), np.float32),
        onescolb=np.ones((128, 1), bf),
        identb=np.eye(128).astype(bf),
        cst=cstm,
    )
    common.update(common2)
    common.update(conv_common)
    common.update(common4)

    NQTL = 6
    maps = []
    for c in range(8):
        b, p = c // 2, c % 2
        r0 = p * RPC - 1024
        bcq_ = np.zeros((128, 16 * NQTL * NKC), np.float32)
        brow0_ = np.zeros((1, 16 * NQTL * 512), np.float32)
        pp_ = np.arange(128)
        for h in range(16):
            sl_ = float(slopes[h])
            for qt in range(NQTL):
                g0 = r0 + qt * 512
                dummy = g0 < 0
                cmax = -1 if dummy else 4 * (g0 // 512) + 3
                for kcc in range(NKC):
                    colv = np.full(128, -1e9, np.float32)
                    if kcc < 32 and kcc <= cmax:
                        colv = (sl_ * (kcc * 128 + pp_ + 2)).astype(np.float32)
                    elif kcc == 32:
                        colv[0] = 0.0
                        colv[1] = sl_ * 1.0
                    bcq_[:, (h * NQTL + qt) * NKC + kcc] = colv
                qrow = np.arange(g0, g0 + 512)
                brow0_[0, (h * NQTL + qt) * 512:(h * NQTL + qt + 1) * 512] = \
                    0.0 if dummy else -(sl_ * (qrow + 2) + 2.0)
        x0q_ = np.zeros((D, 3072), np.float32)
        lo = max(0, r0)
        x0q_[:, (lo - r0):3072] = x0[b].T[:, lo:r0 + 3072]
        if r0 < 0:
            x0q_[:, 0:-r0] = x0[b].T[:, 0:-r0]
        maps.append(dict(
            x0f=np.ascontiguousarray(x0[b].T),
            x0q=x0q_,
            bcq=bcq_, brow0=brow0_,
            M2Av=(BIGMb if p == 0 else Z128),
            M2Bv=(BIGMb if p == 1 else Z128),
            hmask=np.full((128, 1), 0.0 if p == 0 else 1.0, np.float32),
            **common,
        ))
    return maps


def ml_bf16():
    import ml_dtypes
    return ml_dtypes.bfloat16


def kernel(**inputs):
    nc = build_fused()
    maps = fused_inmaps(inputs)
    r = run_bass_kernel_spmd(nc, maps, core_ids=list(range(8)))
    out = np.stack([np.concatenate([np.asarray(r.results[2 * b]["x4f"]).T,
                                    np.asarray(r.results[2 * b + 1]["x4f"]).T], axis=0)
                    for b in range(B)])
    return np.ascontiguousarray(out.astype(np.float32))
